# version 02
# baseline (speedup 1.0000x reference)
import sys
sys.path.insert(0, "/opt/trn_rl_repo")
import math
from collections import defaultdict
import numpy as np
import concourse.bass as bass
import concourse.mybir as mybir

F32 = mybir.dt.float32
BF16 = mybir.dt.bfloat16
ALU = mybir.AluOpType
AF = mybir.ActivationFunctionType
AX = mybir.AxisListType

ENGINES = ("tensor", "vector", "scalar", "gpsimd", "sync")
ESEM = {e: "e_" + e for e in ENGINES}
DSEM = {e: "d_" + e for e in ("sync", "scalar", "gpsimd", "vector")}
NEG_BIG = -1.0e30


class _IntervalMap:
    __slots__ = ("segs",)

    def __init__(self):
        self.segs = []

    def read(self, lo, hi, reader_tok, conflict_readers=()):
        waits = []
        rsem, rval = reader_tok
        covered = []
        for s in self.segs:
            if s[0] < hi and lo < s[1]:
                if s[2] is not None:
                    waits.append(s[2])
                for csem in conflict_readers:
                    cv = s[3].get(csem)
                    if cv is not None:
                        waits.append((csem, cv))
                if s[3].get(rsem, -1) < rval:
                    s[3][rsem] = rval
                covered.append((max(s[0], lo), min(s[1], hi)))
        covered.sort()
        pos = lo
        gaps = []
        for c0, c1 in covered:
            if c0 > pos:
                gaps.append([pos, c0, None, {rsem: rval}])
            pos = max(pos, c1)
        if pos < hi:
            gaps.append([pos, hi, None, {rsem: rval}])
        self.segs.extend(gaps)
        return waits

    def write(self, lo, hi, writer_tok):
        waits = []
        new_segs = []
        for s in self.segs:
            if s[0] < hi and lo < s[1]:
                if s[2] is not None:
                    waits.append(s[2])
                for rsem, rval in s[3].items():
                    waits.append((rsem, rval))
                if s[0] < lo:
                    new_segs.append([s[0], lo, s[2], dict(s[3])])
                if hi < s[1]:
                    new_segs.append([hi, s[1], s[2], dict(s[3])])
            else:
                new_segs.append(s)
        new_segs.append([lo, hi, writer_tok, {}])
        self.segs = new_segs
        return waits


class Buf:
    __slots__ = ("h", "space", "base", "esize", "n", "quant")

    def __init__(self, h, space, base, esize, n, quant=1):
        self.h = h
        self.space = space
        self.base = base
        self.esize = esize
        self.n = n
        self.quant = quant

    def __getitem__(self, idx):
        return self.h[idx]

    @property
    def ap(self):
        return self.h[:, :]

    def r(self, lo=0, hi=None):
        if hi is None:
            hi = self.n
        b0 = self.base + lo * self.esize
        b1 = self.base + hi * self.esize
        if self.quant > 1:
            b0 = (b0 // self.quant) * self.quant
            b1 = ((b1 + self.quant - 1) // self.quant) * self.quant
        return (self.space, b0, b1)


def dreg(name, lo=0, hi=1 << 50):
    return ("D_" + name, lo, hi)


class EK:
    def __init__(self, nc):
        self.nc = nc
        self.sched = {e: [] for e in ENGINES}
        self.cnt = defaultdict(int)
        self.vc = {e: defaultdict(int) for e in ENGINES}
        self.maps = defaultdict(_IntervalMap)
        self.nwaits = 0
        self.nops = defaultdict(int)

    @staticmethod
    def _norm(r):
        if isinstance(r, Buf):
            return r.r()
        if isinstance(r, tuple):
            if isinstance(r[0], Buf):
                return r[0].r(r[1], r[2])
            return r
        raise TypeError(f"bad region {r!r}")

    _PSUM_READ_CONFLICTS = {"vector": ("e_scalar",), "scalar": ("e_vector",)}

    def _acquire(self, engine, reads, writes, tok):
        need = {}
        for r in reads or ():
            key, lo, hi = self._norm(r)
            cr = self._PSUM_READ_CONFLICTS.get(engine, ()) if key == "PS" else ()
            for sem, val in self.maps[key].read(lo, hi, tok, cr):
                if (sem, val) == tok:
                    continue
                if need.get(sem, 0) < val:
                    need[sem] = val
        for w in writes or ():
            key, lo, hi = self._norm(w)
            for sem, val in self.maps[key].write(lo, hi, tok):
                if (sem, val) == tok:
                    continue
                if need.get(sem, 0) < val:
                    need[sem] = val
        waits = []
        own = ESEM[engine]
        for sem, val in sorted(need.items()):
            if engine == "tensor" and sem == own:
                continue
            if sem.startswith("d_"):
                val = self.cnt[sem]
            if self.vc[engine][sem] >= val:
                continue
            self.vc[engine][sem] = val
            waits.append((sem, val))
        self.nwaits += len(waits)
        return waits

    def op(self, engine, fn, reads=(), writes=(), tag=None):
        sem = ESEM[engine]
        self.cnt[sem] += 1
        tokval = self.cnt[sem]
        waits = self._acquire(engine, reads, writes, (sem, tokval))
        self.sched[engine].append(("op", fn, waits, sem, 1))
        self.nops[engine] += 1
        return (sem, tokval)

    def group(self, engine, fns, reads=(), writes=(), tag=None):
        sem = ESEM[engine]
        self.cnt[sem] += 1
        tokval = self.cnt[sem]
        waits = self._acquire(engine, reads, writes, (sem, tokval))
        self.sched[engine].append(("group", fns, waits, sem, 1))
        self.nops[engine] += 1
        return (sem, tokval)

    def dma(self, engine, fn, reads=(), writes=(), slot=None, tag=None):
        assert slot is not None, "dma needs a slot key"
        sem = "d_" + engine + "_" + slot
        tokval = self.cnt[sem] + 16
        waits = self._acquire(engine, reads, writes, (sem, tokval))
        self.cnt[sem] = tokval
        self.sched[engine].append(("op", fn, waits, sem, 16))
        self.nops["dma_" + engine] += 1
        return (sem, tokval)

    def wait_manual(self, engine, tok):
        sem, val = tok
        if self.vc[engine][sem] >= val:
            return
        self.vc[engine][sem] = val
        self.sched[engine].append(("wait", None, [(sem, val)], None, 0))

    def finish(self, engine="sync"):
        for sem, val in sorted(self.cnt.items()):
            self.wait_manual(engine, (sem, val))

    def build_and_emit(self):
        nc = self.nc
        import contextlib
        stack = contextlib.ExitStack()
        sems = {}
        used = set(self.cnt.keys())
        for e in ENGINES:
            for step in self.sched[e]:
                for w in step[2]:
                    used.add(w[0])
        for name in sorted(used):
            sems[name] = stack.enter_context(nc.semaphore(name))
        with stack:
            with nc.Block() as block:
                for eng_name in ENGINES:
                    steps = self.sched[eng_name]
                    if not steps:
                        continue

                    def body(eng, steps=steps):
                        for kind, fn, waits, sem, amt in steps:
                            for wsem, wval in waits:
                                eng.wait_ge(sems[wsem], wval)
                            if kind == "op":
                                fn(eng).then_inc(sems[sem], amt)
                            elif kind == "group":
                                fn(eng)[-1].then_inc(sems[sem], amt)

                    getattr(block, eng_name)(body)
        return sems


class Mem:
    MAN_BASE = 20480
    SB_LIMIT = 229344

    def __init__(self, nc):
        self.nc = nc
        self.cursor = 0
        self.bufs = {}
        self.marks = []

    def sb(self, name, parts, n, dtype, align=64):
        esize = {BF16: 2, mybir.dt.float16: 2, mybir.dt.uint8: 1, mybir.dt.int8: 1}.get(dtype, 4)
        base = self.MAN_BASE + (self.cursor + align - 1) // align * align
        nbytes = n * esize
        assert base + nbytes <= self.SB_LIMIT, (
            f"SBUF overflow at {name}: base={base} need={nbytes}")
        h = self.nc.alloc_sbuf_tensor_at(name, [parts, n], dtype, offset=base)
        self.cursor = (base - self.MAN_BASE) + nbytes
        b = Buf(h, "SB", base, esize, n)
        self.bufs[name] = b
        return b

    def check(self):
        assert self.nc.sbuf_base <= self.MAN_BASE, (
            f"bass auto allocations ({self.nc.sbuf_base}) collided with manual region")

    def push(self):
        self.marks.append(self.cursor)

    def pop(self):
        self.cursor = self.marks.pop()

    def push(self):
        self.marks.append(self.cursor)

    def pop(self):
        self.cursor = self.marks.pop()


def alloc_psum(nc):
    pf_h = nc.alloc_psum_tensor("pf", [128, 3072], F32)
    pt_h = nc.alloc_psum_tensor("pt", [128, 2048], BF16)
    pf = Buf(pf_h, "PS", 0, 4, 3072, quant=2048)
    pt = Buf(pt_h, "PS", 3072 * 4, 2, 2048, quant=2048)
    return pf, pt


class Ctx:
    def __init__(self, nc, T):
        self.nc = nc
        self.T = T
        self.NT = T // 256
        self.ek = EK(nc)
        self.mem = Mem(nc)
        self.pf, self.pt = alloc_psum(nc)
        self.dram = {}
        self.c = {}

    def dt(self, name, shape, dtype, kind=None):
        if kind:
            h = self.nc.dram_tensor(name, shape, dtype, kind=kind)
        else:
            h = self.nc.dram_tensor(name, shape, dtype)
        ap = h.ap()
        self.dram[name] = ap
        return ap


def pfb(ctx, bank, lo=0, hi=512):
    return ctx.pf[:, bank * 512 + lo: bank * 512 + hi], (ctx.pf, bank * 512 + lo, bank * 512 + hi)


def ptb(ctx, bank, lo=0, hi=1024):
    return ctx.pt[:, bank * 1024 + lo: bank * 1024 + hi], (ctx.pt, bank * 1024 + lo, bank * 1024 + hi)


def emit_consts(ctx):
    ek, mem, nc = ctx.ek, ctx.mem, ctx.nc
    c = ctx.c
    c["ones_col"] = mem.sb("ones_col", 1, 128, BF16)
    c["ones128"] = mem.sb("ones128", 128, 1, BF16)
    c["ones_colf"] = mem.sb("ones_colf", 1, 128, F32)
    c["ones128f"] = mem.sb("ones128f", 128, 1, F32)
    c["cpool"] = mem.sb("cpool", 128, 8, F32)
    c["negb8"] = mem.sb("negb8", 128, 8, F32)
    c["ident"] = mem.sb("ident", 128, 128, BF16)
    c["identf"] = mem.sb("identf", 128, 128, F32)
    c["zeros256"] = mem.sb("zeros256", 128, 256, F32)
    c["maskA0"] = mem.sb("maskA0", 128, 256, F32)
    c["maskA1"] = mem.sb("maskA1", 128, 256, F32)
    c["iota_j"] = mem.sb("iota_j", 128, 128, F32)
    c["iota_p"] = mem.sb("iota_p", 128, 128, F32)
    c["strip"] = mem.sb("strip", 128, 1152, F32)

    ek.op("gpsimd", lambda g: g.memset(c["ones_col"].ap, 1.0), writes=[c["ones_col"]])
    ek.op("gpsimd", lambda g: g.memset(c["ones128"].ap, 1.0), writes=[c["ones128"]])
    ek.op("gpsimd", lambda g: g.memset(c["ones_colf"].ap, 1.0), writes=[c["ones_colf"]])
    ek.op("gpsimd", lambda g: g.memset(c["ones128f"].ap, 1.0), writes=[c["ones128f"]])
    for col, val in [(0, 1e-5), (1, 1.0), (2, 1e-3), (3, NEG_BIG), (4, -2.0794415416798357)]:
        ek.op("gpsimd", lambda g, col=col, val=val: g.memset(c["cpool"][:, col:col + 1], val),
              writes=[(c["cpool"], col, col + 1)])
    ek.op("gpsimd", lambda g: g.memset(c["negb8"].ap, NEG_BIG), writes=[c["negb8"]])
    ek.op("gpsimd", lambda g: g.memset(c["zeros256"].ap, 0.0), writes=[c["zeros256"]])
    ek.op("gpsimd", lambda g: g.iota(c["iota_j"].ap, [[1, 128]], channel_multiplier=0,
                                     allow_small_or_imprecise_dtypes=True), writes=[c["iota_j"]])
    ek.op("gpsimd", lambda g: g.iota(c["iota_p"].ap, [[0, 128]], channel_multiplier=1,
                                     allow_small_or_imprecise_dtypes=True), writes=[c["iota_p"]])
    ek.op("vector", lambda v: v.tensor_tensor(c["ident"].ap, c["iota_j"].ap, c["iota_p"].ap, ALU.is_equal),
          reads=[c["iota_j"], c["iota_p"]], writes=[c["ident"]])
    ek.op("vector", lambda v: v.tensor_tensor(c["identf"].ap, c["iota_j"].ap, c["iota_p"].ap, ALU.is_equal),
          reads=[c["iota_j"], c["iota_p"]], writes=[c["identf"]])
    ek.op("vector", lambda v: v.tensor_tensor(c["maskA0"][:, 0:128], c["iota_j"].ap, c["iota_p"].ap, ALU.is_ge),
          reads=[c["iota_j"], c["iota_p"]], writes=[(c["maskA0"], 0, 128)])
    ek.op("vector", lambda v: v.memset(c["maskA0"][:, 128:256], 1.0), writes=[(c["maskA0"], 128, 256)])
    ek.op("vector", lambda v: v.memset(c["maskA1"][:, 0:128], 0.0), writes=[(c["maskA1"], 0, 128)])
    ek.op("vector", lambda v: v.tensor_tensor(c["maskA1"][:, 128:256], c["iota_j"].ap, c["iota_p"].ap, ALU.is_ge),
          reads=[c["iota_j"], c["iota_p"]], writes=[(c["maskA1"], 128, 256)])
    c["iota_w"] = mem.sb("iota_w", 128, 1152, F32)
    ek.op("gpsimd", lambda g: g.iota(c["iota_w"].ap, [[1, 1152]], channel_multiplier=0,
                                     allow_small_or_imprecise_dtypes=True),
          writes=[c["iota_w"]])
    ek.op("vector", lambda v: v.tensor_scalar(c["strip"].ap, c["iota_w"].ap,
                                              c["iota_p"][:, 0:1], 512.0, ALU.subtract, ALU.is_gt),
          reads=[c["iota_w"], (c["iota_p"], 0, 1)], writes=[c["strip"]])
    ek.op("vector", lambda v: v.tensor_single_scalar(c["strip"].ap, c["strip"].ap, NEG_BIG, ALU.mult),
          reads=[c["strip"]], writes=[c["strip"]])


def load_weight_cast(ctx, dst_buf, src_ap, name, nrow_chunks, ncols, col0=0, tag=None):
    ek = ctx.ek
    src = src_ap.rearrange("(k p) n -> p k n", p=128)[:, :, col0:col0 + ncols]
    dst = dst_buf.h[:, 0:nrow_chunks * ncols].rearrange("p (k n) -> p k n", k=nrow_chunks)
    ek.dma("gpsimd", lambda g: g.dma_start(dst, src),
           reads=[dreg(name)], writes=[(dst_buf, 0, nrow_chunks * ncols)],
           slot=dst_buf.h.name, tag=tag)


def mm_chain(ctx, out_ap, out_reg, lhsT_list, rhs_list, reads, extra_first=None):
    ek = ctx.ek
    n = len(lhsT_list)

    def fns(t):
        insts = []
        for i in range(n):
            insts.append(t.matmul(out_ap, lhsT_list[i], rhs_list[i],
                                  start=(i == 0), stop=(i == n - 1)))
        return insts

    return ek.group("tensor", fns, reads=reads, writes=[out_reg])


def emit_ln_from_xtile(ctx, bufs, t0, W, phase_tag=""):
    ek, c, T = ctx.ek, ctx.c, ctx.T
    xt, xn = bufs["xtile"], bufs["xn"]
    tmp, st = bufs["stat_tmp"], bufs["stat_small"]
    nhalf = 4
    s1_ap, s1_reg = pfb(ctx, 4, 0, W)
    s2_ap, s2_reg = pfb(ctx, 5, 0, W)

    def s1_fns(t):
        return [t.matmul(s1_ap[0:1, :], c["ones128f"].ap, xt[:, cc * W:(cc + 1) * W],
                         start=(cc == 0), stop=(cc == 7)) for cc in range(8)]

    ek.group("tensor", s1_fns, reads=[(xt, 0, 8 * W), c["ones128f"]], writes=[s1_reg])
    for half in range(2):
        toks = []
        for i in range(nhalf):
            cc = half * nhalf + i
            ek.op("scalar", lambda s, cc=cc, i=i: s.activation(
                tmp[:, i * W:(i + 1) * W], xt[:, cc * W:(cc + 1) * W], AF.Square),
                reads=[(xt, cc * W, (cc + 1) * W)], writes=[(tmp, i * W, (i + 1) * W)])

        def s2_fns(t, half=half):
            return [t.matmul(s2_ap[0:1, :], c["ones128f"].ap, tmp[:, i * W:(i + 1) * W],
                             start=(half == 0 and i == 0), stop=(half == 1 and i == nhalf - 1))
                    for i in range(nhalf)]

        ek.group("tensor", s2_fns, reads=[(tmp, 0, nhalf * W), c["ones128f"]], writes=[s2_reg])
    m = st
    ek.op("vector", lambda v: v.tensor_single_scalar(m[0:1, 0:W], s1_ap[0:1, :], 1.0 / 1024.0, ALU.mult),
          reads=[s1_reg], writes=[(st, 0, W)])
    ek.op("vector", lambda v: v.tensor_single_scalar(m[0:1, 3 * W:4 * W], s2_ap[0:1, :], 1.0 / 1024.0, ALU.mult),
          reads=[s2_reg], writes=[(st, 3 * W, 4 * W)])
    ek.op("vector", lambda v: v.tensor_mul(m[0:1, W:2 * W], m[0:1, 0:W], m[0:1, 0:W]),
          reads=[(st, 0, W)], writes=[(st, W, 2 * W)])
    ek.op("vector", lambda v: v.tensor_sub(m[0:1, W:2 * W], m[0:1, 3 * W:4 * W], m[0:1, W:2 * W]),
          reads=[(st, W, 2 * W), (st, 3 * W, 4 * W)], writes=[(st, W, 2 * W)])
    ek.op("scalar", lambda s: s.activation(m[0:1, W:2 * W], m[0:1, W:2 * W], AF.Ln,
                                           bias=ctx.c["cpool"][0:1, 0:1], scale=1.0),
          reads=[(st, W, 2 * W), (ctx.c["cpool"], 0, 1)], writes=[(st, W, 2 * W)])
    ek.op("scalar", lambda s: s.activation(m[0:1, 2 * W:3 * W], m[0:1, W:2 * W], AF.Exp, bias=0.0, scale=-0.5),
          reads=[(st, W, 2 * W)], writes=[(st, 2 * W, 3 * W)])
    mb_ap, mb_reg = pfb(ctx, 4, 0, W)
    rb_ap, rb_reg = pfb(ctx, 5, 0, W)
    ek.group("tensor", lambda t: [t.matmul(mb_ap, c["ones_colf"].ap, m[0:1, 0:W], start=True, stop=True)],
             reads=[(st, 0, W), c["ones_colf"]], writes=[mb_reg])
    ek.group("tensor", lambda t: [t.matmul(rb_ap, c["ones_colf"].ap, m[0:1, 2 * W:3 * W], start=True, stop=True)],
             reads=[(st, 2 * W, 3 * W), c["ones_colf"]], writes=[rb_reg])
    for cc in range(8):
        dst_lo = cc * T + t0
        ek.op("vector", lambda v, cc=cc: v.tensor_sub(tmp[:, 0:W] if False else tmp[:, (cc % 4) * W:(cc % 4) * W + W],
                                                      xt[:, cc * W:(cc + 1) * W], mb_ap),
              reads=[(xt, cc * W, (cc + 1) * W), mb_reg],
              writes=[(tmp, (cc % 4) * W, (cc % 4) * W + W)])
        ek.op("vector", lambda v, cc=cc, dst_lo=dst_lo: v.tensor_mul(
            xn[:, dst_lo:dst_lo + W], tmp[:, (cc % 4) * W:(cc % 4) * W + W], rb_ap),
            reads=[(tmp, (cc % 4) * W, (cc % 4) * W + W), rb_reg],
            writes=[(xn, dst_lo, dst_lo + W)])


def emit_residual_tile(ctx, bufs, t0, W, y_src, x_prev_name, x_next_name, store=True):
    ek, T = ctx.ek, ctx.T
    xt, ysb = bufs["xtile"], bufs[y_src]
    xprev = ctx.dram[x_prev_name]
    src = xprev.rearrange("(c p) t -> p c t", p=128)[:, :, t0:t0 + W]
    ek.dma("sync", lambda sy: sy.dma_start(xt.h[:, 0:8 * W].rearrange("p (c w) -> p c w", c=8), src),
           reads=[dreg(x_prev_name, t0, t0 + W)], writes=[(xt, 0, 8 * W)], slot="xt")
    for cc in range(8):
        ek.op("vector", lambda v, cc=cc: v.tensor_add(
            xt[:, cc * W:(cc + 1) * W], ysb[:, cc * W:(cc + 1) * W], xt[:, cc * W:(cc + 1) * W]),
            reads=[(ysb, cc * W, (cc + 1) * W), (xt, cc * W, (cc + 1) * W)],
            writes=[(xt, cc * W, (cc + 1) * W)])
    if store:
        xnext = ctx.dram[x_next_name]
        dst = xnext.rearrange("(c p) t -> p c t", p=128)[:, :, t0:t0 + W]
        ek.dma("sync", lambda sy: sy.dma_start(dst, xt.h[:, 0:8 * W].rearrange("p (c w) -> p c w", c=8)),
               reads=[(xt, 0, 8 * W)], writes=[dreg(x_next_name, t0, t0 + W)], slot="xt")


def emit_prologue(ctx, bufs, x_in_name):
    T = ctx.T
    W = 256
    for t0 in range(0, T, W):
        xin = ctx.dram[x_in_name]
        src = xin.rearrange("(c p) t -> p c t", p=128)[:, :, t0:t0 + W]
        xt = bufs["xtile"]
        ctx.ek.dma("sync", lambda sy, src=src: sy.dma_start(
            xt.h[:, 0:8 * W].rearrange("p (c w) -> p c w", c=8), src),
            reads=[dreg(x_in_name, t0, t0 + W)], writes=[(xt, 0, 8 * W)], slot="xt")
        emit_ln_from_xtile(ctx, bufs, t0, W)


def alloc_shared_bufs(ctx):
    mem, T = ctx.mem, ctx.T
    b = {}
    b["xn"] = mem.sb("xn", 128, 8 * T, BF16)
    b["xtile"] = mem.sb("xtile", 128, 8 * 256, F32)
    b["stat_tmp"] = mem.sb("stat_tmp", 128, 4 * 256, F32)
    b["stat_small"] = mem.sb("stat_small", 1, 4 * 256, F32)
    return b


def alloc_phase_a_bufs(ctx, b):
    mem = ctx.mem
    b["y_sb"] = mem.sb("y_sb_a", 128, 8 * 256, F32)
    b["wq"] = mem.sb("wq", 128, 8 * 512, BF16)
    b["wk"] = mem.sb("wk", 128, 8 * 512, BF16)
    b["wv"] = mem.sb("wv", 128, 8 * 512, BF16)
    b["wg"] = mem.sb("wg", 128, 8 * 512, BF16)
    b["wgk1"] = mem.sb("wgk1", 128, 8 * 16, BF16)
    b["wgk2"] = mem.sb("wgk2", 64, 512, BF16)
    b["bgk2"] = mem.sb("bgk2", 128, 4, F32)
    b["wout"] = mem.sb("wout", 128, 4 * 1024, BF16)
    b["rmsw"] = mem.sb("rmsw", 128, 2, F32)
    b["q_sb"] = mem.sb("q_sb", 128, 4 * 256, BF16)
    b["k_sb"] = mem.sb("k_sb", 128, 4 * 256, BF16)
    b["v_sb"] = mem.sb("v_sb", 128, 8 * 256, BF16)
    b["gate"] = mem.sb("gate", 128, 8 * 512, BF16)
    b["z1"] = mem.sb("z1", 64, 512, BF16)
    b["g_sb"] = mem.sb("g_sb", 128, 4 * 512, F32)
    b["b_sb"] = mem.sb("b_sb", 128, 4 * 256, F32)
    b["ga"] = mem.sb("ga", 128, 512, F32)
    b["gb"] = mem.sb("gb", 128, 512, F32)
    b["ebuf"] = mem.sb("ebuf", 128, 2 * 256, F32)
    b["ebc"] = mem.sb("ebc", 128, 4, F32)
    b["qt"] = mem.sb("qt", 128, 256, BF16)
    b["kh"] = mem.sb("kh", 128, 256, BF16)
    b["kb"] = mem.sb("kb", 128, 256, BF16)
    b["at_sb"] = mem.sb("at_sb", 128, 2 * 256, BF16)
    b["vt_sb"] = mem.sb("vt_sb", 128, 2 * 256, BF16)
    b["kbt_sb"] = mem.sb("kbt_sb", 128, 2 * 128, BF16)
    b["rb_sb"] = mem.sb("rb_sb", 128, 256, F32)
    b["og"] = mem.sb("og", 128, 8 * 256, BF16)
    b["S"] = mem.sb("S", 128, 4 * 256, F32)
    b["S_bf"] = mem.sb("S_bf", 128, 4 * 256, BF16)


def emit_phase_a_weights(ctx, b):
    ek = ctx.ek
    load_weight_cast(ctx, b["wq"], ctx.dram["L0_attn_Wq"], "L0_attn_Wq", 8, 512)
    load_weight_cast(ctx, b["wk"], ctx.dram["L0_attn_Wk"], "L0_attn_Wk", 8, 512)
    load_weight_cast(ctx, b["wgk1"], ctx.dram["L0_attn_Wgk1"], "L0_attn_Wgk1", 8, 16)
    ek.op("vector", lambda v: v.memset(b["wgk2"][0:64, :], 0.0), writes=[b["wgk2"]])
    ek.op("vector", lambda v: v.memset(b["z1"][0:64, :], 0.0), writes=[b["z1"]])
    ek.dma("gpsimd", lambda g: g.dma_start(b["wgk2"][0:16, :], ctx.dram["L0_attn_Wgk2"][:, :]),
           reads=[dreg("L0_attn_Wgk2")], writes=[(b["wgk2"], 0, 512)], slot="wgk2")
    ek.dma("gpsimd", lambda g: g.dma_start(
        b["bgk2"].ap, ctx.dram["L0_attn_bgk2"].rearrange("(fc p) -> p fc", p=128),
        allow_slow_non_contiguous=True),
        reads=[dreg("L0_attn_bgk2")], writes=[b["bgk2"]], slot="bgk2")
    ek.dma("gpsimd", lambda g: g.dma_start(
        b["rmsw"].ap, ctx.dram["L0_attn_rms_w"].rearrange("(vh p) -> p vh", p=128),
        allow_slow_non_contiguous=True),
        reads=[dreg("L0_attn_rms_w")], writes=[b["rmsw"]], slot="rmsw")


def _proj_group(ctx, out_ap, out_reg, wbuf, wcol_of_dc, xn, t0, W, nk=8):
    T = ctx.T
    lhsTs = [wbuf[:, wcol_of_dc(dc): wcol_of_dc(dc) + 128] for dc in range(nk)]
    rhss = [xn[:, dc * T + t0: dc * T + t0 + W] for dc in range(nk)]
    reads = [(wbuf, min(wcol_of_dc(dc) for dc in range(nk)),
              max(wcol_of_dc(dc) + 128 for dc in range(nk)))]
    reads += [(xn, dc * T + t0, dc * T + t0 + W) for dc in range(nk)]
    return mm_chain(ctx, out_ap, out_reg, lhsTs, rhss, reads)


def emit_phase_a(ctx, b, x_prev_name, x_next_name):
    ek, c, T, NT = ctx.ek, ctx.c, ctx.T, ctx.NT
    xn = b["xn"]
    emit_phase_a_weights(ctx, b)
    ek.op("vector", lambda v: v.memset(b["S"].ap, 0.0), writes=[b["S"]])
    ek.op("vector", lambda v: v.memset(b["S_bf"].ap, 0.0), writes=[b["S_bf"]])

    for cc in range(NT):
        t0 = cc * 256
        if cc % 2 == 0:
            _emit_a_gate_g(ctx, b, t0, min(512, T - t0))
        for fc in range(4):
            q_ap, q_reg = pfb(ctx, fc // 2, (fc % 2) * 256, (fc % 2) * 256 + 256)
            _proj_group(ctx, q_ap, q_reg, b["wq"], lambda dc, fc=fc: dc * 512 + fc * 128, xn, t0, 256)
            ek.op("scalar", lambda s, fc=fc, q_ap=q_ap: s.copy(b["q_sb"][:, fc * 256:(fc + 1) * 256], q_ap),
                  reads=[q_reg], writes=[(b["q_sb"], fc * 256, (fc + 1) * 256)])
        for fc in range(4):
            k_ap, k_reg = pfb(ctx, 2 + fc // 2, (fc % 2) * 256, (fc % 2) * 256 + 256)
            _proj_group(ctx, k_ap, k_reg, b["wk"], lambda dc, fc=fc: dc * 512 + fc * 128, xn, t0, 256)
            ek.op("scalar", lambda s, fc=fc, k_ap=k_ap: s.copy(b["k_sb"][:, fc * 256:(fc + 1) * 256], k_ap),
                  reads=[k_reg], writes=[(b["k_sb"], fc * 256, (fc + 1) * 256)])
        for vh2 in range(2):
            load_weight_cast(ctx, b["wv"], ctx.dram["L0_attn_Wv"], "L0_attn_Wv", 8, 512,
                             col0=vh2 * 512)
            for vc4 in range(4):
                vc = vh2 * 4 + vc4
                v_ap, v_reg = pfb(ctx, vc4 // 2, (vc4 % 2) * 256, (vc4 % 2) * 256 + 256)
                _proj_group(ctx, v_ap, v_reg, b["wv"], lambda dc, vc4=vc4: dc * 512 + vc4 * 128, xn, t0, 256)
                ek.op("scalar", lambda s, vc=vc, v_ap=v_ap: s.copy(b["v_sb"][:, vc * 256:(vc + 1) * 256], v_ap),
                      reads=[v_reg], writes=[(b["v_sb"], vc * 256, (vc + 1) * 256)])
        for hc in range(4):
            _emit_a_head(ctx, b, cc, hc)
        for vh2 in range(2):
            src_ = ctx.dram["L0_attn_Wout"].rearrange("(k p) n -> p k n", p=128)[:, vh2 * 4:(vh2 + 1) * 4, :]
            dst_ = b["wout"].h[:, 0:4 * 1024].rearrange("p (k n) -> p k n", k=4)
            ek.dma("gpsimd", lambda g, src_=src_, dst_=dst_: g.dma_start(dst_, src_),
                   reads=[dreg("L0_attn_Wout")], writes=[(b["wout"], 0, 4 * 1024)], slot="wout")
            for dc in range(8):
                y_ap, y_reg = pfb(ctx, dc // 2, (dc % 2) * 256, (dc % 2) * 256 + 256)
                lhsTs = [b["wout"][:, vc4 * 1024 + dc * 128: vc4 * 1024 + dc * 128 + 128] for vc4 in range(4)]
                rhss = [b["og"][:, (vh2 * 4 + vc4) * 256:(vh2 * 4 + vc4 + 1) * 256] for vc4 in range(4)]
                mm_chain(ctx, y_ap, y_reg, lhsTs, rhss,
                         reads=[b["wout"], b["og"]])
                if vh2 == 0:
                    ek.op("scalar", lambda s, dc=dc, y_ap=y_ap: s.copy(
                        b["y_sb"][:, dc * 256:(dc + 1) * 256], y_ap),
                        reads=[y_reg], writes=[(b["y_sb"], dc * 256, (dc + 1) * 256)])
                else:
                    ek.op("vector", lambda v, dc=dc, y_ap=y_ap: v.tensor_add(
                        b["y_sb"][:, dc * 256:(dc + 1) * 256], b["y_sb"][:, dc * 256:(dc + 1) * 256], y_ap),
                        reads=[y_reg, (b["y_sb"], dc * 256, (dc + 1) * 256)],
                        writes=[(b["y_sb"], dc * 256, (dc + 1) * 256)])
        emit_residual_tile(ctx, b, t0, 256, "y_sb", x_prev_name, x_next_name)
        emit_ln_from_xtile(ctx, b, t0, 256)


def _emit_a_gate_g(ctx, b, t0, W):
    ek, c, T = ctx.ek, ctx.c, ctx.T
    xn = b["xn"]
    for fh in range(2):
        load_weight_cast(ctx, b["wg"], ctx.dram["L0_attn_Wg"], "L0_attn_Wg", 8, 512,
                         col0=fh * 512)
        for fc4 in range(4):
            fc = fh * 4 + fc4
            g_ap, g_reg = pfb(ctx, fc4, 0, W)
            _proj_group(ctx, g_ap, g_reg, b["wg"], lambda dc, fc4=fc4: dc * 512 + fc4 * 128, xn, t0, W)
            ek.op("scalar", lambda s, fc=fc, g_ap=g_ap: s.activation(
                b["gate"][:, fc * W:(fc + 1) * W], g_ap, AF.Silu),
                reads=[g_reg], writes=[(b["gate"], fc * W, (fc + 1) * W)])
    z1_ap, z1_reg = pfb(ctx, 4, 0, W)
    lhsTs = [b["wgk1"][:, dc * 16:(dc + 1) * 16] for dc in range(8)]
    rhss = [xn[:, dc * T + t0: dc * T + t0 + W] for dc in range(8)]
    mm_chain(ctx, z1_ap[0:16, :], z1_reg, lhsTs, rhss,
             reads=[b["wgk1"]] + [(xn, dc * T + t0, dc * T + t0 + W) for dc in range(8)])
    ek.op("scalar", lambda s: s.copy(b["z1"][0:16, 0:W], z1_ap[0:16, :]),
          reads=[z1_reg], writes=[b["z1"]])
    for fc in range(4):
        z_ap, z_reg = pfb(ctx, 5, 0, W)
        ek.group("tensor", lambda t, fc=fc, z_ap=z_ap: [
            t.matmul(z_ap, b["wgk2"][0:64, fc * 128:(fc + 1) * 128], b["z1"][0:64, 0:W],
                     start=True, stop=True)],
            reads=[b["wgk2"], b["z1"]], writes=[z_reg])
        bias = b["bgk2"][:, fc:fc + 1]
        ek.op("vector", lambda v, z_ap=z_ap, bias=bias: v.tensor_scalar(
            b["ga"][:, 0:W], z_ap, bias, 0.0, ALU.add, ALU.min),
            reads=[z_reg, b["bgk2"]], writes=[(b["ga"], 0, W)])
        ek.op("scalar", lambda s, z_ap=z_ap, bias=bias: s.activation(
            b["gb"][:, 0:W], z_ap, AF.Abs, bias=bias, scale=1.0),
            reads=[z_reg, b["bgk2"]], writes=[(b["gb"], 0, W)])
        ek.op("scalar", lambda s: s.activation(b["gb"][:, 0:W], b["gb"][:, 0:W], AF.Exp, scale=-1.0),
              reads=[(b["gb"], 0, W)], writes=[(b["gb"], 0, W)])
        ek.op("scalar", lambda s: s.activation(b["gb"][:, 0:W], b["gb"][:, 0:W], AF.Ln,
                                               bias=ctx.c["cpool"][:, 1:2]),
              reads=[(b["gb"], 0, W), (ctx.c["cpool"], 1, 2)], writes=[(b["gb"], 0, W)])
        ek.op("vector", lambda v: v.tensor_sub(b["ga"][:, 0:W], b["ga"][:, 0:W], b["gb"][:, 0:W]),
              reads=[(b["ga"], 0, W), (b["gb"], 0, W)], writes=[(b["ga"], 0, W)])
        ek.op("vector", lambda v, fc=fc: v.tensor_scalar(
            b["g_sb"][:, fc * 512:fc * 512 + W], b["ga"][:, 0:W], 1.0 / 16.0, -3.0, ALU.mult, ALU.max),
            reads=[(b["ga"], 0, W)], writes=[(b["g_sb"], fc * 512, fc * 512 + W)])


def _emit_a_head(ctx, b, cc, hc):
    ek, c = ctx.ek, ctx.c
    coff = (cc % 2) * 256
    ek.op("vector", lambda v: v.tensor_tensor_scan(
        b["b_sb"][:, hc * 256:(hc + 1) * 256], b["g_sb"][:, hc * 512 + coff: hc * 512 + coff + 256],
        c["zeros256"].ap, 0.0, ALU.add, ALU.add),
        reads=[(b["g_sb"], hc * 512 + coff, hc * 512 + coff + 256), c["zeros256"]],
        writes=[(b["b_sb"], hc * 256, (hc + 1) * 256)])
    bs = b["b_sb"][:, hc * 256:(hc + 1) * 256]
    bs_reg = (b["b_sb"], hc * 256, (hc + 1) * 256)
    ek.op("scalar", lambda s: s.activation(b["ebuf"][:, 0:256], bs, AF.Exp),
          reads=[bs_reg], writes=[(b["ebuf"], 0, 256)])
    ek.op("scalar", lambda s: s.activation(b["ebuf"][:, 256:512], bs, AF.Exp, scale=-1.0),
          reads=[bs_reg], writes=[(b["ebuf"], 256, 512)])
    ek.op("scalar", lambda s: s.activation(
        b["ebc"][:, hc:hc + 1], b["b_sb"][:, hc * 256 + 255: hc * 256 + 256], AF.Exp),
        reads=[bs_reg], writes=[(b["ebc"], hc, hc + 1)])
    ek.op("vector", lambda v: v.tensor_mul(b["qt"].ap, b["q_sb"][:, hc * 256:(hc + 1) * 256], b["ebuf"][:, 0:256]),
          reads=[(b["q_sb"], hc * 256, (hc + 1) * 256), (b["ebuf"], 0, 256)], writes=[b["qt"]])
    ek.op("vector", lambda v: v.tensor_mul(b["kh"].ap, b["k_sb"][:, hc * 256:(hc + 1) * 256], b["ebuf"][:, 256:512]),
          reads=[(b["k_sb"], hc * 256, (hc + 1) * 256), (b["ebuf"], 256, 512)], writes=[b["kh"]])
    ek.op("vector", lambda v: v.tensor_scalar_mul(b["kb"].ap, b["kh"].ap, b["ebc"][:, hc:hc + 1]),
          reads=[b["kh"], (b["ebc"], hc, hc + 1)], writes=[b["kb"]])
    for sblk in range(2):
        at_ap, at_reg = pfb(ctx, 4, sblk * 256, (sblk + 1) * 256)
        ek.group("tensor", lambda t, sblk=sblk, at_ap=at_ap: [
            t.matmul(at_ap, b["kh"][:, sblk * 128:(sblk + 1) * 128], b["qt"].ap, start=True, stop=True)],
            reads=[b["kh"], b["qt"]], writes=[at_reg])
        mask = c["maskA0"] if sblk == 0 else c["maskA1"]
        ek.op("vector", lambda v, sblk=sblk, at_ap=at_ap, mask=mask: v.tensor_mul(
            b["at_sb"][:, sblk * 256:(sblk + 1) * 256], at_ap, mask.ap),
            reads=[at_reg, mask], writes=[(b["at_sb"], sblk * 256, (sblk + 1) * 256)])
    for sblk in range(2):
        for vh in range(2):
            pt_ap, pt_reg = ptb(ctx, 0, vh * 128, (vh + 1) * 128)
            ek.op("tensor", lambda t, sblk=sblk, vh=vh, pt_ap=pt_ap: t.transpose(
                pt_ap, b["v_sb"][:, (2 * hc + vh) * 256 + sblk * 128: (2 * hc + vh) * 256 + (sblk + 1) * 128],
                c["ident"].ap),
                reads=[(b["v_sb"], (2 * hc + vh) * 256 + sblk * 128, (2 * hc + vh) * 256 + (sblk + 1) * 128),
                       c["ident"]],
                writes=[pt_reg])
        ek.op("vector", lambda v, sblk=sblk: v.tensor_copy(
            b["vt_sb"][:, sblk * 256:(sblk + 1) * 256], ctx.pt[:, 0:256]),
            reads=[(ctx.pt, 0, 256)], writes=[(b["vt_sb"], sblk * 256, (sblk + 1) * 256)])
        pt2_ap, pt2_reg = ptb(ctx, 1, 0, 128)
        ek.op("tensor", lambda t, sblk=sblk, pt2_ap=pt2_ap: t.transpose(
            pt2_ap, b["kb"][:, sblk * 128:(sblk + 1) * 128], c["ident"].ap),
            reads=[(b["kb"], sblk * 128, (sblk + 1) * 128), c["ident"]], writes=[pt2_reg])
        ek.op("vector", lambda v, sblk=sblk: v.tensor_copy(
            b["kbt_sb"][:, sblk * 128:(sblk + 1) * 128], ctx.pt[:, 3072 - 3072 + 1024:1024 + 128] if False else pt2_ap),
            reads=[pt2_reg], writes=[(b["kbt_sb"], sblk * 128, (sblk + 1) * 128)])
    for vh in range(2):
        o_ap, o_reg = pfb(ctx, 0, vh * 256, (vh + 1) * 256)
        lhsTs = [b["S_bf"][:, hc * 256 + vh * 128: hc * 256 + (vh + 1) * 128],
                 b["vt_sb"][:, 0 * 256 + vh * 128: 0 * 256 + (vh + 1) * 128],
                 b["vt_sb"][:, 1 * 256 + vh * 128: 1 * 256 + (vh + 1) * 128]]
        rhss = [b["qt"].ap, b["at_sb"][:, 0:256], b["at_sb"][:, 256:512]]
        mm_chain(ctx, o_ap, o_reg, lhsTs, rhss,
                 reads=[(b["S_bf"], hc * 256, (hc + 1) * 256), b["vt_sb"], b["qt"], b["at_sb"]])
    ds_ap, ds_reg = pfb(ctx, 1, 0, 256)
    lhsTs = [b["kbt_sb"][:, 0:128], b["kbt_sb"][:, 128:256]]
    rhss = [b["vt_sb"][:, 0:256], b["vt_sb"][:, 256:512]]
    mm_chain(ctx, ds_ap, ds_reg, lhsTs, rhss, reads=[b["kbt_sb"], b["vt_sb"]])
    for vh in range(2):
        o_ap = ctx.pf[:, 0 * 512 + vh * 256: 0 * 512 + (vh + 1) * 256]
        o_reg = (ctx.pf, vh * 256, (vh + 1) * 256)
        ek.op("scalar", lambda s, vh=vh, o_ap=o_ap: s.activation(
            b["stat_tmp"][:, vh * 256:(vh + 1) * 256], o_ap, AF.Square),
            reads=[o_reg], writes=[(b["stat_tmp"], vh * 256, (vh + 1) * 256)])
    ss_ap, ss_reg = pfb(ctx, 5, 0, 256)
    mm_chain(ctx, ss_ap[0:1, :], ss_reg,
             [c["ones128f"].ap, c["ones128f"].ap],
             [b["stat_tmp"][:, 0:256], b["stat_tmp"][:, 256:512]],
             reads=[c["ones128f"], (b["stat_tmp"], 0, 512)])
    ek.op("scalar", lambda s: s.activation(b["stat_small"][0:1, 0:256], ss_ap[0:1, :], AF.Ln,
                                           bias=ctx.c["cpool"][0:1, 0:1], scale=1.0 / 256.0),
          reads=[ss_reg, (ctx.c["cpool"], 0, 1)], writes=[(b["stat_small"], 0, 256)])
    ek.op("scalar", lambda s: s.activation(b["stat_small"][0:1, 256:512], b["stat_small"][0:1, 0:256],
                                           AF.Exp, scale=-0.5),
          reads=[(b["stat_small"], 0, 256)], writes=[(b["stat_small"], 256, 512)])
    rb_ap, rb_reg = pfb(ctx, 5, 256, 512)
    ek.group("tensor", lambda t: [t.matmul(rb_ap, c["ones_colf"].ap, b["stat_small"][0:1, 256:512],
                                           start=True, stop=True)],
             reads=[c["ones_colf"], (b["stat_small"], 256, 512)], writes=[rb_reg])
    ek.op("vector", lambda v: v.tensor_copy(b["rb_sb"].ap, rb_ap), reads=[rb_reg], writes=[b["rb_sb"]])
    for vh in range(2):
        o_ap = ctx.pf[:, vh * 256: (vh + 1) * 256]
        o_reg = (ctx.pf, vh * 256, (vh + 1) * 256)
        ek.op("vector", lambda v, vh=vh, o_ap=o_ap: v.scalar_tensor_tensor(
            b["stat_tmp"][:, vh * 256:(vh + 1) * 256], o_ap, b["rmsw"][:, vh:vh + 1], b["rb_sb"].ap,
            ALU.mult, ALU.mult),
            reads=[o_reg, b["rmsw"], b["rb_sb"]], writes=[(b["stat_tmp"], vh * 256, (vh + 1) * 256)])
        gcol = (2 * hc + vh) * 512 + (cc % 2) * 256
        ek.op("vector", lambda v, vh=vh, gcol=gcol: v.tensor_mul(
            b["og"][:, (2 * hc + vh) * 256:(2 * hc + vh + 1) * 256],
            b["stat_tmp"][:, vh * 256:(vh + 1) * 256], b["gate"][:, gcol:gcol + 256]),
            reads=[(b["stat_tmp"], vh * 256, (vh + 1) * 256), (b["gate"], gcol, gcol + 256)],
            writes=[(b["og"], (2 * hc + vh) * 256, (2 * hc + vh + 1) * 256)])
    ek.op("vector", lambda v: v.tensor_scalar_mul(
        b["S"][:, hc * 256:(hc + 1) * 256], b["S"][:, hc * 256:(hc + 1) * 256], b["ebc"][:, hc:hc + 1]),
        reads=[(b["S"], hc * 256, (hc + 1) * 256), (b["ebc"], hc, hc + 1)],
        writes=[(b["S"], hc * 256, (hc + 1) * 256)])
    ek.op("vector", lambda v: v.tensor_add(
        b["S"][:, hc * 256:(hc + 1) * 256], b["S"][:, hc * 256:(hc + 1) * 256], ds_ap),
        reads=[(b["S"], hc * 256, (hc + 1) * 256), ds_reg],
        writes=[(b["S"], hc * 256, (hc + 1) * 256)])
    ek.op("vector", lambda v: v.tensor_copy(b["S_bf"][:, hc * 256:(hc + 1) * 256], b["S"][:, hc * 256:(hc + 1) * 256]),
          reads=[(b["S"], hc * 256, (hc + 1) * 256)], writes=[(b["S_bf"], hc * 256, (hc + 1) * 256)])


def alloc_routing_bufs(ctx, b, sfx, tstride=1024):
    mem = ctx.mem
    b["rwT"] = mem.sb("rwT" + sfx, 128, 8 * 8, BF16)
    b["rtmp"] = mem.sb("rtmp" + sfx, 8, 1024, BF16)
    b["bias_b"] = mem.sb("bias_b" + sfx, 128, 8, F32)
    b["bias_row"] = mem.sb("bias_row" + sfx, 1, 8, F32)
    b["l_sb"] = mem.sb("l_sb" + sfx, 128, 8, F32)
    b["l_ms"] = mem.sb("l_ms" + sfx, 128, 8, F32)
    b["msk"] = mem.sb("msk" + sfx, 128, 8, mybir.dt.uint8)
    b["m1"] = mem.sb("m1" + sfx, 128, 4, F32)
    b["el"] = mem.sb("el" + sfx, 128, 8, F32)
    b["cw16"] = mem.sb("cw16" + sfx, 128, 8, BF16)
    b["cwrow"] = mem.sb("cwrow" + sfx, 1, 8 * tstride, BF16)
    b["cw_stride"] = tstride
    b["cwb"] = mem.sb("cwb" + sfx, 128, 1024, BF16)


def alloc_phase_b_bufs(ctx, b, with_stash=False):
    mem = ctx.mem
    alloc_routing_bufs(ctx, b, "_b")
    b["y_sb"] = mem.sb("y_sb_b", 128, 8 * 256, F32)
    b["w1s"] = mem.sb("w1s", 128, 2 * 8 * 128, BF16)
    b["w3s"] = mem.sb("w3s", 128, 2 * 8 * 128, BF16)
    b["eo"] = mem.sb("eo", 128, 8 * 1024, BF16)
    b["w2s"] = mem.sb("w2s", 128, 8 * 1024, BF16)
    b["y_acc"] = mem.sb("y_acc", 128, 8 * 1024, F32)
    if with_stash:
        b["stash_stg"] = mem.sb("stash_stg", 128, 2 * 1024, BF16)


def make_stash_chunks(ctx, b):
    ek = ctx.ek
    stg = b["stash_stg"]
    chunks = []
    bi = 0
    jobs = []
    for nm, dn in (("L1_ffn_gate_w", "gwbf"), ("L1_ffn_up_w", "uwbf")):
        for e in range(8):
            for t in range(32):
                jobs.append((nm, dn, e, t, 0))
    for e in range(8):
        for t in range(8):
            for c4 in range(4):
                jobs.append(("L1_ffn_down_w", "dwbf", e, t, c4 * 1024))
    for (nm, dn, e, t, c0) in jobs:
        half = bi % 2
        bi += 1
        src = ctx.dram[nm][e].rearrange("(t p) n -> p t n", p=128)[:, t, c0:c0 + 1024]
        dst = ctx.dram[dn][e].rearrange("(t p) n -> p t n", p=128)[:, t, c0:c0 + 1024]
        stg_w = stg.h[:, half * 1024:(half + 1) * 1024]
        ncols = 1024 if dn != "dwbf" else 4096
        rlo = e * 4096 * 1024 + t * 128 * ncols

        def emit(src=src, dst=dst, stg_w=stg_w, nm=nm, dn=dn, rlo=rlo, half=half):
            ek.dma("gpsimd", lambda g: g.dma_start(stg_w, src),
                   reads=[dreg(nm, rlo, rlo + 128 * 1024)],
                   writes=[(stg, half * 1024, (half + 1) * 1024)], slot="st" + str(half))
            ek.dma("sync", lambda sy: sy.dma_start(dst, stg_w),
                   reads=[(stg, half * 1024, (half + 1) * 1024)],
                   writes=[dreg(dn, rlo, rlo + 128 * 1024)], slot="ss" + str(half))
        chunks.append(emit)
    return chunks


def make_repack_chunks(ctx, b):
    ek = ctx.ek
    stg = b["rpk_stg"]
    chunks = []
    bi = 0
    for e in range(8):
        for dn, dT in (("gwbf", "gwT"), ("uwbf", "uwT")):
            for icg in range(8):
                for hcp in range(4):
                    half = bi % 2
                    bi += 1

                    def emit(e=e, dn=dn, dT=dT, icg=icg, hcp=hcp, half=half):
                        for hci in range(2):
                            hc = hcp * 2 + hci
                            src = ctx.dram[dn][e, icg * 512:(icg + 1) * 512, hc * 128:(hc + 1) * 128]
                            dst = stg.h[:, half * 1024 + hci * 512: half * 1024 + (hci + 1) * 512]
                            ek.dma("sync", lambda sy, src=src, dst=dst: sy.dma_start(dst, src, transpose=True),
                                   reads=[dreg(dn, e * 4096 * 1024 + icg * 512 * 1024,
                                               e * 4096 * 1024 + (icg + 1) * 512 * 1024)],
                                   writes=[(stg, half * 1024 + hci * 512, half * 1024 + (hci + 1) * 512)],
                                   slot="rpk" + str(half))
                        dstT = ctx.dram[dT][e, icg, :, hcp * 1024:(hcp + 1) * 1024]
                        ek.dma("gpsimd", lambda g, dstT=dstT, half=half: g.dma_start(
                            dstT, stg.h[:, half * 1024:(half + 1) * 1024]),
                            reads=[(stg, half * 1024, (half + 1) * 1024)],
                            writes=[dreg(dT, (e * 8 + icg) * 128 * 4096 + hcp * 128 * 1024,
                                         (e * 8 + icg) * 128 * 4096 + (hcp + 1) * 128 * 1024)],
                            slot="rps" + str(half))
                    chunks.append(emit)
        for ihalf in range(2):
            for ic in range(16):
                half = bi % 2
                bi += 1

                def emit(e=e, ihalf=ihalf, ic=ic, half=half):
                    icg_ = ihalf * 16 + ic
                    src = ctx.dram["dwbf"][e, :, icg_ * 128:(icg_ + 1) * 128]
                    dst = stg.h[:, half * 1024:(half + 1) * 1024]
                    ek.dma("sync", lambda sy, src=src, dst=dst: sy.dma_start(dst, src, transpose=True),
                           reads=[dreg("dwbf", e * 1024 * 4096, (e + 1) * 1024 * 4096)],
                           writes=[(stg, half * 1024, (half + 1) * 1024)],
                           slot="rpk" + str(half))
                    dstT = ctx.dram["dwT"][e, ihalf, :, ic * 1024:(ic + 1) * 1024]
                    ek.dma("gpsimd", lambda g, dstT=dstT, half=half: g.dma_start(
                        dstT, stg.h[:, half * 1024:(half + 1) * 1024]),
                        reads=[(stg, half * 1024, (half + 1) * 1024)],
                        writes=[dreg("dwT", (e * 2 + ihalf) * 128 * 16384 + ic * 128 * 1024,
                                     (e * 2 + ihalf) * 128 * 16384 + (ic + 1) * 128 * 1024)],
                        slot="rps" + str(half))
                chunks.append(emit)
    return chunks


def emit_phase_b_routing(ctx, b, tt0, te):
    ek, c, T = ctx.ek, ctx.c, ctx.T
    xn = b["xn"]
    for tci in range(te // 128):
        tc0 = tt0 + tci * 128
        lg_ap, lg_reg = pfb(ctx, 4, 0, 128)
        lhsTs = [xn[:, dc * T + tc0: dc * T + tc0 + 128] for dc in range(8)]
        rhss = [b["rwT"][:, dc * 8:(dc + 1) * 8] for dc in range(8)]
        mm_chain(ctx, lg_ap[:, 0:8], lg_reg, lhsTs, rhss,
                 reads=[b["rwT"]] + [(xn, dc * T + tc0, dc * T + tc0 + 128) for dc in range(8)])
        ek.op("vector", lambda v, lg_ap=lg_ap: v.tensor_add(b["l_sb"].ap, lg_ap[:, 0:8], b["bias_b"].ap),
              reads=[lg_reg, b["bias_b"]], writes=[b["l_sb"]])
        ek.op("vector", lambda v: v.reduce_max(b["m1"][:, 0:1], b["l_sb"].ap, axis=AX.X),
              reads=[b["l_sb"]], writes=[(b["m1"], 0, 1)])
        ek.op("vector", lambda v: v.tensor_single_scalar(b["msk"].ap, b["l_sb"].ap, b["m1"][:, 0:1], ALU.is_lt),
              reads=[b["l_sb"], (b["m1"], 0, 1)], writes=[b["msk"]])
        ek.op("vector", lambda v: v.memset(b["l_ms"].ap, NEG_BIG), writes=[b["l_ms"]])
        ek.op("vector", lambda v: v.copy_predicated(b["l_ms"].ap, b["msk"].ap, b["l_sb"].ap),
              reads=[b["msk"], b["l_sb"], b["l_ms"]], writes=[b["l_ms"]])
        ek.op("vector", lambda v: v.reduce_max(b["m1"][:, 1:2], b["l_ms"].ap, axis=AX.X),
              reads=[b["l_ms"]], writes=[(b["m1"], 1, 2)])
        ek.op("vector", lambda v: v.tensor_single_scalar(b["m1"][:, 2:3], b["m1"][:, 0:1], -1.0, ALU.mult),
              reads=[(b["m1"], 0, 1)], writes=[(b["m1"], 2, 3)])
        ek.op("scalar", lambda s: s.activation(b["el"].ap, b["l_sb"].ap, AF.Exp, bias=b["m1"][:, 2:3]),
              reads=[b["l_sb"], (b["m1"], 2, 3)], writes=[b["el"]])
        ek.op("vector", lambda v: v.tensor_add(b["m1"][:, 3:4], b["m1"][:, 1:2], b["m1"][:, 2:3]),
              reads=[(b["m1"], 1, 3)], writes=[(b["m1"], 3, 4)])
        ek.op("scalar", lambda s: s.activation(b["m1"][:, 3:4], b["m1"][:, 3:4], AF.Exp),
              reads=[(b["m1"], 3, 4)], writes=[(b["m1"], 3, 4)])
        ek.op("vector", lambda v: v.tensor_single_scalar(b["m1"][:, 3:4], b["m1"][:, 3:4], 1.0, ALU.add),
              reads=[(b["m1"], 3, 4)], writes=[(b["m1"], 3, 4)])
        ek.op("vector", lambda v: v.reciprocal(b["m1"][:, 3:4], b["m1"][:, 3:4]),
              reads=[(b["m1"], 3, 4)], writes=[(b["m1"], 3, 4)])
        ek.op("vector", lambda v: v.tensor_single_scalar(b["l_ms"].ap, b["l_sb"].ap, b["m1"][:, 1:2], ALU.is_ge),
              reads=[b["l_sb"], (b["m1"], 1, 2)], writes=[b["l_ms"]])
        ek.op("vector", lambda v: v.tensor_mul(b["el"].ap, b["el"].ap, b["l_ms"].ap),
              reads=[b["el"], b["l_ms"]], writes=[b["el"]])
        ek.op("vector", lambda v: v.tensor_scalar_mul(b["cw16"].ap, b["el"].ap, b["m1"][:, 3:4]),
              reads=[b["el"], (b["m1"], 3, 4)], writes=[b["cw16"]])
        for e in range(8):
            pt_ap, pt_reg = ptb(ctx, 0, e * 128, (e + 1) * 128)
            ek.op("tensor", lambda t, e=e, pt_ap=pt_ap: t.transpose(
                pt_ap[0:1, :], b["cw16"][:, e:e + 1], c["ident"].ap),
                reads=[(b["cw16"], e, e + 1), c["ident"]], writes=[pt_reg])
        for e in range(8):
            pt_ap = ctx.pt[0:1, e * 128:(e + 1) * 128]
            ek.op("vector", lambda v, e=e, pt_ap=pt_ap, tci=tci: v.tensor_copy(
                b["cwrow"][0:1, e * 1024 + tci * 128: e * 1024 + (tci + 1) * 128], pt_ap),
                reads=[(ctx.pt, e * 128, (e + 1) * 128)],
                writes=[(b["cwrow"], e * 1024 + tci * 128, e * 1024 + (tci + 1) * 128)])


def emit_phase_b(ctx, b, x_prev_name, x_next_name, stash_chunks=None):
    ek, c, T = ctx.ek, ctx.c, ctx.T
    xn = b["xn"]
    stash_chunks = list(stash_chunks or [])
    n_iter = (T // 1024) * 9
    per_iter = -(-len(stash_chunks) // n_iter) if stash_chunks else 0
    si = 0
    ek.dma("gpsimd", lambda g: g.dma_start(b["rtmp"][0:8, :], ctx.dram["L0_ffn_router_w"][:, :]),
           reads=[dreg("L0_ffn_router_w")], writes=[b["rtmp"]], slot="rwT")
    for dc in range(8):
        pt_ap, pt_reg = ptb(ctx, 1, 0, 8)
        ek.op("tensor", lambda t, dc=dc, pt_ap=pt_ap: t.transpose(
            pt_ap, b["rtmp"][0:8, dc * 128:(dc + 1) * 128], c["ident"][0:8, 0:8]),
            reads=[b["rtmp"], c["ident"]], writes=[pt_reg])
        ek.op("vector", lambda v, dc=dc, pt_ap=pt_ap: v.tensor_copy(b["rwT"][:, dc * 8:(dc + 1) * 8], pt_ap),
              reads=[pt_reg], writes=[(b["rwT"], dc * 8, (dc + 1) * 8)])
    ek.dma("sync", lambda sy: sy.dma_start(b["bias_row"][0:1, :], ctx.dram["L0_ffn_experts_bias"][None, :]),
           reads=[dreg("L0_ffn_experts_bias")], writes=[b["bias_row"]], slot="br")
    bb_ap, bb_reg = pfb(ctx, 5, 0, 8)
    ek.group("tensor", lambda t: [t.matmul(bb_ap[:, 0:8], c["ones_colf"].ap, b["bias_row"][0:1, :],
                                           start=True, stop=True)],
             reads=[c["ones_colf"], b["bias_row"]], writes=[bb_reg])
    ek.op("vector", lambda v: v.tensor_copy(b["bias_b"].ap, bb_ap[:, 0:8]), reads=[bb_reg], writes=[b["bias_b"]])
    for tt0 in range(0, T, 1024):
        te = min(1024, T - tt0)
        emit_phase_b_routing(ctx, b, tt0, te)
        for e in range(9):
            for _ in range(per_iter):
                if si < len(stash_chunks):
                    stash_chunks[si]()
                    si += 1
            shared = (e == 8)
            w1_name = "L0_ffn_sw1" if shared else "L0_ffn_w1"
            w3_name = "L0_ffn_sw3" if shared else "L0_ffn_w3"
            w2_name = "L0_ffn_sw2" if shared else "L0_ffn_w2"
            ei = 0 if shared else e
            if not shared:
                for q in range(te // 512):
                    cb_ap, cb_reg = pfb(ctx, 5, 0, 512)
                    ek.group("tensor", lambda t, e=e, q=q, cb_ap=cb_ap: [
                        t.matmul(cb_ap, c["ones_col"].ap,
                                 b["cwrow"][0:1, e * 1024 + q * 512: e * 1024 + (q + 1) * 512],
                                 start=True, stop=True)],
                        reads=[c["ones_col"], (b["cwrow"], e * 1024 + q * 512, e * 1024 + (q + 1) * 512)],
                        writes=[cb_reg])
                    ek.op("vector", lambda v, q=q, cb_ap=cb_ap: v.tensor_copy(
                        b["cwb"][:, q * 512:(q + 1) * 512], cb_ap),
                        reads=[cb_reg], writes=[(b["cwb"], q * 512, (q + 1) * 512)])
            for hh in range(2):
                for hc8 in range(8):
                    hc = hh * 8 + hc8
                    half = hc8 % 2
                    for wbuf, wname in ((b["w1s"], w1_name), (b["w3s"], w3_name)):
                        src = ctx.dram[wname][ei].rearrange("(dc p) n -> p dc n", p=128)[:, :, hc * 128:(hc + 1) * 128]
                        dst = wbuf.h[:, half * 1024:(half + 1) * 1024].rearrange("p (dc n) -> p dc n", dc=8)
                        ek.dma("gpsimd", lambda g, src=src, dst=dst: g.dma_start(dst, src),
                               reads=[dreg(wname, (ei * 16 + hc) * 131072, (ei * 16 + hc + 1) * 131072)],
                               writes=[(wbuf, half * 1024, (half + 1) * 1024)],
                               slot=wbuf.h.name + str(half))
                    for ts_ in range(te // 512):
                        tq0 = tt0 + ts_ * 512
                        g_ap, g_reg = pfb(ctx, 4, 0, 512)
                        u_ap, u_reg = pfb(ctx, 5, 0, 512)
                        for (wbuf, o_ap, o_reg) in ((b["w1s"], g_ap, g_reg), (b["w3s"], u_ap, u_reg)):
                            lhsTs = [wbuf[:, half * 1024 + dc * 128: half * 1024 + (dc + 1) * 128] for dc in range(8)]
                            rhss = [xn[:, dc * T + tq0: dc * T + tq0 + 512] for dc in range(8)]
                            mm_chain(ctx, o_ap, o_reg, lhsTs, rhss,
                                     reads=[(wbuf, half * 1024, (half + 1) * 1024)] +
                                           [(xn, dc * T + tq0, dc * T + tq0 + 512) for dc in range(8)])
                        st = b["stat_tmp"]
                        ek.op("scalar", lambda s, g_ap=g_ap, st=st: s.activation(st[:, 0:512], g_ap, AF.Silu),
                              reads=[g_reg], writes=[(st, 0, 512)])
                        eo_lo = hc8 * 1024 + ts_ * 512
                        if shared:
                            ek.op("vector", lambda v, u_ap=u_ap, eo_lo=eo_lo, st=st: v.scalar_tensor_tensor(
                                b["eo"][:, eo_lo:eo_lo + 512], u_ap, 1.0, st[:, 0:512], ALU.mult, ALU.mult),
                                reads=[u_reg, (st, 0, 512)], writes=[(b["eo"], eo_lo, eo_lo + 512)])
                        else:
                            ek.op("vector", lambda v, u_ap=u_ap, st=st: v.scalar_tensor_tensor(
                                st[:, 512:1024], u_ap, 1.0, st[:, 0:512], ALU.mult, ALU.mult),
                                reads=[u_reg, (st, 0, 512)], writes=[(st, 512, 1024)])
                            ek.op("vector", lambda v, eo_lo=eo_lo, ts_=ts_, st=st: v.tensor_mul(
                                b["eo"][:, eo_lo:eo_lo + 512], st[:, 512:1024], b["cwb"][:, ts_ * 512:(ts_ + 1) * 512]),
                                reads=[(st, 512, 1024), (b["cwb"], ts_ * 512, (ts_ + 1) * 512)],
                                writes=[(b["eo"], eo_lo, eo_lo + 512)])
                for hc8 in range(8):
                    hc = hh * 8 + hc8
                    src = ctx.dram[w2_name][ei].rearrange("(hc p) n -> p hc n", p=128)[:, hc:hc + 1, :]
                    dst = b["w2s"].h[:, hc8 * 1024:(hc8 + 1) * 1024].rearrange("p (o n) -> p o n", o=1)
                    ek.dma("gpsimd", lambda g, src=src, dst=dst: g.dma_start(dst, src),
                           reads=[dreg(w2_name, (ei * 16 + hc) * 131072, (ei * 16 + hc + 1) * 131072)],
                           writes=[(b["w2s"], hc8 * 1024, (hc8 + 1) * 1024)],
                           slot="w2s" + str(hc8 % 2))
                for ts4 in range(te // 256):
                    for dc in range(8):
                        y_ap, y_reg = pfb(ctx, dc // 2, (dc % 2) * 256, (dc % 2) * 256 + 256)
                        lhsTs = [b["w2s"][:, hc8_ * 1024 + dc * 128: hc8_ * 1024 + (dc + 1) * 128] for hc8_ in range(8)]
                        rhss = [b["eo"][:, hc8_ * 1024 + ts4 * 256: hc8_ * 1024 + ts4 * 256 + 256] for hc8_ in range(8)]
                        mm_chain(ctx, y_ap, y_reg, lhsTs, rhss,
                                 reads=[b["w2s"], b["eo"]])
                        ya_lo = dc * 1024 + ts4 * 256
                        if e == 0 and hh == 0:
                            ek.op("vector", lambda v, y_ap=y_ap, ya_lo=ya_lo: v.tensor_copy(
                                b["y_acc"][:, ya_lo:ya_lo + 256], y_ap),
                                reads=[y_reg], writes=[(b["y_acc"], ya_lo, ya_lo + 256)])
                        else:
                            ek.op("vector", lambda v, y_ap=y_ap, ya_lo=ya_lo: v.tensor_add(
                                b["y_acc"][:, ya_lo:ya_lo + 256], b["y_acc"][:, ya_lo:ya_lo + 256], y_ap),
                                reads=[y_reg, (b["y_acc"], ya_lo, ya_lo + 256)],
                                writes=[(b["y_acc"], ya_lo, ya_lo + 256)])
        for cc4 in range(te // 256):
            t0 = tt0 + cc4 * 256
            emit_residual_tile_cols(ctx, b, t0, 256, "y_acc",
                                    lambda dc, cc4=cc4: dc * 1024 + cc4 * 256,
                                    x_prev_name, x_next_name)
            emit_ln_from_xtile(ctx, b, t0, 256)


def emit_residual_tile_cols(ctx, bufs, t0, W, y_src, ycol, x_prev_name, x_next_name, store=True):
    ek, T = ctx.ek, ctx.T
    xt, ysb = bufs["xtile"], bufs[y_src]
    xprev = ctx.dram[x_prev_name]
    src = xprev.rearrange("(c p) t -> p c t", p=128)[:, :, t0:t0 + W]
    ek.dma("sync", lambda sy: sy.dma_start(xt.h[:, 0:8 * W].rearrange("p (c w) -> p c w", c=8), src),
           reads=[dreg(x_prev_name, t0, t0 + W)], writes=[(xt, 0, 8 * W)], slot="xt")
    for cc in range(8):
        yl = ycol(cc)
        ek.op("vector", lambda v, cc=cc, yl=yl: v.tensor_add(
            xt[:, cc * W:(cc + 1) * W], ysb[:, yl:yl + W], xt[:, cc * W:(cc + 1) * W]),
            reads=[(ysb, yl, yl + W), (xt, cc * W, (cc + 1) * W)],
            writes=[(xt, cc * W, (cc + 1) * W)])
    if store:
        xnext = ctx.dram[x_next_name]
        dst = xnext.rearrange("(c p) t -> p c t", p=128)[:, :, t0:t0 + W]
        ek.dma("sync", lambda sy: sy.dma_start(dst, xt.h[:, 0:8 * W].rearrange("p (c w) -> p c w", c=8)),
               reads=[(xt, 0, 8 * W)], writes=[dreg(x_next_name, t0, t0 + W)], slot="xt")


def alloc_phase_c_bufs(ctx, b, with_repack=False):
    mem, T = ctx.mem, ctx.T
    b["hf"] = mem.sb("hf", 128, 8 * T, BF16)
    b["gma"] = mem.sb("gma", 128, 8, F32)
    b["bta"] = mem.sb("bta", 128, 8, F32)
    if with_repack:
        b["rpk_stg"] = mem.sb("rpk_stg", 128, 2 * 1024, BF16)


def _alloc_c_head_bufs(ctx, b):
    mem, T = ctx.mem, ctx.T
    b["wqh"] = mem.sb("wqh", 128, 512, BF16)
    b["wkh"] = mem.sb("wkh", 128, 512, BF16)
    b["wvh"] = mem.sb("wvh", 128, 512, BF16)
    b["q_f"] = mem.sb("q_f", 64, T, BF16)
    b["k_f"] = mem.sb("k_f", 64, T, BF16)
    b["v_tm"] = mem.sb("v_tm", 128, (T // 128) * 64, BF16)
    b["qsq"] = mem.sb("qsq", 128, T // 128, F32)
    b["negM"] = mem.sb("negM", 128, T // 128, F32)
    b["mxk"] = mem.sb("mxk", 128, 2, F32)
    b["P"] = mem.sb("P", 128, 4 * 512, BF16)
    b["PT"] = mem.sb("PT", 128, 8 * 512, BF16)
    b["rsum"] = mem.sb("rsum", 128, 4 * 16, F32)
    b["rtot"] = mem.sb("rtot", 128, 4, F32)
    b["rs1"] = mem.sb("rs1", 1, 512, F32)
    b["ou_sb"] = mem.sb("ou_sb", 64, 512, F32)


def _alloc_c_tail_bufs(ctx, b):
    mem = ctx.mem
    b["wo"] = mem.sb("wo", 128, 8 * 1024, BF16)
    b["yln"] = mem.sb("yln", 128, 8 * 256, F32)
    b["y_sb"] = mem.sb("y_sb_c", 128, 8 * 256, F32)


def emit_phase_c(ctx, b, x_prev_name, x_next_name, repack_chunks=None):
    ek, c, T = ctx.ek, ctx.c, ctx.T
    xn = b["xn"]
    repack_chunks = list(repack_chunks or [])
    per_head = -(-len(repack_chunks) // 16) if repack_chunks else 0
    ri = 0
    ek.dma("gpsimd", lambda g: g.dma_start(
        b["gma"].ap, ctx.dram["L1_attn_gamma"].rearrange("(dc p) -> p dc", p=128),
        allow_slow_non_contiguous=True),
        reads=[dreg("L1_attn_gamma")], writes=[b["gma"]], slot="gma")
    ek.dma("gpsimd", lambda g: g.dma_start(
        b["bta"].ap, ctx.dram["L1_attn_beta"].rearrange("(dc p) -> p dc", p=128),
        allow_slow_non_contiguous=True),
        reads=[dreg("L1_attn_beta")], writes=[b["bta"]], slot="bta")
    ctx.mem.push()
    _alloc_c_head_bufs(ctx, b)
    for h in range(16):
        for _ in range(per_head):
            if ri < len(repack_chunks):
                repack_chunks[ri]()
                ri += 1
        _emit_c_head(ctx, b, h)
    ctx.mem.pop()
    _alloc_c_tail_bufs(ctx, b)
    load_weight_cast(ctx, b["wo"], ctx.dram["L1_attn_Wo"], "L1_attn_Wo", 8, 1024)
    for ti in range(T // 256):
        t0 = ti * 256
        if ti == 0 and "c_hf0" in ctx.dram:
            ek.dma("sync", lambda sy: sy.dma_start(ctx.dram["c_hf0"][:, :], b["hf"].h[:, 0:8 * T]),
                   reads=[(b["hf"], 0, 8 * T)], writes=[dreg("c_hf0")], slot="dbg2")
            ek.dma("sync", lambda sy: sy.dma_start(ctx.dram["c_wo0"][:, :], b["wo"].h[:, :]),
                   reads=[b["wo"]], writes=[dreg("c_wo0")], slot="dbg2")
            ek.dma("sync", lambda sy: sy.dma_start(ctx.dram["c_xn0"][:, :], b["xn"].h[:, 0:8 * T]),
                   reads=[(b["xn"], 0, 8 * T)], writes=[dreg("c_xn0")], slot="dbg2")
        for dc in range(8):
            y_ap, y_reg = pfb(ctx, dc // 2, (dc % 2) * 256, (dc % 2) * 256 + 256)
            lhsTs = [b["wo"][:, hec * 1024 + dc * 128: hec * 1024 + (dc + 1) * 128] for hec in range(8)]
            rhss = [b["hf"][:, hec * T + t0: hec * T + t0 + 256] for hec in range(8)]
            mm_chain(ctx, y_ap, y_reg, lhsTs, rhss,
                     reads=[b["wo"], b["hf"]])
            ek.op("vector", lambda v, dc=dc, y_ap=y_ap, t0=t0: v.tensor_add(
                b["y_sb"][:, dc * 256:(dc + 1) * 256], y_ap, xn[:, dc * T + t0: dc * T + t0 + 256]),
                reads=[y_reg, (xn, dc * T + t0, dc * T + t0 + 256)],
                writes=[(b["y_sb"], dc * 256, (dc + 1) * 256)])
        if ti == 0 and "c_ysb0" in ctx.dram:
            ek.dma("sync", lambda sy: sy.dma_start(ctx.dram["c_ysb0"][:, :], b["y_sb"].h[:, 0:8 * 256]),
                   reads=[(b["y_sb"], 0, 8 * 256)], writes=[dreg("c_ysb0")], slot="dbg2")
        _emit_c_inner_ln(ctx, b, 256)
        if ti == 0 and "c_yln0" in ctx.dram:
            ek.dma("sync", lambda sy: sy.dma_start(ctx.dram["c_yln0"][:, :], b["yln"].h[:, 0:8 * 256]),
                   reads=[(b["yln"], 0, 8 * 256)], writes=[dreg("c_yln0")], slot="dbg2")
        emit_residual_tile(ctx, b, t0, 256, "yln", x_prev_name, x_next_name)
        emit_ln_from_xtile(ctx, b, t0, 256)


def _emit_c_inner_ln(ctx, b, W):
    ek, c = ctx.ek, ctx.c
    ysb, tmp, st = b["y_sb"], b["stat_tmp"], b["stat_small"]
    s1_ap, s1_reg = pfb(ctx, 4, 0, W)
    s2_ap, s2_reg = pfb(ctx, 5, 0, W)

    def s1_fns(t):
        return [t.matmul(s1_ap[0:1, :], c["ones128f"].ap, ysb[:, cc * W:(cc + 1) * W],
                         start=(cc == 0), stop=(cc == 7)) for cc in range(8)]

    ek.group("tensor", s1_fns, reads=[(ysb, 0, 8 * W), c["ones128f"]], writes=[s1_reg])
    for half in range(2):
        for i in range(4):
            cc = half * 4 + i
            ek.op("scalar", lambda s, cc=cc, i=i: s.activation(
                tmp[:, i * W:(i + 1) * W], ysb[:, cc * W:(cc + 1) * W], AF.Square),
                reads=[(ysb, cc * W, (cc + 1) * W)], writes=[(tmp, i * W, (i + 1) * W)])

        def s2_fns(t, half=half):
            return [t.matmul(s2_ap[0:1, :], c["ones128f"].ap, tmp[:, i * W:(i + 1) * W],
                             start=(half == 0 and i == 0), stop=(half == 1 and i == 3))
                    for i in range(4)]

        ek.group("tensor", s2_fns, reads=[(tmp, 0, 4 * W), c["ones128f"]], writes=[s2_reg])
    m = st
    ek.op("vector", lambda v: v.tensor_single_scalar(m[0:1, 0:W], s1_ap[0:1, :], 1.0 / 1024.0, ALU.mult),
          reads=[s1_reg], writes=[(st, 0, W)])
    ek.op("vector", lambda v: v.tensor_single_scalar(m[0:1, 3 * W:4 * W], s2_ap[0:1, :], 1.0 / 1024.0, ALU.mult),
          reads=[s2_reg], writes=[(st, 3 * W, 4 * W)])
    ek.op("vector", lambda v: v.tensor_mul(m[0:1, W:2 * W], m[0:1, 0:W], m[0:1, 0:W]),
          reads=[(st, 0, W)], writes=[(st, W, 2 * W)])
    ek.op("vector", lambda v: v.tensor_sub(m[0:1, W:2 * W], m[0:1, 3 * W:4 * W], m[0:1, W:2 * W]),
          reads=[(st, W, 2 * W), (st, 3 * W, 4 * W)], writes=[(st, W, 2 * W)])
    ek.op("scalar", lambda s: s.activation(m[0:1, W:2 * W], m[0:1, W:2 * W], AF.Ln,
                                           bias=c["cpool"][0:1, 2:3], scale=1.0),
          reads=[(st, W, 2 * W), (c["cpool"], 2, 3)], writes=[(st, W, 2 * W)])
    ek.op("scalar", lambda s: s.activation(m[0:1, 2 * W:3 * W], m[0:1, W:2 * W], AF.Exp, scale=-0.5),
          reads=[(st, W, 2 * W)], writes=[(st, 2 * W, 3 * W)])
    mb_ap, mb_reg = pfb(ctx, 4, 0, W)
    rb_ap, rb_reg = pfb(ctx, 5, 0, W)
    ek.group("tensor", lambda t: [t.matmul(mb_ap, c["ones_colf"].ap, m[0:1, 0:W], start=True, stop=True)],
             reads=[(st, 0, W), c["ones_colf"]], writes=[mb_reg])
    ek.group("tensor", lambda t: [t.matmul(rb_ap, c["ones_colf"].ap, m[0:1, 2 * W:3 * W], start=True, stop=True)],
             reads=[(st, 2 * W, 3 * W), c["ones_colf"]], writes=[rb_reg])
    for cc in range(8):
        sl = (cc % 4) * W
        ek.op("vector", lambda v, cc=cc, sl=sl: v.tensor_sub(
            tmp[:, sl:sl + W], ysb[:, cc * W:(cc + 1) * W], mb_ap),
            reads=[(ysb, cc * W, (cc + 1) * W), mb_reg], writes=[(tmp, sl, sl + W)])
        ek.op("vector", lambda v, cc=cc, sl=sl: v.scalar_tensor_tensor(
            tmp[:, sl:sl + W], tmp[:, sl:sl + W], b["gma"][:, cc:cc + 1], rb_ap, ALU.mult, ALU.mult),
            reads=[(tmp, sl, sl + W), b["gma"], rb_reg], writes=[(tmp, sl, sl + W)])
        ek.op("vector", lambda v, cc=cc, sl=sl: v.tensor_scalar_add(
            b["yln"][:, cc * W:(cc + 1) * W], tmp[:, sl:sl + W], b["bta"][:, cc:cc + 1]),
            reads=[(tmp, sl, sl + W), b["bta"]], writes=[(b["yln"], cc * W, (cc + 1) * W)])


def _emit_c_head(ctx, b, h):
    ek, c, T = ctx.ek, ctx.c, ctx.T
    xn = b["xn"]
    half = h % 2
    hc2 = h // 2
    prow = (h % 2) * 64
    for wbuf, wname in ((b["wqh"], "L1_attn_Wq"), (b["wkh"], "L1_attn_Wk"), (b["wvh"], "L1_attn_Wv")):
        src = ctx.dram[wname][h].rearrange("(dc p) e -> p dc e", p=128)
        dst = wbuf.h[:, 0:512].rearrange("p (dc e) -> p dc e", dc=8)
        ek.dma("gpsimd", lambda g, src=src, dst=dst: g.dma_start(dst, src),
               reads=[dreg(wname, h, h + 1)], writes=[(wbuf, 0, 512)],
               slot=wbuf.h.name)
    for (wbuf, dstbuf, bank) in ((b["wkh"], b["k_f"], 0), (b["wqh"], b["q_f"], 1)):
        for tt in range(T // 512):
            o_ap, o_reg = pfb(ctx, bank, 0, 512)
            lhsTs = [wbuf[:, dc * 64: (dc + 1) * 64] for dc in range(8)]
            rhss = [xn[:, dc * T + tt * 512: dc * T + (tt + 1) * 512] for dc in range(8)]
            mm_chain(ctx, o_ap[0:64, :], o_reg, lhsTs, rhss,
                     reads=[(wbuf, 0, 512)] +
                           [(xn, dc * T + tt * 512, dc * T + (tt + 1) * 512) for dc in range(8)])
            ek.op("scalar", lambda s, tt=tt, o_ap=o_ap, dstbuf=dstbuf: s.copy(
                dstbuf[0:64, tt * 512:(tt + 1) * 512], o_ap[0:64, :]),
                reads=[o_reg], writes=[(dstbuf, tt * 512, (tt + 1) * 512)])
    for (srcbuf, dstbuf, bank) in ((b["wvh"], b["v_tm"], 2), (None, b["P"], 3)):
        for tt in range(T // 512):
            if srcbuf is not None:
                o_ap, o_reg = pfb(ctx, bank, 0, 512)
                lhsTs = [srcbuf[:, dc * 64: (dc + 1) * 64] for dc in range(8)]
                rhss = [xn[:, dc * T + tt * 512: dc * T + (tt + 1) * 512] for dc in range(8)]
                mm_chain(ctx, o_ap[0:64, :], o_reg, lhsTs, rhss,
                         reads=[(srcbuf, 0, 512)] +
                               [(xn, dc * T + tt * 512, dc * T + (tt + 1) * 512) for dc in range(8)])
                ek.op("scalar", lambda s, o_ap=o_ap: s.copy(b["stat_tmp"].h[0:64, 0:512].bitcast(BF16)
                                                            if False else b["P"][0:64, 0:512], o_ap[0:64, :]),
                      reads=[o_reg], writes=[(b["P"], 0, 512)])
                fsrc, freg = b["P"], (b["P"], 0, 512)
            else:
                fsrc, freg = b["q_f"], (b["q_f"], tt * 512, (tt + 1) * 512)
            for j in range(4):
                pt_ap, pt_reg = ptb(ctx, 0, j * 128, j * 128 + 64)
                off = (0 if srcbuf is not None else tt * 512) + j * 128
                ek.op("tensor", lambda t, fsrc=fsrc, off=off, pt_ap=pt_ap: t.transpose(
                    pt_ap, fsrc[0:64, off:off + 128], c["ident"][0:64, 0:64]),
                    reads=[freg, c["ident"]], writes=[pt_reg])
            for j in range(4):
                tc = tt * 4 + j
                ek.op("vector", lambda v, j=j, tc=tc, dstbuf=dstbuf: v.tensor_copy(
                    dstbuf[:, tc * 64:(tc + 1) * 64], ctx.pt[:, j * 128: j * 128 + 64]),
                    reads=[(ctx.pt, j * 128, j * 128 + 64)], writes=[(dstbuf, tc * 64, (tc + 1) * 64)])
    ntc = T // 128
    sqw = min(1024, ntc * 64)
    for tc in range(ntc):
        g0 = (tc % 16) * 64
        if tc % 16 == 0:
            ek.op("scalar", lambda s, tc=tc: s.activation(
                b["stat_tmp"].h[:, 0:sqw], b["P"][:, tc * 64: tc * 64 + sqw], AF.Square),
                reads=[(b["P"], tc * 64, tc * 64 + sqw)], writes=[(b["stat_tmp"], 0, sqw)])
        ek.op("vector", lambda v, tc=tc, g0=g0: v.reduce_sum(
            b["qsq"][:, tc:tc + 1], b["stat_tmp"].h[:, g0:g0 + 64], axis=AX.X),
            reads=[(b["stat_tmp"], g0, g0 + 64)], writes=[(b["qsq"], tc, tc + 1)])
    ek.op("vector", lambda v: v.memset(b["mxk"][:, 0:1], 0.0), writes=[(b["mxk"], 0, 1)])
    for tt in range(T // 512):
        ek.op("scalar", lambda s, tt=tt: s.activation(
            b["stat_tmp"].h[0:64, 0:512], b["k_f"][0:64, tt * 512:(tt + 1) * 512], AF.Square),
            reads=[(b["k_f"], tt * 512, (tt + 1) * 512)], writes=[(b["stat_tmp"], 0, 512)])
        kn_ap, kn_reg = pfb(ctx, 4, 0, 512)
        ek.group("tensor", lambda t, kn_ap=kn_ap: [
            t.matmul(kn_ap[0:1, :], c["ones128f"][0:64, :], b["stat_tmp"].h[0:64, 0:512],
                     start=True, stop=True)],
            reads=[(b["stat_tmp"], 0, 512), c["ones128f"]], writes=[kn_reg])
        ek.op("vector", lambda v, kn_ap=kn_ap: v.reduce_max(
            b["stat_small"][0:1, 0:1], kn_ap[0:1, :], axis=AX.X),
            reads=[kn_reg], writes=[(b["stat_small"], 0, 1)])
        ek.op("vector", lambda v: v.tensor_max(b["mxk"][0:1, 0:1], b["mxk"][0:1, 0:1], b["stat_small"][0:1, 0:1]),
              reads=[(b["mxk"], 0, 1), (b["stat_small"], 0, 1)], writes=[(b["mxk"], 0, 1)])
    mk_ap, mk_reg = pfb(ctx, 4, 0, 128)
    ek.group("tensor", lambda t, mk_ap=mk_ap: [
        t.matmul(mk_ap[:, 0:1], c["ones_colf"].ap, b["mxk"][0:1, 0:1], start=True, stop=True)],
        reads=[c["ones_colf"], (b["mxk"], 0, 1)], writes=[mk_reg])
    ek.op("vector", lambda v, mk_ap=mk_ap: v.tensor_copy(b["mxk"][:, 1:2], mk_ap[:, 0:1]),
          reads=[mk_reg], writes=[(b["mxk"], 1, 2)])
    ek.op("vector", lambda v: v.tensor_scalar_mul(b["negM"].ap, b["qsq"].ap, b["mxk"][:, 1:2]),
          reads=[b["qsq"], (b["mxk"], 1, 2)], writes=[b["negM"]])
    ek.op("scalar", lambda s: s.activation(b["negM"].ap, b["negM"].ap, AF.Ln),
          reads=[b["negM"]], writes=[b["negM"]])
    ek.op("scalar", lambda s: s.activation(b["negM"].ap, b["negM"].ap, AF.Exp,
                                           bias=c["cpool"][:, 4:5], scale=0.5),
          reads=[b["negM"], (c["cpool"], 4, 5)], writes=[b["negM"]])
    ek.op("vector", lambda v: v.tensor_single_scalar(b["negM"].ap, b["negM"].ap, -1.0, ALU.mult),
          reads=[b["negM"]], writes=[b["negM"]])
    for g in range(T // 512):
        _emit_c_group(ctx, b, h, g, hc2, prow)


def _emit_c_group(ctx, b, h, g, hc2, prow):
    ek, c, T = ctx.ek, ctx.c, ctx.T
    nsup = (g + 2) // 2
    ou_ap, ou_reg = pfb(ctx, 2, 0, 512)
    for tqb in range(4):
        ek.op("vector", lambda v, tqb=tqb: v.memset(b["rsum"][:, tqb * 16:(tqb + 1) * 16], 0.0),
              writes=[(b["rsum"], tqb * 16, (tqb + 1) * 16)])
    first = True
    for sup in range(nsup):
        for sh in range(2):
            k0 = sup * 1024 + sh * 512
            if k0 >= (g + 1) * 512:
                continue
            for tqb in range(4):
                q0 = g * 512 + tqb * 128
                pcol = tqb * 512
                if k0 > q0:
                    ek.op("vector", lambda v, pcol=pcol: v.memset(b["P"][:, pcol:pcol + 512], 0.0),
                          writes=[(b["P"], pcol, pcol + 512)])
                    continue
                s_ap, s_reg = pfb(ctx, tqb % 2, 0, 512)
                ek.group("tensor", lambda t, q0=q0, k0=k0, s_ap=s_ap: [
                    t.matmul(s_ap, b["q_f"][0:64, q0:q0 + 128], b["k_f"][0:64, k0:k0 + 512],
                             start=True, stop=True)],
                    reads=[(b["q_f"], q0, q0 + 128), (b["k_f"], k0, k0 + 512)], writes=[s_reg])
                if k0 + 512 > q0:
                    off = k0 - q0 + 512
                    ek.op("vector", lambda v, s_ap=s_ap, off=off, s_reg=s_reg: v.tensor_add(
                        s_ap, s_ap, c["strip"][:, off:off + 512]),
                        reads=[s_reg, c["strip"]], writes=[s_reg])
                tcq = g * 4 + tqb
                rcol = tqb * 16 + sup * 2 + sh
                ek.op("scalar", lambda s, s_ap=s_ap, tcq=tcq, pcol=pcol, rcol=rcol: s.activation(
                    b["P"][:, pcol:pcol + 512], s_ap, AF.Exp, bias=b["negM"][:, tcq:tcq + 1], scale=0.125,
                    accum_out=b["rsum"][:, rcol:rcol + 1]),
                    reads=[s_reg, (b["negM"], tcq, tcq + 1)],
                    writes=[(b["P"], pcol, pcol + 512), (b["rsum"], rcol, rcol + 1)])
            for tkc in range(4):
                for tqb in range(4):
                    pt_ap, pt_reg = ptb(ctx, 0, tqb * 128, (tqb + 1) * 128)
                    ek.op("tensor", lambda t, tkc=tkc, tqb=tqb, pt_ap=pt_ap: t.transpose(
                        pt_ap, b["P"][:, tqb * 512 + tkc * 128: tqb * 512 + (tkc + 1) * 128], c["ident"].ap),
                        reads=[(b["P"], tqb * 512 + tkc * 128, tqb * 512 + (tkc + 1) * 128), c["ident"]],
                        writes=[pt_reg])
                ptcol = (sh * 4 + tkc) * 512
                ek.op("vector", lambda v, ptcol=ptcol: v.tensor_copy(
                    b["PT"][:, ptcol:ptcol + 512], ctx.pt[:, 0:512]),
                    reads=[(ctx.pt, 0, 512)], writes=[(b["PT"], ptcol, ptcol + 512)])
        kmax = min((g + 1) * 512, sup * 1024 + 1024) - sup * 1024
        nch = kmax // 128
        last_sup = (sup == nsup - 1)

        def av_fns(t, sup=sup, nch=nch, first=first, last_sup=last_sup):
            insts = []
            for ch in range(nch):
                tc = (sup * 1024) // 128 + ch
                insts.append(t.matmul(ou_ap[0:64, :], b["v_tm"][:, tc * 64:(tc + 1) * 64],
                                      b["PT"][:, ch * 512:(ch + 1) * 512],
                                      start=(first and ch == 0),
                                      stop=(last_sup and ch == nch - 1)))
            return insts

        ek.group("tensor", av_fns,
                 reads=[b["v_tm"], (b["PT"], 0, nch * 512)], writes=[ou_reg])
        first = False
    for tqb in range(4):
        ek.op("vector", lambda v, tqb=tqb: v.reduce_sum(
            b["rtot"][:, tqb:tqb + 1], b["rsum"][:, tqb * 16:(tqb + 1) * 16], axis=AX.X),
            reads=[(b["rsum"], tqb * 16, (tqb + 1) * 16)], writes=[(b["rtot"], tqb, tqb + 1)])
    for tqb in range(4):
        rt_ap, rt_reg = pfb(ctx, 3, tqb * 128, (tqb + 1) * 128)
        ek.op("tensor", lambda t, tqb=tqb, rt_ap=rt_ap: t.transpose(
            rt_ap[0:1, :], b["rtot"][:, tqb:tqb + 1], c["identf"].ap),
            reads=[(b["rtot"], tqb, tqb + 1), c["identf"]], writes=[rt_reg])
    ek.op("vector", lambda v: v.tensor_copy(b["rs1"][0:1, :], ctx.pf[0:1, 3 * 512: 3 * 512 + 512]),
          reads=[(ctx.pf, 3 * 512, 4 * 512)], writes=[b["rs1"]])
    ek.op("vector", lambda v: v.reciprocal(b["rs1"][0:1, :], b["rs1"][0:1, :]),
          reads=[b["rs1"]], writes=[b["rs1"]])
    rb_ap, rb_reg = pfb(ctx, 3, 0, 512)
    ek.group("tensor", lambda t, rb_ap=rb_ap: [
        t.matmul(rb_ap[0:64, :], c["ones_colf"][0:1, 0:64], b["rs1"][0:1, :], start=True, stop=True)],
        reads=[c["ones_colf"], b["rs1"]], writes=[rb_reg])
    ek.op("scalar", lambda s: s.copy(b["ou_sb"][0:64, :], ou_ap[0:64, :]),
          reads=[ou_reg], writes=[b["ou_sb"]])
    hcol = hc2 * T + g * 512
    if prow == 0:
        ek.op("vector", lambda v, hcol=hcol, rb_ap=rb_ap: v.tensor_mul(
            b["hf"].h[0:64, hcol:hcol + 512], b["ou_sb"][0:64, :], rb_ap[0:64, :]),
            reads=[b["ou_sb"], rb_reg], writes=[(b["hf"], hcol, hcol + 512)])
    else:
        ek.op("vector", lambda v, hcol=hcol, rb_ap=rb_ap: v.tensor_mul(
            b["hf"].h[64:128, hcol:hcol + 512], b["ou_sb"][0:64, :], rb_ap[0:64, :]),
            reads=[b["ou_sb"], rb_reg], writes=[(b["hf"], hcol, hcol + 512)])


def _unused():
    pass


def alloc_phase_d_bufs(ctx, b):
    mem = ctx.mem
    alloc_routing_bufs(ctx, b, "_d", tstride=512)
    b["wealth_b"] = mem.sb("wealth_b", 128, 8, F32)
    b["wealth_row"] = mem.sb("wealth_row", 1, 8, F32)
    b["bids"] = mem.sb("bids", 128, 8 * 8, F32)
    b["wslab"] = mem.sb("wslab", 128, 16384, BF16)
    b["dwb"] = mem.sb("dwb", 128, 16384, BF16)
    b["eoh"] = mem.sb("eoh", 128, 16 * 512, BF16)
    b["yacc_d"] = mem.sb("yacc_d", 128, 8 * 512, F32)


def emit_phase_d_routing(ctx, b, tt0, te):
    ek, c, T = ctx.ek, ctx.c, ctx.T
    xn = b["xn"]
    ntc = te // 128
    for tci in range(ntc):
        tc0 = tt0 + tci * 128
        lg_ap, lg_reg = pfb(ctx, 4, 0, 128)
        lhsTs = [xn[:, dc * T + tc0: dc * T + tc0 + 128] for dc in range(8)]
        rhss = [b["rwT"][:, dc * 8:(dc + 1) * 8] for dc in range(8)]
        mm_chain(ctx, lg_ap[:, 0:8], lg_reg, lhsTs, rhss,
                 reads=[b["rwT"]] + [(xn, dc * T + tc0, dc * T + tc0 + 128) for dc in range(8)])
        ek.op("vector", lambda v, lg_ap=lg_ap, tci=tci: v.tensor_add(
            b["bids"][:, tci * 8:(tci + 1) * 8], lg_ap[:, 0:8], b["bias_b"].ap),
            reads=[lg_reg, b["bias_b"]], writes=[(b["bids"], tci * 8, (tci + 1) * 8)])
        ek.op("scalar", lambda s, tci=tci: s.activation(
            b["bids"][:, tci * 8:(tci + 1) * 8], b["bids"][:, tci * 8:(tci + 1) * 8], AF.Sigmoid),
            reads=[(b["bids"], tci * 8, (tci + 1) * 8)], writes=[(b["bids"], tci * 8, (tci + 1) * 8)])
        ek.op("vector", lambda v, tci=tci: v.tensor_mul(
            b["bids"][:, tci * 8:(tci + 1) * 8], b["bids"][:, tci * 8:(tci + 1) * 8], b["wealth_b"].ap),
            reads=[(b["bids"], tci * 8, (tci + 1) * 8), b["wealth_b"]],
            writes=[(b["bids"], tci * 8, (tci + 1) * 8)])
    for tci in range(ntc):
        bsl = (b["bids"], tci * 8, (tci + 1) * 8)
        bap = b["bids"][:, tci * 8:(tci + 1) * 8]
        ek.op("vector", lambda v, bap=bap: v.reduce_max(b["m1"][:, 0:1], bap, axis=AX.X),
              reads=[bsl], writes=[(b["m1"], 0, 1)])
        ek.op("vector", lambda v, bap=bap: v.tensor_single_scalar(b["msk"].ap, bap, b["m1"][:, 0:1], ALU.is_lt),
              reads=[bsl, (b["m1"], 0, 1)], writes=[b["msk"]])
        ek.op("vector", lambda v: v.memset(b["l_ms"].ap, NEG_BIG), writes=[b["l_ms"]])
        ek.op("vector", lambda v, bap=bap: v.copy_predicated(b["l_ms"].ap, b["msk"].ap, bap),
              reads=[b["msk"], bsl, b["l_ms"]], writes=[b["l_ms"]])
        ek.op("vector", lambda v: v.reduce_max(b["m1"][:, 1:2], b["l_ms"].ap, axis=AX.X),
              reads=[b["l_ms"]], writes=[(b["m1"], 1, 2)])
        ek.op("vector", lambda v: v.tensor_single_scalar(b["m1"][:, 2:3], b["m1"][:, 0:1], -1.0, ALU.mult),
              reads=[(b["m1"], 0, 1)], writes=[(b["m1"], 2, 3)])
        ek.op("scalar", lambda s, bap=bap: s.activation(b["el"].ap, bap, AF.Exp, bias=b["m1"][:, 2:3]),
              reads=[bsl, (b["m1"], 2, 3)], writes=[b["el"]])
        ek.op("vector", lambda v: v.tensor_add(b["m1"][:, 3:4], b["m1"][:, 1:2], b["m1"][:, 2:3]),
              reads=[(b["m1"], 1, 3)], writes=[(b["m1"], 3, 4)])
        ek.op("scalar", lambda s: s.activation(b["m1"][:, 3:4], b["m1"][:, 3:4], AF.Exp),
              reads=[(b["m1"], 3, 4)], writes=[(b["m1"], 3, 4)])
        ek.op("vector", lambda v: v.tensor_single_scalar(b["m1"][:, 3:4], b["m1"][:, 3:4], 1.0, ALU.add),
              reads=[(b["m1"], 3, 4)], writes=[(b["m1"], 3, 4)])
        ek.op("vector", lambda v: v.reciprocal(b["m1"][:, 3:4], b["m1"][:, 3:4]),
              reads=[(b["m1"], 3, 4)], writes=[(b["m1"], 3, 4)])
        ek.op("vector", lambda v, bap=bap: v.tensor_single_scalar(b["l_ms"].ap, bap, b["m1"][:, 1:2], ALU.is_ge),
              reads=[bsl, (b["m1"], 1, 2)], writes=[b["l_ms"]])
        ek.op("vector", lambda v: v.tensor_mul(b["el"].ap, b["el"].ap, b["l_ms"].ap),
              reads=[b["el"], b["l_ms"]], writes=[b["el"]])
        ek.op("vector", lambda v: v.tensor_scalar_mul(b["cw16"].ap, b["el"].ap, b["m1"][:, 3:4]),
              reads=[b["el"], (b["m1"], 3, 4)], writes=[b["cw16"]])
        for e in range(8):
            pt_ap, pt_reg = ptb(ctx, 0, e * 128, (e + 1) * 128)
            ek.op("tensor", lambda t, e=e, pt_ap=pt_ap: t.transpose(
                pt_ap[0:1, :], b["cw16"][:, e:e + 1], c["ident"].ap),
                reads=[(b["cw16"], e, e + 1), c["ident"]], writes=[pt_reg])
        for e in range(8):
            pt_ap = ctx.pt[0:1, e * 128:(e + 1) * 128]
            ek.op("vector", lambda v, e=e, pt_ap=pt_ap, tci=tci: v.tensor_copy(
                b["cwrow"][0:1, e * 512 + tci * 128: e * 512 + (tci + 1) * 128], pt_ap),
                reads=[(ctx.pt, e * 128, (e + 1) * 128)],
                writes=[(b["cwrow"], e * 512 + tci * 128, e * 512 + (tci + 1) * 128)])


def emit_phase_d(ctx, b, x_prev_name, x_next_name):
    ek, c, T = ctx.ek, ctx.c, ctx.T
    xn = b["xn"]
    ek.dma("gpsimd", lambda g: g.dma_start(b["rtmp"][0:8, :], ctx.dram["L1_ffn_conf_w"][:, :]),
           reads=[dreg("L1_ffn_conf_w")], writes=[b["rtmp"]], slot="rwT")
    for dc in range(8):
        pt_ap, pt_reg = ptb(ctx, 1, 0, 8)
        ek.op("tensor", lambda t, dc=dc, pt_ap=pt_ap: t.transpose(
            pt_ap, b["rtmp"][0:8, dc * 128:(dc + 1) * 128], c["ident"][0:8, 0:8]),
            reads=[b["rtmp"], c["ident"]], writes=[pt_reg])
        ek.op("vector", lambda v, dc=dc, pt_ap=pt_ap: v.tensor_copy(b["rwT"][:, dc * 8:(dc + 1) * 8], pt_ap),
              reads=[pt_reg], writes=[(b["rwT"], dc * 8, (dc + 1) * 8)])
    ek.dma("sync", lambda sy: sy.dma_start(b["bias_row"][0:1, :], ctx.dram["L1_ffn_conf_b"][None, :]),
           reads=[dreg("L1_ffn_conf_b")], writes=[b["bias_row"]], slot="br")
    ek.dma("sync", lambda sy: sy.dma_start(b["wealth_row"][0:1, :], ctx.dram["L1_ffn_wealth"][None, :]),
           reads=[dreg("L1_ffn_wealth")], writes=[b["wealth_row"]], slot="br")
    for (row, dst) in ((b["bias_row"], b["bias_b"]), (b["wealth_row"], b["wealth_b"])):
        bb_ap, bb_reg = pfb(ctx, 5, 0, 8)
        ek.group("tensor", lambda t, row=row, bb_ap=bb_ap: [
            t.matmul(bb_ap[:, 0:8], c["ones_colf"].ap, row[0:1, :], start=True, stop=True)],
            reads=[c["ones_colf"], row], writes=[bb_reg])
        ek.op("vector", lambda v, dst=dst, bb_ap=bb_ap: v.tensor_copy(dst.ap, bb_ap[:, 0:8]),
              reads=[bb_reg], writes=[dst])

    for tt0 in range(0, T, 512):
        te = min(512, T - tt0)
        emit_phase_d_routing(ctx, b, tt0, te)
        for e in range(8):
            for q in range(te // 512):
                cb_ap, cb_reg = pfb(ctx, 5, 0, 512)
                ek.group("tensor", lambda t, e=e, q=q, cb_ap=cb_ap: [
                    t.matmul(cb_ap, c["ones_col"].ap,
                             b["cwrow"][0:1, e * 512 + q * 512: e * 512 + (q + 1) * 512],
                             start=True, stop=True)],
                    reads=[c["ones_col"], (b["cwrow"], e * 512 + q * 512, e * 512 + (q + 1) * 512)],
                    writes=[cb_reg])
                ek.op("vector", lambda v, q=q, cb_ap=cb_ap: v.tensor_copy(
                    b["cwb"][:, q * 512:(q + 1) * 512], cb_ap),
                    reads=[cb_reg], writes=[(b["cwb"], q * 512, (q + 1) * 512)])
            for ihalf in range(2):
                _emit_d_half(ctx, b, tt0, te, e, ihalf)
        for cc4 in range(te // 256):
            t0 = tt0 + cc4 * 256
            emit_residual_tile_cols(ctx, b, t0, 256, "yacc_d",
                                    lambda dc, cc4=cc4: dc * 512 + cc4 * 256,
                                    x_prev_name, x_next_name)
            emit_ln_from_xtile(ctx, b, t0, 256)


def _emit_d_half(ctx, b, tt0, te, e, ihalf):
    ek, c, T = ctx.ek, ctx.c, ctx.T
    xn = b["xn"]
    i0 = ihalf * 2048
    dlo = (e * 2 + ihalf) * 128 * 16384
    ek.dma("gpsimd", lambda g: g.dma_start(b["dwb"].ap, ctx.dram["dwT"][e, ihalf]),
           reads=[dreg("dwT", dlo, dlo + 128 * 16384)],
           writes=[b["dwb"]], slot="dwb")
    for icg in range(4):
        half = icg % 2
        wcols = 8192 * half
        icg_g = ihalf * 4 + icg
        for wi, dT in ((0, "gwT"), (1, "uwT")):
            col = wcols + wi * 4096
            rlo = (e * 8 + icg_g) * 128 * 4096
            ek.dma("sync", lambda sy, dT=dT, icg_g=icg_g, col=col: sy.dma_start(
                b["wslab"].h[:, col:col + 4096], ctx.dram[dT][e, icg_g]),
                reads=[dreg(dT, rlo, rlo + 128 * 4096)],
                writes=[(b["wslab"], col, col + 4096)], slot="ws" + str(half) + str(wi))
        for ic in range(4):
            for ts_ in range(te // 512):
                tq0 = tt0 + ts_ * 512
                g_ap, g_reg = pfb(ctx, 4, 0, 512)
                u_ap, u_reg = pfb(ctx, 5, 0, 512)
                for wi, (o_ap, o_reg) in ((0, (g_ap, g_reg)), (1, (u_ap, u_reg))):
                    lhsTs = [b["wslab"][:, wcols + wi * 4096 + hc * 512 + ic * 128:
                                        wcols + wi * 4096 + hc * 512 + ic * 128 + 128] for hc in range(8)]
                    rhss = [xn[:, hc * T + tq0: hc * T + tq0 + 512] for hc in range(8)]
                    mm_chain(ctx, o_ap, o_reg, lhsTs, rhss,
                             reads=[(b["wslab"], wcols + wi * 4096, wcols + (wi + 1) * 4096)] +
                                   [(xn, hc * T + tq0, hc * T + tq0 + 512) for hc in range(8)])
                st = b["stat_tmp"]
                ek.op("scalar", lambda s, g_ap=g_ap, st=st: s.activation(st[:, 512:1024], g_ap, AF.Silu),
                      reads=[g_reg], writes=[(st, 512, 1024)])
                eo_lo = (icg * 4 + ic) * 512 + ts_ * 512
                ek.op("vector", lambda v, u_ap=u_ap, st=st: v.scalar_tensor_tensor(
                    st[:, 0:512], u_ap, 1.0, st[:, 512:1024], ALU.mult, ALU.mult),
                    reads=[u_reg, (st, 512, 1024)], writes=[(st, 0, 512)])
                ek.op("vector", lambda v, eo_lo=eo_lo, ts_=ts_: v.tensor_mul(
                    b["eoh"][:, eo_lo:eo_lo + 512], b["stat_tmp"][:, 0:512],
                    b["cwb"][:, ts_ * 512:(ts_ + 1) * 512]),
                    reads=[(b["stat_tmp"], 0, 512), (b["cwb"], ts_ * 512, (ts_ + 1) * 512)],
                    writes=[(b["eoh"], eo_lo, eo_lo + 512)])
    for ts_ in range(te // 512):
        tq0s = ts_ * 512
        for dc in range(8):
            y_ap = ctx.pf[:, (dc % 4) * 512: (dc % 4) * 512 + 512]
            y_reg = (ctx.pf, (dc % 4) * 512, (dc % 4) * 512 + 512)
            lhsTs = [b["dwb"][:, ic * 1024 + dc * 128: ic * 1024 + (dc + 1) * 128] for ic in range(16)]
            rhss = [b["eoh"][:, ic * 512 + tq0s: ic * 512 + tq0s + 512] for ic in range(16)]
            mm_chain(ctx, y_ap, y_reg, lhsTs, rhss,
                     reads=[b["dwb"], b["eoh"]])
            ya_lo = dc * 512 + tq0s
            fresh = (e == 0 and ihalf == 0)
            if fresh:
                ek.op("vector", lambda v, y_ap=y_ap, ya_lo=ya_lo: v.tensor_copy(
                    b["yacc_d"][:, ya_lo:ya_lo + 512], y_ap),
                    reads=[y_reg], writes=[(b["yacc_d"], ya_lo, ya_lo + 512)])
            else:
                ek.op("vector", lambda v, y_ap=y_ap, ya_lo=ya_lo: v.tensor_add(
                    b["yacc_d"][:, ya_lo:ya_lo + 512], b["yacc_d"][:, ya_lo:ya_lo + 512], y_ap),
                    reads=[y_reg, (b["yacc_d"], ya_lo, ya_lo + 512)],
                    writes=[(b["yacc_d"], ya_lo, ya_lo + 512)])


INPUT_SHAPES = {
    "L0_attn_Wq": [1024, 512], "L0_attn_Wk": [1024, 512], "L0_attn_Wv": [1024, 1024],
    "L0_attn_Wg": [1024, 1024], "L0_attn_Wgk1": [1024, 16], "L0_attn_Wgk2": [16, 512],
    "L0_attn_bgk2": [512], "L0_attn_Wout": [1024, 1024], "L0_attn_rms_w": [256],
    "L0_ffn_router_w": [8, 1024], "L0_ffn_experts_bias": [8],
    "L0_ffn_w1": [8, 1024, 2048], "L0_ffn_w3": [8, 1024, 2048], "L0_ffn_w2": [8, 2048, 1024],
    "L0_ffn_sw1": [1, 1024, 2048], "L0_ffn_sw3": [1, 1024, 2048], "L0_ffn_sw2": [1, 2048, 1024],
    "L1_attn_Wq": [16, 1024, 64], "L1_attn_Wk": [16, 1024, 64], "L1_attn_Wv": [16, 1024, 64],
    "L1_attn_Wo": [1024, 1024], "L1_attn_gamma": [1024], "L1_attn_beta": [1024],
    "L1_ffn_conf_w": [8, 1024], "L1_ffn_conf_b": [8],
    "L1_ffn_gate_w": [8, 4096, 1024], "L1_ffn_up_w": [8, 4096, 1024],
    "L1_ffn_down_w": [8, 1024, 4096], "L1_ffn_wealth": [8],
}


def build_full(T, phases="ABCD", nc=None):
    if nc is None:
        nc = bass.Bass("TRN2", target_bir_lowering=False)
    ctx = Ctx(nc, T)
    ctx.dt("x_in", [1024, T], F32, kind="ExternalInput")
    ctx.dt("x_out", [1024, T], F32, kind="ExternalOutput")
    for n, sh in INPUT_SHAPES.items():
        ctx.dt(n, sh, F32, kind="ExternalInput")
    ctx.dt("x1", [1024, T], F32)
    ctx.dt("x2", [1024, T], F32)
    ctx.dt("x3", [1024, T], F32)
    ctx.dt("gwbf", [8, 4096, 1024], BF16)
    ctx.dt("uwbf", [8, 4096, 1024], BF16)
    ctx.dt("dwbf", [8, 1024, 4096], BF16)
    ctx.dt("gwT", [8, 8, 128, 4096], BF16)
    ctx.dt("uwT", [8, 8, 128, 4096], BF16)
    ctx.dt("dwT", [8, 2, 128, 16384], BF16)
    emit_consts(ctx)
    b = alloc_shared_bufs(ctx)
    mem = ctx.mem
    peaks = {}
    xs = ["x_in"]
    for ph in phases[:-1]:
        xs.append({"A": "x1", "B": "x2", "C": "x3"}[ph])
    xs.append("x_out")

    i = 0
    if "A" in phases:
        mem.push()
        alloc_phase_a_bufs(ctx, b)
        peaks["A"] = mem.cursor
        pb = dict(b)
        emit_prologue(ctx, pb, xs[i])
        emit_phase_a(ctx, pb, xs[i], xs[i + 1])
        mem.pop()
        i += 1
    need_d = "D" in phases
    if "B" in phases:
        mem.push()
        alloc_phase_b_bufs(ctx, b, with_stash=need_d)
        peaks["B"] = mem.cursor
        pb = dict(b)
        sc = make_stash_chunks(ctx, pb) if need_d else None
        if i == 0:
            emit_prologue(ctx, pb, xs[i])
        emit_phase_b(ctx, pb, xs[i], xs[i + 1], stash_chunks=sc)
        mem.pop()
        i += 1
    if "C" in phases:
        mem.push()
        alloc_phase_c_bufs(ctx, b, with_repack=need_d)
        pb = dict(b)
        rc = make_repack_chunks(ctx, pb) if need_d else None
        if i == 0:
            emit_prologue(ctx, pb, xs[i])
        emit_phase_c(ctx, pb, xs[i], xs[i + 1], repack_chunks=rc)
        peaks["C"] = mem.cursor
        mem.pop()
        i += 1
    if "D" in phases:
        mem.push()
        alloc_phase_d_bufs(ctx, b)
        pb = dict(b)
        if i == 0:
            emit_prologue(ctx, pb, xs[i])
        if "B" not in phases:
            pb["stash_stg"] = mem.sb("stash_stg", 128, 2 * 1024, BF16)
            for chk in make_stash_chunks(ctx, pb):
                chk()
        if "C" not in phases:
            pb["rpk_stg"] = mem.sb("rpk_stg", 128, 2 * 4096, BF16)
            for chk in make_repack_chunks(ctx, pb):
                chk()
        peaks["D"] = mem.cursor
        emit_phase_d(ctx, pb, xs[i], xs[i + 1])
        mem.pop()
        i += 1
    ctx.ek.finish("sync")
    ctx.ek.build_and_emit()
    mem.check()
    return nc, ctx, peaks


_T = 4096
_NCORES = 8


def kernel(**inputs):
    from concourse.bass_utils import run_bass_kernel_spmd

    x = np.ascontiguousarray(inputs["L0_attn_x"], dtype=np.float32)
    assert x.shape == (_NCORES, _T, 1024)
    weights = {n: np.ascontiguousarray(inputs[n], dtype=np.float32)
               for n in INPUT_SHAPES}

    nc, ctx, _peaks = build_full(_T)
    in_maps = []
    for i in range(_NCORES):
        m = {"x_in": np.ascontiguousarray(x[i].T)}
        m.update(weights)
        in_maps.append(m)
    res = run_bass_kernel_spmd(nc, in_maps, core_ids=list(range(_NCORES)))
    out = np.stack([np.asarray(res.results[i]["x_out"]).T for i in range(_NCORES)], axis=0)
    return out.astype(np.float32)


# version 04
# speedup vs baseline: 1.0572x; 1.0572x over previous
import sys
sys.path.insert(0, "/opt/trn_rl_repo")
import math
from collections import defaultdict
import numpy as np
import concourse.bass as bass
import concourse.mybir as mybir

F32 = mybir.dt.float32
BF16 = mybir.dt.bfloat16
ALU = mybir.AluOpType
AF = mybir.ActivationFunctionType
AX = mybir.AxisListType

ENGINES = ("tensor", "vector", "scalar", "gpsimd", "sync")
ESEM = {e: "e_" + e for e in ENGINES}
DSEM = {e: "d_" + e for e in ("sync", "scalar", "gpsimd", "vector")}
NEG_BIG = -1.0e30


class _IntervalMap:
    __slots__ = ("segs",)

    def __init__(self):
        self.segs = []

    def read(self, lo, hi, reader_tok, conflict_readers=()):
        waits = []
        rsem, rval = reader_tok
        covered = []
        for s in self.segs:
            if s[0] < hi and lo < s[1]:
                if s[2] is not None:
                    waits.append(s[2])
                for csem in conflict_readers:
                    cv = s[3].get(csem)
                    if cv is not None:
                        waits.append((csem, cv))
                if s[3].get(rsem, -1) < rval:
                    s[3][rsem] = rval
                covered.append((max(s[0], lo), min(s[1], hi)))
        covered.sort()
        pos = lo
        gaps = []
        for c0, c1 in covered:
            if c0 > pos:
                gaps.append([pos, c0, None, {rsem: rval}])
            pos = max(pos, c1)
        if pos < hi:
            gaps.append([pos, hi, None, {rsem: rval}])
        self.segs.extend(gaps)
        return waits

    def write(self, lo, hi, writer_tok):
        waits = []
        new_segs = []
        for s in self.segs:
            if s[0] < hi and lo < s[1]:
                if s[2] is not None:
                    waits.append(s[2])
                for rsem, rval in s[3].items():
                    waits.append((rsem, rval))
                if s[0] < lo:
                    new_segs.append([s[0], lo, s[2], dict(s[3])])
                if hi < s[1]:
                    new_segs.append([hi, s[1], s[2], dict(s[3])])
            else:
                new_segs.append(s)
        new_segs.append([lo, hi, writer_tok, {}])
        self.segs = new_segs
        return waits


class Buf:
    __slots__ = ("h", "space", "base", "esize", "n", "quant")

    def __init__(self, h, space, base, esize, n, quant=1):
        self.h = h
        self.space = space
        self.base = base
        self.esize = esize
        self.n = n
        self.quant = quant

    def __getitem__(self, idx):
        return self.h[idx]

    @property
    def ap(self):
        return self.h[:, :]

    def r(self, lo=0, hi=None):
        if hi is None:
            hi = self.n
        b0 = self.base + lo * self.esize
        b1 = self.base + hi * self.esize
        if self.quant > 1:
            b0 = (b0 // self.quant) * self.quant
            b1 = ((b1 + self.quant - 1) // self.quant) * self.quant
        return (self.space, b0, b1)


def dreg(name, lo=0, hi=1 << 50):
    return ("D_" + name, lo, hi)


class EK:
    def __init__(self, nc):
        self.nc = nc
        self.sched = {e: [] for e in ENGINES}
        self.cnt = defaultdict(int)
        self.vc = {e: defaultdict(int) for e in ENGINES}
        self.maps = defaultdict(_IntervalMap)
        self.nwaits = 0
        self.nops = defaultdict(int)

    @staticmethod
    def _norm(r):
        if isinstance(r, Buf):
            return r.r()
        if isinstance(r, tuple):
            if isinstance(r[0], Buf):
                return r[0].r(r[1], r[2])
            return r
        raise TypeError(f"bad region {r!r}")

    _PSUM_READ_CONFLICTS = {"vector": ("e_scalar",), "scalar": ("e_vector",)}

    def _acquire(self, engine, reads, writes, tok):
        need = {}
        for r in reads or ():
            key, lo, hi = self._norm(r)
            cr = self._PSUM_READ_CONFLICTS.get(engine, ()) if key == "PS" else ()
            for sem, val in self.maps[key].read(lo, hi, tok, cr):
                if (sem, val) == tok:
                    continue
                if need.get(sem, 0) < val:
                    need[sem] = val
        for w in writes or ():
            key, lo, hi = self._norm(w)
            for sem, val in self.maps[key].write(lo, hi, tok):
                if (sem, val) == tok:
                    continue
                if need.get(sem, 0) < val:
                    need[sem] = val
        waits = []
        own = ESEM[engine]
        for sem, val in sorted(need.items()):
            if engine == "tensor" and sem == own:
                continue
            if sem.startswith("d_"):
                val = self.cnt[sem]
            if self.vc[engine][sem] >= val:
                continue
            self.vc[engine][sem] = val
            waits.append((sem, val))
        self.nwaits += len(waits)
        return waits

    def op(self, engine, fn, reads=(), writes=(), tag=None):
        sem = ESEM[engine]
        self.cnt[sem] += 1
        tokval = self.cnt[sem]
        waits = self._acquire(engine, reads, writes, (sem, tokval))
        self.sched[engine].append(("op", fn, waits, sem, 1))
        self.nops[engine] += 1
        return (sem, tokval)

    def group(self, engine, fns, reads=(), writes=(), tag=None):
        sem = ESEM[engine]
        self.cnt[sem] += 1
        tokval = self.cnt[sem]
        waits = self._acquire(engine, reads, writes, (sem, tokval))
        self.sched[engine].append(("group", fns, waits, sem, 1))
        self.nops[engine] += 1
        return (sem, tokval)

    def dma(self, engine, fn, reads=(), writes=(), slot=None, tag=None):
        assert slot is not None, "dma needs a slot key"
        sem = "d_" + engine + "_" + slot
        tokval = self.cnt[sem] + 16
        waits = self._acquire(engine, reads, writes, (sem, tokval))
        self.cnt[sem] = tokval
        self.sched[engine].append(("op", fn, waits, sem, 16))
        self.nops["dma_" + engine] += 1
        return (sem, tokval)

    def wait_manual(self, engine, tok):
        sem, val = tok
        if self.vc[engine][sem] >= val:
            return
        self.vc[engine][sem] = val
        self.sched[engine].append(("wait", None, [(sem, val)], None, 0))

    def finish(self, engine="sync"):
        for sem, val in sorted(self.cnt.items()):
            self.wait_manual(engine, (sem, val))

    def build_and_emit(self):
        nc = self.nc
        import contextlib
        stack = contextlib.ExitStack()
        sems = {}
        used = set(self.cnt.keys())
        for e in ENGINES:
            for step in self.sched[e]:
                for w in step[2]:
                    used.add(w[0])
        for name in sorted(used):
            sems[name] = stack.enter_context(nc.semaphore(name))
        with stack:
            with nc.Block() as block:
                for eng_name in ENGINES:
                    steps = self.sched[eng_name]
                    if not steps:
                        continue

                    def body(eng, steps=steps):
                        for kind, fn, waits, sem, amt in steps:
                            for wsem, wval in waits:
                                eng.wait_ge(sems[wsem], wval)
                            if kind == "op":
                                fn(eng).then_inc(sems[sem], amt)
                            elif kind == "group":
                                fn(eng)[-1].then_inc(sems[sem], amt)

                    getattr(block, eng_name)(body)
        return sems


class Mem:
    MAN_BASE = 20480
    SB_LIMIT = 229344

    def __init__(self, nc):
        self.nc = nc
        self.cursor = 0
        self.bufs = {}
        self.marks = []

    def sb(self, name, parts, n, dtype, align=64):
        esize = {BF16: 2, mybir.dt.float16: 2, mybir.dt.uint8: 1, mybir.dt.int8: 1}.get(dtype, 4)
        base = self.MAN_BASE + (self.cursor + align - 1) // align * align
        nbytes = n * esize
        assert base + nbytes <= self.SB_LIMIT, (
            f"SBUF overflow at {name}: base={base} need={nbytes}")
        h = self.nc.alloc_sbuf_tensor_at(name, [parts, n], dtype, offset=base)
        self.cursor = (base - self.MAN_BASE) + nbytes
        b = Buf(h, "SB", base, esize, n)
        self.bufs[name] = b
        return b

    def check(self):
        assert self.nc.sbuf_base <= self.MAN_BASE, (
            f"bass auto allocations ({self.nc.sbuf_base}) collided with manual region")

    def push(self):
        self.marks.append(self.cursor)

    def pop(self):
        self.cursor = self.marks.pop()

    def push(self):
        self.marks.append(self.cursor)

    def pop(self):
        self.cursor = self.marks.pop()


def alloc_psum(nc):
    pf_h = nc.alloc_psum_tensor("pf", [128, 3072], F32)
    pt_h = nc.alloc_psum_tensor("pt", [128, 2048], BF16)
    pf = Buf(pf_h, "PS", 0, 4, 3072, quant=2048)
    pt = Buf(pt_h, "PS", 3072 * 4, 2, 2048, quant=2048)
    return pf, pt


class Ctx:
    def __init__(self, nc, T):
        self.nc = nc
        self.T = T
        self.NT = T // 256
        self.ek = EK(nc)
        self.mem = Mem(nc)
        self.pf, self.pt = alloc_psum(nc)
        self.dram = {}
        self.c = {}

    def dt(self, name, shape, dtype, kind=None):
        if kind:
            h = self.nc.dram_tensor(name, shape, dtype, kind=kind)
        else:
            h = self.nc.dram_tensor(name, shape, dtype)
        ap = h.ap()
        self.dram[name] = ap
        return ap


def pfb(ctx, bank, lo=0, hi=512):
    return ctx.pf[:, bank * 512 + lo: bank * 512 + hi], (ctx.pf, bank * 512 + lo, bank * 512 + hi)


def ptb(ctx, bank, lo=0, hi=1024):
    return ctx.pt[:, bank * 1024 + lo: bank * 1024 + hi], (ctx.pt, bank * 1024 + lo, bank * 1024 + hi)


def emit_consts(ctx):
    ek, mem, nc = ctx.ek, ctx.mem, ctx.nc
    c = ctx.c
    c["ones_col"] = mem.sb("ones_col", 1, 128, BF16)
    c["ones128"] = mem.sb("ones128", 128, 1, BF16)
    c["ones_colf"] = mem.sb("ones_colf", 1, 128, F32)
    c["ones128f"] = mem.sb("ones128f", 128, 1, F32)
    c["cpool"] = mem.sb("cpool", 128, 8, F32)
    c["negb8"] = mem.sb("negb8", 128, 8, F32)
    c["ident"] = mem.sb("ident", 128, 128, BF16)
    c["identf"] = mem.sb("identf", 128, 128, F32)
    c["zeros256"] = mem.sb("zeros256", 128, 256, F32)
    c["maskA0"] = mem.sb("maskA0", 128, 256, F32)
    c["maskA1"] = mem.sb("maskA1", 128, 256, F32)
    c["iota_j"] = mem.sb("iota_j", 128, 128, F32)
    c["iota_p"] = mem.sb("iota_p", 128, 128, F32)

    ek.op("gpsimd", lambda g: g.memset(c["ones_col"].ap, 1.0), writes=[c["ones_col"]])
    ek.op("gpsimd", lambda g: g.memset(c["ones128"].ap, 1.0), writes=[c["ones128"]])
    ek.op("gpsimd", lambda g: g.memset(c["ones_colf"].ap, 1.0), writes=[c["ones_colf"]])
    ek.op("gpsimd", lambda g: g.memset(c["ones128f"].ap, 1.0), writes=[c["ones128f"]])
    for col, val in [(0, 1e-5), (1, 1.0), (2, 1e-3), (3, NEG_BIG), (4, -2.0794415416798357)]:
        ek.op("gpsimd", lambda g, col=col, val=val: g.memset(c["cpool"][:, col:col + 1], val),
              writes=[(c["cpool"], col, col + 1)])
    ek.op("gpsimd", lambda g: g.memset(c["negb8"].ap, NEG_BIG), writes=[c["negb8"]])
    ek.op("gpsimd", lambda g: g.memset(c["zeros256"].ap, 0.0), writes=[c["zeros256"]])
    ek.op("gpsimd", lambda g: g.iota(c["iota_j"].ap, [[1, 128]], channel_multiplier=0,
                                     allow_small_or_imprecise_dtypes=True), writes=[c["iota_j"]])
    ek.op("gpsimd", lambda g: g.iota(c["iota_p"].ap, [[0, 128]], channel_multiplier=1,
                                     allow_small_or_imprecise_dtypes=True), writes=[c["iota_p"]])
    ek.op("vector", lambda v: v.tensor_tensor(c["ident"].ap, c["iota_j"].ap, c["iota_p"].ap, ALU.is_equal),
          reads=[c["iota_j"], c["iota_p"]], writes=[c["ident"]])
    ek.op("vector", lambda v: v.tensor_tensor(c["identf"].ap, c["iota_j"].ap, c["iota_p"].ap, ALU.is_equal),
          reads=[c["iota_j"], c["iota_p"]], writes=[c["identf"]])
    ek.op("vector", lambda v: v.tensor_tensor(c["maskA0"][:, 0:128], c["iota_j"].ap, c["iota_p"].ap, ALU.is_ge),
          reads=[c["iota_j"], c["iota_p"]], writes=[(c["maskA0"], 0, 128)])
    ek.op("vector", lambda v: v.memset(c["maskA0"][:, 128:256], 1.0), writes=[(c["maskA0"], 128, 256)])
    ek.op("vector", lambda v: v.memset(c["maskA1"][:, 0:128], 0.0), writes=[(c["maskA1"], 0, 128)])
    ek.op("vector", lambda v: v.tensor_tensor(c["maskA1"][:, 128:256], c["iota_j"].ap, c["iota_p"].ap, ALU.is_ge),
          reads=[c["iota_j"], c["iota_p"]], writes=[(c["maskA1"], 128, 256)])


def load_weight_cast(ctx, dst_buf, src_ap, name, nrow_chunks, ncols, col0=0, tag=None):
    ek = ctx.ek
    src = src_ap.rearrange("(k p) n -> p k n", p=128)[:, :, col0:col0 + ncols]
    dst = dst_buf.h[:, 0:nrow_chunks * ncols].rearrange("p (k n) -> p k n", k=nrow_chunks)
    ek.dma("gpsimd", lambda g: g.dma_start(dst, src),
           reads=[dreg(name)], writes=[(dst_buf, 0, nrow_chunks * ncols)],
           slot=dst_buf.h.name, tag=tag)


def mm_chain(ctx, out_ap, out_reg, lhsT_list, rhs_list, reads, extra_first=None):
    ek = ctx.ek
    n = len(lhsT_list)

    def fns(t):
        insts = []
        for i in range(n):
            insts.append(t.matmul(out_ap, lhsT_list[i], rhs_list[i],
                                  start=(i == 0), stop=(i == n - 1)))
        return insts

    return ek.group("tensor", fns, reads=reads, writes=[out_reg])


def emit_ln_from_xtile(ctx, bufs, t0, W, phase_tag=""):
    ek, c, T = ctx.ek, ctx.c, ctx.T
    xt, xn = bufs["xtile"], bufs["xn"]
    tmp, st = bufs["stat_tmp"], bufs["stat_small"]
    nhalf = 4
    s1_ap, s1_reg = pfb(ctx, 4, 0, W)
    s2_ap, s2_reg = pfb(ctx, 5, 0, W)

    def s1_fns(t):
        return [t.matmul(s1_ap[0:1, :], c["ones128f"].ap, xt[:, cc * W:(cc + 1) * W],
                         start=(cc == 0), stop=(cc == 7)) for cc in range(8)]

    ek.group("tensor", s1_fns, reads=[(xt, 0, 8 * W), c["ones128f"]], writes=[s1_reg])
    for half in range(2):
        toks = []
        for i in range(nhalf):
            cc = half * nhalf + i
            ek.op("scalar", lambda s, cc=cc, i=i: s.activation(
                tmp[:, i * W:(i + 1) * W], xt[:, cc * W:(cc + 1) * W], AF.Square),
                reads=[(xt, cc * W, (cc + 1) * W)], writes=[(tmp, i * W, (i + 1) * W)])

        def s2_fns(t, half=half):
            return [t.matmul(s2_ap[0:1, :], c["ones128f"].ap, tmp[:, i * W:(i + 1) * W],
                             start=(half == 0 and i == 0), stop=(half == 1 and i == nhalf - 1))
                    for i in range(nhalf)]

        ek.group("tensor", s2_fns, reads=[(tmp, 0, nhalf * W), c["ones128f"]], writes=[s2_reg])
    m = st
    ek.op("vector", lambda v: v.tensor_single_scalar(m[0:1, 0:W], s1_ap[0:1, :], 1.0 / 1024.0, ALU.mult),
          reads=[s1_reg], writes=[(st, 0, W)])
    ek.op("vector", lambda v: v.tensor_single_scalar(m[0:1, 3 * W:4 * W], s2_ap[0:1, :], 1.0 / 1024.0, ALU.mult),
          reads=[s2_reg], writes=[(st, 3 * W, 4 * W)])
    ek.op("vector", lambda v: v.tensor_mul(m[0:1, W:2 * W], m[0:1, 0:W], m[0:1, 0:W]),
          reads=[(st, 0, W)], writes=[(st, W, 2 * W)])
    ek.op("vector", lambda v: v.tensor_sub(m[0:1, W:2 * W], m[0:1, 3 * W:4 * W], m[0:1, W:2 * W]),
          reads=[(st, W, 2 * W), (st, 3 * W, 4 * W)], writes=[(st, W, 2 * W)])
    ek.op("scalar", lambda s: s.activation(m[0:1, W:2 * W], m[0:1, W:2 * W], AF.Ln,
                                           bias=ctx.c["cpool"][0:1, 0:1], scale=1.0),
          reads=[(st, W, 2 * W), (ctx.c["cpool"], 0, 1)], writes=[(st, W, 2 * W)])
    ek.op("scalar", lambda s: s.activation(m[0:1, 2 * W:3 * W], m[0:1, W:2 * W], AF.Exp, bias=0.0, scale=-0.5),
          reads=[(st, W, 2 * W)], writes=[(st, 2 * W, 3 * W)])
    mb_ap, mb_reg = pfb(ctx, 4, 0, W)
    rb_ap, rb_reg = pfb(ctx, 5, 0, W)
    ek.group("tensor", lambda t: [t.matmul(mb_ap, c["ones_colf"].ap, m[0:1, 0:W], start=True, stop=True)],
             reads=[(st, 0, W), c["ones_colf"]], writes=[mb_reg])
    ek.group("tensor", lambda t: [t.matmul(rb_ap, c["ones_colf"].ap, m[0:1, 2 * W:3 * W], start=True, stop=True)],
             reads=[(st, 2 * W, 3 * W), c["ones_colf"]], writes=[rb_reg])
    for cc in range(8):
        dst_lo = cc * T + t0
        ek.op("vector", lambda v, cc=cc: v.tensor_sub(tmp[:, 0:W] if False else tmp[:, (cc % 4) * W:(cc % 4) * W + W],
                                                      xt[:, cc * W:(cc + 1) * W], mb_ap),
              reads=[(xt, cc * W, (cc + 1) * W), mb_reg],
              writes=[(tmp, (cc % 4) * W, (cc % 4) * W + W)])
        ek.op("vector", lambda v, cc=cc, dst_lo=dst_lo: v.tensor_mul(
            xn[:, dst_lo:dst_lo + W], tmp[:, (cc % 4) * W:(cc % 4) * W + W], rb_ap),
            reads=[(tmp, (cc % 4) * W, (cc % 4) * W + W), rb_reg],
            writes=[(xn, dst_lo, dst_lo + W)])


def emit_residual_tile(ctx, bufs, t0, W, y_src, x_prev_name, x_next_name, store=True):
    ek, T = ctx.ek, ctx.T
    xt, ysb = bufs["xtile"], bufs[y_src]
    xprev = ctx.dram[x_prev_name]
    src = xprev.rearrange("(c p) t -> p c t", p=128)[:, :, t0:t0 + W]
    ek.dma("sync", lambda sy: sy.dma_start(xt.h[:, 0:8 * W].rearrange("p (c w) -> p c w", c=8), src),
           reads=[dreg(x_prev_name, t0, t0 + W)], writes=[(xt, 0, 8 * W)], slot="xt")
    for cc in range(8):
        ek.op("vector", lambda v, cc=cc: v.tensor_add(
            xt[:, cc * W:(cc + 1) * W], ysb[:, cc * W:(cc + 1) * W], xt[:, cc * W:(cc + 1) * W]),
            reads=[(ysb, cc * W, (cc + 1) * W), (xt, cc * W, (cc + 1) * W)],
            writes=[(xt, cc * W, (cc + 1) * W)])
    if store:
        xnext = ctx.dram[x_next_name]
        dst = xnext.rearrange("(c p) t -> p c t", p=128)[:, :, t0:t0 + W]
        ek.dma("sync", lambda sy: sy.dma_start(dst, xt.h[:, 0:8 * W].rearrange("p (c w) -> p c w", c=8)),
               reads=[(xt, 0, 8 * W)], writes=[dreg(x_next_name, t0, t0 + W)], slot="xt")


def emit_prologue(ctx, bufs, x_in_name):
    T = ctx.T
    W = 256
    for t0 in range(0, T, W):
        xin = ctx.dram[x_in_name]
        src = xin.rearrange("(c p) t -> p c t", p=128)[:, :, t0:t0 + W]
        xt = bufs["xtile"]
        ctx.ek.dma("sync", lambda sy, src=src: sy.dma_start(
            xt.h[:, 0:8 * W].rearrange("p (c w) -> p c w", c=8), src),
            reads=[dreg(x_in_name, t0, t0 + W)], writes=[(xt, 0, 8 * W)], slot="xt")
        emit_ln_from_xtile(ctx, bufs, t0, W)


def alloc_shared_bufs(ctx):
    mem, T = ctx.mem, ctx.T
    b = {}
    b["xn"] = mem.sb("xn", 128, 8 * T, BF16)
    b["xtile"] = mem.sb("xtile", 128, 8 * 256, F32)
    b["stat_tmp"] = mem.sb("stat_tmp", 128, 4 * 256, F32)
    b["stat_small"] = mem.sb("stat_small", 1, 4 * 256, F32)
    return b


def alloc_phase_a_bufs(ctx, b):
    mem = ctx.mem
    b["y_sb"] = mem.sb("y_sb_a", 128, 8 * 256, F32)
    b["wq"] = mem.sb("wq", 128, 8 * 512, BF16)
    b["wk"] = mem.sb("wk", 128, 8 * 512, BF16)
    b["wv"] = mem.sb("wv", 128, 8 * 512, BF16)
    b["wg"] = mem.sb("wg", 128, 8 * 512, BF16)
    b["wgk1"] = mem.sb("wgk1", 128, 8 * 16, BF16)
    b["wgk2"] = mem.sb("wgk2", 64, 512, BF16)
    b["bgk2"] = mem.sb("bgk2", 128, 4, F32)
    b["wout"] = mem.sb("wout", 128, 4 * 1024, BF16)
    b["rmsw"] = mem.sb("rmsw", 128, 2, F32)
    b["q_sb"] = mem.sb("q_sb", 128, 4 * 256, BF16)
    b["k_sb"] = mem.sb("k_sb", 128, 4 * 256, BF16)
    b["v_sb"] = mem.sb("v_sb", 128, 8 * 256, BF16)
    b["gate"] = mem.sb("gate", 128, 8 * 512, BF16)
    b["z1"] = mem.sb("z1", 64, 512, BF16)
    b["g_sb"] = mem.sb("g_sb", 128, 4 * 512, F32)
    b["b_sb"] = mem.sb("b_sb", 128, 4 * 256, F32)
    b["ga"] = mem.sb("ga", 128, 512, F32)
    b["gb"] = mem.sb("gb", 128, 512, F32)
    b["ebuf"] = mem.sb("ebuf", 128, 2 * 256, F32)
    b["ebc"] = mem.sb("ebc", 128, 4, F32)
    b["qt"] = mem.sb("qt", 128, 256, BF16)
    b["kh"] = mem.sb("kh", 128, 256, BF16)
    b["kb"] = mem.sb("kb", 128, 256, BF16)
    b["at_sb"] = mem.sb("at_sb", 128, 2 * 256, BF16)
    b["vt_sb"] = mem.sb("vt_sb", 128, 2 * 256, BF16)
    b["kbt_sb"] = mem.sb("kbt_sb", 128, 2 * 128, BF16)
    b["rb_sb"] = mem.sb("rb_sb", 128, 256, F32)
    b["og"] = mem.sb("og", 128, 8 * 256, BF16)
    b["S"] = mem.sb("S", 128, 4 * 256, F32)
    b["S_bf"] = mem.sb("S_bf", 128, 4 * 256, BF16)


def emit_phase_a_weights(ctx, b):
    ek = ctx.ek
    load_weight_cast(ctx, b["wq"], ctx.dram["L0_attn_Wq"], "L0_attn_Wq", 8, 512)
    load_weight_cast(ctx, b["wk"], ctx.dram["L0_attn_Wk"], "L0_attn_Wk", 8, 512)
    load_weight_cast(ctx, b["wgk1"], ctx.dram["L0_attn_Wgk1"], "L0_attn_Wgk1", 8, 16)
    ek.op("vector", lambda v: v.memset(b["wgk2"][0:64, :], 0.0), writes=[b["wgk2"]])
    ek.op("vector", lambda v: v.memset(b["z1"][0:64, :], 0.0), writes=[b["z1"]])
    ek.dma("gpsimd", lambda g: g.dma_start(b["wgk2"][0:16, :], ctx.dram["L0_attn_Wgk2"][:, :]),
           reads=[dreg("L0_attn_Wgk2")], writes=[(b["wgk2"], 0, 512)], slot="wgk2")
    ek.dma("gpsimd", lambda g: g.dma_start(
        b["bgk2"].ap, ctx.dram["L0_attn_bgk2"].rearrange("(fc p) -> p fc", p=128),
        allow_slow_non_contiguous=True),
        reads=[dreg("L0_attn_bgk2")], writes=[b["bgk2"]], slot="bgk2")
    ek.dma("gpsimd", lambda g: g.dma_start(
        b["rmsw"].ap, ctx.dram["L0_attn_rms_w"].rearrange("(vh p) -> p vh", p=128),
        allow_slow_non_contiguous=True),
        reads=[dreg("L0_attn_rms_w")], writes=[b["rmsw"]], slot="rmsw")


def _proj_group(ctx, out_ap, out_reg, wbuf, wcol_of_dc, xn, t0, W, nk=8):
    T = ctx.T
    lhsTs = [wbuf[:, wcol_of_dc(dc): wcol_of_dc(dc) + 128] for dc in range(nk)]
    rhss = [xn[:, dc * T + t0: dc * T + t0 + W] for dc in range(nk)]
    reads = [(wbuf, min(wcol_of_dc(dc) for dc in range(nk)),
              max(wcol_of_dc(dc) + 128 for dc in range(nk)))]
    reads += [(xn, dc * T + t0, dc * T + t0 + W) for dc in range(nk)]
    return mm_chain(ctx, out_ap, out_reg, lhsTs, rhss, reads)


def emit_phase_a(ctx, b, x_prev_name, x_next_name):
    ek, c, T, NT = ctx.ek, ctx.c, ctx.T, ctx.NT
    xn = b["xn"]
    emit_phase_a_weights(ctx, b)
    ek.op("vector", lambda v: v.memset(b["S"].ap, 0.0), writes=[b["S"]])
    ek.op("vector", lambda v: v.memset(b["S_bf"].ap, 0.0), writes=[b["S_bf"]])

    for cc in range(NT):
        t0 = cc * 256
        if cc % 2 == 0:
            _emit_a_gate_g(ctx, b, t0, min(512, T - t0))
        for fc in range(4):
            q_ap, q_reg = pfb(ctx, fc // 2, (fc % 2) * 256, (fc % 2) * 256 + 256)
            _proj_group(ctx, q_ap, q_reg, b["wq"], lambda dc, fc=fc: dc * 512 + fc * 128, xn, t0, 256)
            ek.op("scalar", lambda s, fc=fc, q_ap=q_ap: s.copy(b["q_sb"][:, fc * 256:(fc + 1) * 256], q_ap),
                  reads=[q_reg], writes=[(b["q_sb"], fc * 256, (fc + 1) * 256)])
        for fc in range(4):
            k_ap, k_reg = pfb(ctx, 2 + fc // 2, (fc % 2) * 256, (fc % 2) * 256 + 256)
            _proj_group(ctx, k_ap, k_reg, b["wk"], lambda dc, fc=fc: dc * 512 + fc * 128, xn, t0, 256)
            ek.op("scalar", lambda s, fc=fc, k_ap=k_ap: s.copy(b["k_sb"][:, fc * 256:(fc + 1) * 256], k_ap),
                  reads=[k_reg], writes=[(b["k_sb"], fc * 256, (fc + 1) * 256)])
        for vh2 in range(2):
            load_weight_cast(ctx, b["wv"], ctx.dram["L0_attn_Wv"], "L0_attn_Wv", 8, 512,
                             col0=vh2 * 512)
            for vc4 in range(4):
                vc = vh2 * 4 + vc4
                v_ap, v_reg = pfb(ctx, vc4 // 2, (vc4 % 2) * 256, (vc4 % 2) * 256 + 256)
                _proj_group(ctx, v_ap, v_reg, b["wv"], lambda dc, vc4=vc4: dc * 512 + vc4 * 128, xn, t0, 256)
                ek.op("scalar", lambda s, vc=vc, v_ap=v_ap: s.copy(b["v_sb"][:, vc * 256:(vc + 1) * 256], v_ap),
                      reads=[v_reg], writes=[(b["v_sb"], vc * 256, (vc + 1) * 256)])
        for hc in range(4):
            _emit_a_head(ctx, b, cc, hc)
        for vh2 in range(2):
            src_ = ctx.dram["L0_attn_Wout"].rearrange("(k p) n -> p k n", p=128)[:, vh2 * 4:(vh2 + 1) * 4, :]
            dst_ = b["wout"].h[:, 0:4 * 1024].rearrange("p (k n) -> p k n", k=4)
            ek.dma("gpsimd", lambda g, src_=src_, dst_=dst_: g.dma_start(dst_, src_),
                   reads=[dreg("L0_attn_Wout")], writes=[(b["wout"], 0, 4 * 1024)], slot="wout")
            for dc in range(8):
                y_ap, y_reg = pfb(ctx, dc // 2, (dc % 2) * 256, (dc % 2) * 256 + 256)
                lhsTs = [b["wout"][:, vc4 * 1024 + dc * 128: vc4 * 1024 + dc * 128 + 128] for vc4 in range(4)]
                rhss = [b["og"][:, (vh2 * 4 + vc4) * 256:(vh2 * 4 + vc4 + 1) * 256] for vc4 in range(4)]
                mm_chain(ctx, y_ap, y_reg, lhsTs, rhss,
                         reads=[b["wout"], b["og"]])
                if vh2 == 0:
                    ek.op("scalar", lambda s, dc=dc, y_ap=y_ap: s.copy(
                        b["y_sb"][:, dc * 256:(dc + 1) * 256], y_ap),
                        reads=[y_reg], writes=[(b["y_sb"], dc * 256, (dc + 1) * 256)])
                else:
                    ek.op("vector", lambda v, dc=dc, y_ap=y_ap: v.tensor_add(
                        b["y_sb"][:, dc * 256:(dc + 1) * 256], b["y_sb"][:, dc * 256:(dc + 1) * 256], y_ap),
                        reads=[y_reg, (b["y_sb"], dc * 256, (dc + 1) * 256)],
                        writes=[(b["y_sb"], dc * 256, (dc + 1) * 256)])
        emit_residual_tile(ctx, b, t0, 256, "y_sb", x_prev_name, x_next_name)
        emit_ln_from_xtile(ctx, b, t0, 256)


def _emit_a_gate_g(ctx, b, t0, W):
    ek, c, T = ctx.ek, ctx.c, ctx.T
    xn = b["xn"]
    for fh in range(2):
        load_weight_cast(ctx, b["wg"], ctx.dram["L0_attn_Wg"], "L0_attn_Wg", 8, 512,
                         col0=fh * 512)
        for fc4 in range(4):
            fc = fh * 4 + fc4
            g_ap, g_reg = pfb(ctx, fc4, 0, W)
            _proj_group(ctx, g_ap, g_reg, b["wg"], lambda dc, fc4=fc4: dc * 512 + fc4 * 128, xn, t0, W)
            ek.op("scalar", lambda s, fc=fc, g_ap=g_ap: s.activation(
                b["gate"][:, fc * W:(fc + 1) * W], g_ap, AF.Silu),
                reads=[g_reg], writes=[(b["gate"], fc * W, (fc + 1) * W)])
    z1_ap, z1_reg = pfb(ctx, 4, 0, W)
    lhsTs = [b["wgk1"][:, dc * 16:(dc + 1) * 16] for dc in range(8)]
    rhss = [xn[:, dc * T + t0: dc * T + t0 + W] for dc in range(8)]
    mm_chain(ctx, z1_ap[0:16, :], z1_reg, lhsTs, rhss,
             reads=[b["wgk1"]] + [(xn, dc * T + t0, dc * T + t0 + W) for dc in range(8)])
    ek.op("scalar", lambda s: s.copy(b["z1"][0:16, 0:W], z1_ap[0:16, :]),
          reads=[z1_reg], writes=[b["z1"]])
    for fc in range(4):
        z_ap, z_reg = pfb(ctx, 5, 0, W)
        ek.group("tensor", lambda t, fc=fc, z_ap=z_ap: [
            t.matmul(z_ap, b["wgk2"][0:64, fc * 128:(fc + 1) * 128], b["z1"][0:64, 0:W],
                     start=True, stop=True)],
            reads=[b["wgk2"], b["z1"]], writes=[z_reg])
        bias = b["bgk2"][:, fc:fc + 1]
        ek.op("vector", lambda v, z_ap=z_ap, bias=bias: v.tensor_scalar(
            b["ga"][:, 0:W], z_ap, bias, 0.0, ALU.add, ALU.min),
            reads=[z_reg, b["bgk2"]], writes=[(b["ga"], 0, W)])
        ek.op("scalar", lambda s, z_ap=z_ap, bias=bias: s.activation(
            b["gb"][:, 0:W], z_ap, AF.Abs, bias=bias, scale=1.0),
            reads=[z_reg, b["bgk2"]], writes=[(b["gb"], 0, W)])
        ek.op("scalar", lambda s: s.activation(b["gb"][:, 0:W], b["gb"][:, 0:W], AF.Exp, scale=-1.0),
              reads=[(b["gb"], 0, W)], writes=[(b["gb"], 0, W)])
        ek.op("scalar", lambda s: s.activation(b["gb"][:, 0:W], b["gb"][:, 0:W], AF.Ln,
                                               bias=ctx.c["cpool"][:, 1:2]),
              reads=[(b["gb"], 0, W), (ctx.c["cpool"], 1, 2)], writes=[(b["gb"], 0, W)])
        ek.op("vector", lambda v: v.tensor_sub(b["ga"][:, 0:W], b["ga"][:, 0:W], b["gb"][:, 0:W]),
              reads=[(b["ga"], 0, W), (b["gb"], 0, W)], writes=[(b["ga"], 0, W)])
        ek.op("vector", lambda v, fc=fc: v.tensor_scalar(
            b["g_sb"][:, fc * 512:fc * 512 + W], b["ga"][:, 0:W], 1.0 / 16.0, -3.0, ALU.mult, ALU.max),
            reads=[(b["ga"], 0, W)], writes=[(b["g_sb"], fc * 512, fc * 512 + W)])


def _emit_a_head(ctx, b, cc, hc):
    ek, c = ctx.ek, ctx.c
    coff = (cc % 2) * 256
    ek.op("vector", lambda v: v.tensor_tensor_scan(
        b["b_sb"][:, hc * 256:(hc + 1) * 256], b["g_sb"][:, hc * 512 + coff: hc * 512 + coff + 256],
        c["zeros256"].ap, 0.0, ALU.add, ALU.add),
        reads=[(b["g_sb"], hc * 512 + coff, hc * 512 + coff + 256), c["zeros256"]],
        writes=[(b["b_sb"], hc * 256, (hc + 1) * 256)])
    bs = b["b_sb"][:, hc * 256:(hc + 1) * 256]
    bs_reg = (b["b_sb"], hc * 256, (hc + 1) * 256)
    ek.op("scalar", lambda s: s.activation(b["ebuf"][:, 0:256], bs, AF.Exp),
          reads=[bs_reg], writes=[(b["ebuf"], 0, 256)])
    ek.op("scalar", lambda s: s.activation(b["ebuf"][:, 256:512], bs, AF.Exp, scale=-1.0),
          reads=[bs_reg], writes=[(b["ebuf"], 256, 512)])
    ek.op("scalar", lambda s: s.activation(
        b["ebc"][:, hc:hc + 1], b["b_sb"][:, hc * 256 + 255: hc * 256 + 256], AF.Exp),
        reads=[bs_reg], writes=[(b["ebc"], hc, hc + 1)])
    ek.op("vector", lambda v: v.tensor_mul(b["qt"].ap, b["q_sb"][:, hc * 256:(hc + 1) * 256], b["ebuf"][:, 0:256]),
          reads=[(b["q_sb"], hc * 256, (hc + 1) * 256), (b["ebuf"], 0, 256)], writes=[b["qt"]])
    ek.op("vector", lambda v: v.tensor_mul(b["kh"].ap, b["k_sb"][:, hc * 256:(hc + 1) * 256], b["ebuf"][:, 256:512]),
          reads=[(b["k_sb"], hc * 256, (hc + 1) * 256), (b["ebuf"], 256, 512)], writes=[b["kh"]])
    ek.op("vector", lambda v: v.tensor_scalar_mul(b["kb"].ap, b["kh"].ap, b["ebc"][:, hc:hc + 1]),
          reads=[b["kh"], (b["ebc"], hc, hc + 1)], writes=[b["kb"]])
    for sblk in range(2):
        at_ap, at_reg = pfb(ctx, 4, sblk * 256, (sblk + 1) * 256)
        ek.group("tensor", lambda t, sblk=sblk, at_ap=at_ap: [
            t.matmul(at_ap, b["kh"][:, sblk * 128:(sblk + 1) * 128], b["qt"].ap, start=True, stop=True)],
            reads=[b["kh"], b["qt"]], writes=[at_reg])
        mask = c["maskA0"] if sblk == 0 else c["maskA1"]
        ek.op("vector", lambda v, sblk=sblk, at_ap=at_ap, mask=mask: v.tensor_mul(
            b["at_sb"][:, sblk * 256:(sblk + 1) * 256], at_ap, mask.ap),
            reads=[at_reg, mask], writes=[(b["at_sb"], sblk * 256, (sblk + 1) * 256)])
    for sblk in range(2):
        for vh in range(2):
            pt_ap, pt_reg = ptb(ctx, 0, vh * 128, (vh + 1) * 128)
            ek.op("tensor", lambda t, sblk=sblk, vh=vh, pt_ap=pt_ap: t.transpose(
                pt_ap, b["v_sb"][:, (2 * hc + vh) * 256 + sblk * 128: (2 * hc + vh) * 256 + (sblk + 1) * 128],
                c["ident"].ap),
                reads=[(b["v_sb"], (2 * hc + vh) * 256 + sblk * 128, (2 * hc + vh) * 256 + (sblk + 1) * 128),
                       c["ident"]],
                writes=[pt_reg])
        ek.op("vector", lambda v, sblk=sblk: v.tensor_copy(
            b["vt_sb"][:, sblk * 256:(sblk + 1) * 256], ctx.pt[:, 0:256]),
            reads=[(ctx.pt, 0, 256)], writes=[(b["vt_sb"], sblk * 256, (sblk + 1) * 256)])
        pt2_ap, pt2_reg = ptb(ctx, 1, 0, 128)
        ek.op("tensor", lambda t, sblk=sblk, pt2_ap=pt2_ap: t.transpose(
            pt2_ap, b["kb"][:, sblk * 128:(sblk + 1) * 128], c["ident"].ap),
            reads=[(b["kb"], sblk * 128, (sblk + 1) * 128), c["ident"]], writes=[pt2_reg])
        ek.op("vector", lambda v, sblk=sblk: v.tensor_copy(
            b["kbt_sb"][:, sblk * 128:(sblk + 1) * 128], ctx.pt[:, 3072 - 3072 + 1024:1024 + 128] if False else pt2_ap),
            reads=[pt2_reg], writes=[(b["kbt_sb"], sblk * 128, (sblk + 1) * 128)])
    for vh in range(2):
        o_ap, o_reg = pfb(ctx, 0, vh * 256, (vh + 1) * 256)
        lhsTs = [b["S_bf"][:, hc * 256 + vh * 128: hc * 256 + (vh + 1) * 128],
                 b["vt_sb"][:, 0 * 256 + vh * 128: 0 * 256 + (vh + 1) * 128],
                 b["vt_sb"][:, 1 * 256 + vh * 128: 1 * 256 + (vh + 1) * 128]]
        rhss = [b["qt"].ap, b["at_sb"][:, 0:256], b["at_sb"][:, 256:512]]
        mm_chain(ctx, o_ap, o_reg, lhsTs, rhss,
                 reads=[(b["S_bf"], hc * 256, (hc + 1) * 256), b["vt_sb"], b["qt"], b["at_sb"]])
    ds_ap, ds_reg = pfb(ctx, 1, 0, 256)
    lhsTs = [b["kbt_sb"][:, 0:128], b["kbt_sb"][:, 128:256]]
    rhss = [b["vt_sb"][:, 0:256], b["vt_sb"][:, 256:512]]
    mm_chain(ctx, ds_ap, ds_reg, lhsTs, rhss, reads=[b["kbt_sb"], b["vt_sb"]])
    for vh in range(2):
        o_ap = ctx.pf[:, 0 * 512 + vh * 256: 0 * 512 + (vh + 1) * 256]
        o_reg = (ctx.pf, vh * 256, (vh + 1) * 256)
        ek.op("scalar", lambda s, vh=vh, o_ap=o_ap: s.activation(
            b["stat_tmp"][:, vh * 256:(vh + 1) * 256], o_ap, AF.Square),
            reads=[o_reg], writes=[(b["stat_tmp"], vh * 256, (vh + 1) * 256)])
    ss_ap, ss_reg = pfb(ctx, 5, 0, 256)
    mm_chain(ctx, ss_ap[0:1, :], ss_reg,
             [c["ones128f"].ap, c["ones128f"].ap],
             [b["stat_tmp"][:, 0:256], b["stat_tmp"][:, 256:512]],
             reads=[c["ones128f"], (b["stat_tmp"], 0, 512)])
    ek.op("scalar", lambda s: s.activation(b["stat_small"][0:1, 0:256], ss_ap[0:1, :], AF.Ln,
                                           bias=ctx.c["cpool"][0:1, 0:1], scale=1.0 / 256.0),
          reads=[ss_reg, (ctx.c["cpool"], 0, 1)], writes=[(b["stat_small"], 0, 256)])
    ek.op("scalar", lambda s: s.activation(b["stat_small"][0:1, 256:512], b["stat_small"][0:1, 0:256],
                                           AF.Exp, scale=-0.5),
          reads=[(b["stat_small"], 0, 256)], writes=[(b["stat_small"], 256, 512)])
    rb_ap, rb_reg = pfb(ctx, 5, 256, 512)
    ek.group("tensor", lambda t: [t.matmul(rb_ap, c["ones_colf"].ap, b["stat_small"][0:1, 256:512],
                                           start=True, stop=True)],
             reads=[c["ones_colf"], (b["stat_small"], 256, 512)], writes=[rb_reg])
    ek.op("vector", lambda v: v.tensor_copy(b["rb_sb"].ap, rb_ap), reads=[rb_reg], writes=[b["rb_sb"]])
    for vh in range(2):
        o_ap = ctx.pf[:, vh * 256: (vh + 1) * 256]
        o_reg = (ctx.pf, vh * 256, (vh + 1) * 256)
        ek.op("vector", lambda v, vh=vh, o_ap=o_ap: v.scalar_tensor_tensor(
            b["stat_tmp"][:, vh * 256:(vh + 1) * 256], o_ap, b["rmsw"][:, vh:vh + 1], b["rb_sb"].ap,
            ALU.mult, ALU.mult),
            reads=[o_reg, b["rmsw"], b["rb_sb"]], writes=[(b["stat_tmp"], vh * 256, (vh + 1) * 256)])
        gcol = (2 * hc + vh) * 512 + (cc % 2) * 256
        ek.op("vector", lambda v, vh=vh, gcol=gcol: v.tensor_mul(
            b["og"][:, (2 * hc + vh) * 256:(2 * hc + vh + 1) * 256],
            b["stat_tmp"][:, vh * 256:(vh + 1) * 256], b["gate"][:, gcol:gcol + 256]),
            reads=[(b["stat_tmp"], vh * 256, (vh + 1) * 256), (b["gate"], gcol, gcol + 256)],
            writes=[(b["og"], (2 * hc + vh) * 256, (2 * hc + vh + 1) * 256)])
    ek.op("vector", lambda v: v.tensor_scalar_mul(
        b["S"][:, hc * 256:(hc + 1) * 256], b["S"][:, hc * 256:(hc + 1) * 256], b["ebc"][:, hc:hc + 1]),
        reads=[(b["S"], hc * 256, (hc + 1) * 256), (b["ebc"], hc, hc + 1)],
        writes=[(b["S"], hc * 256, (hc + 1) * 256)])
    ek.op("vector", lambda v: v.tensor_add(
        b["S"][:, hc * 256:(hc + 1) * 256], b["S"][:, hc * 256:(hc + 1) * 256], ds_ap),
        reads=[(b["S"], hc * 256, (hc + 1) * 256), ds_reg],
        writes=[(b["S"], hc * 256, (hc + 1) * 256)])
    ek.op("vector", lambda v: v.tensor_copy(b["S_bf"][:, hc * 256:(hc + 1) * 256], b["S"][:, hc * 256:(hc + 1) * 256]),
          reads=[(b["S"], hc * 256, (hc + 1) * 256)], writes=[(b["S_bf"], hc * 256, (hc + 1) * 256)])


def alloc_routing_bufs(ctx, b, sfx, tstride=1024):
    mem = ctx.mem
    b["rwT"] = mem.sb("rwT" + sfx, 128, 8 * 8, BF16)
    b["rtmp"] = mem.sb("rtmp" + sfx, 8, 1024, BF16)
    b["bias_b"] = mem.sb("bias_b" + sfx, 128, 8, F32)
    b["bias_row"] = mem.sb("bias_row" + sfx, 1, 8, F32)
    b["l_sb"] = mem.sb("l_sb" + sfx, 128, 8, F32)
    b["l_ms"] = mem.sb("l_ms" + sfx, 128, 8, F32)
    b["msk"] = mem.sb("msk" + sfx, 128, 8, mybir.dt.uint8)
    b["m1"] = mem.sb("m1" + sfx, 128, 4, F32)
    b["el"] = mem.sb("el" + sfx, 128, 8, F32)
    b["cw16"] = mem.sb("cw16" + sfx, 128, 8, BF16)
    b["cwrow"] = mem.sb("cwrow" + sfx, 1, 8 * tstride, BF16)
    b["cw_stride"] = tstride
    b["cwb"] = mem.sb("cwb" + sfx, 128, 1024, BF16)


def alloc_phase_b_bufs(ctx, b, with_stash=False):
    mem = ctx.mem
    alloc_routing_bufs(ctx, b, "_b")
    b["y_sb"] = mem.sb("y_sb_b", 128, 8 * 256, F32)
    b["w1s"] = mem.sb("w1s", 128, 2 * 8 * 128, BF16)
    b["w3s"] = mem.sb("w3s", 128, 2 * 8 * 128, BF16)
    b["eo"] = mem.sb("eo", 128, 8 * 1024, BF16)
    b["w2s"] = mem.sb("w2s", 128, 8 * 1024, BF16)
    b["y_acc"] = mem.sb("y_acc", 128, 8 * 1024, F32)
    if with_stash:
        b["stash_stg"] = mem.sb("stash_stg", 128, 2 * 1024, BF16)


def make_stash_chunks(ctx, b):
    ek = ctx.ek
    stg = b["stash_stg"]
    chunks = []
    bi = 0
    jobs = []
    for nm, dn in (("L1_ffn_gate_w", "gwbf"), ("L1_ffn_up_w", "uwbf")):
        for e in range(8):
            for t in range(32):
                jobs.append((nm, dn, e, t, 0))
    for e in range(8):
        for t in range(8):
            for c4 in range(4):
                jobs.append(("L1_ffn_down_w", "dwbf", e, t, c4 * 1024))
    for (nm, dn, e, t, c0) in jobs:
        half = bi % 2
        bi += 1
        src = ctx.dram[nm][e].rearrange("(t p) n -> p t n", p=128)[:, t, c0:c0 + 1024]
        dst = ctx.dram[dn][e].rearrange("(t p) n -> p t n", p=128)[:, t, c0:c0 + 1024]
        stg_w = stg.h[:, half * 1024:(half + 1) * 1024]
        ncols = 1024 if dn != "dwbf" else 4096
        rlo = e * 4096 * 1024 + t * 128 * ncols

        def emit(src=src, dst=dst, stg_w=stg_w, nm=nm, dn=dn, rlo=rlo, half=half):
            ek.dma("gpsimd", lambda g: g.dma_start(stg_w, src),
                   reads=[dreg(nm, rlo, rlo + 128 * 1024)],
                   writes=[(stg, half * 1024, (half + 1) * 1024)], slot="st" + str(half))
            ek.dma("sync", lambda sy: sy.dma_start(dst, stg_w),
                   reads=[(stg, half * 1024, (half + 1) * 1024)],
                   writes=[dreg(dn, rlo, rlo + 128 * 1024)], slot="ss" + str(half))
        chunks.append(emit)
    return chunks


def make_repack_chunks(ctx, b):
    ek = ctx.ek
    stg = b["rpk_stg"]
    chunks = []
    bi = 0
    for e in range(8):
        for dn, dT in (("gwbf", "gwT"), ("uwbf", "uwT")):
            for icg in range(8):
                for hcp in range(4):
                    half = bi % 2
                    bi += 1

                    def emit(e=e, dn=dn, dT=dT, icg=icg, hcp=hcp, half=half):
                        for hci in range(2):
                            hc = hcp * 2 + hci
                            src = ctx.dram[dn][e, icg * 512:(icg + 1) * 512, hc * 128:(hc + 1) * 128]
                            dst = stg.h[:, half * 1024 + hci * 512: half * 1024 + (hci + 1) * 512]
                            ek.dma("sync", lambda sy, src=src, dst=dst: sy.dma_start(dst, src, transpose=True),
                                   reads=[dreg(dn, e * 4096 * 1024 + icg * 512 * 1024,
                                               e * 4096 * 1024 + (icg + 1) * 512 * 1024)],
                                   writes=[(stg, half * 1024 + hci * 512, half * 1024 + (hci + 1) * 512)],
                                   slot="rpk" + str(half))
                        dstT = ctx.dram[dT][e, icg, :, hcp * 1024:(hcp + 1) * 1024]
                        ek.dma("gpsimd", lambda g, dstT=dstT, half=half: g.dma_start(
                            dstT, stg.h[:, half * 1024:(half + 1) * 1024]),
                            reads=[(stg, half * 1024, (half + 1) * 1024)],
                            writes=[dreg(dT, (e * 8 + icg) * 128 * 4096 + hcp * 128 * 1024,
                                         (e * 8 + icg) * 128 * 4096 + (hcp + 1) * 128 * 1024)],
                            slot="rps" + str(half))
                    chunks.append(emit)
        for ihalf in range(2):
            for ic in range(16):
                half = bi % 2
                bi += 1

                def emit(e=e, ihalf=ihalf, ic=ic, half=half):
                    icg_ = ihalf * 16 + ic
                    src = ctx.dram["dwbf"][e, :, icg_ * 128:(icg_ + 1) * 128]
                    dst = stg.h[:, half * 1024:(half + 1) * 1024]
                    ek.dma("sync", lambda sy, src=src, dst=dst: sy.dma_start(dst, src, transpose=True),
                           reads=[dreg("dwbf", e * 1024 * 4096, (e + 1) * 1024 * 4096)],
                           writes=[(stg, half * 1024, (half + 1) * 1024)],
                           slot="rpk" + str(half))
                    dstT = ctx.dram["dwT"][e, ihalf, :, ic * 1024:(ic + 1) * 1024]
                    ek.dma("gpsimd", lambda g, dstT=dstT, half=half: g.dma_start(
                        dstT, stg.h[:, half * 1024:(half + 1) * 1024]),
                        reads=[(stg, half * 1024, (half + 1) * 1024)],
                        writes=[dreg("dwT", (e * 2 + ihalf) * 128 * 16384 + ic * 128 * 1024,
                                     (e * 2 + ihalf) * 128 * 16384 + (ic + 1) * 128 * 1024)],
                        slot="rps" + str(half))
                chunks.append(emit)
    return chunks


def emit_phase_b_routing(ctx, b, tt0, te):
    ek, c, T = ctx.ek, ctx.c, ctx.T
    xn = b["xn"]
    for tci in range(te // 128):
        tc0 = tt0 + tci * 128
        lg_ap, lg_reg = pfb(ctx, 4, 0, 128)
        lhsTs = [xn[:, dc * T + tc0: dc * T + tc0 + 128] for dc in range(8)]
        rhss = [b["rwT"][:, dc * 8:(dc + 1) * 8] for dc in range(8)]
        mm_chain(ctx, lg_ap[:, 0:8], lg_reg, lhsTs, rhss,
                 reads=[b["rwT"]] + [(xn, dc * T + tc0, dc * T + tc0 + 128) for dc in range(8)])
        ek.op("vector", lambda v, lg_ap=lg_ap: v.tensor_add(b["l_sb"].ap, lg_ap[:, 0:8], b["bias_b"].ap),
              reads=[lg_reg, b["bias_b"]], writes=[b["l_sb"]])
        ek.op("vector", lambda v: v.reduce_max(b["m1"][:, 0:1], b["l_sb"].ap, axis=AX.X),
              reads=[b["l_sb"]], writes=[(b["m1"], 0, 1)])
        ek.op("vector", lambda v: v.tensor_single_scalar(b["msk"].ap, b["l_sb"].ap, b["m1"][:, 0:1], ALU.is_lt),
              reads=[b["l_sb"], (b["m1"], 0, 1)], writes=[b["msk"]])
        ek.op("vector", lambda v: v.memset(b["l_ms"].ap, NEG_BIG), writes=[b["l_ms"]])
        ek.op("vector", lambda v: v.copy_predicated(b["l_ms"].ap, b["msk"].ap, b["l_sb"].ap),
              reads=[b["msk"], b["l_sb"], b["l_ms"]], writes=[b["l_ms"]])
        ek.op("vector", lambda v: v.reduce_max(b["m1"][:, 1:2], b["l_ms"].ap, axis=AX.X),
              reads=[b["l_ms"]], writes=[(b["m1"], 1, 2)])
        ek.op("vector", lambda v: v.tensor_single_scalar(b["m1"][:, 2:3], b["m1"][:, 0:1], -1.0, ALU.mult),
              reads=[(b["m1"], 0, 1)], writes=[(b["m1"], 2, 3)])
        ek.op("scalar", lambda s: s.activation(b["el"].ap, b["l_sb"].ap, AF.Exp, bias=b["m1"][:, 2:3]),
              reads=[b["l_sb"], (b["m1"], 2, 3)], writes=[b["el"]])
        ek.op("vector", lambda v: v.tensor_add(b["m1"][:, 3:4], b["m1"][:, 1:2], b["m1"][:, 2:3]),
              reads=[(b["m1"], 1, 3)], writes=[(b["m1"], 3, 4)])
        ek.op("scalar", lambda s: s.activation(b["m1"][:, 3:4], b["m1"][:, 3:4], AF.Exp),
              reads=[(b["m1"], 3, 4)], writes=[(b["m1"], 3, 4)])
        ek.op("vector", lambda v: v.tensor_single_scalar(b["m1"][:, 3:4], b["m1"][:, 3:4], 1.0, ALU.add),
              reads=[(b["m1"], 3, 4)], writes=[(b["m1"], 3, 4)])
        ek.op("vector", lambda v: v.reciprocal(b["m1"][:, 3:4], b["m1"][:, 3:4]),
              reads=[(b["m1"], 3, 4)], writes=[(b["m1"], 3, 4)])
        ek.op("vector", lambda v: v.tensor_single_scalar(b["l_ms"].ap, b["l_sb"].ap, b["m1"][:, 1:2], ALU.is_ge),
              reads=[b["l_sb"], (b["m1"], 1, 2)], writes=[b["l_ms"]])
        ek.op("vector", lambda v: v.tensor_mul(b["el"].ap, b["el"].ap, b["l_ms"].ap),
              reads=[b["el"], b["l_ms"]], writes=[b["el"]])
        ek.op("vector", lambda v: v.tensor_scalar_mul(b["cw16"].ap, b["el"].ap, b["m1"][:, 3:4]),
              reads=[b["el"], (b["m1"], 3, 4)], writes=[b["cw16"]])
        for e in range(8):
            pt_ap, pt_reg = ptb(ctx, 0, e * 128, (e + 1) * 128)
            ek.op("tensor", lambda t, e=e, pt_ap=pt_ap: t.transpose(
                pt_ap[0:1, :], b["cw16"][:, e:e + 1], c["ident"].ap),
                reads=[(b["cw16"], e, e + 1), c["ident"]], writes=[pt_reg])
        for e in range(8):
            pt_ap = ctx.pt[0:1, e * 128:(e + 1) * 128]
            ek.op("vector", lambda v, e=e, pt_ap=pt_ap, tci=tci: v.tensor_copy(
                b["cwrow"][0:1, e * 1024 + tci * 128: e * 1024 + (tci + 1) * 128], pt_ap),
                reads=[(ctx.pt, e * 128, (e + 1) * 128)],
                writes=[(b["cwrow"], e * 1024 + tci * 128, e * 1024 + (tci + 1) * 128)])


def emit_phase_b(ctx, b, x_prev_name, x_next_name, stash_chunks=None):
    ek, c, T = ctx.ek, ctx.c, ctx.T
    xn = b["xn"]
    stash_chunks = list(stash_chunks or [])
    n_iter = (T // 1024) * 9
    per_iter = -(-len(stash_chunks) // n_iter) if stash_chunks else 0
    si = 0
    ek.dma("gpsimd", lambda g: g.dma_start(b["rtmp"][0:8, :], ctx.dram["L0_ffn_router_w"][:, :]),
           reads=[dreg("L0_ffn_router_w")], writes=[b["rtmp"]], slot="rwT")
    for dc in range(8):
        pt_ap, pt_reg = ptb(ctx, 1, 0, 8)
        ek.op("tensor", lambda t, dc=dc, pt_ap=pt_ap: t.transpose(
            pt_ap, b["rtmp"][0:8, dc * 128:(dc + 1) * 128], c["ident"][0:8, 0:8]),
            reads=[b["rtmp"], c["ident"]], writes=[pt_reg])
        ek.op("vector", lambda v, dc=dc, pt_ap=pt_ap: v.tensor_copy(b["rwT"][:, dc * 8:(dc + 1) * 8], pt_ap),
              reads=[pt_reg], writes=[(b["rwT"], dc * 8, (dc + 1) * 8)])
    ek.dma("sync", lambda sy: sy.dma_start(b["bias_row"][0:1, :], ctx.dram["L0_ffn_experts_bias"][None, :]),
           reads=[dreg("L0_ffn_experts_bias")], writes=[b["bias_row"]], slot="br")
    bb_ap, bb_reg = pfb(ctx, 5, 0, 8)
    ek.group("tensor", lambda t: [t.matmul(bb_ap[:, 0:8], c["ones_colf"].ap, b["bias_row"][0:1, :],
                                           start=True, stop=True)],
             reads=[c["ones_colf"], b["bias_row"]], writes=[bb_reg])
    ek.op("vector", lambda v: v.tensor_copy(b["bias_b"].ap, bb_ap[:, 0:8]), reads=[bb_reg], writes=[b["bias_b"]])
    for tt0 in range(0, T, 1024):
        te = min(1024, T - tt0)
        emit_phase_b_routing(ctx, b, tt0, te)
        for e in range(9):
            for _ in range(per_iter):
                if si < len(stash_chunks):
                    stash_chunks[si]()
                    si += 1
            shared = (e == 8)
            w1_name = "L0_ffn_sw1" if shared else "L0_ffn_w1"
            w3_name = "L0_ffn_sw3" if shared else "L0_ffn_w3"
            w2_name = "L0_ffn_sw2" if shared else "L0_ffn_w2"
            ei = 0 if shared else e
            if not shared:
                for q in range(te // 512):
                    cb_ap, cb_reg = pfb(ctx, 5, 0, 512)
                    ek.group("tensor", lambda t, e=e, q=q, cb_ap=cb_ap: [
                        t.matmul(cb_ap, c["ones_col"].ap,
                                 b["cwrow"][0:1, e * 1024 + q * 512: e * 1024 + (q + 1) * 512],
                                 start=True, stop=True)],
                        reads=[c["ones_col"], (b["cwrow"], e * 1024 + q * 512, e * 1024 + (q + 1) * 512)],
                        writes=[cb_reg])
                    ek.op("vector", lambda v, q=q, cb_ap=cb_ap: v.tensor_copy(
                        b["cwb"][:, q * 512:(q + 1) * 512], cb_ap),
                        reads=[cb_reg], writes=[(b["cwb"], q * 512, (q + 1) * 512)])
            for hh in range(2):
                for hc8 in range(8):
                    hc = hh * 8 + hc8
                    half = hc8 % 2
                    for wbuf, wname in ((b["w1s"], w1_name), (b["w3s"], w3_name)):
                        src = ctx.dram[wname][ei].rearrange("(dc p) n -> p dc n", p=128)[:, :, hc * 128:(hc + 1) * 128]
                        dst = wbuf.h[:, half * 1024:(half + 1) * 1024].rearrange("p (dc n) -> p dc n", dc=8)
                        ek.dma("gpsimd", lambda g, src=src, dst=dst: g.dma_start(dst, src),
                               reads=[dreg(wname, (ei * 16 + hc) * 131072, (ei * 16 + hc + 1) * 131072)],
                               writes=[(wbuf, half * 1024, (half + 1) * 1024)],
                               slot=wbuf.h.name + str(half))
                    for ts_ in range(te // 512):
                        tq0 = tt0 + ts_ * 512
                        g_ap, g_reg = pfb(ctx, 4, 0, 512)
                        u_ap, u_reg = pfb(ctx, 5, 0, 512)
                        for (wbuf, o_ap, o_reg) in ((b["w1s"], g_ap, g_reg), (b["w3s"], u_ap, u_reg)):
                            lhsTs = [wbuf[:, half * 1024 + dc * 128: half * 1024 + (dc + 1) * 128] for dc in range(8)]
                            rhss = [xn[:, dc * T + tq0: dc * T + tq0 + 512] for dc in range(8)]
                            mm_chain(ctx, o_ap, o_reg, lhsTs, rhss,
                                     reads=[(wbuf, half * 1024, (half + 1) * 1024)] +
                                           [(xn, dc * T + tq0, dc * T + tq0 + 512) for dc in range(8)])
                        st = b["stat_tmp"]
                        ek.op("scalar", lambda s, g_ap=g_ap, st=st: s.activation(st[:, 0:512], g_ap, AF.Silu),
                              reads=[g_reg], writes=[(st, 0, 512)])
                        eo_lo = hc8 * 1024 + ts_ * 512
                        if shared:
                            ek.op("vector", lambda v, u_ap=u_ap, eo_lo=eo_lo, st=st: v.scalar_tensor_tensor(
                                b["eo"][:, eo_lo:eo_lo + 512], u_ap, 1.0, st[:, 0:512], ALU.mult, ALU.mult),
                                reads=[u_reg, (st, 0, 512)], writes=[(b["eo"], eo_lo, eo_lo + 512)])
                        else:
                            ek.op("vector", lambda v, u_ap=u_ap, st=st: v.scalar_tensor_tensor(
                                st[:, 512:1024], u_ap, 1.0, st[:, 0:512], ALU.mult, ALU.mult),
                                reads=[u_reg, (st, 0, 512)], writes=[(st, 512, 1024)])
                            ek.op("vector", lambda v, eo_lo=eo_lo, ts_=ts_, st=st: v.tensor_mul(
                                b["eo"][:, eo_lo:eo_lo + 512], st[:, 512:1024], b["cwb"][:, ts_ * 512:(ts_ + 1) * 512]),
                                reads=[(st, 512, 1024), (b["cwb"], ts_ * 512, (ts_ + 1) * 512)],
                                writes=[(b["eo"], eo_lo, eo_lo + 512)])
                for hc8 in range(8):
                    hc = hh * 8 + hc8
                    src = ctx.dram[w2_name][ei].rearrange("(hc p) n -> p hc n", p=128)[:, hc:hc + 1, :]
                    dst = b["w2s"].h[:, hc8 * 1024:(hc8 + 1) * 1024].rearrange("p (o n) -> p o n", o=1)
                    ek.dma("gpsimd", lambda g, src=src, dst=dst: g.dma_start(dst, src),
                           reads=[dreg(w2_name, (ei * 16 + hc) * 131072, (ei * 16 + hc + 1) * 131072)],
                           writes=[(b["w2s"], hc8 * 1024, (hc8 + 1) * 1024)],
                           slot="w2s" + str(hc8 % 2))
                for ts4 in range(te // 256):
                    for dc in range(8):
                        y_ap, y_reg = pfb(ctx, dc // 2, (dc % 2) * 256, (dc % 2) * 256 + 256)
                        lhsTs = [b["w2s"][:, hc8_ * 1024 + dc * 128: hc8_ * 1024 + (dc + 1) * 128] for hc8_ in range(8)]
                        rhss = [b["eo"][:, hc8_ * 1024 + ts4 * 256: hc8_ * 1024 + ts4 * 256 + 256] for hc8_ in range(8)]
                        mm_chain(ctx, y_ap, y_reg, lhsTs, rhss,
                                 reads=[b["w2s"], b["eo"]])
                        ya_lo = dc * 1024 + ts4 * 256
                        if e == 0 and hh == 0:
                            ek.op("vector", lambda v, y_ap=y_ap, ya_lo=ya_lo: v.tensor_copy(
                                b["y_acc"][:, ya_lo:ya_lo + 256], y_ap),
                                reads=[y_reg], writes=[(b["y_acc"], ya_lo, ya_lo + 256)])
                        else:
                            ek.op("vector", lambda v, y_ap=y_ap, ya_lo=ya_lo: v.tensor_add(
                                b["y_acc"][:, ya_lo:ya_lo + 256], b["y_acc"][:, ya_lo:ya_lo + 256], y_ap),
                                reads=[y_reg, (b["y_acc"], ya_lo, ya_lo + 256)],
                                writes=[(b["y_acc"], ya_lo, ya_lo + 256)])
        for cc4 in range(te // 256):
            t0 = tt0 + cc4 * 256
            emit_residual_tile_cols(ctx, b, t0, 256, "y_acc",
                                    lambda dc, cc4=cc4: dc * 1024 + cc4 * 256,
                                    x_prev_name, x_next_name)
            emit_ln_from_xtile(ctx, b, t0, 256)


def emit_residual_tile_cols(ctx, bufs, t0, W, y_src, ycol, x_prev_name, x_next_name, store=True):
    ek, T = ctx.ek, ctx.T
    xt, ysb = bufs["xtile"], bufs[y_src]
    xprev = ctx.dram[x_prev_name]
    src = xprev.rearrange("(c p) t -> p c t", p=128)[:, :, t0:t0 + W]
    ek.dma("sync", lambda sy: sy.dma_start(xt.h[:, 0:8 * W].rearrange("p (c w) -> p c w", c=8), src),
           reads=[dreg(x_prev_name, t0, t0 + W)], writes=[(xt, 0, 8 * W)], slot="xt")
    for cc in range(8):
        yl = ycol(cc)
        ek.op("vector", lambda v, cc=cc, yl=yl: v.tensor_add(
            xt[:, cc * W:(cc + 1) * W], ysb[:, yl:yl + W], xt[:, cc * W:(cc + 1) * W]),
            reads=[(ysb, yl, yl + W), (xt, cc * W, (cc + 1) * W)],
            writes=[(xt, cc * W, (cc + 1) * W)])
    if store:
        xnext = ctx.dram[x_next_name]
        dst = xnext.rearrange("(c p) t -> p c t", p=128)[:, :, t0:t0 + W]
        ek.dma("sync", lambda sy: sy.dma_start(dst, xt.h[:, 0:8 * W].rearrange("p (c w) -> p c w", c=8)),
               reads=[(xt, 0, 8 * W)], writes=[dreg(x_next_name, t0, t0 + W)], slot="xt")


def alloc_phase_c_bufs(ctx, b, with_repack=False):
    mem, T = ctx.mem, ctx.T
    b["hf"] = mem.sb("hf", 128, 8 * T, BF16)
    b["gma"] = mem.sb("gma", 128, 8, F32)
    b["bta"] = mem.sb("bta", 128, 8, F32)
    if with_repack:
        b["rpk_stg"] = mem.sb("rpk_stg", 128, 2 * 1024, BF16)


def _alloc_c_head_bufs(ctx, b):
    mem, T = ctx.mem, ctx.T
    b["wqh"] = mem.sb("wqh", 128, 512, BF16)
    b["wkh"] = mem.sb("wkh", 128, 512, BF16)
    b["wvh"] = mem.sb("wvh", 128, 512, BF16)
    b["q_f"] = mem.sb("q_f", 65, T, BF16)
    b["k_f"] = mem.sb("k_f", 65, T, BF16)
    b["v_tm"] = mem.sb("v_tm", 128, (T // 128) * 65, BF16)
    b["vstage"] = mem.sb("vstage", 64, 512, BF16)
    b["qrow"] = mem.sb("qrow", 1, T, BF16)
    b["mxk"] = mem.sb("mxk", 1, 2, F32)
    b["PT"] = mem.sb("PT", 128, 2 * 512, BF16)
    b["rs1"] = mem.sb("rs1", 1, 512, F32)
    b["ou_sb"] = mem.sb("ou_sb", 64, 512, F32)
    b["cstrip"] = mem.sb("cstrip", 128, 4 * 512, F32)


def _alloc_c_tail_bufs(ctx, b):
    mem = ctx.mem
    b["wo"] = mem.sb("wo", 128, 8 * 1024, BF16)
    b["yln"] = mem.sb("yln", 128, 8 * 256, F32)
    b["y_sb"] = mem.sb("y_sb_c", 128, 8 * 256, F32)


def emit_phase_c(ctx, b, x_prev_name, x_next_name, repack_chunks=None):
    ek, c, T = ctx.ek, ctx.c, ctx.T
    xn = b["xn"]
    repack_chunks = list(repack_chunks or [])
    per_head = -(-len(repack_chunks) // 16) if repack_chunks else 0
    ri = 0
    ek.dma("gpsimd", lambda g: g.dma_start(
        b["gma"].ap, ctx.dram["L1_attn_gamma"].rearrange("(dc p) -> p dc", p=128),
        allow_slow_non_contiguous=True),
        reads=[dreg("L1_attn_gamma")], writes=[b["gma"]], slot="gma")
    ek.dma("gpsimd", lambda g: g.dma_start(
        b["bta"].ap, ctx.dram["L1_attn_beta"].rearrange("(dc p) -> p dc", p=128),
        allow_slow_non_contiguous=True),
        reads=[dreg("L1_attn_beta")], writes=[b["bta"]], slot="bta")
    ctx.mem.push()
    _alloc_c_head_bufs(ctx, b)
    cs = b["cstrip"]
    ek.op("gpsimd", lambda g: g.iota(cs[:, 3 * 512:4 * 512], [[1, 512]], channel_multiplier=0,
                                     allow_small_or_imprecise_dtypes=True),
          writes=[(cs, 3 * 512, 4 * 512)])
    for j in (0, 1, 2, 3):
        dst = cs[:, j * 512:(j + 1) * 512]
        ek.op("vector", lambda v, dst=dst, j=j: v.tensor_scalar(
            dst, cs[:, 3 * 512:4 * 512], c["iota_p"][:, 0:1], float(128 * j),
            ALU.subtract, ALU.is_ge),
            reads=[(cs, 3 * 512, 4 * 512), (c["iota_p"], 0, 1)],
            writes=[(cs, j * 512, (j + 1) * 512)])
        ek.op("vector", lambda v, dst=dst: v.tensor_single_scalar(dst, dst, 1.0, ALU.subtract),
              reads=[(cs, j * 512, (j + 1) * 512)], writes=[(cs, j * 512, (j + 1) * 512)])
        ek.op("vector", lambda v, dst=dst: v.tensor_single_scalar(dst, dst, 1.0e30, ALU.mult),
              reads=[(cs, j * 512, (j + 1) * 512)], writes=[(cs, j * 512, (j + 1) * 512)])
    ek.op("vector", lambda v: v.memset(b["v_tm"].ap, 1.0), writes=[b["v_tm"]])
    for h in range(16):
        for _ in range(per_head):
            if ri < len(repack_chunks):
                repack_chunks[ri]()
                ri += 1
        _emit_c_head(ctx, b, h)
    ctx.mem.pop()
    _alloc_c_tail_bufs(ctx, b)
    load_weight_cast(ctx, b["wo"], ctx.dram["L1_attn_Wo"], "L1_attn_Wo", 8, 1024)
    for ti in range(T // 256):
        t0 = ti * 256
        if ti == 0 and "c_hf0" in ctx.dram:
            ek.dma("sync", lambda sy: sy.dma_start(ctx.dram["c_hf0"][:, :], b["hf"].h[:, 0:8 * T]),
                   reads=[(b["hf"], 0, 8 * T)], writes=[dreg("c_hf0")], slot="dbg2")
            ek.dma("sync", lambda sy: sy.dma_start(ctx.dram["c_wo0"][:, :], b["wo"].h[:, :]),
                   reads=[b["wo"]], writes=[dreg("c_wo0")], slot="dbg2")
            ek.dma("sync", lambda sy: sy.dma_start(ctx.dram["c_xn0"][:, :], b["xn"].h[:, 0:8 * T]),
                   reads=[(b["xn"], 0, 8 * T)], writes=[dreg("c_xn0")], slot="dbg2")
        for dc in range(8):
            y_ap, y_reg = pfb(ctx, dc // 2, (dc % 2) * 256, (dc % 2) * 256 + 256)
            lhsTs = [b["wo"][:, hec * 1024 + dc * 128: hec * 1024 + (dc + 1) * 128] for hec in range(8)]
            rhss = [b["hf"][:, hec * T + t0: hec * T + t0 + 256] for hec in range(8)]
            mm_chain(ctx, y_ap, y_reg, lhsTs, rhss,
                     reads=[b["wo"], b["hf"]])
            ek.op("vector", lambda v, dc=dc, y_ap=y_ap, t0=t0: v.tensor_add(
                b["y_sb"][:, dc * 256:(dc + 1) * 256], y_ap, xn[:, dc * T + t0: dc * T + t0 + 256]),
                reads=[y_reg, (xn, dc * T + t0, dc * T + t0 + 256)],
                writes=[(b["y_sb"], dc * 256, (dc + 1) * 256)])
        if ti == 0 and "c_ysb0" in ctx.dram:
            ek.dma("sync", lambda sy: sy.dma_start(ctx.dram["c_ysb0"][:, :], b["y_sb"].h[:, 0:8 * 256]),
                   reads=[(b["y_sb"], 0, 8 * 256)], writes=[dreg("c_ysb0")], slot="dbg2")
        _emit_c_inner_ln(ctx, b, 256)
        if ti == 0 and "c_yln0" in ctx.dram:
            ek.dma("sync", lambda sy: sy.dma_start(ctx.dram["c_yln0"][:, :], b["yln"].h[:, 0:8 * 256]),
                   reads=[(b["yln"], 0, 8 * 256)], writes=[dreg("c_yln0")], slot="dbg2")
        emit_residual_tile(ctx, b, t0, 256, "yln", x_prev_name, x_next_name)
        emit_ln_from_xtile(ctx, b, t0, 256)


def _emit_c_inner_ln(ctx, b, W):
    ek, c = ctx.ek, ctx.c
    ysb, tmp, st = b["y_sb"], b["stat_tmp"], b["stat_small"]
    s1_ap, s1_reg = pfb(ctx, 4, 0, W)
    s2_ap, s2_reg = pfb(ctx, 5, 0, W)

    def s1_fns(t):
        return [t.matmul(s1_ap[0:1, :], c["ones128f"].ap, ysb[:, cc * W:(cc + 1) * W],
                         start=(cc == 0), stop=(cc == 7)) for cc in range(8)]

    ek.group("tensor", s1_fns, reads=[(ysb, 0, 8 * W), c["ones128f"]], writes=[s1_reg])
    for half in range(2):
        for i in range(4):
            cc = half * 4 + i
            ek.op("scalar", lambda s, cc=cc, i=i: s.activation(
                tmp[:, i * W:(i + 1) * W], ysb[:, cc * W:(cc + 1) * W], AF.Square),
                reads=[(ysb, cc * W, (cc + 1) * W)], writes=[(tmp, i * W, (i + 1) * W)])

        def s2_fns(t, half=half):
            return [t.matmul(s2_ap[0:1, :], c["ones128f"].ap, tmp[:, i * W:(i + 1) * W],
                             start=(half == 0 and i == 0), stop=(half == 1 and i == 3))
                    for i in range(4)]

        ek.group("tensor", s2_fns, reads=[(tmp, 0, 4 * W), c["ones128f"]], writes=[s2_reg])
    m = st
    ek.op("vector", lambda v: v.tensor_single_scalar(m[0:1, 0:W], s1_ap[0:1, :], 1.0 / 1024.0, ALU.mult),
          reads=[s1_reg], writes=[(st, 0, W)])
    ek.op("vector", lambda v: v.tensor_single_scalar(m[0:1, 3 * W:4 * W], s2_ap[0:1, :], 1.0 / 1024.0, ALU.mult),
          reads=[s2_reg], writes=[(st, 3 * W, 4 * W)])
    ek.op("vector", lambda v: v.tensor_mul(m[0:1, W:2 * W], m[0:1, 0:W], m[0:1, 0:W]),
          reads=[(st, 0, W)], writes=[(st, W, 2 * W)])
    ek.op("vector", lambda v: v.tensor_sub(m[0:1, W:2 * W], m[0:1, 3 * W:4 * W], m[0:1, W:2 * W]),
          reads=[(st, W, 2 * W), (st, 3 * W, 4 * W)], writes=[(st, W, 2 * W)])
    ek.op("scalar", lambda s: s.activation(m[0:1, W:2 * W], m[0:1, W:2 * W], AF.Ln,
                                           bias=c["cpool"][0:1, 2:3], scale=1.0),
          reads=[(st, W, 2 * W), (c["cpool"], 2, 3)], writes=[(st, W, 2 * W)])
    ek.op("scalar", lambda s: s.activation(m[0:1, 2 * W:3 * W], m[0:1, W:2 * W], AF.Exp, scale=-0.5),
          reads=[(st, W, 2 * W)], writes=[(st, 2 * W, 3 * W)])
    mb_ap, mb_reg = pfb(ctx, 4, 0, W)
    rb_ap, rb_reg = pfb(ctx, 5, 0, W)
    ek.group("tensor", lambda t: [t.matmul(mb_ap, c["ones_colf"].ap, m[0:1, 0:W], start=True, stop=True)],
             reads=[(st, 0, W), c["ones_colf"]], writes=[mb_reg])
    ek.group("tensor", lambda t: [t.matmul(rb_ap, c["ones_colf"].ap, m[0:1, 2 * W:3 * W], start=True, stop=True)],
             reads=[(st, 2 * W, 3 * W), c["ones_colf"]], writes=[rb_reg])
    for cc in range(8):
        sl = (cc % 4) * W
        ek.op("vector", lambda v, cc=cc, sl=sl: v.tensor_sub(
            tmp[:, sl:sl + W], ysb[:, cc * W:(cc + 1) * W], mb_ap),
            reads=[(ysb, cc * W, (cc + 1) * W), mb_reg], writes=[(tmp, sl, sl + W)])
        ek.op("vector", lambda v, cc=cc, sl=sl: v.scalar_tensor_tensor(
            tmp[:, sl:sl + W], tmp[:, sl:sl + W], b["gma"][:, cc:cc + 1], rb_ap, ALU.mult, ALU.mult),
            reads=[(tmp, sl, sl + W), b["gma"], rb_reg], writes=[(tmp, sl, sl + W)])
        ek.op("vector", lambda v, cc=cc, sl=sl: v.tensor_scalar_add(
            b["yln"][:, cc * W:(cc + 1) * W], tmp[:, sl:sl + W], b["bta"][:, cc:cc + 1]),
            reads=[(tmp, sl, sl + W), b["bta"]], writes=[(b["yln"], cc * W, (cc + 1) * W)])


def _emit_c_head(ctx, b, h):
    ek, c, T = ctx.ek, ctx.c, ctx.T
    xn = b["xn"]
    hc2 = h // 2
    prow = (h % 2) * 64
    for wbuf, wname in ((b["wqh"], "L1_attn_Wq"), (b["wkh"], "L1_attn_Wk"), (b["wvh"], "L1_attn_Wv")):
        src = ctx.dram[wname][h].rearrange("(dc p) e -> p dc e", p=128)
        dst = wbuf.h[:, 0:512].rearrange("p (dc e) -> p dc e", dc=8)
        ek.dma("gpsimd", lambda g, src=src, dst=dst: g.dma_start(dst, src),
               reads=[dreg(wname, h, h + 1)], writes=[(wbuf, 0, 512)],
               slot=wbuf.h.name)
    for (wbuf, dstbuf, bank, scale) in ((b["wkh"], b["k_f"], 0, 1.0), (b["wqh"], b["q_f"], 1, 0.125)):
        for tt in range(T // 512):
            o_ap, o_reg = pfb(ctx, bank, 0, 512)
            lhsTs = [wbuf[:, dc * 64: (dc + 1) * 64] for dc in range(8)]
            rhss = [xn[:, dc * T + tt * 512: dc * T + (tt + 1) * 512] for dc in range(8)]
            mm_chain(ctx, o_ap[0:64, :], o_reg, lhsTs, rhss,
                     reads=[(wbuf, 0, 512)] +
                           [(xn, dc * T + tt * 512, dc * T + (tt + 1) * 512) for dc in range(8)])
            ek.op("scalar", lambda s, tt=tt, o_ap=o_ap, dstbuf=dstbuf, scale=scale: s.activation(
                dstbuf.h[0:64, tt * 512:(tt + 1) * 512], o_ap[0:64, :], AF.Copy, scale=scale),
                reads=[o_reg], writes=[(dstbuf, tt * 512, (tt + 1) * 512)])
    for tt in range(T // 512):
        o_ap, o_reg = pfb(ctx, 2, 0, 512)
        lhsTs = [b["wvh"][:, dc * 64: (dc + 1) * 64] for dc in range(8)]
        rhss = [xn[:, dc * T + tt * 512: dc * T + (tt + 1) * 512] for dc in range(8)]
        mm_chain(ctx, o_ap[0:64, :], o_reg, lhsTs, rhss,
                 reads=[(b["wvh"], 0, 512)] +
                       [(xn, dc * T + tt * 512, dc * T + (tt + 1) * 512) for dc in range(8)])
        ek.op("scalar", lambda s, o_ap=o_ap: s.copy(b["vstage"][0:64, :], o_ap[0:64, :]),
              reads=[o_reg], writes=[b["vstage"]])
        for j in range(4):
            pt_ap, pt_reg = ptb(ctx, 0, j * 128, j * 128 + 64)
            ek.op("tensor", lambda t, j=j, pt_ap=pt_ap: t.transpose(
                pt_ap, b["vstage"][0:64, j * 128:(j + 1) * 128], c["ident"][0:64, 0:64]),
                reads=[b["vstage"], c["ident"]], writes=[pt_reg])
        for j in range(4):
            tc = tt * 4 + j
            ek.op("vector", lambda v, j=j, tc=tc: v.tensor_copy(
                b["v_tm"][:, tc * 65: tc * 65 + 64], ctx.pt[:, j * 128: j * 128 + 64]),
                reads=[(ctx.pt, j * 128, j * 128 + 64)], writes=[(b["v_tm"], tc * 65, tc * 65 + 64)])
    for tt in range(T // 512):
        ek.op("scalar", lambda s, tt=tt: s.activation(
            b["stat_tmp"].h[0:64, 0:512], b["q_f"][0:64, tt * 512:(tt + 1) * 512], AF.Square),
            reads=[(b["q_f"], tt * 512, (tt + 1) * 512)], writes=[(b["stat_tmp"], 0, 512)])
        qs_ap, qs_reg = pfb(ctx, 4, 0, 512)
        ek.group("tensor", lambda t, qs_ap=qs_ap: [
            t.matmul(qs_ap[0:1, :], c["ones128f"][0:64, :], b["stat_tmp"].h[0:64, 0:512],
                     start=True, stop=True)],
            reads=[(b["stat_tmp"], 0, 512), c["ones128f"]], writes=[qs_reg])
        ek.op("vector", lambda v, tt=tt, qs_ap=qs_ap: v.tensor_copy(
            b["qrow"][0:1, tt * 512:(tt + 1) * 512], qs_ap[0:1, :]),
            reads=[qs_reg], writes=[(b["qrow"], tt * 512, (tt + 1) * 512)])
    ek.op("vector", lambda v: v.memset(b["mxk"][0:1, 0:1], 0.0), writes=[(b["mxk"], 0, 1)])
    for tt in range(T // 512):
        ek.op("scalar", lambda s, tt=tt: s.activation(
            b["stat_tmp"].h[0:64, 0:512], b["k_f"][0:64, tt * 512:(tt + 1) * 512], AF.Square),
            reads=[(b["k_f"], tt * 512, (tt + 1) * 512)], writes=[(b["stat_tmp"], 0, 512)])
        kn_ap, kn_reg = pfb(ctx, 5, 0, 512)
        ek.group("tensor", lambda t, kn_ap=kn_ap: [
            t.matmul(kn_ap[0:1, :], c["ones128f"][0:64, :], b["stat_tmp"].h[0:64, 0:512],
                     start=True, stop=True)],
            reads=[(b["stat_tmp"], 0, 512), c["ones128f"]], writes=[kn_reg])
        ek.op("vector", lambda v, kn_ap=kn_ap: v.reduce_max(
            b["stat_small"][0:1, 0:1], kn_ap[0:1, :], axis=AX.X),
            reads=[kn_reg], writes=[(b["stat_small"], 0, 1)])
        ek.op("vector", lambda v: v.tensor_max(b["mxk"][0:1, 0:1], b["mxk"][0:1, 0:1], b["stat_small"][0:1, 0:1]),
              reads=[(b["mxk"], 0, 1), (b["stat_small"], 0, 1)], writes=[(b["mxk"], 0, 1)])
    ek.op("vector", lambda v: v.tensor_scalar_mul(b["qrow"][0:1, :], b["qrow"][0:1, :], b["mxk"][0:1, 0:1]),
          reads=[(b["qrow"], 0, T), (b["mxk"], 0, 1)], writes=[(b["qrow"], 0, T)])
    ek.op("scalar", lambda s: s.activation(b["qrow"][0:1, :], b["qrow"][0:1, :], AF.Ln),
          reads=[(b["qrow"], 0, T)], writes=[(b["qrow"], 0, T)])
    ek.op("scalar", lambda s: s.activation(b["qrow"][0:1, :], b["qrow"][0:1, :], AF.Exp, scale=0.5),
          reads=[(b["qrow"], 0, T)], writes=[(b["qrow"], 0, T)])
    ek.op("vector", lambda v: v.tensor_single_scalar(b["q_f"].h[64:65, :], b["qrow"][0:1, :], -1.0, ALU.mult),
          reads=[(b["qrow"], 0, T)], writes=[(b["q_f"], 0, T)])
    ek.op("vector", lambda v: v.memset(b["k_f"].h[64:65, :], 1.0), writes=[(b["k_f"], 0, T)])
    for g in range(T // 512):
        _emit_c_group(ctx, b, h, g, hc2, prow)


def _emit_c_group(ctx, b, h, g, hc2, prow):
    ek, c, T = ctx.ek, ctx.c, ctx.T
    nk = (g + 1) * 4
    q0 = g * 512
    ou_ap, ou_reg = pfb(ctx, 2, 0, 512)

    def emit_av(ki, start, stop):
        slot = ki % 2
        ek.op("tensor", lambda t, ki=ki, slot=slot, start=start, stop=stop: t.matmul(
            ou_ap[0:65, :], b["v_tm"][:, ki * 65:(ki + 1) * 65],
            b["PT"][:, slot * 512:(slot + 1) * 512], start=start, stop=stop),
            reads=[(b["v_tm"], ki * 65, (ki + 1) * 65), (b["PT"], slot * 512, (slot + 1) * 512)],
            writes=[ou_reg])

    for ki in range(nk):
        bank = ki % 2
        s_ap, s_reg = pfb(ctx, bank, 0, 512)
        ek.group("tensor", lambda t, ki=ki, s_ap=s_ap: [t.matmul(
            s_ap, b["k_f"].h[0:65, ki * 128:(ki + 1) * 128], b["q_f"].h[0:65, q0:q0 + 512],
            start=True, stop=True)],
            reads=[(b["k_f"], ki * 128, (ki + 1) * 128), (b["q_f"], q0, q0 + 512)],
            writes=[s_reg])
        if ki >= g * 4:
            j = ki - g * 4
            ek.op("vector", lambda v, s_ap=s_ap, j=j: v.tensor_add(
                s_ap, s_ap, b["cstrip"][:, j * 512:(j + 1) * 512]),
                reads=[s_reg, (b["cstrip"], j * 512, (j + 1) * 512)], writes=[s_reg])
        slot = ki % 2
        ek.op("scalar", lambda s, s_ap=s_ap, slot=slot: s.activation(
            b["PT"][:, slot * 512:(slot + 1) * 512], s_ap, AF.Exp),
            reads=[s_reg], writes=[(b["PT"], slot * 512, (slot + 1) * 512)])
        if ki > 0:
            emit_av(ki - 1, ki - 1 == 0, False)
    emit_av(nk - 1, nk == 1, True)
    ek.op("vector", lambda v: v.tensor_copy(b["rs1"][0:1, :], ou_ap[64:65, :]),
          reads=[ou_reg], writes=[b["rs1"]])
    ek.op("vector", lambda v: v.reciprocal(b["rs1"][0:1, :], b["rs1"][0:1, :]),
          reads=[b["rs1"]], writes=[b["rs1"]])
    rb_ap, rb_reg = pfb(ctx, 3, 0, 512)
    ek.group("tensor", lambda t, rb_ap=rb_ap: [
        t.matmul(rb_ap[0:64, :], c["ones_colf"][0:1, 0:64], b["rs1"][0:1, :], start=True, stop=True)],
        reads=[c["ones_colf"], b["rs1"]], writes=[rb_reg])
    ek.op("scalar", lambda s: s.copy(b["ou_sb"][0:64, :], ou_ap[0:64, :]),
          reads=[ou_reg], writes=[b["ou_sb"]])
    hcol = hc2 * T + g * 512
    ek.op("vector", lambda v, hcol=hcol, rb_ap=rb_ap: v.tensor_mul(
        b["hf"].h[prow:prow + 64, hcol:hcol + 512], b["ou_sb"][0:64, :], rb_ap[0:64, :]),
        reads=[b["ou_sb"], rb_reg], writes=[(b["hf"], hcol, hcol + 512)])


def _unused():
    pass


def alloc_phase_d_bufs(ctx, b):
    mem = ctx.mem
    alloc_routing_bufs(ctx, b, "_d", tstride=512)
    b["wealth_b"] = mem.sb("wealth_b", 128, 8, F32)
    b["wealth_row"] = mem.sb("wealth_row", 1, 8, F32)
    b["bids"] = mem.sb("bids", 128, 8 * 8, F32)
    b["wslab"] = mem.sb("wslab", 128, 16384, BF16)
    b["dwb"] = mem.sb("dwb", 128, 16384, BF16)
    b["eoh"] = mem.sb("eoh", 128, 16 * 512, BF16)
    b["yacc_d"] = mem.sb("yacc_d", 128, 8 * 512, F32)


def emit_phase_d_routing(ctx, b, tt0, te):
    ek, c, T = ctx.ek, ctx.c, ctx.T
    xn = b["xn"]
    ntc = te // 128
    for tci in range(ntc):
        tc0 = tt0 + tci * 128
        lg_ap, lg_reg = pfb(ctx, 4, 0, 128)
        lhsTs = [xn[:, dc * T + tc0: dc * T + tc0 + 128] for dc in range(8)]
        rhss = [b["rwT"][:, dc * 8:(dc + 1) * 8] for dc in range(8)]
        mm_chain(ctx, lg_ap[:, 0:8], lg_reg, lhsTs, rhss,
                 reads=[b["rwT"]] + [(xn, dc * T + tc0, dc * T + tc0 + 128) for dc in range(8)])
        ek.op("vector", lambda v, lg_ap=lg_ap, tci=tci: v.tensor_add(
            b["bids"][:, tci * 8:(tci + 1) * 8], lg_ap[:, 0:8], b["bias_b"].ap),
            reads=[lg_reg, b["bias_b"]], writes=[(b["bids"], tci * 8, (tci + 1) * 8)])
        ek.op("scalar", lambda s, tci=tci: s.activation(
            b["bids"][:, tci * 8:(tci + 1) * 8], b["bids"][:, tci * 8:(tci + 1) * 8], AF.Sigmoid),
            reads=[(b["bids"], tci * 8, (tci + 1) * 8)], writes=[(b["bids"], tci * 8, (tci + 1) * 8)])
        ek.op("vector", lambda v, tci=tci: v.tensor_mul(
            b["bids"][:, tci * 8:(tci + 1) * 8], b["bids"][:, tci * 8:(tci + 1) * 8], b["wealth_b"].ap),
            reads=[(b["bids"], tci * 8, (tci + 1) * 8), b["wealth_b"]],
            writes=[(b["bids"], tci * 8, (tci + 1) * 8)])
    for tci in range(ntc):
        bsl = (b["bids"], tci * 8, (tci + 1) * 8)
        bap = b["bids"][:, tci * 8:(tci + 1) * 8]
        ek.op("vector", lambda v, bap=bap: v.reduce_max(b["m1"][:, 0:1], bap, axis=AX.X),
              reads=[bsl], writes=[(b["m1"], 0, 1)])
        ek.op("vector", lambda v, bap=bap: v.tensor_single_scalar(b["msk"].ap, bap, b["m1"][:, 0:1], ALU.is_lt),
              reads=[bsl, (b["m1"], 0, 1)], writes=[b["msk"]])
        ek.op("vector", lambda v: v.memset(b["l_ms"].ap, NEG_BIG), writes=[b["l_ms"]])
        ek.op("vector", lambda v, bap=bap: v.copy_predicated(b["l_ms"].ap, b["msk"].ap, bap),
              reads=[b["msk"], bsl, b["l_ms"]], writes=[b["l_ms"]])
        ek.op("vector", lambda v: v.reduce_max(b["m1"][:, 1:2], b["l_ms"].ap, axis=AX.X),
              reads=[b["l_ms"]], writes=[(b["m1"], 1, 2)])
        ek.op("vector", lambda v: v.tensor_single_scalar(b["m1"][:, 2:3], b["m1"][:, 0:1], -1.0, ALU.mult),
              reads=[(b["m1"], 0, 1)], writes=[(b["m1"], 2, 3)])
        ek.op("scalar", lambda s, bap=bap: s.activation(b["el"].ap, bap, AF.Exp, bias=b["m1"][:, 2:3]),
              reads=[bsl, (b["m1"], 2, 3)], writes=[b["el"]])
        ek.op("vector", lambda v: v.tensor_add(b["m1"][:, 3:4], b["m1"][:, 1:2], b["m1"][:, 2:3]),
              reads=[(b["m1"], 1, 3)], writes=[(b["m1"], 3, 4)])
        ek.op("scalar", lambda s: s.activation(b["m1"][:, 3:4], b["m1"][:, 3:4], AF.Exp),
              reads=[(b["m1"], 3, 4)], writes=[(b["m1"], 3, 4)])
        ek.op("vector", lambda v: v.tensor_single_scalar(b["m1"][:, 3:4], b["m1"][:, 3:4], 1.0, ALU.add),
              reads=[(b["m1"], 3, 4)], writes=[(b["m1"], 3, 4)])
        ek.op("vector", lambda v: v.reciprocal(b["m1"][:, 3:4], b["m1"][:, 3:4]),
              reads=[(b["m1"], 3, 4)], writes=[(b["m1"], 3, 4)])
        ek.op("vector", lambda v, bap=bap: v.tensor_single_scalar(b["l_ms"].ap, bap, b["m1"][:, 1:2], ALU.is_ge),
              reads=[bsl, (b["m1"], 1, 2)], writes=[b["l_ms"]])
        ek.op("vector", lambda v: v.tensor_mul(b["el"].ap, b["el"].ap, b["l_ms"].ap),
              reads=[b["el"], b["l_ms"]], writes=[b["el"]])
        ek.op("vector", lambda v: v.tensor_scalar_mul(b["cw16"].ap, b["el"].ap, b["m1"][:, 3:4]),
              reads=[b["el"], (b["m1"], 3, 4)], writes=[b["cw16"]])
        for e in range(8):
            pt_ap, pt_reg = ptb(ctx, 0, e * 128, (e + 1) * 128)
            ek.op("tensor", lambda t, e=e, pt_ap=pt_ap: t.transpose(
                pt_ap[0:1, :], b["cw16"][:, e:e + 1], c["ident"].ap),
                reads=[(b["cw16"], e, e + 1), c["ident"]], writes=[pt_reg])
        for e in range(8):
            pt_ap = ctx.pt[0:1, e * 128:(e + 1) * 128]
            ek.op("vector", lambda v, e=e, pt_ap=pt_ap, tci=tci: v.tensor_copy(
                b["cwrow"][0:1, e * 512 + tci * 128: e * 512 + (tci + 1) * 128], pt_ap),
                reads=[(ctx.pt, e * 128, (e + 1) * 128)],
                writes=[(b["cwrow"], e * 512 + tci * 128, e * 512 + (tci + 1) * 128)])


def emit_phase_d(ctx, b, x_prev_name, x_next_name):
    ek, c, T = ctx.ek, ctx.c, ctx.T
    xn = b["xn"]
    ek.dma("gpsimd", lambda g: g.dma_start(b["rtmp"][0:8, :], ctx.dram["L1_ffn_conf_w"][:, :]),
           reads=[dreg("L1_ffn_conf_w")], writes=[b["rtmp"]], slot="rwT")
    for dc in range(8):
        pt_ap, pt_reg = ptb(ctx, 1, 0, 8)
        ek.op("tensor", lambda t, dc=dc, pt_ap=pt_ap: t.transpose(
            pt_ap, b["rtmp"][0:8, dc * 128:(dc + 1) * 128], c["ident"][0:8, 0:8]),
            reads=[b["rtmp"], c["ident"]], writes=[pt_reg])
        ek.op("vector", lambda v, dc=dc, pt_ap=pt_ap: v.tensor_copy(b["rwT"][:, dc * 8:(dc + 1) * 8], pt_ap),
              reads=[pt_reg], writes=[(b["rwT"], dc * 8, (dc + 1) * 8)])
    ek.dma("sync", lambda sy: sy.dma_start(b["bias_row"][0:1, :], ctx.dram["L1_ffn_conf_b"][None, :]),
           reads=[dreg("L1_ffn_conf_b")], writes=[b["bias_row"]], slot="br")
    ek.dma("sync", lambda sy: sy.dma_start(b["wealth_row"][0:1, :], ctx.dram["L1_ffn_wealth"][None, :]),
           reads=[dreg("L1_ffn_wealth")], writes=[b["wealth_row"]], slot="br")
    for (row, dst) in ((b["bias_row"], b["bias_b"]), (b["wealth_row"], b["wealth_b"])):
        bb_ap, bb_reg = pfb(ctx, 5, 0, 8)
        ek.group("tensor", lambda t, row=row, bb_ap=bb_ap: [
            t.matmul(bb_ap[:, 0:8], c["ones_colf"].ap, row[0:1, :], start=True, stop=True)],
            reads=[c["ones_colf"], row], writes=[bb_reg])
        ek.op("vector", lambda v, dst=dst, bb_ap=bb_ap: v.tensor_copy(dst.ap, bb_ap[:, 0:8]),
              reads=[bb_reg], writes=[dst])

    for tt0 in range(0, T, 512):
        te = min(512, T - tt0)
        emit_phase_d_routing(ctx, b, tt0, te)
        for e in range(8):
            for q in range(te // 512):
                cb_ap, cb_reg = pfb(ctx, 5, 0, 512)
                ek.group("tensor", lambda t, e=e, q=q, cb_ap=cb_ap: [
                    t.matmul(cb_ap, c["ones_col"].ap,
                             b["cwrow"][0:1, e * 512 + q * 512: e * 512 + (q + 1) * 512],
                             start=True, stop=True)],
                    reads=[c["ones_col"], (b["cwrow"], e * 512 + q * 512, e * 512 + (q + 1) * 512)],
                    writes=[cb_reg])
                ek.op("vector", lambda v, q=q, cb_ap=cb_ap: v.tensor_copy(
                    b["cwb"][:, q * 512:(q + 1) * 512], cb_ap),
                    reads=[cb_reg], writes=[(b["cwb"], q * 512, (q + 1) * 512)])
            for ihalf in range(2):
                _emit_d_half(ctx, b, tt0, te, e, ihalf)
        for cc4 in range(te // 256):
            t0 = tt0 + cc4 * 256
            emit_residual_tile_cols(ctx, b, t0, 256, "yacc_d",
                                    lambda dc, cc4=cc4: dc * 512 + cc4 * 256,
                                    x_prev_name, x_next_name)
            emit_ln_from_xtile(ctx, b, t0, 256)


def _emit_d_half(ctx, b, tt0, te, e, ihalf):
    ek, c, T = ctx.ek, ctx.c, ctx.T
    xn = b["xn"]
    i0 = ihalf * 2048
    dlo = (e * 2 + ihalf) * 128 * 16384
    ek.dma("gpsimd", lambda g: g.dma_start(b["dwb"].ap, ctx.dram["dwT"][e, ihalf]),
           reads=[dreg("dwT", dlo, dlo + 128 * 16384)],
           writes=[b["dwb"]], slot="dwb")
    for icg in range(4):
        half = icg % 2
        wcols = 8192 * half
        icg_g = ihalf * 4 + icg
        for wi, dT in ((0, "gwT"), (1, "uwT")):
            col = wcols + wi * 4096
            rlo = (e * 8 + icg_g) * 128 * 4096
            ek.dma("sync", lambda sy, dT=dT, icg_g=icg_g, col=col: sy.dma_start(
                b["wslab"].h[:, col:col + 4096], ctx.dram[dT][e, icg_g]),
                reads=[dreg(dT, rlo, rlo + 128 * 4096)],
                writes=[(b["wslab"], col, col + 4096)], slot="ws" + str(half) + str(wi))
        for ic in range(4):
            for ts_ in range(te // 512):
                tq0 = tt0 + ts_ * 512
                g_ap, g_reg = pfb(ctx, 4, 0, 512)
                u_ap, u_reg = pfb(ctx, 5, 0, 512)
                for wi, (o_ap, o_reg) in ((0, (g_ap, g_reg)), (1, (u_ap, u_reg))):
                    lhsTs = [b["wslab"][:, wcols + wi * 4096 + hc * 512 + ic * 128:
                                        wcols + wi * 4096 + hc * 512 + ic * 128 + 128] for hc in range(8)]
                    rhss = [xn[:, hc * T + tq0: hc * T + tq0 + 512] for hc in range(8)]
                    mm_chain(ctx, o_ap, o_reg, lhsTs, rhss,
                             reads=[(b["wslab"], wcols + wi * 4096, wcols + (wi + 1) * 4096)] +
                                   [(xn, hc * T + tq0, hc * T + tq0 + 512) for hc in range(8)])
                st = b["stat_tmp"]
                ek.op("scalar", lambda s, g_ap=g_ap, st=st: s.activation(st[:, 512:1024], g_ap, AF.Silu),
                      reads=[g_reg], writes=[(st, 512, 1024)])
                eo_lo = (icg * 4 + ic) * 512 + ts_ * 512
                ek.op("vector", lambda v, u_ap=u_ap, st=st: v.scalar_tensor_tensor(
                    st[:, 0:512], u_ap, 1.0, st[:, 512:1024], ALU.mult, ALU.mult),
                    reads=[u_reg, (st, 512, 1024)], writes=[(st, 0, 512)])
                ek.op("vector", lambda v, eo_lo=eo_lo, ts_=ts_: v.tensor_mul(
                    b["eoh"][:, eo_lo:eo_lo + 512], b["stat_tmp"][:, 0:512],
                    b["cwb"][:, ts_ * 512:(ts_ + 1) * 512]),
                    reads=[(b["stat_tmp"], 0, 512), (b["cwb"], ts_ * 512, (ts_ + 1) * 512)],
                    writes=[(b["eoh"], eo_lo, eo_lo + 512)])
    for ts_ in range(te // 512):
        tq0s = ts_ * 512
        for dc in range(8):
            y_ap = ctx.pf[:, (dc % 4) * 512: (dc % 4) * 512 + 512]
            y_reg = (ctx.pf, (dc % 4) * 512, (dc % 4) * 512 + 512)
            lhsTs = [b["dwb"][:, ic * 1024 + dc * 128: ic * 1024 + (dc + 1) * 128] for ic in range(16)]
            rhss = [b["eoh"][:, ic * 512 + tq0s: ic * 512 + tq0s + 512] for ic in range(16)]
            mm_chain(ctx, y_ap, y_reg, lhsTs, rhss,
                     reads=[b["dwb"], b["eoh"]])
            ya_lo = dc * 512 + tq0s
            fresh = (e == 0 and ihalf == 0)
            if fresh:
                ek.op("vector", lambda v, y_ap=y_ap, ya_lo=ya_lo: v.tensor_copy(
                    b["yacc_d"][:, ya_lo:ya_lo + 512], y_ap),
                    reads=[y_reg], writes=[(b["yacc_d"], ya_lo, ya_lo + 512)])
            else:
                ek.op("vector", lambda v, y_ap=y_ap, ya_lo=ya_lo: v.tensor_add(
                    b["yacc_d"][:, ya_lo:ya_lo + 512], b["yacc_d"][:, ya_lo:ya_lo + 512], y_ap),
                    reads=[y_reg, (b["yacc_d"], ya_lo, ya_lo + 512)],
                    writes=[(b["yacc_d"], ya_lo, ya_lo + 512)])


INPUT_SHAPES = {
    "L0_attn_Wq": [1024, 512], "L0_attn_Wk": [1024, 512], "L0_attn_Wv": [1024, 1024],
    "L0_attn_Wg": [1024, 1024], "L0_attn_Wgk1": [1024, 16], "L0_attn_Wgk2": [16, 512],
    "L0_attn_bgk2": [512], "L0_attn_Wout": [1024, 1024], "L0_attn_rms_w": [256],
    "L0_ffn_router_w": [8, 1024], "L0_ffn_experts_bias": [8],
    "L0_ffn_w1": [8, 1024, 2048], "L0_ffn_w3": [8, 1024, 2048], "L0_ffn_w2": [8, 2048, 1024],
    "L0_ffn_sw1": [1, 1024, 2048], "L0_ffn_sw3": [1, 1024, 2048], "L0_ffn_sw2": [1, 2048, 1024],
    "L1_attn_Wq": [16, 1024, 64], "L1_attn_Wk": [16, 1024, 64], "L1_attn_Wv": [16, 1024, 64],
    "L1_attn_Wo": [1024, 1024], "L1_attn_gamma": [1024], "L1_attn_beta": [1024],
    "L1_ffn_conf_w": [8, 1024], "L1_ffn_conf_b": [8],
    "L1_ffn_gate_w": [8, 4096, 1024], "L1_ffn_up_w": [8, 4096, 1024],
    "L1_ffn_down_w": [8, 1024, 4096], "L1_ffn_wealth": [8],
}


def build_full(T, phases="ABCD", nc=None):
    if nc is None:
        nc = bass.Bass("TRN2", target_bir_lowering=False)
    ctx = Ctx(nc, T)
    ctx.dt("x_in", [1024, T], F32, kind="ExternalInput")
    ctx.dt("x_out", [1024, T], F32, kind="ExternalOutput")
    for n, sh in INPUT_SHAPES.items():
        ctx.dt(n, sh, F32, kind="ExternalInput")
    ctx.dt("x1", [1024, T], F32)
    ctx.dt("x2", [1024, T], F32)
    ctx.dt("x3", [1024, T], F32)
    ctx.dt("gwbf", [8, 4096, 1024], BF16)
    ctx.dt("uwbf", [8, 4096, 1024], BF16)
    ctx.dt("dwbf", [8, 1024, 4096], BF16)
    ctx.dt("gwT", [8, 8, 128, 4096], BF16)
    ctx.dt("uwT", [8, 8, 128, 4096], BF16)
    ctx.dt("dwT", [8, 2, 128, 16384], BF16)
    emit_consts(ctx)
    b = alloc_shared_bufs(ctx)
    mem = ctx.mem
    peaks = {}
    xs = ["x_in"]
    for ph in phases[:-1]:
        xs.append({"A": "x1", "B": "x2", "C": "x3"}[ph])
    xs.append("x_out")

    i = 0
    if "A" in phases:
        mem.push()
        alloc_phase_a_bufs(ctx, b)
        peaks["A"] = mem.cursor
        pb = dict(b)
        emit_prologue(ctx, pb, xs[i])
        emit_phase_a(ctx, pb, xs[i], xs[i + 1])
        mem.pop()
        i += 1
    need_d = "D" in phases
    if "B" in phases:
        mem.push()
        alloc_phase_b_bufs(ctx, b, with_stash=need_d)
        peaks["B"] = mem.cursor
        pb = dict(b)
        sc = make_stash_chunks(ctx, pb) if need_d else None
        if i == 0:
            emit_prologue(ctx, pb, xs[i])
        emit_phase_b(ctx, pb, xs[i], xs[i + 1], stash_chunks=sc)
        mem.pop()
        i += 1
    if "C" in phases:
        mem.push()
        alloc_phase_c_bufs(ctx, b, with_repack=need_d)
        pb = dict(b)
        rc = make_repack_chunks(ctx, pb) if need_d else None
        if i == 0:
            emit_prologue(ctx, pb, xs[i])
        emit_phase_c(ctx, pb, xs[i], xs[i + 1], repack_chunks=rc)
        peaks["C"] = mem.cursor
        mem.pop()
        i += 1
    if "D" in phases:
        mem.push()
        alloc_phase_d_bufs(ctx, b)
        pb = dict(b)
        if i == 0:
            emit_prologue(ctx, pb, xs[i])
        if "B" not in phases:
            pb["stash_stg"] = mem.sb("stash_stg", 128, 2 * 1024, BF16)
            for chk in make_stash_chunks(ctx, pb):
                chk()
        if "C" not in phases:
            pb["rpk_stg"] = mem.sb("rpk_stg", 128, 2 * 4096, BF16)
            for chk in make_repack_chunks(ctx, pb):
                chk()
        peaks["D"] = mem.cursor
        emit_phase_d(ctx, pb, xs[i], xs[i + 1])
        mem.pop()
        i += 1
    ctx.ek.finish("sync")
    ctx.ek.build_and_emit()
    mem.check()
    return nc, ctx, peaks


_T = 4096
_NCORES = 8


def kernel(**inputs):
    from concourse.bass_utils import run_bass_kernel_spmd

    x = np.ascontiguousarray(inputs["L0_attn_x"], dtype=np.float32)
    assert x.shape == (_NCORES, _T, 1024)
    weights = {n: np.ascontiguousarray(inputs[n], dtype=np.float32)
               for n in INPUT_SHAPES}

    nc, ctx, _peaks = build_full(_T)
    in_maps = []
    for i in range(_NCORES):
        m = {"x_in": np.ascontiguousarray(x[i].T)}
        m.update(weights)
        in_maps.append(m)
    res = run_bass_kernel_spmd(nc, in_maps, core_ids=list(range(_NCORES)))
    out = np.stack([np.asarray(res.results[i]["x_out"]).T for i in range(_NCORES)], axis=0)
    return out.astype(np.float32)


# version 05
# speedup vs baseline: 1.1065x; 1.0466x over previous
import sys
sys.path.insert(0, "/opt/trn_rl_repo")
import math
from collections import defaultdict
import numpy as np
import concourse.bass as bass
import concourse.mybir as mybir

F32 = mybir.dt.float32
BF16 = mybir.dt.bfloat16
ALU = mybir.AluOpType
AF = mybir.ActivationFunctionType
AX = mybir.AxisListType

ENGINES = ("tensor", "vector", "scalar", "gpsimd", "sync")
ESEM = {e: "e_" + e for e in ENGINES}
DSEM = {e: "d_" + e for e in ("sync", "scalar", "gpsimd", "vector")}
NEG_BIG = -1.0e30


class _IntervalMap:
    __slots__ = ("segs",)

    def __init__(self):
        self.segs = []

    def read(self, lo, hi, reader_tok, conflict_readers=()):
        waits = []
        rsem, rval = reader_tok
        covered = []
        for s in self.segs:
            if s[0] < hi and lo < s[1]:
                if s[2] is not None:
                    waits.append(s[2])
                for csem in conflict_readers:
                    cv = s[3].get(csem)
                    if cv is not None:
                        waits.append((csem, cv))
                if s[3].get(rsem, -1) < rval:
                    s[3][rsem] = rval
                covered.append((max(s[0], lo), min(s[1], hi)))
        covered.sort()
        pos = lo
        gaps = []
        for c0, c1 in covered:
            if c0 > pos:
                gaps.append([pos, c0, None, {rsem: rval}])
            pos = max(pos, c1)
        if pos < hi:
            gaps.append([pos, hi, None, {rsem: rval}])
        self.segs.extend(gaps)
        return waits

    def write(self, lo, hi, writer_tok):
        waits = []
        new_segs = []
        for s in self.segs:
            if s[0] < hi and lo < s[1]:
                if s[2] is not None:
                    waits.append(s[2])
                for rsem, rval in s[3].items():
                    waits.append((rsem, rval))
                if s[0] < lo:
                    new_segs.append([s[0], lo, s[2], dict(s[3])])
                if hi < s[1]:
                    new_segs.append([hi, s[1], s[2], dict(s[3])])
            else:
                new_segs.append(s)
        new_segs.append([lo, hi, writer_tok, {}])
        self.segs = new_segs
        return waits


class Buf:
    __slots__ = ("h", "space", "base", "esize", "n", "quant")

    def __init__(self, h, space, base, esize, n, quant=1):
        self.h = h
        self.space = space
        self.base = base
        self.esize = esize
        self.n = n
        self.quant = quant

    def __getitem__(self, idx):
        return self.h[idx]

    @property
    def ap(self):
        return self.h[:, :]

    def r(self, lo=0, hi=None):
        if hi is None:
            hi = self.n
        b0 = self.base + lo * self.esize
        b1 = self.base + hi * self.esize
        if self.quant > 1:
            b0 = (b0 // self.quant) * self.quant
            b1 = ((b1 + self.quant - 1) // self.quant) * self.quant
        return (self.space, b0, b1)


def dreg(name, lo=0, hi=1 << 50):
    return ("D_" + name, lo, hi)


class EK:
    def __init__(self, nc):
        self.nc = nc
        self.sched = {e: [] for e in ENGINES}
        self.cnt = defaultdict(int)
        self.vc = {e: defaultdict(int) for e in ENGINES}
        self.maps = defaultdict(_IntervalMap)
        self.nwaits = 0
        self.nops = defaultdict(int)

    @staticmethod
    def _norm(r):
        if isinstance(r, Buf):
            return r.r()
        if isinstance(r, tuple):
            if isinstance(r[0], Buf):
                return r[0].r(r[1], r[2])
            return r
        raise TypeError(f"bad region {r!r}")

    _PSUM_READ_CONFLICTS = {"vector": ("e_scalar",), "scalar": ("e_vector",)}

    def _acquire(self, engine, reads, writes, tok):
        need = {}
        for r in reads or ():
            key, lo, hi = self._norm(r)
            cr = self._PSUM_READ_CONFLICTS.get(engine, ()) if key == "PS" else ()
            for sem, val in self.maps[key].read(lo, hi, tok, cr):
                if (sem, val) == tok:
                    continue
                if need.get(sem, 0) < val:
                    need[sem] = val
        for w in writes or ():
            key, lo, hi = self._norm(w)
            for sem, val in self.maps[key].write(lo, hi, tok):
                if (sem, val) == tok:
                    continue
                if need.get(sem, 0) < val:
                    need[sem] = val
        waits = []
        own = ESEM[engine]
        for sem, val in sorted(need.items()):
            if engine == "tensor" and sem == own:
                continue
            if sem.startswith("d_"):
                val = self.cnt[sem]
            if self.vc[engine][sem] >= val:
                continue
            self.vc[engine][sem] = val
            waits.append((sem, val))
        self.nwaits += len(waits)
        return waits

    def op(self, engine, fn, reads=(), writes=(), tag=None):
        sem = ESEM[engine]
        self.cnt[sem] += 1
        tokval = self.cnt[sem]
        waits = self._acquire(engine, reads, writes, (sem, tokval))
        self.sched[engine].append(("op", fn, waits, sem, 1))
        self.nops[engine] += 1
        return (sem, tokval)

    def group(self, engine, fns, reads=(), writes=(), tag=None):
        sem = ESEM[engine]
        self.cnt[sem] += 1
        tokval = self.cnt[sem]
        waits = self._acquire(engine, reads, writes, (sem, tokval))
        self.sched[engine].append(("group", fns, waits, sem, 1))
        self.nops[engine] += 1
        return (sem, tokval)

    def dma(self, engine, fn, reads=(), writes=(), slot=None, tag=None):
        assert slot is not None, "dma needs a slot key"
        sem = "d_" + engine + "_" + slot
        tokval = self.cnt[sem] + 16
        waits = self._acquire(engine, reads, writes, (sem, tokval))
        self.cnt[sem] = tokval
        self.sched[engine].append(("op", fn, waits, sem, 16))
        self.nops["dma_" + engine] += 1
        return (sem, tokval)

    def wait_manual(self, engine, tok):
        sem, val = tok
        if self.vc[engine][sem] >= val:
            return
        self.vc[engine][sem] = val
        self.sched[engine].append(("wait", None, [(sem, val)], None, 0))

    def finish(self, engine="sync"):
        for sem, val in sorted(self.cnt.items()):
            self.wait_manual(engine, (sem, val))

    def build_and_emit(self):
        nc = self.nc
        import contextlib
        stack = contextlib.ExitStack()
        sems = {}
        used = set(self.cnt.keys())
        for e in ENGINES:
            for step in self.sched[e]:
                for w in step[2]:
                    used.add(w[0])
        for name in sorted(used):
            sems[name] = stack.enter_context(nc.semaphore(name))
        with stack:
            with nc.Block() as block:
                for eng_name in ENGINES:
                    steps = self.sched[eng_name]
                    if not steps:
                        continue

                    def body(eng, steps=steps):
                        for kind, fn, waits, sem, amt in steps:
                            for wsem, wval in waits:
                                eng.wait_ge(sems[wsem], wval)
                            if kind == "op":
                                fn(eng).then_inc(sems[sem], amt)
                            elif kind == "group":
                                fn(eng)[-1].then_inc(sems[sem], amt)

                    getattr(block, eng_name)(body)
        return sems


class Mem:
    MAN_BASE = 20480
    SB_LIMIT = 229344

    def __init__(self, nc):
        self.nc = nc
        self.cursor = 0
        self.bufs = {}
        self.marks = []

    def sb(self, name, parts, n, dtype, align=64):
        esize = {BF16: 2, mybir.dt.float16: 2, mybir.dt.uint8: 1, mybir.dt.int8: 1}.get(dtype, 4)
        base = self.MAN_BASE + (self.cursor + align - 1) // align * align
        nbytes = n * esize
        assert base + nbytes <= self.SB_LIMIT, (
            f"SBUF overflow at {name}: base={base} need={nbytes}")
        h = self.nc.alloc_sbuf_tensor_at(name, [parts, n], dtype, offset=base)
        self.cursor = (base - self.MAN_BASE) + nbytes
        b = Buf(h, "SB", base, esize, n)
        self.bufs[name] = b
        return b

    def check(self):
        assert self.nc.sbuf_base <= self.MAN_BASE, (
            f"bass auto allocations ({self.nc.sbuf_base}) collided with manual region")

    def push(self):
        self.marks.append(self.cursor)

    def pop(self):
        self.cursor = self.marks.pop()

    def push(self):
        self.marks.append(self.cursor)

    def pop(self):
        self.cursor = self.marks.pop()


def alloc_psum(nc):
    pf_h = nc.alloc_psum_tensor("pf", [128, 3072], F32)
    pt_h = nc.alloc_psum_tensor("pt", [128, 2048], BF16)
    pf = Buf(pf_h, "PS", 0, 4, 3072, quant=2048)
    pt = Buf(pt_h, "PS", 3072 * 4, 2, 2048, quant=2048)
    return pf, pt


class Ctx:
    def __init__(self, nc, T):
        self.nc = nc
        self.T = T
        self.NT = T // 256
        self.ek = EK(nc)
        self.mem = Mem(nc)
        self.pf, self.pt = alloc_psum(nc)
        self.dram = {}
        self.c = {}

    def dt(self, name, shape, dtype, kind=None):
        if kind:
            h = self.nc.dram_tensor(name, shape, dtype, kind=kind)
        else:
            h = self.nc.dram_tensor(name, shape, dtype)
        ap = h.ap()
        self.dram[name] = ap
        return ap


def pfb(ctx, bank, lo=0, hi=512):
    return ctx.pf[:, bank * 512 + lo: bank * 512 + hi], (ctx.pf, bank * 512 + lo, bank * 512 + hi)


def ptb(ctx, bank, lo=0, hi=1024):
    return ctx.pt[:, bank * 1024 + lo: bank * 1024 + hi], (ctx.pt, bank * 1024 + lo, bank * 1024 + hi)


def emit_consts(ctx):
    ek, mem, nc = ctx.ek, ctx.mem, ctx.nc
    c = ctx.c
    c["ones_col"] = mem.sb("ones_col", 1, 128, BF16)
    c["ones128"] = mem.sb("ones128", 128, 1, BF16)
    c["ones_colf"] = mem.sb("ones_colf", 1, 128, F32)
    c["ones128f"] = mem.sb("ones128f", 128, 1, F32)
    c["cpool"] = mem.sb("cpool", 128, 8, F32)
    c["negb8"] = mem.sb("negb8", 128, 8, F32)
    c["ident"] = mem.sb("ident", 128, 128, BF16)
    c["identf"] = mem.sb("identf", 128, 128, F32)
    c["zeros256"] = mem.sb("zeros256", 128, 256, F32)
    c["maskA0"] = mem.sb("maskA0", 128, 256, F32)
    c["maskA1"] = mem.sb("maskA1", 128, 256, F32)
    c["iota_j"] = mem.sb("iota_j", 128, 128, F32)
    c["iota_p"] = mem.sb("iota_p", 128, 128, F32)

    ek.op("gpsimd", lambda g: g.memset(c["ones_col"].ap, 1.0), writes=[c["ones_col"]])
    ek.op("gpsimd", lambda g: g.memset(c["ones128"].ap, 1.0), writes=[c["ones128"]])
    ek.op("gpsimd", lambda g: g.memset(c["ones_colf"].ap, 1.0), writes=[c["ones_colf"]])
    ek.op("gpsimd", lambda g: g.memset(c["ones128f"].ap, 1.0), writes=[c["ones128f"]])
    for col, val in [(0, 1e-5), (1, 1.0), (2, 1e-3), (3, NEG_BIG), (4, -2.0794415416798357)]:
        ek.op("gpsimd", lambda g, col=col, val=val: g.memset(c["cpool"][:, col:col + 1], val),
              writes=[(c["cpool"], col, col + 1)])
    ek.op("gpsimd", lambda g: g.memset(c["negb8"].ap, NEG_BIG), writes=[c["negb8"]])
    ek.op("gpsimd", lambda g: g.memset(c["zeros256"].ap, 0.0), writes=[c["zeros256"]])
    ek.op("gpsimd", lambda g: g.iota(c["iota_j"].ap, [[1, 128]], channel_multiplier=0,
                                     allow_small_or_imprecise_dtypes=True), writes=[c["iota_j"]])
    ek.op("gpsimd", lambda g: g.iota(c["iota_p"].ap, [[0, 128]], channel_multiplier=1,
                                     allow_small_or_imprecise_dtypes=True), writes=[c["iota_p"]])
    ek.op("vector", lambda v: v.tensor_tensor(c["ident"].ap, c["iota_j"].ap, c["iota_p"].ap, ALU.is_equal),
          reads=[c["iota_j"], c["iota_p"]], writes=[c["ident"]])
    ek.op("vector", lambda v: v.tensor_tensor(c["identf"].ap, c["iota_j"].ap, c["iota_p"].ap, ALU.is_equal),
          reads=[c["iota_j"], c["iota_p"]], writes=[c["identf"]])
    ek.op("vector", lambda v: v.tensor_tensor(c["maskA0"][:, 0:128], c["iota_j"].ap, c["iota_p"].ap, ALU.is_ge),
          reads=[c["iota_j"], c["iota_p"]], writes=[(c["maskA0"], 0, 128)])
    ek.op("vector", lambda v: v.memset(c["maskA0"][:, 128:256], 1.0), writes=[(c["maskA0"], 128, 256)])
    ek.op("vector", lambda v: v.memset(c["maskA1"][:, 0:128], 0.0), writes=[(c["maskA1"], 0, 128)])
    ek.op("vector", lambda v: v.tensor_tensor(c["maskA1"][:, 128:256], c["iota_j"].ap, c["iota_p"].ap, ALU.is_ge),
          reads=[c["iota_j"], c["iota_p"]], writes=[(c["maskA1"], 128, 256)])


def load_weight_cast(ctx, dst_buf, src_ap, name, nrow_chunks, ncols, col0=0, tag=None):
    ek = ctx.ek
    src = src_ap.rearrange("(k p) n -> p k n", p=128)[:, :, col0:col0 + ncols]
    dst = dst_buf.h[:, 0:nrow_chunks * ncols].rearrange("p (k n) -> p k n", k=nrow_chunks)
    ek.dma("gpsimd", lambda g: g.dma_start(dst, src),
           reads=[dreg(name)], writes=[(dst_buf, 0, nrow_chunks * ncols)],
           slot=dst_buf.h.name, tag=tag)


def mm_chain(ctx, out_ap, out_reg, lhsT_list, rhs_list, reads, extra_first=None):
    ek = ctx.ek
    n = len(lhsT_list)

    def fns(t):
        insts = []
        for i in range(n):
            insts.append(t.matmul(out_ap, lhsT_list[i], rhs_list[i],
                                  start=(i == 0), stop=(i == n - 1)))
        return insts

    return ek.group("tensor", fns, reads=reads, writes=[out_reg])


def emit_ln_from_xtile(ctx, bufs, t0, W, phase_tag=""):
    ek, c, T = ctx.ek, ctx.c, ctx.T
    xt, xn = bufs["xtile"], bufs["xn"]
    tmp, st = bufs["stat_tmp"], bufs["stat_small"]
    nhalf = 4
    s1_ap, s1_reg = pfb(ctx, 4, 0, W)
    s2_ap, s2_reg = pfb(ctx, 5, 0, W)

    def s1_fns(t):
        return [t.matmul(s1_ap[0:1, :], c["ones128f"].ap, xt[:, cc * W:(cc + 1) * W],
                         start=(cc == 0), stop=(cc == 7)) for cc in range(8)]

    ek.group("tensor", s1_fns, reads=[(xt, 0, 8 * W), c["ones128f"]], writes=[s1_reg])
    for half in range(2):
        toks = []
        for i in range(nhalf):
            cc = half * nhalf + i
            ek.op("scalar", lambda s, cc=cc, i=i: s.activation(
                tmp[:, i * W:(i + 1) * W], xt[:, cc * W:(cc + 1) * W], AF.Square),
                reads=[(xt, cc * W, (cc + 1) * W)], writes=[(tmp, i * W, (i + 1) * W)])

        def s2_fns(t, half=half):
            return [t.matmul(s2_ap[0:1, :], c["ones128f"].ap, tmp[:, i * W:(i + 1) * W],
                             start=(half == 0 and i == 0), stop=(half == 1 and i == nhalf - 1))
                    for i in range(nhalf)]

        ek.group("tensor", s2_fns, reads=[(tmp, 0, nhalf * W), c["ones128f"]], writes=[s2_reg])
    m = st
    ek.op("vector", lambda v: v.tensor_single_scalar(m[0:1, 0:W], s1_ap[0:1, :], 1.0 / 1024.0, ALU.mult),
          reads=[s1_reg], writes=[(st, 0, W)])
    ek.op("vector", lambda v: v.tensor_single_scalar(m[0:1, 3 * W:4 * W], s2_ap[0:1, :], 1.0 / 1024.0, ALU.mult),
          reads=[s2_reg], writes=[(st, 3 * W, 4 * W)])
    ek.op("vector", lambda v: v.tensor_mul(m[0:1, W:2 * W], m[0:1, 0:W], m[0:1, 0:W]),
          reads=[(st, 0, W)], writes=[(st, W, 2 * W)])
    ek.op("vector", lambda v: v.tensor_sub(m[0:1, W:2 * W], m[0:1, 3 * W:4 * W], m[0:1, W:2 * W]),
          reads=[(st, W, 2 * W), (st, 3 * W, 4 * W)], writes=[(st, W, 2 * W)])
    ek.op("scalar", lambda s: s.activation(m[0:1, W:2 * W], m[0:1, W:2 * W], AF.Ln,
                                           bias=ctx.c["cpool"][0:1, 0:1], scale=1.0),
          reads=[(st, W, 2 * W), (ctx.c["cpool"], 0, 1)], writes=[(st, W, 2 * W)])
    ek.op("scalar", lambda s: s.activation(m[0:1, 2 * W:3 * W], m[0:1, W:2 * W], AF.Exp, bias=0.0, scale=-0.5),
          reads=[(st, W, 2 * W)], writes=[(st, 2 * W, 3 * W)])
    mb_ap, mb_reg = pfb(ctx, 4, 0, W)
    rb_ap, rb_reg = pfb(ctx, 5, 0, W)
    ek.group("tensor", lambda t: [t.matmul(mb_ap, c["ones_colf"].ap, m[0:1, 0:W], start=True, stop=True)],
             reads=[(st, 0, W), c["ones_colf"]], writes=[mb_reg])
    ek.group("tensor", lambda t: [t.matmul(rb_ap, c["ones_colf"].ap, m[0:1, 2 * W:3 * W], start=True, stop=True)],
             reads=[(st, 2 * W, 3 * W), c["ones_colf"]], writes=[rb_reg])
    for cc in range(8):
        dst_lo = cc * T + t0
        ek.op("vector", lambda v, cc=cc: v.tensor_sub(tmp[:, 0:W] if False else tmp[:, (cc % 4) * W:(cc % 4) * W + W],
                                                      xt[:, cc * W:(cc + 1) * W], mb_ap),
              reads=[(xt, cc * W, (cc + 1) * W), mb_reg],
              writes=[(tmp, (cc % 4) * W, (cc % 4) * W + W)])
        ek.op("vector", lambda v, cc=cc, dst_lo=dst_lo: v.tensor_mul(
            xn[:, dst_lo:dst_lo + W], tmp[:, (cc % 4) * W:(cc % 4) * W + W], rb_ap),
            reads=[(tmp, (cc % 4) * W, (cc % 4) * W + W), rb_reg],
            writes=[(xn, dst_lo, dst_lo + W)])


def emit_residual_tile(ctx, bufs, t0, W, y_src, x_prev_name, x_next_name, store=True):
    ek, T = ctx.ek, ctx.T
    xt, ysb = bufs["xtile"], bufs[y_src]
    xprev = ctx.dram[x_prev_name]
    src = xprev.rearrange("(c p) t -> p c t", p=128)[:, :, t0:t0 + W]
    ek.dma("sync", lambda sy: sy.dma_start(xt.h[:, 0:8 * W].rearrange("p (c w) -> p c w", c=8), src),
           reads=[dreg(x_prev_name, t0, t0 + W)], writes=[(xt, 0, 8 * W)], slot="xt")
    for cc in range(8):
        ek.op("vector", lambda v, cc=cc: v.tensor_add(
            xt[:, cc * W:(cc + 1) * W], ysb[:, cc * W:(cc + 1) * W], xt[:, cc * W:(cc + 1) * W]),
            reads=[(ysb, cc * W, (cc + 1) * W), (xt, cc * W, (cc + 1) * W)],
            writes=[(xt, cc * W, (cc + 1) * W)])
    if store:
        xnext = ctx.dram[x_next_name]
        dst = xnext.rearrange("(c p) t -> p c t", p=128)[:, :, t0:t0 + W]
        ek.dma("sync", lambda sy: sy.dma_start(dst, xt.h[:, 0:8 * W].rearrange("p (c w) -> p c w", c=8)),
               reads=[(xt, 0, 8 * W)], writes=[dreg(x_next_name, t0, t0 + W)], slot="xt")


def emit_prologue(ctx, bufs, x_in_name):
    T = ctx.T
    W = 256
    for t0 in range(0, T, W):
        xin = ctx.dram[x_in_name]
        src = xin.rearrange("(c p) t -> p c t", p=128)[:, :, t0:t0 + W]
        xt = bufs["xtile"]
        ctx.ek.dma("sync", lambda sy, src=src: sy.dma_start(
            xt.h[:, 0:8 * W].rearrange("p (c w) -> p c w", c=8), src),
            reads=[dreg(x_in_name, t0, t0 + W)], writes=[(xt, 0, 8 * W)], slot="xt")
        emit_ln_from_xtile(ctx, bufs, t0, W)


def alloc_shared_bufs(ctx):
    mem, T = ctx.mem, ctx.T
    b = {}
    b["xn"] = mem.sb("xn", 128, 8 * T, BF16)
    b["xtile"] = mem.sb("xtile", 128, 8 * 256, F32)
    b["stat_tmp"] = mem.sb("stat_tmp", 128, 4 * 256, F32)
    b["stat_small"] = mem.sb("stat_small", 1, 4 * 256, F32)
    return b


def alloc_phase_a_bufs(ctx, b):
    mem = ctx.mem
    b["y_sb"] = mem.sb("y_sb_a", 128, 8 * 256, F32)
    b["wq"] = mem.sb("wq", 128, 8 * 512, BF16)
    b["wk"] = mem.sb("wk", 128, 8 * 512, BF16)
    b["wv"] = mem.sb("wv", 128, 8 * 1024, BF16)
    b["wg"] = mem.sb("wg", 128, 8 * 512, BF16)
    b["wgk1"] = mem.sb("wgk1", 128, 8 * 16, BF16)
    b["wgk2"] = mem.sb("wgk2", 64, 512, BF16)
    b["bgk2"] = mem.sb("bgk2", 128, 4, F32)
    b["wout"] = mem.sb("wout", 128, 8 * 1024, BF16)
    b["rmsw"] = mem.sb("rmsw", 128, 2, F32)
    b["q_sb"] = mem.sb("q_sb", 128, 4 * 256, BF16)
    b["k_sb"] = mem.sb("k_sb", 128, 4 * 256, BF16)
    b["v_sb"] = mem.sb("v_sb", 128, 8 * 256, BF16)
    b["gate"] = mem.sb("gate", 128, 8 * 512, BF16)
    b["z1"] = mem.sb("z1", 64, 512, BF16)
    b["g_sb"] = mem.sb("g_sb", 128, 4 * 512, F32)
    b["b_sb"] = mem.sb("b_sb", 128, 4 * 256, F32)
    b["ga"] = mem.sb("ga", 128, 512, F32)
    b["gb"] = mem.sb("gb", 128, 512, F32)
    b["ebuf"] = mem.sb("ebuf", 128, 2 * 256, F32)
    b["ebc"] = mem.sb("ebc", 128, 4, F32)
    b["qt"] = mem.sb("qt", 128, 256, BF16)
    b["kh"] = mem.sb("kh", 128, 256, BF16)
    b["kb"] = mem.sb("kb", 128, 256, BF16)
    b["at_sb"] = mem.sb("at_sb", 128, 2 * 256, BF16)
    b["vt_sb"] = mem.sb("vt_sb", 128, 2 * 256, BF16)
    b["kbt_sb"] = mem.sb("kbt_sb", 128, 2 * 128, BF16)
    b["rb_sb"] = mem.sb("rb_sb", 128, 256, F32)
    b["og"] = mem.sb("og", 128, 8 * 256, BF16)
    b["S"] = mem.sb("S", 128, 4 * 256, F32)
    b["S_bf"] = mem.sb("S_bf", 128, 4 * 256, BF16)


def emit_phase_a_weights(ctx, b):
    ek = ctx.ek
    load_weight_cast(ctx, b["wq"], ctx.dram["L0_attn_Wq"], "L0_attn_Wq", 8, 512)
    load_weight_cast(ctx, b["wk"], ctx.dram["L0_attn_Wk"], "L0_attn_Wk", 8, 512)
    load_weight_cast(ctx, b["wv"], ctx.dram["L0_attn_Wv"], "L0_attn_Wv", 8, 1024)
    src_o = ctx.dram["L0_attn_Wout"].rearrange("(k p) n -> p k n", p=128)
    dst_o = b["wout"].h[:, 0:8 * 1024].rearrange("p (k n) -> p k n", k=8)
    ek.dma("gpsimd", lambda g: g.dma_start(dst_o, src_o),
           reads=[dreg("L0_attn_Wout")], writes=[(b["wout"], 0, 8 * 1024)], slot="wout")
    load_weight_cast(ctx, b["wgk1"], ctx.dram["L0_attn_Wgk1"], "L0_attn_Wgk1", 8, 16)
    ek.op("vector", lambda v: v.memset(b["wgk2"][0:64, :], 0.0), writes=[b["wgk2"]])
    ek.op("vector", lambda v: v.memset(b["z1"][0:64, :], 0.0), writes=[b["z1"]])
    ek.dma("gpsimd", lambda g: g.dma_start(b["wgk2"][0:16, :], ctx.dram["L0_attn_Wgk2"][:, :]),
           reads=[dreg("L0_attn_Wgk2")], writes=[(b["wgk2"], 0, 512)], slot="wgk2")
    ek.dma("gpsimd", lambda g: g.dma_start(
        b["bgk2"].ap, ctx.dram["L0_attn_bgk2"].rearrange("(fc p) -> p fc", p=128),
        allow_slow_non_contiguous=True),
        reads=[dreg("L0_attn_bgk2")], writes=[b["bgk2"]], slot="bgk2")
    ek.dma("gpsimd", lambda g: g.dma_start(
        b["rmsw"].ap, ctx.dram["L0_attn_rms_w"].rearrange("(vh p) -> p vh", p=128),
        allow_slow_non_contiguous=True),
        reads=[dreg("L0_attn_rms_w")], writes=[b["rmsw"]], slot="rmsw")


def _proj_group(ctx, out_ap, out_reg, wbuf, wcol_of_dc, xn, t0, W, nk=8):
    T = ctx.T
    lhsTs = [wbuf[:, wcol_of_dc(dc): wcol_of_dc(dc) + 128] for dc in range(nk)]
    rhss = [xn[:, dc * T + t0: dc * T + t0 + W] for dc in range(nk)]
    reads = [(wbuf, min(wcol_of_dc(dc) for dc in range(nk)),
              max(wcol_of_dc(dc) + 128 for dc in range(nk)))]
    reads += [(xn, dc * T + t0, dc * T + t0 + W) for dc in range(nk)]
    return mm_chain(ctx, out_ap, out_reg, lhsTs, rhss, reads)


def emit_phase_a(ctx, b, x_prev_name, x_next_name):
    ek, c, T, NT = ctx.ek, ctx.c, ctx.T, ctx.NT
    xn = b["xn"]
    emit_phase_a_weights(ctx, b)
    ek.op("vector", lambda v: v.memset(b["S"].ap, 0.0), writes=[b["S"]])
    ek.op("vector", lambda v: v.memset(b["S_bf"].ap, 0.0), writes=[b["S_bf"]])

    for cc in range(NT):
        t0 = cc * 256
        if cc % 2 == 0:
            _emit_a_gate_g(ctx, b, t0, min(512, T - t0))
        for fc in range(4):
            q_ap, q_reg = pfb(ctx, fc // 2, (fc % 2) * 256, (fc % 2) * 256 + 256)
            _proj_group(ctx, q_ap, q_reg, b["wq"], lambda dc, fc=fc: dc * 512 + fc * 128, xn, t0, 256)
            ek.op("scalar", lambda s, fc=fc, q_ap=q_ap: s.copy(b["q_sb"][:, fc * 256:(fc + 1) * 256], q_ap),
                  reads=[q_reg], writes=[(b["q_sb"], fc * 256, (fc + 1) * 256)])
        for fc in range(4):
            k_ap, k_reg = pfb(ctx, 2 + fc // 2, (fc % 2) * 256, (fc % 2) * 256 + 256)
            _proj_group(ctx, k_ap, k_reg, b["wk"], lambda dc, fc=fc: dc * 512 + fc * 128, xn, t0, 256)
            ek.op("scalar", lambda s, fc=fc, k_ap=k_ap: s.copy(b["k_sb"][:, fc * 256:(fc + 1) * 256], k_ap),
                  reads=[k_reg], writes=[(b["k_sb"], fc * 256, (fc + 1) * 256)])
        for vc in range(8):
            v_ap, v_reg = pfb(ctx, (vc % 4) // 2, (vc % 2) * 256, (vc % 2) * 256 + 256)
            _proj_group(ctx, v_ap, v_reg, b["wv"], lambda dc, vc=vc: dc * 1024 + vc * 128, xn, t0, 256)
            ek.op("scalar", lambda s, vc=vc, v_ap=v_ap: s.copy(b["v_sb"][:, vc * 256:(vc + 1) * 256], v_ap),
                  reads=[v_reg], writes=[(b["v_sb"], vc * 256, (vc + 1) * 256)])
        for hc in range(4):
            _emit_a_head(ctx, b, cc, hc)
        for dc in range(8):
            y_ap, y_reg = pfb(ctx, dc // 2, (dc % 2) * 256, (dc % 2) * 256 + 256)
            lhsTs = [b["wout"][:, vc * 1024 + dc * 128: vc * 1024 + dc * 128 + 128] for vc in range(8)]
            rhss = [b["og"][:, vc * 256:(vc + 1) * 256] for vc in range(8)]
            mm_chain(ctx, y_ap, y_reg, lhsTs, rhss,
                     reads=[b["wout"], b["og"]])
            ek.op("scalar", lambda s, dc=dc, y_ap=y_ap: s.copy(
                b["y_sb"][:, dc * 256:(dc + 1) * 256], y_ap),
                reads=[y_reg], writes=[(b["y_sb"], dc * 256, (dc + 1) * 256)])
        emit_residual_tile(ctx, b, t0, 256, "y_sb", x_prev_name, x_next_name)
        emit_ln_from_xtile(ctx, b, t0, 256)


def _emit_a_gate_g(ctx, b, t0, W):
    ek, c, T = ctx.ek, ctx.c, ctx.T
    xn = b["xn"]
    for fh in range(2):
        load_weight_cast(ctx, b["wg"], ctx.dram["L0_attn_Wg"], "L0_attn_Wg", 8, 512,
                         col0=fh * 512)
        for fc4 in range(4):
            fc = fh * 4 + fc4
            g_ap, g_reg = pfb(ctx, fc4, 0, W)
            _proj_group(ctx, g_ap, g_reg, b["wg"], lambda dc, fc4=fc4: dc * 512 + fc4 * 128, xn, t0, W)
            ek.op("scalar", lambda s, fc=fc, g_ap=g_ap: s.activation(
                b["gate"][:, fc * W:(fc + 1) * W], g_ap, AF.Silu),
                reads=[g_reg], writes=[(b["gate"], fc * W, (fc + 1) * W)])
    z1_ap, z1_reg = pfb(ctx, 4, 0, W)
    lhsTs = [b["wgk1"][:, dc * 16:(dc + 1) * 16] for dc in range(8)]
    rhss = [xn[:, dc * T + t0: dc * T + t0 + W] for dc in range(8)]
    mm_chain(ctx, z1_ap[0:16, :], z1_reg, lhsTs, rhss,
             reads=[b["wgk1"]] + [(xn, dc * T + t0, dc * T + t0 + W) for dc in range(8)])
    ek.op("scalar", lambda s: s.copy(b["z1"][0:16, 0:W], z1_ap[0:16, :]),
          reads=[z1_reg], writes=[b["z1"]])
    for fc in range(4):
        z_ap, z_reg = pfb(ctx, 5, 0, W)
        ek.group("tensor", lambda t, fc=fc, z_ap=z_ap: [
            t.matmul(z_ap, b["wgk2"][0:64, fc * 128:(fc + 1) * 128], b["z1"][0:64, 0:W],
                     start=True, stop=True)],
            reads=[b["wgk2"], b["z1"]], writes=[z_reg])
        bias = b["bgk2"][:, fc:fc + 1]
        ek.op("vector", lambda v, z_ap=z_ap, bias=bias: v.tensor_scalar(
            b["ga"][:, 0:W], z_ap, bias, 0.0, ALU.add, ALU.min),
            reads=[z_reg, b["bgk2"]], writes=[(b["ga"], 0, W)])
        ek.op("scalar", lambda s, z_ap=z_ap, bias=bias: s.activation(
            b["gb"][:, 0:W], z_ap, AF.Abs, bias=bias, scale=1.0),
            reads=[z_reg, b["bgk2"]], writes=[(b["gb"], 0, W)])
        ek.op("scalar", lambda s: s.activation(b["gb"][:, 0:W], b["gb"][:, 0:W], AF.Exp, scale=-1.0),
              reads=[(b["gb"], 0, W)], writes=[(b["gb"], 0, W)])
        ek.op("scalar", lambda s: s.activation(b["gb"][:, 0:W], b["gb"][:, 0:W], AF.Ln,
                                               bias=ctx.c["cpool"][:, 1:2]),
              reads=[(b["gb"], 0, W), (ctx.c["cpool"], 1, 2)], writes=[(b["gb"], 0, W)])
        ek.op("vector", lambda v: v.tensor_sub(b["ga"][:, 0:W], b["ga"][:, 0:W], b["gb"][:, 0:W]),
              reads=[(b["ga"], 0, W), (b["gb"], 0, W)], writes=[(b["ga"], 0, W)])
        ek.op("vector", lambda v, fc=fc: v.tensor_scalar(
            b["g_sb"][:, fc * 512:fc * 512 + W], b["ga"][:, 0:W], 1.0 / 16.0, -3.0, ALU.mult, ALU.max),
            reads=[(b["ga"], 0, W)], writes=[(b["g_sb"], fc * 512, fc * 512 + W)])


def _emit_a_head(ctx, b, cc, hc):
    ek, c = ctx.ek, ctx.c
    coff = (cc % 2) * 256
    ek.op("vector", lambda v: v.tensor_tensor_scan(
        b["b_sb"][:, hc * 256:(hc + 1) * 256], b["g_sb"][:, hc * 512 + coff: hc * 512 + coff + 256],
        c["zeros256"].ap, 0.0, ALU.add, ALU.add),
        reads=[(b["g_sb"], hc * 512 + coff, hc * 512 + coff + 256), c["zeros256"]],
        writes=[(b["b_sb"], hc * 256, (hc + 1) * 256)])
    bs = b["b_sb"][:, hc * 256:(hc + 1) * 256]
    bs_reg = (b["b_sb"], hc * 256, (hc + 1) * 256)
    ek.op("scalar", lambda s: s.activation(b["ebuf"][:, 0:256], bs, AF.Exp),
          reads=[bs_reg], writes=[(b["ebuf"], 0, 256)])
    ek.op("scalar", lambda s: s.activation(b["ebuf"][:, 256:512], bs, AF.Exp, scale=-1.0),
          reads=[bs_reg], writes=[(b["ebuf"], 256, 512)])
    ek.op("scalar", lambda s: s.activation(
        b["ebc"][:, hc:hc + 1], b["b_sb"][:, hc * 256 + 255: hc * 256 + 256], AF.Exp),
        reads=[bs_reg], writes=[(b["ebc"], hc, hc + 1)])
    ek.op("vector", lambda v: v.tensor_mul(b["qt"].ap, b["q_sb"][:, hc * 256:(hc + 1) * 256], b["ebuf"][:, 0:256]),
          reads=[(b["q_sb"], hc * 256, (hc + 1) * 256), (b["ebuf"], 0, 256)], writes=[b["qt"]])
    ek.op("vector", lambda v: v.tensor_mul(b["kh"].ap, b["k_sb"][:, hc * 256:(hc + 1) * 256], b["ebuf"][:, 256:512]),
          reads=[(b["k_sb"], hc * 256, (hc + 1) * 256), (b["ebuf"], 256, 512)], writes=[b["kh"]])
    ek.op("vector", lambda v: v.tensor_scalar_mul(b["kb"].ap, b["kh"].ap, b["ebc"][:, hc:hc + 1]),
          reads=[b["kh"], (b["ebc"], hc, hc + 1)], writes=[b["kb"]])
    for sblk in range(2):
        at_ap, at_reg = pfb(ctx, 4, sblk * 256, (sblk + 1) * 256)
        ek.group("tensor", lambda t, sblk=sblk, at_ap=at_ap: [
            t.matmul(at_ap, b["kh"][:, sblk * 128:(sblk + 1) * 128], b["qt"].ap, start=True, stop=True)],
            reads=[b["kh"], b["qt"]], writes=[at_reg])
        mask = c["maskA0"] if sblk == 0 else c["maskA1"]
        ek.op("vector", lambda v, sblk=sblk, at_ap=at_ap, mask=mask: v.tensor_mul(
            b["at_sb"][:, sblk * 256:(sblk + 1) * 256], at_ap, mask.ap),
            reads=[at_reg, mask], writes=[(b["at_sb"], sblk * 256, (sblk + 1) * 256)])
    for sblk in range(2):
        for vh in range(2):
            pt_ap, pt_reg = ptb(ctx, 0, vh * 128, (vh + 1) * 128)
            ek.op("tensor", lambda t, sblk=sblk, vh=vh, pt_ap=pt_ap: t.transpose(
                pt_ap, b["v_sb"][:, (2 * hc + vh) * 256 + sblk * 128: (2 * hc + vh) * 256 + (sblk + 1) * 128],
                c["ident"].ap),
                reads=[(b["v_sb"], (2 * hc + vh) * 256 + sblk * 128, (2 * hc + vh) * 256 + (sblk + 1) * 128),
                       c["ident"]],
                writes=[pt_reg])
        ek.op("vector", lambda v, sblk=sblk: v.tensor_copy(
            b["vt_sb"][:, sblk * 256:(sblk + 1) * 256], ctx.pt[:, 0:256]),
            reads=[(ctx.pt, 0, 256)], writes=[(b["vt_sb"], sblk * 256, (sblk + 1) * 256)])
        pt2_ap, pt2_reg = ptb(ctx, 1, 0, 128)
        ek.op("tensor", lambda t, sblk=sblk, pt2_ap=pt2_ap: t.transpose(
            pt2_ap, b["kb"][:, sblk * 128:(sblk + 1) * 128], c["ident"].ap),
            reads=[(b["kb"], sblk * 128, (sblk + 1) * 128), c["ident"]], writes=[pt2_reg])
        ek.op("vector", lambda v, sblk=sblk: v.tensor_copy(
            b["kbt_sb"][:, sblk * 128:(sblk + 1) * 128], ctx.pt[:, 3072 - 3072 + 1024:1024 + 128] if False else pt2_ap),
            reads=[pt2_reg], writes=[(b["kbt_sb"], sblk * 128, (sblk + 1) * 128)])
    for vh in range(2):
        o_ap, o_reg = pfb(ctx, 0, vh * 256, (vh + 1) * 256)
        lhsTs = [b["S_bf"][:, hc * 256 + vh * 128: hc * 256 + (vh + 1) * 128],
                 b["vt_sb"][:, 0 * 256 + vh * 128: 0 * 256 + (vh + 1) * 128],
                 b["vt_sb"][:, 1 * 256 + vh * 128: 1 * 256 + (vh + 1) * 128]]
        rhss = [b["qt"].ap, b["at_sb"][:, 0:256], b["at_sb"][:, 256:512]]
        mm_chain(ctx, o_ap, o_reg, lhsTs, rhss,
                 reads=[(b["S_bf"], hc * 256, (hc + 1) * 256), b["vt_sb"], b["qt"], b["at_sb"]])
    ds_ap, ds_reg = pfb(ctx, 1, 0, 256)
    lhsTs = [b["kbt_sb"][:, 0:128], b["kbt_sb"][:, 128:256]]
    rhss = [b["vt_sb"][:, 0:256], b["vt_sb"][:, 256:512]]
    mm_chain(ctx, ds_ap, ds_reg, lhsTs, rhss, reads=[b["kbt_sb"], b["vt_sb"]])
    for vh in range(2):
        o_ap = ctx.pf[:, 0 * 512 + vh * 256: 0 * 512 + (vh + 1) * 256]
        o_reg = (ctx.pf, vh * 256, (vh + 1) * 256)
        ek.op("scalar", lambda s, vh=vh, o_ap=o_ap: s.activation(
            b["stat_tmp"][:, vh * 256:(vh + 1) * 256], o_ap, AF.Square),
            reads=[o_reg], writes=[(b["stat_tmp"], vh * 256, (vh + 1) * 256)])
    ss_ap, ss_reg = pfb(ctx, 5, 0, 256)
    mm_chain(ctx, ss_ap[0:1, :], ss_reg,
             [c["ones128f"].ap, c["ones128f"].ap],
             [b["stat_tmp"][:, 0:256], b["stat_tmp"][:, 256:512]],
             reads=[c["ones128f"], (b["stat_tmp"], 0, 512)])
    ek.op("scalar", lambda s: s.activation(b["stat_small"][0:1, 0:256], ss_ap[0:1, :], AF.Ln,
                                           bias=ctx.c["cpool"][0:1, 0:1], scale=1.0 / 256.0),
          reads=[ss_reg, (ctx.c["cpool"], 0, 1)], writes=[(b["stat_small"], 0, 256)])
    ek.op("scalar", lambda s: s.activation(b["stat_small"][0:1, 256:512], b["stat_small"][0:1, 0:256],
                                           AF.Exp, scale=-0.5),
          reads=[(b["stat_small"], 0, 256)], writes=[(b["stat_small"], 256, 512)])
    rb_ap, rb_reg = pfb(ctx, 5, 256, 512)
    ek.group("tensor", lambda t: [t.matmul(rb_ap, c["ones_colf"].ap, b["stat_small"][0:1, 256:512],
                                           start=True, stop=True)],
             reads=[c["ones_colf"], (b["stat_small"], 256, 512)], writes=[rb_reg])
    ek.op("vector", lambda v: v.tensor_copy(b["rb_sb"].ap, rb_ap), reads=[rb_reg], writes=[b["rb_sb"]])
    for vh in range(2):
        o_ap = ctx.pf[:, vh * 256: (vh + 1) * 256]
        o_reg = (ctx.pf, vh * 256, (vh + 1) * 256)
        ek.op("vector", lambda v, vh=vh, o_ap=o_ap: v.scalar_tensor_tensor(
            b["stat_tmp"][:, vh * 256:(vh + 1) * 256], o_ap, b["rmsw"][:, vh:vh + 1], b["rb_sb"].ap,
            ALU.mult, ALU.mult),
            reads=[o_reg, b["rmsw"], b["rb_sb"]], writes=[(b["stat_tmp"], vh * 256, (vh + 1) * 256)])
        gcol = (2 * hc + vh) * 512 + (cc % 2) * 256
        ek.op("vector", lambda v, vh=vh, gcol=gcol: v.tensor_mul(
            b["og"][:, (2 * hc + vh) * 256:(2 * hc + vh + 1) * 256],
            b["stat_tmp"][:, vh * 256:(vh + 1) * 256], b["gate"][:, gcol:gcol + 256]),
            reads=[(b["stat_tmp"], vh * 256, (vh + 1) * 256), (b["gate"], gcol, gcol + 256)],
            writes=[(b["og"], (2 * hc + vh) * 256, (2 * hc + vh + 1) * 256)])
    ek.op("vector", lambda v: v.tensor_scalar_mul(
        b["S"][:, hc * 256:(hc + 1) * 256], b["S"][:, hc * 256:(hc + 1) * 256], b["ebc"][:, hc:hc + 1]),
        reads=[(b["S"], hc * 256, (hc + 1) * 256), (b["ebc"], hc, hc + 1)],
        writes=[(b["S"], hc * 256, (hc + 1) * 256)])
    ek.op("vector", lambda v: v.tensor_add(
        b["S"][:, hc * 256:(hc + 1) * 256], b["S"][:, hc * 256:(hc + 1) * 256], ds_ap),
        reads=[(b["S"], hc * 256, (hc + 1) * 256), ds_reg],
        writes=[(b["S"], hc * 256, (hc + 1) * 256)])
    ek.op("vector", lambda v: v.tensor_copy(b["S_bf"][:, hc * 256:(hc + 1) * 256], b["S"][:, hc * 256:(hc + 1) * 256]),
          reads=[(b["S"], hc * 256, (hc + 1) * 256)], writes=[(b["S_bf"], hc * 256, (hc + 1) * 256)])


def alloc_routing_bufs(ctx, b, sfx, tstride=1024):
    mem = ctx.mem
    b["rwT"] = mem.sb("rwT" + sfx, 128, 8 * 8, BF16)
    b["rtmp"] = mem.sb("rtmp" + sfx, 8, 1024, BF16)
    b["bias_b"] = mem.sb("bias_b" + sfx, 128, 8, F32)
    b["bias_row"] = mem.sb("bias_row" + sfx, 1, 8, F32)
    b["l_sb"] = mem.sb("l_sb" + sfx, 128, 8, F32)
    b["l_ms"] = mem.sb("l_ms" + sfx, 128, 8, F32)
    b["msk"] = mem.sb("msk" + sfx, 128, 8, mybir.dt.uint8)
    b["m1"] = mem.sb("m1" + sfx, 128, 4, F32)
    b["el"] = mem.sb("el" + sfx, 128, 8, F32)
    b["cw16"] = mem.sb("cw16" + sfx, 128, 8, BF16)
    b["cwrow"] = mem.sb("cwrow" + sfx, 1, 8 * tstride, BF16)
    b["cw_stride"] = tstride
    b["cwb"] = mem.sb("cwb" + sfx, 128, 1024, BF16)


def alloc_phase_b_bufs(ctx, b, with_stash=False):
    mem = ctx.mem
    alloc_routing_bufs(ctx, b, "_b")
    b["y_sb"] = mem.sb("y_sb_b", 128, 8 * 256, F32)
    b["w1s"] = mem.sb("w1s", 128, 2 * 8 * 128, BF16)
    b["w3s"] = mem.sb("w3s", 128, 2 * 8 * 128, BF16)
    b["eo"] = mem.sb("eo", 128, 8 * 1024, BF16)
    b["w2s"] = mem.sb("w2s", 128, 8 * 1024, BF16)
    b["y_acc"] = mem.sb("y_acc", 128, 8 * 1024, F32)
    if with_stash:
        b["stash_stg"] = mem.sb("stash_stg", 128, 2 * 1024, BF16)


def make_stash_chunks(ctx, b):
    ek = ctx.ek
    stg = b["stash_stg"]
    chunks = []
    bi = 0
    jobs = []
    for nm, dn in (("L1_ffn_gate_w", "gwbf"), ("L1_ffn_up_w", "uwbf")):
        for e in range(8):
            for t in range(32):
                jobs.append((nm, dn, e, t, 0))
    for e in range(8):
        for t in range(8):
            for c4 in range(4):
                jobs.append(("L1_ffn_down_w", "dwbf", e, t, c4 * 1024))
    for (nm, dn, e, t, c0) in jobs:
        half = bi % 2
        bi += 1
        src = ctx.dram[nm][e].rearrange("(t p) n -> p t n", p=128)[:, t, c0:c0 + 1024]
        dst = ctx.dram[dn][e].rearrange("(t p) n -> p t n", p=128)[:, t, c0:c0 + 1024]
        stg_w = stg.h[:, half * 1024:(half + 1) * 1024]
        ncols = 1024 if dn != "dwbf" else 4096
        rlo = e * 4096 * 1024 + t * 128 * ncols

        def emit(src=src, dst=dst, stg_w=stg_w, nm=nm, dn=dn, rlo=rlo, half=half):
            ek.dma("gpsimd", lambda g: g.dma_start(stg_w, src),
                   reads=[dreg(nm, rlo, rlo + 128 * 1024)],
                   writes=[(stg, half * 1024, (half + 1) * 1024)], slot="st" + str(half))
            ek.dma("sync", lambda sy: sy.dma_start(dst, stg_w),
                   reads=[(stg, half * 1024, (half + 1) * 1024)],
                   writes=[dreg(dn, rlo, rlo + 128 * 1024)], slot="ss" + str(half))
        chunks.append(emit)
    return chunks


def make_repack_chunks(ctx, b):
    ek = ctx.ek
    stg = b["rpk_stg"]
    chunks = []
    bi = 0
    for e in range(8):
        for dn, dT in (("gwbf", "gwT"), ("uwbf", "uwT")):
            for icg in range(8):
                for hcp in range(4):
                    half = bi % 2
                    bi += 1

                    def emit(e=e, dn=dn, dT=dT, icg=icg, hcp=hcp, half=half):
                        for hci in range(2):
                            hc = hcp * 2 + hci
                            src = ctx.dram[dn][e, icg * 512:(icg + 1) * 512, hc * 128:(hc + 1) * 128]
                            dst = stg.h[:, half * 1024 + hci * 512: half * 1024 + (hci + 1) * 512]
                            ek.dma("sync", lambda sy, src=src, dst=dst: sy.dma_start(dst, src, transpose=True),
                                   reads=[dreg(dn, e * 4096 * 1024 + icg * 512 * 1024,
                                               e * 4096 * 1024 + (icg + 1) * 512 * 1024)],
                                   writes=[(stg, half * 1024 + hci * 512, half * 1024 + (hci + 1) * 512)],
                                   slot="rpk" + str(half))
                        dstT = ctx.dram[dT][e, icg, :, hcp * 1024:(hcp + 1) * 1024]
                        ek.dma("gpsimd", lambda g, dstT=dstT, half=half: g.dma_start(
                            dstT, stg.h[:, half * 1024:(half + 1) * 1024]),
                            reads=[(stg, half * 1024, (half + 1) * 1024)],
                            writes=[dreg(dT, (e * 8 + icg) * 128 * 4096 + hcp * 128 * 1024,
                                         (e * 8 + icg) * 128 * 4096 + (hcp + 1) * 128 * 1024)],
                            slot="rps" + str(half))
                    chunks.append(emit)
        for ihalf in range(2):
            for ic in range(16):
                half = bi % 2
                bi += 1

                def emit(e=e, ihalf=ihalf, ic=ic, half=half):
                    icg_ = ihalf * 16 + ic
                    src = ctx.dram["dwbf"][e, :, icg_ * 128:(icg_ + 1) * 128]
                    dst = stg.h[:, half * 1024:(half + 1) * 1024]
                    ek.dma("sync", lambda sy, src=src, dst=dst: sy.dma_start(dst, src, transpose=True),
                           reads=[dreg("dwbf", e * 1024 * 4096, (e + 1) * 1024 * 4096)],
                           writes=[(stg, half * 1024, (half + 1) * 1024)],
                           slot="rpk" + str(half))
                    dstT = ctx.dram["dwT"][e, ihalf, :, ic * 1024:(ic + 1) * 1024]
                    ek.dma("gpsimd", lambda g, dstT=dstT, half=half: g.dma_start(
                        dstT, stg.h[:, half * 1024:(half + 1) * 1024]),
                        reads=[(stg, half * 1024, (half + 1) * 1024)],
                        writes=[dreg("dwT", (e * 2 + ihalf) * 128 * 16384 + ic * 128 * 1024,
                                     (e * 2 + ihalf) * 128 * 16384 + (ic + 1) * 128 * 1024)],
                        slot="rps" + str(half))
                chunks.append(emit)
    return chunks


def emit_phase_b_routing(ctx, b, tt0, te):
    ek, c, T = ctx.ek, ctx.c, ctx.T
    xn = b["xn"]
    for tci in range(te // 128):
        tc0 = tt0 + tci * 128
        lg_ap, lg_reg = pfb(ctx, 4, 0, 128)
        lhsTs = [xn[:, dc * T + tc0: dc * T + tc0 + 128] for dc in range(8)]
        rhss = [b["rwT"][:, dc * 8:(dc + 1) * 8] for dc in range(8)]
        mm_chain(ctx, lg_ap[:, 0:8], lg_reg, lhsTs, rhss,
                 reads=[b["rwT"]] + [(xn, dc * T + tc0, dc * T + tc0 + 128) for dc in range(8)])
        ek.op("vector", lambda v, lg_ap=lg_ap: v.tensor_add(b["l_sb"].ap, lg_ap[:, 0:8], b["bias_b"].ap),
              reads=[lg_reg, b["bias_b"]], writes=[b["l_sb"]])
        ek.op("vector", lambda v: v.reduce_max(b["m1"][:, 0:1], b["l_sb"].ap, axis=AX.X),
              reads=[b["l_sb"]], writes=[(b["m1"], 0, 1)])
        ek.op("vector", lambda v: v.tensor_single_scalar(b["msk"].ap, b["l_sb"].ap, b["m1"][:, 0:1], ALU.is_lt),
              reads=[b["l_sb"], (b["m1"], 0, 1)], writes=[b["msk"]])
        ek.op("vector", lambda v: v.memset(b["l_ms"].ap, NEG_BIG), writes=[b["l_ms"]])
        ek.op("vector", lambda v: v.copy_predicated(b["l_ms"].ap, b["msk"].ap, b["l_sb"].ap),
              reads=[b["msk"], b["l_sb"], b["l_ms"]], writes=[b["l_ms"]])
        ek.op("vector", lambda v: v.reduce_max(b["m1"][:, 1:2], b["l_ms"].ap, axis=AX.X),
              reads=[b["l_ms"]], writes=[(b["m1"], 1, 2)])
        ek.op("vector", lambda v: v.tensor_single_scalar(b["m1"][:, 2:3], b["m1"][:, 0:1], -1.0, ALU.mult),
              reads=[(b["m1"], 0, 1)], writes=[(b["m1"], 2, 3)])
        ek.op("scalar", lambda s: s.activation(b["el"].ap, b["l_sb"].ap, AF.Exp, bias=b["m1"][:, 2:3]),
              reads=[b["l_sb"], (b["m1"], 2, 3)], writes=[b["el"]])
        ek.op("vector", lambda v: v.tensor_add(b["m1"][:, 3:4], b["m1"][:, 1:2], b["m1"][:, 2:3]),
              reads=[(b["m1"], 1, 3)], writes=[(b["m1"], 3, 4)])
        ek.op("scalar", lambda s: s.activation(b["m1"][:, 3:4], b["m1"][:, 3:4], AF.Exp),
              reads=[(b["m1"], 3, 4)], writes=[(b["m1"], 3, 4)])
        ek.op("vector", lambda v: v.tensor_single_scalar(b["m1"][:, 3:4], b["m1"][:, 3:4], 1.0, ALU.add),
              reads=[(b["m1"], 3, 4)], writes=[(b["m1"], 3, 4)])
        ek.op("vector", lambda v: v.reciprocal(b["m1"][:, 3:4], b["m1"][:, 3:4]),
              reads=[(b["m1"], 3, 4)], writes=[(b["m1"], 3, 4)])
        ek.op("vector", lambda v: v.tensor_single_scalar(b["l_ms"].ap, b["l_sb"].ap, b["m1"][:, 1:2], ALU.is_ge),
              reads=[b["l_sb"], (b["m1"], 1, 2)], writes=[b["l_ms"]])
        ek.op("vector", lambda v: v.tensor_mul(b["el"].ap, b["el"].ap, b["l_ms"].ap),
              reads=[b["el"], b["l_ms"]], writes=[b["el"]])
        ek.op("vector", lambda v: v.tensor_scalar_mul(b["cw16"].ap, b["el"].ap, b["m1"][:, 3:4]),
              reads=[b["el"], (b["m1"], 3, 4)], writes=[b["cw16"]])
        for e in range(8):
            pt_ap, pt_reg = ptb(ctx, 0, e * 128, (e + 1) * 128)
            ek.op("tensor", lambda t, e=e, pt_ap=pt_ap: t.transpose(
                pt_ap[0:1, :], b["cw16"][:, e:e + 1], c["ident"].ap),
                reads=[(b["cw16"], e, e + 1), c["ident"]], writes=[pt_reg])
        for e in range(8):
            pt_ap = ctx.pt[0:1, e * 128:(e + 1) * 128]
            ek.op("vector", lambda v, e=e, pt_ap=pt_ap, tci=tci: v.tensor_copy(
                b["cwrow"][0:1, e * 1024 + tci * 128: e * 1024 + (tci + 1) * 128], pt_ap),
                reads=[(ctx.pt, e * 128, (e + 1) * 128)],
                writes=[(b["cwrow"], e * 1024 + tci * 128, e * 1024 + (tci + 1) * 128)])


def emit_phase_b(ctx, b, x_prev_name, x_next_name, stash_chunks=None):
    ek, c, T = ctx.ek, ctx.c, ctx.T
    xn = b["xn"]
    stash_chunks = list(stash_chunks or [])
    n_iter = (T // 1024) * 9
    per_iter = -(-len(stash_chunks) // n_iter) if stash_chunks else 0
    si = 0
    ek.dma("gpsimd", lambda g: g.dma_start(b["rtmp"][0:8, :], ctx.dram["L0_ffn_router_w"][:, :]),
           reads=[dreg("L0_ffn_router_w")], writes=[b["rtmp"]], slot="rwT")
    for dc in range(8):
        pt_ap, pt_reg = ptb(ctx, 1, 0, 8)
        ek.op("tensor", lambda t, dc=dc, pt_ap=pt_ap: t.transpose(
            pt_ap, b["rtmp"][0:8, dc * 128:(dc + 1) * 128], c["ident"][0:8, 0:8]),
            reads=[b["rtmp"], c["ident"]], writes=[pt_reg])
        ek.op("vector", lambda v, dc=dc, pt_ap=pt_ap: v.tensor_copy(b["rwT"][:, dc * 8:(dc + 1) * 8], pt_ap),
              reads=[pt_reg], writes=[(b["rwT"], dc * 8, (dc + 1) * 8)])
    ek.dma("sync", lambda sy: sy.dma_start(b["bias_row"][0:1, :], ctx.dram["L0_ffn_experts_bias"][None, :]),
           reads=[dreg("L0_ffn_experts_bias")], writes=[b["bias_row"]], slot="br")
    bb_ap, bb_reg = pfb(ctx, 5, 0, 8)
    ek.group("tensor", lambda t: [t.matmul(bb_ap[:, 0:8], c["ones_colf"].ap, b["bias_row"][0:1, :],
                                           start=True, stop=True)],
             reads=[c["ones_colf"], b["bias_row"]], writes=[bb_reg])
    ek.op("vector", lambda v: v.tensor_copy(b["bias_b"].ap, bb_ap[:, 0:8]), reads=[bb_reg], writes=[b["bias_b"]])
    for tt0 in range(0, T, 1024):
        te = min(1024, T - tt0)
        emit_phase_b_routing(ctx, b, tt0, te)
        for e in range(9):
            for _ in range(per_iter):
                if si < len(stash_chunks):
                    stash_chunks[si]()
                    si += 1
            shared = (e == 8)
            w1_name = "L0_ffn_sw1" if shared else "L0_ffn_w1"
            w3_name = "L0_ffn_sw3" if shared else "L0_ffn_w3"
            w2_name = "L0_ffn_sw2" if shared else "L0_ffn_w2"
            ei = 0 if shared else e
            if not shared:
                for q in range(te // 512):
                    cb_ap, cb_reg = pfb(ctx, 5, 0, 512)
                    ek.group("tensor", lambda t, e=e, q=q, cb_ap=cb_ap: [
                        t.matmul(cb_ap, c["ones_col"].ap,
                                 b["cwrow"][0:1, e * 1024 + q * 512: e * 1024 + (q + 1) * 512],
                                 start=True, stop=True)],
                        reads=[c["ones_col"], (b["cwrow"], e * 1024 + q * 512, e * 1024 + (q + 1) * 512)],
                        writes=[cb_reg])
                    ek.op("vector", lambda v, q=q, cb_ap=cb_ap: v.tensor_copy(
                        b["cwb"][:, q * 512:(q + 1) * 512], cb_ap),
                        reads=[cb_reg], writes=[(b["cwb"], q * 512, (q + 1) * 512)])
            for hh in range(2):
                for hc8 in range(8):
                    hc = hh * 8 + hc8
                    half = hc8 % 2
                    sei = 8 if shared else e
                    for wbuf, wname, bfn in ((b["w1s"], w1_name, "w1bf"), (b["w3s"], w3_name, "w3bf")):
                        dst = wbuf.h[:, half * 1024:(half + 1) * 1024].rearrange("p (dc n) -> p dc n", dc=8)
                        srcbf = ctx.dram[bfn][sei].rearrange("(dc p) n -> p dc n", p=128)[:, :, hc * 128:(hc + 1) * 128]
                        if tt0 == 0:
                            src = ctx.dram[wname][ei].rearrange("(dc p) n -> p dc n", p=128)[:, :, hc * 128:(hc + 1) * 128]
                            ek.dma("gpsimd", lambda g, src=src, dst=dst: g.dma_start(dst, src),
                                   reads=[dreg(wname, (ei * 16 + hc) * 131072, (ei * 16 + hc + 1) * 131072)],
                                   writes=[(wbuf, half * 1024, (half + 1) * 1024)],
                                   slot=wbuf.h.name + str(half))
                            srcs = wbuf.h[:, half * 1024:(half + 1) * 1024].rearrange("p (dc n) -> p dc n", dc=8)
                            ek.dma("sync", lambda sy, srcbf=srcbf, srcs=srcs: sy.dma_start(srcbf, srcs),
                                   reads=[(wbuf, half * 1024, (half + 1) * 1024)],
                                   writes=[dreg(bfn, (sei * 16 + hc) * 131072, (sei * 16 + hc + 1) * 131072)],
                                   slot="bs" + wbuf.h.name[-2:] + str(half))
                        else:
                            ek.dma("scalar", lambda sc, srcbf=srcbf, dst=dst: sc.dma_start(dst, srcbf),
                                   reads=[dreg(bfn, (sei * 16 + hc) * 131072, (sei * 16 + hc + 1) * 131072)],
                                   writes=[(wbuf, half * 1024, (half + 1) * 1024)],
                                   slot=wbuf.h.name + str(half))
                    for ts_ in range(te // 512):
                        tq0 = tt0 + ts_ * 512
                        g_ap, g_reg = pfb(ctx, 4, 0, 512)
                        u_ap, u_reg = pfb(ctx, 5, 0, 512)
                        for (wbuf, o_ap, o_reg) in ((b["w1s"], g_ap, g_reg), (b["w3s"], u_ap, u_reg)):
                            lhsTs = [wbuf[:, half * 1024 + dc * 128: half * 1024 + (dc + 1) * 128] for dc in range(8)]
                            rhss = [xn[:, dc * T + tq0: dc * T + tq0 + 512] for dc in range(8)]
                            mm_chain(ctx, o_ap, o_reg, lhsTs, rhss,
                                     reads=[(wbuf, half * 1024, (half + 1) * 1024)] +
                                           [(xn, dc * T + tq0, dc * T + tq0 + 512) for dc in range(8)])
                        st = b["stat_tmp"]
                        ek.op("scalar", lambda s, g_ap=g_ap, st=st: s.activation(st[:, 0:512], g_ap, AF.Silu),
                              reads=[g_reg], writes=[(st, 0, 512)])
                        eo_lo = hc8 * 1024 + ts_ * 512
                        if shared:
                            ek.op("vector", lambda v, u_ap=u_ap, eo_lo=eo_lo, st=st: v.scalar_tensor_tensor(
                                b["eo"][:, eo_lo:eo_lo + 512], u_ap, 1.0, st[:, 0:512], ALU.mult, ALU.mult),
                                reads=[u_reg, (st, 0, 512)], writes=[(b["eo"], eo_lo, eo_lo + 512)])
                        else:
                            ek.op("vector", lambda v, u_ap=u_ap, st=st: v.scalar_tensor_tensor(
                                st[:, 512:1024], u_ap, 1.0, st[:, 0:512], ALU.mult, ALU.mult),
                                reads=[u_reg, (st, 0, 512)], writes=[(st, 512, 1024)])
                            ek.op("vector", lambda v, eo_lo=eo_lo, ts_=ts_, st=st: v.tensor_mul(
                                b["eo"][:, eo_lo:eo_lo + 512], st[:, 512:1024], b["cwb"][:, ts_ * 512:(ts_ + 1) * 512]),
                                reads=[(st, 512, 1024), (b["cwb"], ts_ * 512, (ts_ + 1) * 512)],
                                writes=[(b["eo"], eo_lo, eo_lo + 512)])
                sei = 8 if shared else e
                for hc8 in range(8):
                    hc = hh * 8 + hc8
                    dst = b["w2s"].h[:, hc8 * 1024:(hc8 + 1) * 1024].rearrange("p (o n) -> p o n", o=1)
                    srcbf = ctx.dram["w2bf"][sei].rearrange("(hc p) n -> p hc n", p=128)[:, hc:hc + 1, :]
                    if tt0 == 0:
                        src = ctx.dram[w2_name][ei].rearrange("(hc p) n -> p hc n", p=128)[:, hc:hc + 1, :]
                        ek.dma("gpsimd", lambda g, src=src, dst=dst: g.dma_start(dst, src),
                               reads=[dreg(w2_name, (ei * 16 + hc) * 131072, (ei * 16 + hc + 1) * 131072)],
                               writes=[(b["w2s"], hc8 * 1024, (hc8 + 1) * 1024)],
                               slot="w2s" + str(hc8 % 2))
                        srcs = b["w2s"].h[:, hc8 * 1024:(hc8 + 1) * 1024].rearrange("p (o n) -> p o n", o=1)
                        ek.dma("sync", lambda sy, srcbf=srcbf, srcs=srcs: sy.dma_start(srcbf, srcs),
                               reads=[(b["w2s"], hc8 * 1024, (hc8 + 1) * 1024)],
                               writes=[dreg("w2bf", (sei * 16 + hc) * 131072, (sei * 16 + hc + 1) * 131072)],
                               slot="bs2" + str(hc8 % 2))
                    else:
                        ek.dma("scalar", lambda sc, srcbf=srcbf, dst=dst: sc.dma_start(dst, srcbf),
                               reads=[dreg("w2bf", (sei * 16 + hc) * 131072, (sei * 16 + hc + 1) * 131072)],
                               writes=[(b["w2s"], hc8 * 1024, (hc8 + 1) * 1024)],
                               slot="w2s" + str(hc8 % 2))
                for ts4 in range(te // 256):
                    for dc in range(8):
                        y_ap, y_reg = pfb(ctx, dc // 2, (dc % 2) * 256, (dc % 2) * 256 + 256)
                        lhsTs = [b["w2s"][:, hc8_ * 1024 + dc * 128: hc8_ * 1024 + (dc + 1) * 128] for hc8_ in range(8)]
                        rhss = [b["eo"][:, hc8_ * 1024 + ts4 * 256: hc8_ * 1024 + ts4 * 256 + 256] for hc8_ in range(8)]
                        mm_chain(ctx, y_ap, y_reg, lhsTs, rhss,
                                 reads=[b["w2s"], b["eo"]])
                        ya_lo = dc * 1024 + ts4 * 256
                        if e == 0 and hh == 0:
                            ek.op("vector", lambda v, y_ap=y_ap, ya_lo=ya_lo: v.tensor_copy(
                                b["y_acc"][:, ya_lo:ya_lo + 256], y_ap),
                                reads=[y_reg], writes=[(b["y_acc"], ya_lo, ya_lo + 256)])
                        else:
                            ek.op("vector", lambda v, y_ap=y_ap, ya_lo=ya_lo: v.tensor_add(
                                b["y_acc"][:, ya_lo:ya_lo + 256], b["y_acc"][:, ya_lo:ya_lo + 256], y_ap),
                                reads=[y_reg, (b["y_acc"], ya_lo, ya_lo + 256)],
                                writes=[(b["y_acc"], ya_lo, ya_lo + 256)])
        for cc4 in range(te // 256):
            t0 = tt0 + cc4 * 256
            emit_residual_tile_cols(ctx, b, t0, 256, "y_acc",
                                    lambda dc, cc4=cc4: dc * 1024 + cc4 * 256,
                                    x_prev_name, x_next_name)
            emit_ln_from_xtile(ctx, b, t0, 256)


def emit_residual_tile_cols(ctx, bufs, t0, W, y_src, ycol, x_prev_name, x_next_name, store=True):
    ek, T = ctx.ek, ctx.T
    xt, ysb = bufs["xtile"], bufs[y_src]
    xprev = ctx.dram[x_prev_name]
    src = xprev.rearrange("(c p) t -> p c t", p=128)[:, :, t0:t0 + W]
    ek.dma("sync", lambda sy: sy.dma_start(xt.h[:, 0:8 * W].rearrange("p (c w) -> p c w", c=8), src),
           reads=[dreg(x_prev_name, t0, t0 + W)], writes=[(xt, 0, 8 * W)], slot="xt")
    for cc in range(8):
        yl = ycol(cc)
        ek.op("vector", lambda v, cc=cc, yl=yl: v.tensor_add(
            xt[:, cc * W:(cc + 1) * W], ysb[:, yl:yl + W], xt[:, cc * W:(cc + 1) * W]),
            reads=[(ysb, yl, yl + W), (xt, cc * W, (cc + 1) * W)],
            writes=[(xt, cc * W, (cc + 1) * W)])
    if store:
        xnext = ctx.dram[x_next_name]
        dst = xnext.rearrange("(c p) t -> p c t", p=128)[:, :, t0:t0 + W]
        ek.dma("sync", lambda sy: sy.dma_start(dst, xt.h[:, 0:8 * W].rearrange("p (c w) -> p c w", c=8)),
               reads=[(xt, 0, 8 * W)], writes=[dreg(x_next_name, t0, t0 + W)], slot="xt")


def alloc_phase_c_bufs(ctx, b, with_repack=False):
    mem, T = ctx.mem, ctx.T
    b["hf"] = mem.sb("hf", 128, 8 * T, BF16)
    b["gma"] = mem.sb("gma", 128, 8, F32)
    b["bta"] = mem.sb("bta", 128, 8, F32)
    if with_repack:
        b["rpk_stg"] = mem.sb("rpk_stg", 128, 2 * 1024, BF16)


def _alloc_c_head_bufs(ctx, b):
    mem, T = ctx.mem, ctx.T
    b["wqh"] = mem.sb("wqh", 128, 512, BF16)
    b["wkh"] = mem.sb("wkh", 128, 512, BF16)
    b["wvh"] = mem.sb("wvh", 128, 512, BF16)
    b["q_f"] = mem.sb("q_f", 65, T, BF16)
    b["k_f"] = mem.sb("k_f", 65, T, BF16)
    b["v_tm"] = mem.sb("v_tm", 128, (T // 128) * 65, BF16)
    b["vstage"] = mem.sb("vstage", 64, 512, BF16)
    b["qrow"] = mem.sb("qrow", 1, T, BF16)
    b["mxk"] = mem.sb("mxk", 1, 2, F32)
    b["PT"] = mem.sb("PT", 128, 2 * 512, BF16)
    b["rs1"] = mem.sb("rs1", 1, 512, F32)
    b["ou_sb"] = mem.sb("ou_sb", 64, 512, F32)
    b["cstrip"] = mem.sb("cstrip", 128, 4 * 512, F32)


def _alloc_c_tail_bufs(ctx, b):
    mem = ctx.mem
    b["wo"] = mem.sb("wo", 128, 8 * 1024, BF16)
    b["yln"] = mem.sb("yln", 128, 8 * 256, F32)
    b["y_sb"] = mem.sb("y_sb_c", 128, 8 * 256, F32)


def emit_phase_c(ctx, b, x_prev_name, x_next_name, repack_chunks=None):
    ek, c, T = ctx.ek, ctx.c, ctx.T
    xn = b["xn"]
    repack_chunks = list(repack_chunks or [])
    per_head = -(-len(repack_chunks) // 16) if repack_chunks else 0
    ri = 0
    ek.dma("gpsimd", lambda g: g.dma_start(
        b["gma"].ap, ctx.dram["L1_attn_gamma"].rearrange("(dc p) -> p dc", p=128),
        allow_slow_non_contiguous=True),
        reads=[dreg("L1_attn_gamma")], writes=[b["gma"]], slot="gma")
    ek.dma("gpsimd", lambda g: g.dma_start(
        b["bta"].ap, ctx.dram["L1_attn_beta"].rearrange("(dc p) -> p dc", p=128),
        allow_slow_non_contiguous=True),
        reads=[dreg("L1_attn_beta")], writes=[b["bta"]], slot="bta")
    ctx.mem.push()
    _alloc_c_head_bufs(ctx, b)
    cs = b["cstrip"]
    ek.op("gpsimd", lambda g: g.iota(cs[:, 3 * 512:4 * 512], [[1, 512]], channel_multiplier=0,
                                     allow_small_or_imprecise_dtypes=True),
          writes=[(cs, 3 * 512, 4 * 512)])
    for j in (0, 1, 2, 3):
        dst = cs[:, j * 512:(j + 1) * 512]
        ek.op("vector", lambda v, dst=dst, j=j: v.tensor_scalar(
            dst, cs[:, 3 * 512:4 * 512], c["iota_p"][:, 0:1], float(128 * j),
            ALU.subtract, ALU.is_ge),
            reads=[(cs, 3 * 512, 4 * 512), (c["iota_p"], 0, 1)],
            writes=[(cs, j * 512, (j + 1) * 512)])
        ek.op("vector", lambda v, dst=dst: v.tensor_single_scalar(dst, dst, 1.0, ALU.subtract),
              reads=[(cs, j * 512, (j + 1) * 512)], writes=[(cs, j * 512, (j + 1) * 512)])
        ek.op("vector", lambda v, dst=dst: v.tensor_single_scalar(dst, dst, 1.0e30, ALU.mult),
              reads=[(cs, j * 512, (j + 1) * 512)], writes=[(cs, j * 512, (j + 1) * 512)])
    ek.op("vector", lambda v: v.memset(b["v_tm"].ap, 1.0), writes=[b["v_tm"]])
    for h in range(16):
        for _ in range(per_head):
            if ri < len(repack_chunks):
                repack_chunks[ri]()
                ri += 1
        _emit_c_head(ctx, b, h)
    ctx.mem.pop()
    _alloc_c_tail_bufs(ctx, b)
    load_weight_cast(ctx, b["wo"], ctx.dram["L1_attn_Wo"], "L1_attn_Wo", 8, 1024)
    for ti in range(T // 256):
        t0 = ti * 256
        if ti == 0 and "c_hf0" in ctx.dram:
            ek.dma("sync", lambda sy: sy.dma_start(ctx.dram["c_hf0"][:, :], b["hf"].h[:, 0:8 * T]),
                   reads=[(b["hf"], 0, 8 * T)], writes=[dreg("c_hf0")], slot="dbg2")
            ek.dma("sync", lambda sy: sy.dma_start(ctx.dram["c_wo0"][:, :], b["wo"].h[:, :]),
                   reads=[b["wo"]], writes=[dreg("c_wo0")], slot="dbg2")
            ek.dma("sync", lambda sy: sy.dma_start(ctx.dram["c_xn0"][:, :], b["xn"].h[:, 0:8 * T]),
                   reads=[(b["xn"], 0, 8 * T)], writes=[dreg("c_xn0")], slot="dbg2")
        for dc in range(8):
            y_ap, y_reg = pfb(ctx, dc // 2, (dc % 2) * 256, (dc % 2) * 256 + 256)
            lhsTs = [b["wo"][:, hec * 1024 + dc * 128: hec * 1024 + (dc + 1) * 128] for hec in range(8)]
            rhss = [b["hf"][:, hec * T + t0: hec * T + t0 + 256] for hec in range(8)]
            mm_chain(ctx, y_ap, y_reg, lhsTs, rhss,
                     reads=[b["wo"], b["hf"]])
            ek.op("vector", lambda v, dc=dc, y_ap=y_ap, t0=t0: v.tensor_add(
                b["y_sb"][:, dc * 256:(dc + 1) * 256], y_ap, xn[:, dc * T + t0: dc * T + t0 + 256]),
                reads=[y_reg, (xn, dc * T + t0, dc * T + t0 + 256)],
                writes=[(b["y_sb"], dc * 256, (dc + 1) * 256)])
        if ti == 0 and "c_ysb0" in ctx.dram:
            ek.dma("sync", lambda sy: sy.dma_start(ctx.dram["c_ysb0"][:, :], b["y_sb"].h[:, 0:8 * 256]),
                   reads=[(b["y_sb"], 0, 8 * 256)], writes=[dreg("c_ysb0")], slot="dbg2")
        _emit_c_inner_ln(ctx, b, 256)
        if ti == 0 and "c_yln0" in ctx.dram:
            ek.dma("sync", lambda sy: sy.dma_start(ctx.dram["c_yln0"][:, :], b["yln"].h[:, 0:8 * 256]),
                   reads=[(b["yln"], 0, 8 * 256)], writes=[dreg("c_yln0")], slot="dbg2")
        emit_residual_tile(ctx, b, t0, 256, "yln", x_prev_name, x_next_name)
        emit_ln_from_xtile(ctx, b, t0, 256)


def _emit_c_inner_ln(ctx, b, W):
    ek, c = ctx.ek, ctx.c
    ysb, tmp, st = b["y_sb"], b["stat_tmp"], b["stat_small"]
    s1_ap, s1_reg = pfb(ctx, 4, 0, W)
    s2_ap, s2_reg = pfb(ctx, 5, 0, W)

    def s1_fns(t):
        return [t.matmul(s1_ap[0:1, :], c["ones128f"].ap, ysb[:, cc * W:(cc + 1) * W],
                         start=(cc == 0), stop=(cc == 7)) for cc in range(8)]

    ek.group("tensor", s1_fns, reads=[(ysb, 0, 8 * W), c["ones128f"]], writes=[s1_reg])
    for half in range(2):
        for i in range(4):
            cc = half * 4 + i
            ek.op("scalar", lambda s, cc=cc, i=i: s.activation(
                tmp[:, i * W:(i + 1) * W], ysb[:, cc * W:(cc + 1) * W], AF.Square),
                reads=[(ysb, cc * W, (cc + 1) * W)], writes=[(tmp, i * W, (i + 1) * W)])

        def s2_fns(t, half=half):
            return [t.matmul(s2_ap[0:1, :], c["ones128f"].ap, tmp[:, i * W:(i + 1) * W],
                             start=(half == 0 and i == 0), stop=(half == 1 and i == 3))
                    for i in range(4)]

        ek.group("tensor", s2_fns, reads=[(tmp, 0, 4 * W), c["ones128f"]], writes=[s2_reg])
    m = st
    ek.op("vector", lambda v: v.tensor_single_scalar(m[0:1, 0:W], s1_ap[0:1, :], 1.0 / 1024.0, ALU.mult),
          reads=[s1_reg], writes=[(st, 0, W)])
    ek.op("vector", lambda v: v.tensor_single_scalar(m[0:1, 3 * W:4 * W], s2_ap[0:1, :], 1.0 / 1024.0, ALU.mult),
          reads=[s2_reg], writes=[(st, 3 * W, 4 * W)])
    ek.op("vector", lambda v: v.tensor_mul(m[0:1, W:2 * W], m[0:1, 0:W], m[0:1, 0:W]),
          reads=[(st, 0, W)], writes=[(st, W, 2 * W)])
    ek.op("vector", lambda v: v.tensor_sub(m[0:1, W:2 * W], m[0:1, 3 * W:4 * W], m[0:1, W:2 * W]),
          reads=[(st, W, 2 * W), (st, 3 * W, 4 * W)], writes=[(st, W, 2 * W)])
    ek.op("scalar", lambda s: s.activation(m[0:1, W:2 * W], m[0:1, W:2 * W], AF.Ln,
                                           bias=c["cpool"][0:1, 2:3], scale=1.0),
          reads=[(st, W, 2 * W), (c["cpool"], 2, 3)], writes=[(st, W, 2 * W)])
    ek.op("scalar", lambda s: s.activation(m[0:1, 2 * W:3 * W], m[0:1, W:2 * W], AF.Exp, scale=-0.5),
          reads=[(st, W, 2 * W)], writes=[(st, 2 * W, 3 * W)])
    mb_ap, mb_reg = pfb(ctx, 4, 0, W)
    rb_ap, rb_reg = pfb(ctx, 5, 0, W)
    ek.group("tensor", lambda t: [t.matmul(mb_ap, c["ones_colf"].ap, m[0:1, 0:W], start=True, stop=True)],
             reads=[(st, 0, W), c["ones_colf"]], writes=[mb_reg])
    ek.group("tensor", lambda t: [t.matmul(rb_ap, c["ones_colf"].ap, m[0:1, 2 * W:3 * W], start=True, stop=True)],
             reads=[(st, 2 * W, 3 * W), c["ones_colf"]], writes=[rb_reg])
    for cc in range(8):
        sl = (cc % 4) * W
        ek.op("vector", lambda v, cc=cc, sl=sl: v.tensor_sub(
            tmp[:, sl:sl + W], ysb[:, cc * W:(cc + 1) * W], mb_ap),
            reads=[(ysb, cc * W, (cc + 1) * W), mb_reg], writes=[(tmp, sl, sl + W)])
        ek.op("vector", lambda v, cc=cc, sl=sl: v.scalar_tensor_tensor(
            tmp[:, sl:sl + W], tmp[:, sl:sl + W], b["gma"][:, cc:cc + 1], rb_ap, ALU.mult, ALU.mult),
            reads=[(tmp, sl, sl + W), b["gma"], rb_reg], writes=[(tmp, sl, sl + W)])
        ek.op("vector", lambda v, cc=cc, sl=sl: v.tensor_scalar_add(
            b["yln"][:, cc * W:(cc + 1) * W], tmp[:, sl:sl + W], b["bta"][:, cc:cc + 1]),
            reads=[(tmp, sl, sl + W), b["bta"]], writes=[(b["yln"], cc * W, (cc + 1) * W)])


def _emit_c_head(ctx, b, h):
    ek, c, T = ctx.ek, ctx.c, ctx.T
    xn = b["xn"]
    hc2 = h // 2
    prow = (h % 2) * 64
    for wbuf, wname in ((b["wqh"], "L1_attn_Wq"), (b["wkh"], "L1_attn_Wk"), (b["wvh"], "L1_attn_Wv")):
        src = ctx.dram[wname][h].rearrange("(dc p) e -> p dc e", p=128)
        dst = wbuf.h[:, 0:512].rearrange("p (dc e) -> p dc e", dc=8)
        ek.dma("gpsimd", lambda g, src=src, dst=dst: g.dma_start(dst, src),
               reads=[dreg(wname, h, h + 1)], writes=[(wbuf, 0, 512)],
               slot=wbuf.h.name)
    for (wbuf, dstbuf, bank, scale) in ((b["wkh"], b["k_f"], 0, 1.0), (b["wqh"], b["q_f"], 1, 0.125)):
        for tt in range(T // 512):
            o_ap, o_reg = pfb(ctx, bank, 0, 512)
            lhsTs = [wbuf[:, dc * 64: (dc + 1) * 64] for dc in range(8)]
            rhss = [xn[:, dc * T + tt * 512: dc * T + (tt + 1) * 512] for dc in range(8)]
            mm_chain(ctx, o_ap[0:64, :], o_reg, lhsTs, rhss,
                     reads=[(wbuf, 0, 512)] +
                           [(xn, dc * T + tt * 512, dc * T + (tt + 1) * 512) for dc in range(8)])
            ek.op("scalar", lambda s, tt=tt, o_ap=o_ap, dstbuf=dstbuf, scale=scale: s.activation(
                dstbuf.h[0:64, tt * 512:(tt + 1) * 512], o_ap[0:64, :], AF.Copy, scale=scale),
                reads=[o_reg], writes=[(dstbuf, tt * 512, (tt + 1) * 512)])
    for tt in range(T // 512):
        o_ap, o_reg = pfb(ctx, 2, 0, 512)
        lhsTs = [b["wvh"][:, dc * 64: (dc + 1) * 64] for dc in range(8)]
        rhss = [xn[:, dc * T + tt * 512: dc * T + (tt + 1) * 512] for dc in range(8)]
        mm_chain(ctx, o_ap[0:64, :], o_reg, lhsTs, rhss,
                 reads=[(b["wvh"], 0, 512)] +
                       [(xn, dc * T + tt * 512, dc * T + (tt + 1) * 512) for dc in range(8)])
        ek.op("scalar", lambda s, o_ap=o_ap: s.copy(b["vstage"][0:64, :], o_ap[0:64, :]),
              reads=[o_reg], writes=[b["vstage"]])
        for j in range(4):
            pt_ap, pt_reg = ptb(ctx, 0, j * 128, j * 128 + 64)
            ek.op("tensor", lambda t, j=j, pt_ap=pt_ap: t.transpose(
                pt_ap, b["vstage"][0:64, j * 128:(j + 1) * 128], c["ident"][0:64, 0:64]),
                reads=[b["vstage"], c["ident"]], writes=[pt_reg])
        for j in range(4):
            tc = tt * 4 + j
            ek.op("vector", lambda v, j=j, tc=tc: v.tensor_copy(
                b["v_tm"][:, tc * 65: tc * 65 + 64], ctx.pt[:, j * 128: j * 128 + 64]),
                reads=[(ctx.pt, j * 128, j * 128 + 64)], writes=[(b["v_tm"], tc * 65, tc * 65 + 64)])
    for tt in range(T // 512):
        ek.op("scalar", lambda s, tt=tt: s.activation(
            b["stat_tmp"].h[0:64, 0:512], b["q_f"][0:64, tt * 512:(tt + 1) * 512], AF.Square),
            reads=[(b["q_f"], tt * 512, (tt + 1) * 512)], writes=[(b["stat_tmp"], 0, 512)])
        qs_ap, qs_reg = pfb(ctx, 4, 0, 512)
        ek.group("tensor", lambda t, qs_ap=qs_ap: [
            t.matmul(qs_ap[0:1, :], c["ones128f"][0:64, :], b["stat_tmp"].h[0:64, 0:512],
                     start=True, stop=True)],
            reads=[(b["stat_tmp"], 0, 512), c["ones128f"]], writes=[qs_reg])
        ek.op("vector", lambda v, tt=tt, qs_ap=qs_ap: v.tensor_copy(
            b["qrow"][0:1, tt * 512:(tt + 1) * 512], qs_ap[0:1, :]),
            reads=[qs_reg], writes=[(b["qrow"], tt * 512, (tt + 1) * 512)])
    ek.op("vector", lambda v: v.memset(b["mxk"][0:1, 0:1], 0.0), writes=[(b["mxk"], 0, 1)])
    for tt in range(T // 512):
        ek.op("scalar", lambda s, tt=tt: s.activation(
            b["stat_tmp"].h[0:64, 0:512], b["k_f"][0:64, tt * 512:(tt + 1) * 512], AF.Square),
            reads=[(b["k_f"], tt * 512, (tt + 1) * 512)], writes=[(b["stat_tmp"], 0, 512)])
        kn_ap, kn_reg = pfb(ctx, 5, 0, 512)
        ek.group("tensor", lambda t, kn_ap=kn_ap: [
            t.matmul(kn_ap[0:1, :], c["ones128f"][0:64, :], b["stat_tmp"].h[0:64, 0:512],
                     start=True, stop=True)],
            reads=[(b["stat_tmp"], 0, 512), c["ones128f"]], writes=[kn_reg])
        ek.op("vector", lambda v, kn_ap=kn_ap: v.reduce_max(
            b["stat_small"][0:1, 0:1], kn_ap[0:1, :], axis=AX.X),
            reads=[kn_reg], writes=[(b["stat_small"], 0, 1)])
        ek.op("vector", lambda v: v.tensor_max(b["mxk"][0:1, 0:1], b["mxk"][0:1, 0:1], b["stat_small"][0:1, 0:1]),
              reads=[(b["mxk"], 0, 1), (b["stat_small"], 0, 1)], writes=[(b["mxk"], 0, 1)])
    ek.op("vector", lambda v: v.tensor_scalar_mul(b["qrow"][0:1, :], b["qrow"][0:1, :], b["mxk"][0:1, 0:1]),
          reads=[(b["qrow"], 0, T), (b["mxk"], 0, 1)], writes=[(b["qrow"], 0, T)])
    ek.op("scalar", lambda s: s.activation(b["qrow"][0:1, :], b["qrow"][0:1, :], AF.Ln),
          reads=[(b["qrow"], 0, T)], writes=[(b["qrow"], 0, T)])
    ek.op("scalar", lambda s: s.activation(b["qrow"][0:1, :], b["qrow"][0:1, :], AF.Exp, scale=0.5),
          reads=[(b["qrow"], 0, T)], writes=[(b["qrow"], 0, T)])
    ek.op("vector", lambda v: v.tensor_single_scalar(b["q_f"].h[64:65, :], b["qrow"][0:1, :], -1.0, ALU.mult),
          reads=[(b["qrow"], 0, T)], writes=[(b["q_f"], 0, T)])
    ek.op("vector", lambda v: v.memset(b["k_f"].h[64:65, :], 1.0), writes=[(b["k_f"], 0, T)])
    for g in range(T // 512):
        _emit_c_group(ctx, b, h, g, hc2, prow)


def _emit_c_group(ctx, b, h, g, hc2, prow):
    ek, c, T = ctx.ek, ctx.c, ctx.T
    nk = (g + 1) * 4
    q0 = g * 512
    ou_ap, ou_reg = pfb(ctx, 2, 0, 512)

    def emit_av(ki, start, stop):
        slot = ki % 2
        ek.op("tensor", lambda t, ki=ki, slot=slot, start=start, stop=stop: t.matmul(
            ou_ap[0:65, :], b["v_tm"][:, ki * 65:(ki + 1) * 65],
            b["PT"][:, slot * 512:(slot + 1) * 512], start=start, stop=stop),
            reads=[(b["v_tm"], ki * 65, (ki + 1) * 65), (b["PT"], slot * 512, (slot + 1) * 512)],
            writes=[ou_reg])

    for ki in range(nk):
        bank = ki % 2
        s_ap, s_reg = pfb(ctx, bank, 0, 512)
        ek.group("tensor", lambda t, ki=ki, s_ap=s_ap: [t.matmul(
            s_ap, b["k_f"].h[0:65, ki * 128:(ki + 1) * 128], b["q_f"].h[0:65, q0:q0 + 512],
            start=True, stop=True)],
            reads=[(b["k_f"], ki * 128, (ki + 1) * 128), (b["q_f"], q0, q0 + 512)],
            writes=[s_reg])
        if ki >= g * 4:
            j = ki - g * 4
            ek.op("vector", lambda v, s_ap=s_ap, j=j: v.tensor_add(
                s_ap, s_ap, b["cstrip"][:, j * 512:(j + 1) * 512]),
                reads=[s_reg, (b["cstrip"], j * 512, (j + 1) * 512)], writes=[s_reg])
        slot = ki % 2
        ek.op("scalar", lambda s, s_ap=s_ap, slot=slot: s.activation(
            b["PT"][:, slot * 512:(slot + 1) * 512], s_ap, AF.Exp),
            reads=[s_reg], writes=[(b["PT"], slot * 512, (slot + 1) * 512)])
        if ki > 0:
            emit_av(ki - 1, ki - 1 == 0, False)
    emit_av(nk - 1, nk == 1, True)
    ek.op("vector", lambda v: v.tensor_copy(b["rs1"][0:1, :], ou_ap[64:65, :]),
          reads=[ou_reg], writes=[b["rs1"]])
    ek.op("vector", lambda v: v.reciprocal(b["rs1"][0:1, :], b["rs1"][0:1, :]),
          reads=[b["rs1"]], writes=[b["rs1"]])
    rb_ap, rb_reg = pfb(ctx, 3, 0, 512)
    ek.group("tensor", lambda t, rb_ap=rb_ap: [
        t.matmul(rb_ap[0:64, :], c["ones_colf"][0:1, 0:64], b["rs1"][0:1, :], start=True, stop=True)],
        reads=[c["ones_colf"], b["rs1"]], writes=[rb_reg])
    ek.op("scalar", lambda s: s.copy(b["ou_sb"][0:64, :], ou_ap[0:64, :]),
          reads=[ou_reg], writes=[b["ou_sb"]])
    hcol = hc2 * T + g * 512
    ek.op("vector", lambda v, hcol=hcol, rb_ap=rb_ap: v.tensor_mul(
        b["hf"].h[prow:prow + 64, hcol:hcol + 512], b["ou_sb"][0:64, :], rb_ap[0:64, :]),
        reads=[b["ou_sb"], rb_reg], writes=[(b["hf"], hcol, hcol + 512)])


def _unused():
    pass


def alloc_phase_d_bufs(ctx, b):
    mem = ctx.mem
    alloc_routing_bufs(ctx, b, "_d", tstride=512)
    b["wealth_b"] = mem.sb("wealth_b", 128, 8, F32)
    b["wealth_row"] = mem.sb("wealth_row", 1, 8, F32)
    b["bids"] = mem.sb("bids", 128, 8 * 8, F32)
    b["wslab"] = mem.sb("wslab", 128, 16384, BF16)
    b["dwb"] = mem.sb("dwb", 128, 16384, BF16)
    b["eoh"] = mem.sb("eoh", 128, 16 * 512, BF16)
    b["yacc_d"] = mem.sb("yacc_d", 128, 8 * 512, F32)


def emit_phase_d_routing(ctx, b, tt0, te):
    ek, c, T = ctx.ek, ctx.c, ctx.T
    xn = b["xn"]
    ntc = te // 128
    for tci in range(ntc):
        tc0 = tt0 + tci * 128
        lg_ap, lg_reg = pfb(ctx, 4, 0, 128)
        lhsTs = [xn[:, dc * T + tc0: dc * T + tc0 + 128] for dc in range(8)]
        rhss = [b["rwT"][:, dc * 8:(dc + 1) * 8] for dc in range(8)]
        mm_chain(ctx, lg_ap[:, 0:8], lg_reg, lhsTs, rhss,
                 reads=[b["rwT"]] + [(xn, dc * T + tc0, dc * T + tc0 + 128) for dc in range(8)])
        ek.op("vector", lambda v, lg_ap=lg_ap, tci=tci: v.tensor_add(
            b["bids"][:, tci * 8:(tci + 1) * 8], lg_ap[:, 0:8], b["bias_b"].ap),
            reads=[lg_reg, b["bias_b"]], writes=[(b["bids"], tci * 8, (tci + 1) * 8)])
        ek.op("scalar", lambda s, tci=tci: s.activation(
            b["bids"][:, tci * 8:(tci + 1) * 8], b["bids"][:, tci * 8:(tci + 1) * 8], AF.Sigmoid),
            reads=[(b["bids"], tci * 8, (tci + 1) * 8)], writes=[(b["bids"], tci * 8, (tci + 1) * 8)])
        ek.op("vector", lambda v, tci=tci: v.tensor_mul(
            b["bids"][:, tci * 8:(tci + 1) * 8], b["bids"][:, tci * 8:(tci + 1) * 8], b["wealth_b"].ap),
            reads=[(b["bids"], tci * 8, (tci + 1) * 8), b["wealth_b"]],
            writes=[(b["bids"], tci * 8, (tci + 1) * 8)])
    for tci in range(ntc):
        bsl = (b["bids"], tci * 8, (tci + 1) * 8)
        bap = b["bids"][:, tci * 8:(tci + 1) * 8]
        ek.op("vector", lambda v, bap=bap: v.reduce_max(b["m1"][:, 0:1], bap, axis=AX.X),
              reads=[bsl], writes=[(b["m1"], 0, 1)])
        ek.op("vector", lambda v, bap=bap: v.tensor_single_scalar(b["msk"].ap, bap, b["m1"][:, 0:1], ALU.is_lt),
              reads=[bsl, (b["m1"], 0, 1)], writes=[b["msk"]])
        ek.op("vector", lambda v: v.memset(b["l_ms"].ap, NEG_BIG), writes=[b["l_ms"]])
        ek.op("vector", lambda v, bap=bap: v.copy_predicated(b["l_ms"].ap, b["msk"].ap, bap),
              reads=[b["msk"], bsl, b["l_ms"]], writes=[b["l_ms"]])
        ek.op("vector", lambda v: v.reduce_max(b["m1"][:, 1:2], b["l_ms"].ap, axis=AX.X),
              reads=[b["l_ms"]], writes=[(b["m1"], 1, 2)])
        ek.op("vector", lambda v: v.tensor_single_scalar(b["m1"][:, 2:3], b["m1"][:, 0:1], -1.0, ALU.mult),
              reads=[(b["m1"], 0, 1)], writes=[(b["m1"], 2, 3)])
        ek.op("scalar", lambda s, bap=bap: s.activation(b["el"].ap, bap, AF.Exp, bias=b["m1"][:, 2:3]),
              reads=[bsl, (b["m1"], 2, 3)], writes=[b["el"]])
        ek.op("vector", lambda v: v.tensor_add(b["m1"][:, 3:4], b["m1"][:, 1:2], b["m1"][:, 2:3]),
              reads=[(b["m1"], 1, 3)], writes=[(b["m1"], 3, 4)])
        ek.op("scalar", lambda s: s.activation(b["m1"][:, 3:4], b["m1"][:, 3:4], AF.Exp),
              reads=[(b["m1"], 3, 4)], writes=[(b["m1"], 3, 4)])
        ek.op("vector", lambda v: v.tensor_single_scalar(b["m1"][:, 3:4], b["m1"][:, 3:4], 1.0, ALU.add),
              reads=[(b["m1"], 3, 4)], writes=[(b["m1"], 3, 4)])
        ek.op("vector", lambda v: v.reciprocal(b["m1"][:, 3:4], b["m1"][:, 3:4]),
              reads=[(b["m1"], 3, 4)], writes=[(b["m1"], 3, 4)])
        ek.op("vector", lambda v, bap=bap: v.tensor_single_scalar(b["l_ms"].ap, bap, b["m1"][:, 1:2], ALU.is_ge),
              reads=[bsl, (b["m1"], 1, 2)], writes=[b["l_ms"]])
        ek.op("vector", lambda v: v.tensor_mul(b["el"].ap, b["el"].ap, b["l_ms"].ap),
              reads=[b["el"], b["l_ms"]], writes=[b["el"]])
        ek.op("vector", lambda v: v.tensor_scalar_mul(b["cw16"].ap, b["el"].ap, b["m1"][:, 3:4]),
              reads=[b["el"], (b["m1"], 3, 4)], writes=[b["cw16"]])
        for e in range(8):
            pt_ap, pt_reg = ptb(ctx, 0, e * 128, (e + 1) * 128)
            ek.op("tensor", lambda t, e=e, pt_ap=pt_ap: t.transpose(
                pt_ap[0:1, :], b["cw16"][:, e:e + 1], c["ident"].ap),
                reads=[(b["cw16"], e, e + 1), c["ident"]], writes=[pt_reg])
        for e in range(8):
            pt_ap = ctx.pt[0:1, e * 128:(e + 1) * 128]
            ek.op("vector", lambda v, e=e, pt_ap=pt_ap, tci=tci: v.tensor_copy(
                b["cwrow"][0:1, e * 512 + tci * 128: e * 512 + (tci + 1) * 128], pt_ap),
                reads=[(ctx.pt, e * 128, (e + 1) * 128)],
                writes=[(b["cwrow"], e * 512 + tci * 128, e * 512 + (tci + 1) * 128)])


def emit_phase_d(ctx, b, x_prev_name, x_next_name):
    ek, c, T = ctx.ek, ctx.c, ctx.T
    xn = b["xn"]
    ek.dma("gpsimd", lambda g: g.dma_start(b["rtmp"][0:8, :], ctx.dram["L1_ffn_conf_w"][:, :]),
           reads=[dreg("L1_ffn_conf_w")], writes=[b["rtmp"]], slot="rwT")
    for dc in range(8):
        pt_ap, pt_reg = ptb(ctx, 1, 0, 8)
        ek.op("tensor", lambda t, dc=dc, pt_ap=pt_ap: t.transpose(
            pt_ap, b["rtmp"][0:8, dc * 128:(dc + 1) * 128], c["ident"][0:8, 0:8]),
            reads=[b["rtmp"], c["ident"]], writes=[pt_reg])
        ek.op("vector", lambda v, dc=dc, pt_ap=pt_ap: v.tensor_copy(b["rwT"][:, dc * 8:(dc + 1) * 8], pt_ap),
              reads=[pt_reg], writes=[(b["rwT"], dc * 8, (dc + 1) * 8)])
    ek.dma("sync", lambda sy: sy.dma_start(b["bias_row"][0:1, :], ctx.dram["L1_ffn_conf_b"][None, :]),
           reads=[dreg("L1_ffn_conf_b")], writes=[b["bias_row"]], slot="br")
    ek.dma("sync", lambda sy: sy.dma_start(b["wealth_row"][0:1, :], ctx.dram["L1_ffn_wealth"][None, :]),
           reads=[dreg("L1_ffn_wealth")], writes=[b["wealth_row"]], slot="br")
    for (row, dst) in ((b["bias_row"], b["bias_b"]), (b["wealth_row"], b["wealth_b"])):
        bb_ap, bb_reg = pfb(ctx, 5, 0, 8)
        ek.group("tensor", lambda t, row=row, bb_ap=bb_ap: [
            t.matmul(bb_ap[:, 0:8], c["ones_colf"].ap, row[0:1, :], start=True, stop=True)],
            reads=[c["ones_colf"], row], writes=[bb_reg])
        ek.op("vector", lambda v, dst=dst, bb_ap=bb_ap: v.tensor_copy(dst.ap, bb_ap[:, 0:8]),
              reads=[bb_reg], writes=[dst])

    for tt0 in range(0, T, 512):
        te = min(512, T - tt0)
        emit_phase_d_routing(ctx, b, tt0, te)
        for e in range(8):
            for q in range(te // 512):
                cb_ap, cb_reg = pfb(ctx, 5, 0, 512)
                ek.group("tensor", lambda t, e=e, q=q, cb_ap=cb_ap: [
                    t.matmul(cb_ap, c["ones_col"].ap,
                             b["cwrow"][0:1, e * 512 + q * 512: e * 512 + (q + 1) * 512],
                             start=True, stop=True)],
                    reads=[c["ones_col"], (b["cwrow"], e * 512 + q * 512, e * 512 + (q + 1) * 512)],
                    writes=[cb_reg])
                ek.op("vector", lambda v, q=q, cb_ap=cb_ap: v.tensor_copy(
                    b["cwb"][:, q * 512:(q + 1) * 512], cb_ap),
                    reads=[cb_reg], writes=[(b["cwb"], q * 512, (q + 1) * 512)])
            for ihalf in range(2):
                _emit_d_half(ctx, b, tt0, te, e, ihalf)
        for cc4 in range(te // 256):
            t0 = tt0 + cc4 * 256
            emit_residual_tile_cols(ctx, b, t0, 256, "yacc_d",
                                    lambda dc, cc4=cc4: dc * 512 + cc4 * 256,
                                    x_prev_name, x_next_name)
            emit_ln_from_xtile(ctx, b, t0, 256)


def _emit_d_half(ctx, b, tt0, te, e, ihalf):
    ek, c, T = ctx.ek, ctx.c, ctx.T
    xn = b["xn"]
    i0 = ihalf * 2048
    dlo = (e * 2 + ihalf) * 128 * 16384
    ek.dma("gpsimd", lambda g: g.dma_start(b["dwb"].ap, ctx.dram["dwT"][e, ihalf]),
           reads=[dreg("dwT", dlo, dlo + 128 * 16384)],
           writes=[b["dwb"]], slot="dwb")
    for icg in range(4):
        half = icg % 2
        wcols = 8192 * half
        icg_g = ihalf * 4 + icg
        for wi, dT in ((0, "gwT"), (1, "uwT")):
            col = wcols + wi * 4096
            rlo = (e * 8 + icg_g) * 128 * 4096
            ek.dma("sync", lambda sy, dT=dT, icg_g=icg_g, col=col: sy.dma_start(
                b["wslab"].h[:, col:col + 4096], ctx.dram[dT][e, icg_g]),
                reads=[dreg(dT, rlo, rlo + 128 * 4096)],
                writes=[(b["wslab"], col, col + 4096)], slot="ws" + str(half) + str(wi))
        for ic in range(4):
            for ts_ in range(te // 512):
                tq0 = tt0 + ts_ * 512
                g_ap, g_reg = pfb(ctx, 4, 0, 512)
                u_ap, u_reg = pfb(ctx, 5, 0, 512)
                for wi, (o_ap, o_reg) in ((0, (g_ap, g_reg)), (1, (u_ap, u_reg))):
                    lhsTs = [b["wslab"][:, wcols + wi * 4096 + hc * 512 + ic * 128:
                                        wcols + wi * 4096 + hc * 512 + ic * 128 + 128] for hc in range(8)]
                    rhss = [xn[:, hc * T + tq0: hc * T + tq0 + 512] for hc in range(8)]
                    mm_chain(ctx, o_ap, o_reg, lhsTs, rhss,
                             reads=[(b["wslab"], wcols + wi * 4096, wcols + (wi + 1) * 4096)] +
                                   [(xn, hc * T + tq0, hc * T + tq0 + 512) for hc in range(8)])
                st = b["stat_tmp"]
                ek.op("scalar", lambda s, g_ap=g_ap, st=st: s.activation(st[:, 512:1024], g_ap, AF.Silu),
                      reads=[g_reg], writes=[(st, 512, 1024)])
                eo_lo = (icg * 4 + ic) * 512 + ts_ * 512
                ek.op("vector", lambda v, u_ap=u_ap, st=st: v.scalar_tensor_tensor(
                    st[:, 0:512], u_ap, 1.0, st[:, 512:1024], ALU.mult, ALU.mult),
                    reads=[u_reg, (st, 512, 1024)], writes=[(st, 0, 512)])
                ek.op("vector", lambda v, eo_lo=eo_lo, ts_=ts_: v.tensor_mul(
                    b["eoh"][:, eo_lo:eo_lo + 512], b["stat_tmp"][:, 0:512],
                    b["cwb"][:, ts_ * 512:(ts_ + 1) * 512]),
                    reads=[(b["stat_tmp"], 0, 512), (b["cwb"], ts_ * 512, (ts_ + 1) * 512)],
                    writes=[(b["eoh"], eo_lo, eo_lo + 512)])
    for ts_ in range(te // 512):
        tq0s = ts_ * 512
        for dc in range(8):
            y_ap = ctx.pf[:, (dc % 4) * 512: (dc % 4) * 512 + 512]
            y_reg = (ctx.pf, (dc % 4) * 512, (dc % 4) * 512 + 512)
            lhsTs = [b["dwb"][:, ic * 1024 + dc * 128: ic * 1024 + (dc + 1) * 128] for ic in range(16)]
            rhss = [b["eoh"][:, ic * 512 + tq0s: ic * 512 + tq0s + 512] for ic in range(16)]
            mm_chain(ctx, y_ap, y_reg, lhsTs, rhss,
                     reads=[b["dwb"], b["eoh"]])
            ya_lo = dc * 512 + tq0s
            fresh = (e == 0 and ihalf == 0)
            if fresh:
                ek.op("vector", lambda v, y_ap=y_ap, ya_lo=ya_lo: v.tensor_copy(
                    b["yacc_d"][:, ya_lo:ya_lo + 512], y_ap),
                    reads=[y_reg], writes=[(b["yacc_d"], ya_lo, ya_lo + 512)])
            else:
                ek.op("vector", lambda v, y_ap=y_ap, ya_lo=ya_lo: v.tensor_add(
                    b["yacc_d"][:, ya_lo:ya_lo + 512], b["yacc_d"][:, ya_lo:ya_lo + 512], y_ap),
                    reads=[y_reg, (b["yacc_d"], ya_lo, ya_lo + 512)],
                    writes=[(b["yacc_d"], ya_lo, ya_lo + 512)])


INPUT_SHAPES = {
    "L0_attn_Wq": [1024, 512], "L0_attn_Wk": [1024, 512], "L0_attn_Wv": [1024, 1024],
    "L0_attn_Wg": [1024, 1024], "L0_attn_Wgk1": [1024, 16], "L0_attn_Wgk2": [16, 512],
    "L0_attn_bgk2": [512], "L0_attn_Wout": [1024, 1024], "L0_attn_rms_w": [256],
    "L0_ffn_router_w": [8, 1024], "L0_ffn_experts_bias": [8],
    "L0_ffn_w1": [8, 1024, 2048], "L0_ffn_w3": [8, 1024, 2048], "L0_ffn_w2": [8, 2048, 1024],
    "L0_ffn_sw1": [1, 1024, 2048], "L0_ffn_sw3": [1, 1024, 2048], "L0_ffn_sw2": [1, 2048, 1024],
    "L1_attn_Wq": [16, 1024, 64], "L1_attn_Wk": [16, 1024, 64], "L1_attn_Wv": [16, 1024, 64],
    "L1_attn_Wo": [1024, 1024], "L1_attn_gamma": [1024], "L1_attn_beta": [1024],
    "L1_ffn_conf_w": [8, 1024], "L1_ffn_conf_b": [8],
    "L1_ffn_gate_w": [8, 4096, 1024], "L1_ffn_up_w": [8, 4096, 1024],
    "L1_ffn_down_w": [8, 1024, 4096], "L1_ffn_wealth": [8],
}


def build_full(T, phases="ABCD", nc=None):
    if nc is None:
        nc = bass.Bass("TRN2", target_bir_lowering=False)
    ctx = Ctx(nc, T)
    ctx.dt("x_in", [1024, T], F32, kind="ExternalInput")
    ctx.dt("x_out", [1024, T], F32, kind="ExternalOutput")
    for n, sh in INPUT_SHAPES.items():
        ctx.dt(n, sh, F32, kind="ExternalInput")
    ctx.dt("x1", [1024, T], F32)
    ctx.dt("x2", [1024, T], F32)
    ctx.dt("x3", [1024, T], F32)
    ctx.dt("gwbf", [8, 4096, 1024], BF16)
    ctx.dt("uwbf", [8, 4096, 1024], BF16)
    ctx.dt("dwbf", [8, 1024, 4096], BF16)
    ctx.dt("w1bf", [9, 1024, 2048], BF16)
    ctx.dt("w3bf", [9, 1024, 2048], BF16)
    ctx.dt("w2bf", [9, 2048, 1024], BF16)
    ctx.dt("gwT", [8, 8, 128, 4096], BF16)
    ctx.dt("uwT", [8, 8, 128, 4096], BF16)
    ctx.dt("dwT", [8, 2, 128, 16384], BF16)
    emit_consts(ctx)
    b = alloc_shared_bufs(ctx)
    mem = ctx.mem
    peaks = {}
    xs = ["x_in"]
    for ph in phases[:-1]:
        xs.append({"A": "x1", "B": "x2", "C": "x3"}[ph])
    xs.append("x_out")

    i = 0
    if "A" in phases:
        mem.push()
        alloc_phase_a_bufs(ctx, b)
        peaks["A"] = mem.cursor
        pb = dict(b)
        emit_prologue(ctx, pb, xs[i])
        emit_phase_a(ctx, pb, xs[i], xs[i + 1])
        mem.pop()
        i += 1
    need_d = "D" in phases
    if "B" in phases:
        mem.push()
        alloc_phase_b_bufs(ctx, b, with_stash=need_d)
        peaks["B"] = mem.cursor
        pb = dict(b)
        sc = make_stash_chunks(ctx, pb) if need_d else None
        if i == 0:
            emit_prologue(ctx, pb, xs[i])
        emit_phase_b(ctx, pb, xs[i], xs[i + 1], stash_chunks=sc)
        mem.pop()
        i += 1
    if "C" in phases:
        mem.push()
        alloc_phase_c_bufs(ctx, b, with_repack=need_d)
        pb = dict(b)
        rc = make_repack_chunks(ctx, pb) if need_d else None
        if i == 0:
            emit_prologue(ctx, pb, xs[i])
        emit_phase_c(ctx, pb, xs[i], xs[i + 1], repack_chunks=rc)
        peaks["C"] = mem.cursor
        mem.pop()
        i += 1
    if "D" in phases:
        mem.push()
        alloc_phase_d_bufs(ctx, b)
        pb = dict(b)
        if i == 0:
            emit_prologue(ctx, pb, xs[i])
        if "B" not in phases:
            pb["stash_stg"] = mem.sb("stash_stg", 128, 2 * 1024, BF16)
            for chk in make_stash_chunks(ctx, pb):
                chk()
        if "C" not in phases:
            pb["rpk_stg"] = mem.sb("rpk_stg", 128, 2 * 4096, BF16)
            for chk in make_repack_chunks(ctx, pb):
                chk()
        peaks["D"] = mem.cursor
        emit_phase_d(ctx, pb, xs[i], xs[i + 1])
        mem.pop()
        i += 1
    ctx.ek.finish("sync")
    ctx.ek.build_and_emit()
    mem.check()
    return nc, ctx, peaks


_T = 4096
_NCORES = 8


def kernel(**inputs):
    from concourse.bass_utils import run_bass_kernel_spmd

    x = np.ascontiguousarray(inputs["L0_attn_x"], dtype=np.float32)
    assert x.shape == (_NCORES, _T, 1024)
    weights = {n: np.ascontiguousarray(inputs[n], dtype=np.float32)
               for n in INPUT_SHAPES}

    nc, ctx, _peaks = build_full(_T)
    in_maps = []
    for i in range(_NCORES):
        m = {"x_in": np.ascontiguousarray(x[i].T)}
        m.update(weights)
        in_maps.append(m)
    res = run_bass_kernel_spmd(nc, in_maps, core_ids=list(range(_NCORES)))
    out = np.stack([np.asarray(res.results[i]["x_out"]).T for i in range(_NCORES)], axis=0)
    return out.astype(np.float32)


# version 06 (final)
# speedup vs baseline: 1.1693x; 1.0567x over previous
import sys
sys.path.insert(0, "/opt/trn_rl_repo")
import math
from collections import defaultdict
import numpy as np
import concourse.bass as bass
import concourse.mybir as mybir

F32 = mybir.dt.float32
BF16 = mybir.dt.bfloat16
ALU = mybir.AluOpType
AF = mybir.ActivationFunctionType
AX = mybir.AxisListType

ENGINES = ("tensor", "vector", "scalar", "gpsimd", "sync")
ESEM = {e: "e_" + e for e in ENGINES}
DSEM = {e: "d_" + e for e in ("sync", "scalar", "gpsimd", "vector")}
NEG_BIG = -1.0e30


class _IntervalMap:
    __slots__ = ("segs",)

    def __init__(self):
        self.segs = []

    def read(self, lo, hi, reader_tok, conflict_readers=()):
        waits = []
        rsem, rval = reader_tok
        covered = []
        for s in self.segs:
            if s[0] < hi and lo < s[1]:
                if s[2] is not None:
                    waits.append(s[2])
                for csem in conflict_readers:
                    cv = s[3].get(csem)
                    if cv is not None:
                        waits.append((csem, cv))
                if s[3].get(rsem, -1) < rval:
                    s[3][rsem] = rval
                covered.append((max(s[0], lo), min(s[1], hi)))
        covered.sort()
        pos = lo
        gaps = []
        for c0, c1 in covered:
            if c0 > pos:
                gaps.append([pos, c0, None, {rsem: rval}])
            pos = max(pos, c1)
        if pos < hi:
            gaps.append([pos, hi, None, {rsem: rval}])
        self.segs.extend(gaps)
        return waits

    def write(self, lo, hi, writer_tok):
        waits = []
        new_segs = []
        for s in self.segs:
            if s[0] < hi and lo < s[1]:
                if s[2] is not None:
                    waits.append(s[2])
                for rsem, rval in s[3].items():
                    waits.append((rsem, rval))
                if s[0] < lo:
                    new_segs.append([s[0], lo, s[2], dict(s[3])])
                if hi < s[1]:
                    new_segs.append([hi, s[1], s[2], dict(s[3])])
            else:
                new_segs.append(s)
        new_segs.append([lo, hi, writer_tok, {}])
        self.segs = new_segs
        return waits


class Buf:
    __slots__ = ("h", "space", "base", "esize", "n", "quant")

    def __init__(self, h, space, base, esize, n, quant=1):
        self.h = h
        self.space = space
        self.base = base
        self.esize = esize
        self.n = n
        self.quant = quant

    def __getitem__(self, idx):
        return self.h[idx]

    @property
    def ap(self):
        return self.h[:, :]

    def r(self, lo=0, hi=None):
        if hi is None:
            hi = self.n
        b0 = self.base + lo * self.esize
        b1 = self.base + hi * self.esize
        if self.quant > 1:
            b0 = (b0 // self.quant) * self.quant
            b1 = ((b1 + self.quant - 1) // self.quant) * self.quant
        return (self.space, b0, b1)


def dreg(name, lo=0, hi=1 << 50):
    return ("D_" + name, lo, hi)


class EK:
    def __init__(self, nc):
        self.nc = nc
        self.sched = {e: [] for e in ENGINES}
        self.cnt = defaultdict(int)
        self.vc = {e: defaultdict(int) for e in ENGINES}
        self.maps = defaultdict(_IntervalMap)
        self.nwaits = 0
        self.nops = defaultdict(int)

    @staticmethod
    def _norm(r):
        if isinstance(r, Buf):
            return r.r()
        if isinstance(r, tuple):
            if isinstance(r[0], Buf):
                return r[0].r(r[1], r[2])
            return r
        raise TypeError(f"bad region {r!r}")

    _PSUM_READ_CONFLICTS = {"vector": ("e_scalar",), "scalar": ("e_vector",)}

    def _acquire(self, engine, reads, writes, tok):
        need = {}
        for r in reads or ():
            key, lo, hi = self._norm(r)
            cr = self._PSUM_READ_CONFLICTS.get(engine, ()) if key == "PS" else ()
            for sem, val in self.maps[key].read(lo, hi, tok, cr):
                if (sem, val) == tok:
                    continue
                if need.get(sem, 0) < val:
                    need[sem] = val
        for w in writes or ():
            key, lo, hi = self._norm(w)
            for sem, val in self.maps[key].write(lo, hi, tok):
                if (sem, val) == tok:
                    continue
                if need.get(sem, 0) < val:
                    need[sem] = val
        waits = []
        own = ESEM[engine]
        for sem, val in sorted(need.items()):
            if engine == "tensor" and sem == own:
                continue
            if sem.startswith("d_"):
                val = self.cnt[sem]
            if self.vc[engine][sem] >= val:
                continue
            self.vc[engine][sem] = val
            waits.append((sem, val))
        self.nwaits += len(waits)
        return waits

    def op(self, engine, fn, reads=(), writes=(), tag=None):
        sem = ESEM[engine]
        self.cnt[sem] += 1
        tokval = self.cnt[sem]
        waits = self._acquire(engine, reads, writes, (sem, tokval))
        self.sched[engine].append(("op", fn, waits, sem, 1))
        self.nops[engine] += 1
        return (sem, tokval)

    def group(self, engine, fns, reads=(), writes=(), tag=None):
        sem = ESEM[engine]
        self.cnt[sem] += 1
        tokval = self.cnt[sem]
        waits = self._acquire(engine, reads, writes, (sem, tokval))
        self.sched[engine].append(("group", fns, waits, sem, 1))
        self.nops[engine] += 1
        return (sem, tokval)

    def dma(self, engine, fn, reads=(), writes=(), slot=None, tag=None):
        assert slot is not None, "dma needs a slot key"
        sem = "d_" + engine + "_" + slot
        tokval = self.cnt[sem] + 16
        waits = self._acquire(engine, reads, writes, (sem, tokval))
        self.cnt[sem] = tokval
        self.sched[engine].append(("op", fn, waits, sem, 16))
        self.nops["dma_" + engine] += 1
        return (sem, tokval)

    def wait_manual(self, engine, tok):
        sem, val = tok
        if self.vc[engine][sem] >= val:
            return
        self.vc[engine][sem] = val
        self.sched[engine].append(("wait", None, [(sem, val)], None, 0))

    def finish(self, engine="sync"):
        for sem, val in sorted(self.cnt.items()):
            self.wait_manual(engine, (sem, val))

    def build_and_emit(self):
        nc = self.nc
        import contextlib
        stack = contextlib.ExitStack()
        sems = {}
        used = set(self.cnt.keys())
        for e in ENGINES:
            for step in self.sched[e]:
                for w in step[2]:
                    used.add(w[0])
        for name in sorted(used):
            sems[name] = stack.enter_context(nc.semaphore(name))
        with stack:
            with nc.Block() as block:
                for eng_name in ENGINES:
                    steps = self.sched[eng_name]
                    if not steps:
                        continue

                    def body(eng, steps=steps):
                        for kind, fn, waits, sem, amt in steps:
                            for wsem, wval in waits:
                                eng.wait_ge(sems[wsem], wval)
                            if kind == "op":
                                fn(eng).then_inc(sems[sem], amt)
                            elif kind == "group":
                                fn(eng)[-1].then_inc(sems[sem], amt)

                    getattr(block, eng_name)(body)
        return sems


class Mem:
    MAN_BASE = 20480
    SB_LIMIT = 229344

    def __init__(self, nc):
        self.nc = nc
        self.cursor = 0
        self.bufs = {}
        self.marks = []

    def sb(self, name, parts, n, dtype, align=64):
        esize = {BF16: 2, mybir.dt.float16: 2, mybir.dt.uint8: 1, mybir.dt.int8: 1}.get(dtype, 4)
        base = self.MAN_BASE + (self.cursor + align - 1) // align * align
        nbytes = n * esize
        assert base + nbytes <= self.SB_LIMIT, (
            f"SBUF overflow at {name}: base={base} need={nbytes}")
        h = self.nc.alloc_sbuf_tensor_at(name, [parts, n], dtype, offset=base)
        self.cursor = (base - self.MAN_BASE) + nbytes
        b = Buf(h, "SB", base, esize, n)
        self.bufs[name] = b
        return b

    def check(self):
        assert self.nc.sbuf_base <= self.MAN_BASE, (
            f"bass auto allocations ({self.nc.sbuf_base}) collided with manual region")

    def push(self):
        self.marks.append(self.cursor)

    def pop(self):
        self.cursor = self.marks.pop()

    def push(self):
        self.marks.append(self.cursor)

    def pop(self):
        self.cursor = self.marks.pop()


def alloc_psum(nc):
    pf_h = nc.alloc_psum_tensor("pf", [128, 3072], F32)
    pt_h = nc.alloc_psum_tensor("pt", [128, 2048], BF16)
    pf = Buf(pf_h, "PS", 0, 4, 3072, quant=2048)
    pt = Buf(pt_h, "PS", 3072 * 4, 2, 2048, quant=2048)
    return pf, pt


class Ctx:
    def __init__(self, nc, T):
        self.nc = nc
        self.T = T
        self.NT = T // 256
        self.ek = EK(nc)
        self.mem = Mem(nc)
        self.pf, self.pt = alloc_psum(nc)
        self.dram = {}
        self.c = {}

    def dt(self, name, shape, dtype, kind=None):
        if kind:
            h = self.nc.dram_tensor(name, shape, dtype, kind=kind)
        else:
            h = self.nc.dram_tensor(name, shape, dtype)
        ap = h.ap()
        self.dram[name] = ap
        return ap


def pfb(ctx, bank, lo=0, hi=512):
    return ctx.pf[:, bank * 512 + lo: bank * 512 + hi], (ctx.pf, bank * 512 + lo, bank * 512 + hi)


def ptb(ctx, bank, lo=0, hi=1024):
    return ctx.pt[:, bank * 1024 + lo: bank * 1024 + hi], (ctx.pt, bank * 1024 + lo, bank * 1024 + hi)


def emit_consts(ctx):
    ek, mem, nc = ctx.ek, ctx.mem, ctx.nc
    c = ctx.c
    c["ones_col"] = mem.sb("ones_col", 1, 128, BF16)
    c["ones128"] = mem.sb("ones128", 128, 1, BF16)
    c["ones_colf"] = mem.sb("ones_colf", 1, 128, F32)
    c["ones128f"] = mem.sb("ones128f", 128, 1, F32)
    c["cpool"] = mem.sb("cpool", 128, 8, F32)
    c["negb8"] = mem.sb("negb8", 128, 8, F32)
    c["ident"] = mem.sb("ident", 128, 128, BF16)
    c["identf"] = mem.sb("identf", 128, 128, F32)
    c["zeros256"] = mem.sb("zeros256", 128, 256, F32)
    c["maskA0"] = mem.sb("maskA0", 128, 256, F32)
    c["maskA1"] = mem.sb("maskA1", 128, 256, F32)
    c["iota_j"] = mem.sb("iota_j", 128, 128, F32)
    c["iota_p"] = mem.sb("iota_p", 128, 128, F32)

    ek.op("gpsimd", lambda g: g.memset(c["ones_col"].ap, 1.0), writes=[c["ones_col"]])
    ek.op("gpsimd", lambda g: g.memset(c["ones128"].ap, 1.0), writes=[c["ones128"]])
    ek.op("gpsimd", lambda g: g.memset(c["ones_colf"].ap, 1.0), writes=[c["ones_colf"]])
    ek.op("gpsimd", lambda g: g.memset(c["ones128f"].ap, 1.0), writes=[c["ones128f"]])
    for col, val in [(0, 1e-5), (1, 1.0), (2, 1e-3), (3, NEG_BIG), (4, -2.0794415416798357)]:
        ek.op("gpsimd", lambda g, col=col, val=val: g.memset(c["cpool"][:, col:col + 1], val),
              writes=[(c["cpool"], col, col + 1)])
    ek.op("gpsimd", lambda g: g.memset(c["negb8"].ap, NEG_BIG), writes=[c["negb8"]])
    ek.op("gpsimd", lambda g: g.memset(c["zeros256"].ap, 0.0), writes=[c["zeros256"]])
    ek.op("gpsimd", lambda g: g.iota(c["iota_j"].ap, [[1, 128]], channel_multiplier=0,
                                     allow_small_or_imprecise_dtypes=True), writes=[c["iota_j"]])
    ek.op("gpsimd", lambda g: g.iota(c["iota_p"].ap, [[0, 128]], channel_multiplier=1,
                                     allow_small_or_imprecise_dtypes=True), writes=[c["iota_p"]])
    ek.op("vector", lambda v: v.tensor_tensor(c["ident"].ap, c["iota_j"].ap, c["iota_p"].ap, ALU.is_equal),
          reads=[c["iota_j"], c["iota_p"]], writes=[c["ident"]])
    ek.op("vector", lambda v: v.tensor_tensor(c["identf"].ap, c["iota_j"].ap, c["iota_p"].ap, ALU.is_equal),
          reads=[c["iota_j"], c["iota_p"]], writes=[c["identf"]])
    ek.op("vector", lambda v: v.tensor_tensor(c["maskA0"][:, 0:128], c["iota_j"].ap, c["iota_p"].ap, ALU.is_ge),
          reads=[c["iota_j"], c["iota_p"]], writes=[(c["maskA0"], 0, 128)])
    ek.op("vector", lambda v: v.memset(c["maskA0"][:, 128:256], 1.0), writes=[(c["maskA0"], 128, 256)])
    ek.op("vector", lambda v: v.memset(c["maskA1"][:, 0:128], 0.0), writes=[(c["maskA1"], 0, 128)])
    ek.op("vector", lambda v: v.tensor_tensor(c["maskA1"][:, 128:256], c["iota_j"].ap, c["iota_p"].ap, ALU.is_ge),
          reads=[c["iota_j"], c["iota_p"]], writes=[(c["maskA1"], 128, 256)])


def load_weight_cast(ctx, dst_buf, src_ap, name, nrow_chunks, ncols, col0=0, tag=None):
    ek = ctx.ek
    src = src_ap.rearrange("(k p) n -> p k n", p=128)[:, :, col0:col0 + ncols]
    dst = dst_buf.h[:, 0:nrow_chunks * ncols].rearrange("p (k n) -> p k n", k=nrow_chunks)
    ek.dma("gpsimd", lambda g: g.dma_start(dst, src),
           reads=[dreg(name)], writes=[(dst_buf, 0, nrow_chunks * ncols)],
           slot=dst_buf.h.name, tag=tag)


def mm_chain(ctx, out_ap, out_reg, lhsT_list, rhs_list, reads, extra_first=None):
    ek = ctx.ek
    n = len(lhsT_list)

    def fns(t):
        insts = []
        for i in range(n):
            insts.append(t.matmul(out_ap, lhsT_list[i], rhs_list[i],
                                  start=(i == 0), stop=(i == n - 1)))
        return insts

    return ek.group("tensor", fns, reads=reads, writes=[out_reg])


def emit_ln_from_xtile(ctx, bufs, t0, W, phase_tag=""):
    ek, c, T = ctx.ek, ctx.c, ctx.T
    xt, xn = bufs["xtile"], bufs["xn"]
    tmp, st = bufs["stat_tmp"], bufs["stat_small"]
    nhalf = 4
    s1_ap, s1_reg = pfb(ctx, 4, 0, W)
    s2_ap, s2_reg = pfb(ctx, 5, 0, W)

    def s1_fns(t):
        return [t.matmul(s1_ap[0:1, :], c["ones128f"].ap, xt[:, cc * W:(cc + 1) * W],
                         start=(cc == 0), stop=(cc == 7)) for cc in range(8)]

    ek.group("tensor", s1_fns, reads=[(xt, 0, 8 * W), c["ones128f"]], writes=[s1_reg])
    for half in range(2):
        toks = []
        for i in range(nhalf):
            cc = half * nhalf + i
            ek.op("scalar", lambda s, cc=cc, i=i: s.activation(
                tmp[:, i * W:(i + 1) * W], xt[:, cc * W:(cc + 1) * W], AF.Square),
                reads=[(xt, cc * W, (cc + 1) * W)], writes=[(tmp, i * W, (i + 1) * W)])

        def s2_fns(t, half=half):
            return [t.matmul(s2_ap[0:1, :], c["ones128f"].ap, tmp[:, i * W:(i + 1) * W],
                             start=(half == 0 and i == 0), stop=(half == 1 and i == nhalf - 1))
                    for i in range(nhalf)]

        ek.group("tensor", s2_fns, reads=[(tmp, 0, nhalf * W), c["ones128f"]], writes=[s2_reg])
    m = st
    ek.op("vector", lambda v: v.tensor_single_scalar(m[0:1, 0:W], s1_ap[0:1, :], 1.0 / 1024.0, ALU.mult),
          reads=[s1_reg], writes=[(st, 0, W)])
    ek.op("vector", lambda v: v.tensor_single_scalar(m[0:1, 3 * W:4 * W], s2_ap[0:1, :], 1.0 / 1024.0, ALU.mult),
          reads=[s2_reg], writes=[(st, 3 * W, 4 * W)])
    ek.op("vector", lambda v: v.tensor_mul(m[0:1, W:2 * W], m[0:1, 0:W], m[0:1, 0:W]),
          reads=[(st, 0, W)], writes=[(st, W, 2 * W)])
    ek.op("vector", lambda v: v.tensor_sub(m[0:1, W:2 * W], m[0:1, 3 * W:4 * W], m[0:1, W:2 * W]),
          reads=[(st, W, 2 * W), (st, 3 * W, 4 * W)], writes=[(st, W, 2 * W)])
    ek.op("scalar", lambda s: s.activation(m[0:1, W:2 * W], m[0:1, W:2 * W], AF.Ln,
                                           bias=ctx.c["cpool"][0:1, 0:1], scale=1.0),
          reads=[(st, W, 2 * W), (ctx.c["cpool"], 0, 1)], writes=[(st, W, 2 * W)])
    ek.op("scalar", lambda s: s.activation(m[0:1, 2 * W:3 * W], m[0:1, W:2 * W], AF.Exp, bias=0.0, scale=-0.5),
          reads=[(st, W, 2 * W)], writes=[(st, 2 * W, 3 * W)])
    mb_ap, mb_reg = pfb(ctx, 4, 0, W)
    rb_ap, rb_reg = pfb(ctx, 5, 0, W)
    ek.group("tensor", lambda t: [t.matmul(mb_ap, c["ones_colf"].ap, m[0:1, 0:W], start=True, stop=True)],
             reads=[(st, 0, W), c["ones_colf"]], writes=[mb_reg])
    ek.group("tensor", lambda t: [t.matmul(rb_ap, c["ones_colf"].ap, m[0:1, 2 * W:3 * W], start=True, stop=True)],
             reads=[(st, 2 * W, 3 * W), c["ones_colf"]], writes=[rb_reg])
    for cc in range(8):
        dst_lo = cc * T + t0
        ek.op("vector", lambda v, cc=cc: v.tensor_sub(tmp[:, 0:W] if False else tmp[:, (cc % 4) * W:(cc % 4) * W + W],
                                                      xt[:, cc * W:(cc + 1) * W], mb_ap),
              reads=[(xt, cc * W, (cc + 1) * W), mb_reg],
              writes=[(tmp, (cc % 4) * W, (cc % 4) * W + W)])
        ek.op("vector", lambda v, cc=cc, dst_lo=dst_lo: v.tensor_mul(
            xn[:, dst_lo:dst_lo + W], tmp[:, (cc % 4) * W:(cc % 4) * W + W], rb_ap),
            reads=[(tmp, (cc % 4) * W, (cc % 4) * W + W), rb_reg],
            writes=[(xn, dst_lo, dst_lo + W)])


def emit_residual_tile(ctx, bufs, t0, W, y_src, x_prev_name, x_next_name, store=True):
    ek, T = ctx.ek, ctx.T
    xt, ysb = bufs["xtile"], bufs[y_src]
    xprev = ctx.dram[x_prev_name]
    src = xprev.rearrange("(c p) t -> p c t", p=128)[:, :, t0:t0 + W]
    ek.dma("sync", lambda sy: sy.dma_start(xt.h[:, 0:8 * W].rearrange("p (c w) -> p c w", c=8), src),
           reads=[dreg(x_prev_name, t0, t0 + W)], writes=[(xt, 0, 8 * W)], slot="xt")
    for cc in range(8):
        ek.op("vector", lambda v, cc=cc: v.tensor_add(
            xt[:, cc * W:(cc + 1) * W], ysb[:, cc * W:(cc + 1) * W], xt[:, cc * W:(cc + 1) * W]),
            reads=[(ysb, cc * W, (cc + 1) * W), (xt, cc * W, (cc + 1) * W)],
            writes=[(xt, cc * W, (cc + 1) * W)])
    if store:
        xnext = ctx.dram[x_next_name]
        dst = xnext.rearrange("(c p) t -> p c t", p=128)[:, :, t0:t0 + W]
        ek.dma("sync", lambda sy: sy.dma_start(dst, xt.h[:, 0:8 * W].rearrange("p (c w) -> p c w", c=8)),
               reads=[(xt, 0, 8 * W)], writes=[dreg(x_next_name, t0, t0 + W)], slot="xt")


def emit_prologue(ctx, bufs, x_in_name):
    T = ctx.T
    W = 256
    for t0 in range(0, T, W):
        xin = ctx.dram[x_in_name]
        src = xin.rearrange("(c p) t -> p c t", p=128)[:, :, t0:t0 + W]
        xt = bufs["xtile"]
        ctx.ek.dma("sync", lambda sy, src=src: sy.dma_start(
            xt.h[:, 0:8 * W].rearrange("p (c w) -> p c w", c=8), src),
            reads=[dreg(x_in_name, t0, t0 + W)], writes=[(xt, 0, 8 * W)], slot="xt")
        emit_ln_from_xtile(ctx, bufs, t0, W)


def alloc_shared_bufs(ctx):
    mem, T = ctx.mem, ctx.T
    b = {}
    b["xn"] = mem.sb("xn", 128, 8 * T, BF16)
    b["xtile"] = mem.sb("xtile", 128, 8 * 256, F32)
    b["stat_tmp"] = mem.sb("stat_tmp", 128, 4 * 256, F32)
    b["stat_small"] = mem.sb("stat_small", 1, 4 * 256, F32)
    return b


def alloc_phase_a_bufs(ctx, b):
    mem = ctx.mem
    b["y_sb"] = mem.sb("y_sb_a", 128, 8 * 256, F32)
    b["wq"] = mem.sb("wq", 128, 8 * 512, BF16)
    b["wk"] = mem.sb("wk", 128, 8 * 512, BF16)
    b["wv"] = mem.sb("wv", 128, 8 * 1024, BF16)
    b["wg"] = mem.sb("wg", 128, 8 * 512, BF16)
    b["wgk1"] = mem.sb("wgk1", 128, 8 * 16, BF16)
    b["wgk2"] = mem.sb("wgk2", 64, 512, BF16)
    b["bgk2"] = mem.sb("bgk2", 128, 4, F32)
    b["wout"] = mem.sb("wout", 128, 8 * 1024, BF16)
    b["rmsw"] = mem.sb("rmsw", 128, 2, F32)
    b["q_sb"] = mem.sb("q_sb", 128, 4 * 256, BF16)
    b["k_sb"] = mem.sb("k_sb", 128, 4 * 256, BF16)
    b["v_sb"] = mem.sb("v_sb", 128, 8 * 256, BF16)
    b["gate"] = mem.sb("gate", 128, 8 * 512, BF16)
    b["z1"] = mem.sb("z1", 64, 512, BF16)
    b["g_sb"] = mem.sb("g_sb", 128, 4 * 512, F32)
    b["b_sb"] = mem.sb("b_sb", 128, 4 * 256, F32)
    b["ga"] = mem.sb("ga", 128, 512, F32)
    b["gb"] = mem.sb("gb", 128, 512, F32)
    b["ebuf"] = mem.sb("ebuf", 128, 2 * 256, F32)
    b["ebc"] = mem.sb("ebc", 128, 4, F32)
    b["qt"] = mem.sb("qt", 128, 256, BF16)
    b["kh"] = mem.sb("kh", 128, 256, BF16)
    b["kb"] = mem.sb("kb", 128, 256, BF16)
    b["at_sb"] = mem.sb("at_sb", 128, 2 * 256, BF16)
    b["vt_sb"] = mem.sb("vt_sb", 128, 2 * 256, BF16)
    b["kbt_sb"] = mem.sb("kbt_sb", 128, 2 * 128, BF16)
    b["rb_sb"] = mem.sb("rb_sb", 128, 256, F32)
    b["og"] = mem.sb("og", 128, 8 * 256, BF16)
    b["S"] = mem.sb("S", 128, 4 * 256, F32)
    b["S_bf"] = mem.sb("S_bf", 128, 4 * 256, BF16)


def emit_phase_a_weights(ctx, b):
    ek = ctx.ek
    load_weight_cast(ctx, b["wq"], ctx.dram["L0_attn_Wq"], "L0_attn_Wq", 8, 512)
    load_weight_cast(ctx, b["wk"], ctx.dram["L0_attn_Wk"], "L0_attn_Wk", 8, 512)
    load_weight_cast(ctx, b["wv"], ctx.dram["L0_attn_Wv"], "L0_attn_Wv", 8, 1024)
    src_o = ctx.dram["L0_attn_Wout"].rearrange("(k p) n -> p k n", p=128)
    dst_o = b["wout"].h[:, 0:8 * 1024].rearrange("p (k n) -> p k n", k=8)
    ek.dma("gpsimd", lambda g: g.dma_start(dst_o, src_o),
           reads=[dreg("L0_attn_Wout")], writes=[(b["wout"], 0, 8 * 1024)], slot="wout")
    load_weight_cast(ctx, b["wgk1"], ctx.dram["L0_attn_Wgk1"], "L0_attn_Wgk1", 8, 16)
    ek.op("vector", lambda v: v.memset(b["wgk2"][0:64, :], 0.0), writes=[b["wgk2"]])
    ek.op("vector", lambda v: v.memset(b["z1"][0:64, :], 0.0), writes=[b["z1"]])
    ek.dma("gpsimd", lambda g: g.dma_start(b["wgk2"][0:16, :], ctx.dram["L0_attn_Wgk2"][:, :]),
           reads=[dreg("L0_attn_Wgk2")], writes=[(b["wgk2"], 0, 512)], slot="wgk2")
    ek.dma("gpsimd", lambda g: g.dma_start(
        b["bgk2"].ap, ctx.dram["L0_attn_bgk2"].rearrange("(fc p) -> p fc", p=128),
        allow_slow_non_contiguous=True),
        reads=[dreg("L0_attn_bgk2")], writes=[b["bgk2"]], slot="bgk2")
    ek.dma("gpsimd", lambda g: g.dma_start(
        b["rmsw"].ap, ctx.dram["L0_attn_rms_w"].rearrange("(vh p) -> p vh", p=128),
        allow_slow_non_contiguous=True),
        reads=[dreg("L0_attn_rms_w")], writes=[b["rmsw"]], slot="rmsw")


def _proj_group(ctx, out_ap, out_reg, wbuf, wcol_of_dc, xn, t0, W, nk=8):
    T = ctx.T
    lhsTs = [wbuf[:, wcol_of_dc(dc): wcol_of_dc(dc) + 128] for dc in range(nk)]
    rhss = [xn[:, dc * T + t0: dc * T + t0 + W] for dc in range(nk)]
    reads = [(wbuf, min(wcol_of_dc(dc) for dc in range(nk)),
              max(wcol_of_dc(dc) + 128 for dc in range(nk)))]
    reads += [(xn, dc * T + t0, dc * T + t0 + W) for dc in range(nk)]
    return mm_chain(ctx, out_ap, out_reg, lhsTs, rhss, reads)


def emit_phase_a(ctx, b, x_prev_name, x_next_name):
    ek, c, T, NT = ctx.ek, ctx.c, ctx.T, ctx.NT
    xn = b["xn"]
    emit_phase_a_weights(ctx, b)
    ek.op("vector", lambda v: v.memset(b["S"].ap, 0.0), writes=[b["S"]])
    ek.op("vector", lambda v: v.memset(b["S_bf"].ap, 0.0), writes=[b["S_bf"]])

    for cc in range(NT):
        t0 = cc * 256
        if cc % 2 == 0:
            _emit_a_gate_g(ctx, b, t0, min(512, T - t0))
        for fc in range(4):
            q_ap, q_reg = pfb(ctx, fc // 2, (fc % 2) * 256, (fc % 2) * 256 + 256)
            _proj_group(ctx, q_ap, q_reg, b["wq"], lambda dc, fc=fc: dc * 512 + fc * 128, xn, t0, 256)
            ek.op("scalar", lambda s, fc=fc, q_ap=q_ap: s.copy(b["q_sb"][:, fc * 256:(fc + 1) * 256], q_ap),
                  reads=[q_reg], writes=[(b["q_sb"], fc * 256, (fc + 1) * 256)])
        for fc in range(4):
            k_ap, k_reg = pfb(ctx, 2 + fc // 2, (fc % 2) * 256, (fc % 2) * 256 + 256)
            _proj_group(ctx, k_ap, k_reg, b["wk"], lambda dc, fc=fc: dc * 512 + fc * 128, xn, t0, 256)
            ek.op("scalar", lambda s, fc=fc, k_ap=k_ap: s.copy(b["k_sb"][:, fc * 256:(fc + 1) * 256], k_ap),
                  reads=[k_reg], writes=[(b["k_sb"], fc * 256, (fc + 1) * 256)])
        for vc in range(8):
            v_ap, v_reg = pfb(ctx, (vc % 4) // 2, (vc % 2) * 256, (vc % 2) * 256 + 256)
            _proj_group(ctx, v_ap, v_reg, b["wv"], lambda dc, vc=vc: dc * 1024 + vc * 128, xn, t0, 256)
            ek.op("scalar", lambda s, vc=vc, v_ap=v_ap: s.copy(b["v_sb"][:, vc * 256:(vc + 1) * 256], v_ap),
                  reads=[v_reg], writes=[(b["v_sb"], vc * 256, (vc + 1) * 256)])
        for hc in range(4):
            _emit_a_head(ctx, b, cc, hc)
        for dc in range(8):
            y_ap, y_reg = pfb(ctx, dc // 2, (dc % 2) * 256, (dc % 2) * 256 + 256)
            lhsTs = [b["wout"][:, vc * 1024 + dc * 128: vc * 1024 + dc * 128 + 128] for vc in range(8)]
            rhss = [b["og"][:, vc * 256:(vc + 1) * 256] for vc in range(8)]
            mm_chain(ctx, y_ap, y_reg, lhsTs, rhss,
                     reads=[b["wout"], b["og"]])
            ek.op("scalar", lambda s, dc=dc, y_ap=y_ap: s.copy(
                b["y_sb"][:, dc * 256:(dc + 1) * 256], y_ap),
                reads=[y_reg], writes=[(b["y_sb"], dc * 256, (dc + 1) * 256)])
        emit_residual_tile(ctx, b, t0, 256, "y_sb", x_prev_name, x_next_name)
        emit_ln_from_xtile(ctx, b, t0, 256)


def _emit_a_gate_g(ctx, b, t0, W):
    ek, c, T = ctx.ek, ctx.c, ctx.T
    xn = b["xn"]
    for fh in range(2):
        load_weight_cast(ctx, b["wg"], ctx.dram["L0_attn_Wg"], "L0_attn_Wg", 8, 512,
                         col0=fh * 512)
        for fc4 in range(4):
            fc = fh * 4 + fc4
            g_ap, g_reg = pfb(ctx, fc4, 0, W)
            _proj_group(ctx, g_ap, g_reg, b["wg"], lambda dc, fc4=fc4: dc * 512 + fc4 * 128, xn, t0, W)
            ek.op("scalar", lambda s, fc=fc, g_ap=g_ap: s.activation(
                b["gate"][:, fc * W:(fc + 1) * W], g_ap, AF.Silu),
                reads=[g_reg], writes=[(b["gate"], fc * W, (fc + 1) * W)])
    z1_ap, z1_reg = pfb(ctx, 4, 0, W)
    lhsTs = [b["wgk1"][:, dc * 16:(dc + 1) * 16] for dc in range(8)]
    rhss = [xn[:, dc * T + t0: dc * T + t0 + W] for dc in range(8)]
    mm_chain(ctx, z1_ap[0:16, :], z1_reg, lhsTs, rhss,
             reads=[b["wgk1"]] + [(xn, dc * T + t0, dc * T + t0 + W) for dc in range(8)])
    ek.op("scalar", lambda s: s.copy(b["z1"][0:16, 0:W], z1_ap[0:16, :]),
          reads=[z1_reg], writes=[b["z1"]])
    for fc in range(4):
        z_ap, z_reg = pfb(ctx, 5, 0, W)
        ek.group("tensor", lambda t, fc=fc, z_ap=z_ap: [
            t.matmul(z_ap, b["wgk2"][0:64, fc * 128:(fc + 1) * 128], b["z1"][0:64, 0:W],
                     start=True, stop=True)],
            reads=[b["wgk2"], b["z1"]], writes=[z_reg])
        bias = b["bgk2"][:, fc:fc + 1]
        ek.op("vector", lambda v, z_ap=z_ap, bias=bias: v.tensor_scalar(
            b["ga"][:, 0:W], z_ap, bias, 0.0, ALU.add, ALU.min),
            reads=[z_reg, b["bgk2"]], writes=[(b["ga"], 0, W)])
        ek.op("scalar", lambda s, z_ap=z_ap, bias=bias: s.activation(
            b["gb"][:, 0:W], z_ap, AF.Abs, bias=bias, scale=1.0),
            reads=[z_reg, b["bgk2"]], writes=[(b["gb"], 0, W)])
        ek.op("scalar", lambda s: s.activation(b["gb"][:, 0:W], b["gb"][:, 0:W], AF.Exp, scale=-1.0),
              reads=[(b["gb"], 0, W)], writes=[(b["gb"], 0, W)])
        ek.op("scalar", lambda s: s.activation(b["gb"][:, 0:W], b["gb"][:, 0:W], AF.Ln,
                                               bias=ctx.c["cpool"][:, 1:2]),
              reads=[(b["gb"], 0, W), (ctx.c["cpool"], 1, 2)], writes=[(b["gb"], 0, W)])
        ek.op("vector", lambda v: v.tensor_sub(b["ga"][:, 0:W], b["ga"][:, 0:W], b["gb"][:, 0:W]),
              reads=[(b["ga"], 0, W), (b["gb"], 0, W)], writes=[(b["ga"], 0, W)])
        ek.op("vector", lambda v, fc=fc: v.tensor_scalar(
            b["g_sb"][:, fc * 512:fc * 512 + W], b["ga"][:, 0:W], 1.0 / 16.0, -3.0, ALU.mult, ALU.max),
            reads=[(b["ga"], 0, W)], writes=[(b["g_sb"], fc * 512, fc * 512 + W)])


def _emit_a_head(ctx, b, cc, hc):
    ek, c = ctx.ek, ctx.c
    coff = (cc % 2) * 256
    ek.op("vector", lambda v: v.tensor_tensor_scan(
        b["b_sb"][:, hc * 256:(hc + 1) * 256], b["g_sb"][:, hc * 512 + coff: hc * 512 + coff + 256],
        c["zeros256"].ap, 0.0, ALU.add, ALU.add),
        reads=[(b["g_sb"], hc * 512 + coff, hc * 512 + coff + 256), c["zeros256"]],
        writes=[(b["b_sb"], hc * 256, (hc + 1) * 256)])
    bs = b["b_sb"][:, hc * 256:(hc + 1) * 256]
    bs_reg = (b["b_sb"], hc * 256, (hc + 1) * 256)
    ek.op("scalar", lambda s: s.activation(b["ebuf"][:, 0:256], bs, AF.Exp),
          reads=[bs_reg], writes=[(b["ebuf"], 0, 256)])
    ek.op("scalar", lambda s: s.activation(b["ebuf"][:, 256:512], bs, AF.Exp, scale=-1.0),
          reads=[bs_reg], writes=[(b["ebuf"], 256, 512)])
    ek.op("scalar", lambda s: s.activation(
        b["ebc"][:, hc:hc + 1], b["b_sb"][:, hc * 256 + 255: hc * 256 + 256], AF.Exp),
        reads=[bs_reg], writes=[(b["ebc"], hc, hc + 1)])
    ek.op("vector", lambda v: v.tensor_mul(b["qt"].ap, b["q_sb"][:, hc * 256:(hc + 1) * 256], b["ebuf"][:, 0:256]),
          reads=[(b["q_sb"], hc * 256, (hc + 1) * 256), (b["ebuf"], 0, 256)], writes=[b["qt"]])
    ek.op("vector", lambda v: v.tensor_mul(b["kh"].ap, b["k_sb"][:, hc * 256:(hc + 1) * 256], b["ebuf"][:, 256:512]),
          reads=[(b["k_sb"], hc * 256, (hc + 1) * 256), (b["ebuf"], 256, 512)], writes=[b["kh"]])
    ek.op("vector", lambda v: v.tensor_scalar_mul(b["kb"].ap, b["kh"].ap, b["ebc"][:, hc:hc + 1]),
          reads=[b["kh"], (b["ebc"], hc, hc + 1)], writes=[b["kb"]])
    for sblk in range(2):
        at_ap, at_reg = pfb(ctx, 4, sblk * 256, (sblk + 1) * 256)
        ek.group("tensor", lambda t, sblk=sblk, at_ap=at_ap: [
            t.matmul(at_ap, b["kh"][:, sblk * 128:(sblk + 1) * 128], b["qt"].ap, start=True, stop=True)],
            reads=[b["kh"], b["qt"]], writes=[at_reg])
        mask = c["maskA0"] if sblk == 0 else c["maskA1"]
        ek.op("vector", lambda v, sblk=sblk, at_ap=at_ap, mask=mask: v.tensor_mul(
            b["at_sb"][:, sblk * 256:(sblk + 1) * 256], at_ap, mask.ap),
            reads=[at_reg, mask], writes=[(b["at_sb"], sblk * 256, (sblk + 1) * 256)])
    for sblk in range(2):
        for vh in range(2):
            pt_ap, pt_reg = ptb(ctx, 0, vh * 128, (vh + 1) * 128)
            ek.op("tensor", lambda t, sblk=sblk, vh=vh, pt_ap=pt_ap: t.transpose(
                pt_ap, b["v_sb"][:, (2 * hc + vh) * 256 + sblk * 128: (2 * hc + vh) * 256 + (sblk + 1) * 128],
                c["ident"].ap),
                reads=[(b["v_sb"], (2 * hc + vh) * 256 + sblk * 128, (2 * hc + vh) * 256 + (sblk + 1) * 128),
                       c["ident"]],
                writes=[pt_reg])
        ek.op("vector", lambda v, sblk=sblk: v.tensor_copy(
            b["vt_sb"][:, sblk * 256:(sblk + 1) * 256], ctx.pt[:, 0:256]),
            reads=[(ctx.pt, 0, 256)], writes=[(b["vt_sb"], sblk * 256, (sblk + 1) * 256)])
        pt2_ap, pt2_reg = ptb(ctx, 1, 0, 128)
        ek.op("tensor", lambda t, sblk=sblk, pt2_ap=pt2_ap: t.transpose(
            pt2_ap, b["kb"][:, sblk * 128:(sblk + 1) * 128], c["ident"].ap),
            reads=[(b["kb"], sblk * 128, (sblk + 1) * 128), c["ident"]], writes=[pt2_reg])
        ek.op("vector", lambda v, sblk=sblk: v.tensor_copy(
            b["kbt_sb"][:, sblk * 128:(sblk + 1) * 128], ctx.pt[:, 3072 - 3072 + 1024:1024 + 128] if False else pt2_ap),
            reads=[pt2_reg], writes=[(b["kbt_sb"], sblk * 128, (sblk + 1) * 128)])
    for vh in range(2):
        o_ap, o_reg = pfb(ctx, 0, vh * 256, (vh + 1) * 256)
        lhsTs = [b["S_bf"][:, hc * 256 + vh * 128: hc * 256 + (vh + 1) * 128],
                 b["vt_sb"][:, 0 * 256 + vh * 128: 0 * 256 + (vh + 1) * 128],
                 b["vt_sb"][:, 1 * 256 + vh * 128: 1 * 256 + (vh + 1) * 128]]
        rhss = [b["qt"].ap, b["at_sb"][:, 0:256], b["at_sb"][:, 256:512]]
        mm_chain(ctx, o_ap, o_reg, lhsTs, rhss,
                 reads=[(b["S_bf"], hc * 256, (hc + 1) * 256), b["vt_sb"], b["qt"], b["at_sb"]])
    ds_ap, ds_reg = pfb(ctx, 1, 0, 256)
    lhsTs = [b["kbt_sb"][:, 0:128], b["kbt_sb"][:, 128:256]]
    rhss = [b["vt_sb"][:, 0:256], b["vt_sb"][:, 256:512]]
    mm_chain(ctx, ds_ap, ds_reg, lhsTs, rhss, reads=[b["kbt_sb"], b["vt_sb"]])
    for vh in range(2):
        o_ap = ctx.pf[:, 0 * 512 + vh * 256: 0 * 512 + (vh + 1) * 256]
        o_reg = (ctx.pf, vh * 256, (vh + 1) * 256)
        ek.op("scalar", lambda s, vh=vh, o_ap=o_ap: s.activation(
            b["stat_tmp"][:, vh * 256:(vh + 1) * 256], o_ap, AF.Square),
            reads=[o_reg], writes=[(b["stat_tmp"], vh * 256, (vh + 1) * 256)])
    ss_ap, ss_reg = pfb(ctx, 5, 0, 256)
    mm_chain(ctx, ss_ap[0:1, :], ss_reg,
             [c["ones128f"].ap, c["ones128f"].ap],
             [b["stat_tmp"][:, 0:256], b["stat_tmp"][:, 256:512]],
             reads=[c["ones128f"], (b["stat_tmp"], 0, 512)])
    ek.op("scalar", lambda s: s.activation(b["stat_small"][0:1, 0:256], ss_ap[0:1, :], AF.Ln,
                                           bias=ctx.c["cpool"][0:1, 0:1], scale=1.0 / 256.0),
          reads=[ss_reg, (ctx.c["cpool"], 0, 1)], writes=[(b["stat_small"], 0, 256)])
    ek.op("scalar", lambda s: s.activation(b["stat_small"][0:1, 256:512], b["stat_small"][0:1, 0:256],
                                           AF.Exp, scale=-0.5),
          reads=[(b["stat_small"], 0, 256)], writes=[(b["stat_small"], 256, 512)])
    rb_ap, rb_reg = pfb(ctx, 5, 256, 512)
    ek.group("tensor", lambda t: [t.matmul(rb_ap, c["ones_colf"].ap, b["stat_small"][0:1, 256:512],
                                           start=True, stop=True)],
             reads=[c["ones_colf"], (b["stat_small"], 256, 512)], writes=[rb_reg])
    ek.op("vector", lambda v: v.tensor_copy(b["rb_sb"].ap, rb_ap), reads=[rb_reg], writes=[b["rb_sb"]])
    for vh in range(2):
        o_ap = ctx.pf[:, vh * 256: (vh + 1) * 256]
        o_reg = (ctx.pf, vh * 256, (vh + 1) * 256)
        ek.op("vector", lambda v, vh=vh, o_ap=o_ap: v.scalar_tensor_tensor(
            b["stat_tmp"][:, vh * 256:(vh + 1) * 256], o_ap, b["rmsw"][:, vh:vh + 1], b["rb_sb"].ap,
            ALU.mult, ALU.mult),
            reads=[o_reg, b["rmsw"], b["rb_sb"]], writes=[(b["stat_tmp"], vh * 256, (vh + 1) * 256)])
        gcol = (2 * hc + vh) * 512 + (cc % 2) * 256
        ek.op("vector", lambda v, vh=vh, gcol=gcol: v.tensor_mul(
            b["og"][:, (2 * hc + vh) * 256:(2 * hc + vh + 1) * 256],
            b["stat_tmp"][:, vh * 256:(vh + 1) * 256], b["gate"][:, gcol:gcol + 256]),
            reads=[(b["stat_tmp"], vh * 256, (vh + 1) * 256), (b["gate"], gcol, gcol + 256)],
            writes=[(b["og"], (2 * hc + vh) * 256, (2 * hc + vh + 1) * 256)])
    ek.op("vector", lambda v: v.tensor_scalar_mul(
        b["S"][:, hc * 256:(hc + 1) * 256], b["S"][:, hc * 256:(hc + 1) * 256], b["ebc"][:, hc:hc + 1]),
        reads=[(b["S"], hc * 256, (hc + 1) * 256), (b["ebc"], hc, hc + 1)],
        writes=[(b["S"], hc * 256, (hc + 1) * 256)])
    ek.op("vector", lambda v: v.tensor_add(
        b["S"][:, hc * 256:(hc + 1) * 256], b["S"][:, hc * 256:(hc + 1) * 256], ds_ap),
        reads=[(b["S"], hc * 256, (hc + 1) * 256), ds_reg],
        writes=[(b["S"], hc * 256, (hc + 1) * 256)])
    ek.op("vector", lambda v: v.tensor_copy(b["S_bf"][:, hc * 256:(hc + 1) * 256], b["S"][:, hc * 256:(hc + 1) * 256]),
          reads=[(b["S"], hc * 256, (hc + 1) * 256)], writes=[(b["S_bf"], hc * 256, (hc + 1) * 256)])


def alloc_routing_bufs(ctx, b, sfx, tstride=1024):
    mem = ctx.mem
    b["rwT"] = mem.sb("rwT" + sfx, 128, 8 * 8, BF16)
    b["rtmp"] = mem.sb("rtmp" + sfx, 8, 1024, BF16)
    b["bias_b"] = mem.sb("bias_b" + sfx, 128, 8, F32)
    b["bias_row"] = mem.sb("bias_row" + sfx, 1, 8, F32)
    b["l_sb"] = mem.sb("l_sb" + sfx, 128, 8, F32)
    b["l_ms"] = mem.sb("l_ms" + sfx, 128, 8, F32)
    b["msk"] = mem.sb("msk" + sfx, 128, 8, mybir.dt.uint8)
    b["m1"] = mem.sb("m1" + sfx, 128, 4, F32)
    b["el"] = mem.sb("el" + sfx, 128, 8, F32)
    b["cw16"] = mem.sb("cw16" + sfx, 128, 8, BF16)
    b["cwrow"] = mem.sb("cwrow" + sfx, 1, 8 * tstride, BF16)
    b["cw_stride"] = tstride
    b["cwb"] = mem.sb("cwb" + sfx, 128, 1024, BF16)


def alloc_phase_b_bufs(ctx, b, with_stash=False):
    mem = ctx.mem
    alloc_routing_bufs(ctx, b, "_b")
    b["y_sb"] = mem.sb("y_sb_b", 128, 8 * 256, F32)
    b["w1s"] = mem.sb("w1s", 128, 2 * 8 * 128, BF16)
    b["w3s"] = mem.sb("w3s", 128, 2 * 8 * 128, BF16)
    b["eo"] = mem.sb("eo", 128, 8 * 1024, BF16)
    b["w2s"] = mem.sb("w2s", 128, 8 * 1024, BF16)
    b["y_acc"] = mem.sb("y_acc", 128, 8 * 1024, F32)
    if with_stash:
        b["stash_stg"] = mem.sb("stash_stg", 128, 2 * 1024, BF16)


def make_stash_chunks(ctx, b):
    ek = ctx.ek
    stg = b["stash_stg"]
    chunks = []
    bi = 0
    jobs = []
    for nm, dn in (("L1_ffn_gate_w", "gwbf"), ("L1_ffn_up_w", "uwbf")):
        for e in range(8):
            for t in range(32):
                jobs.append((nm, dn, e, t, 0))
    for e in range(8):
        for t in range(8):
            for c4 in range(4):
                jobs.append(("L1_ffn_down_w", "dwbf", e, t, c4 * 1024))
    for (nm, dn, e, t, c0) in jobs:
        half = bi % 2
        bi += 1
        src = ctx.dram[nm][e].rearrange("(t p) n -> p t n", p=128)[:, t, c0:c0 + 1024]
        dst = ctx.dram[dn][e].rearrange("(t p) n -> p t n", p=128)[:, t, c0:c0 + 1024]
        stg_w = stg.h[:, half * 1024:(half + 1) * 1024]
        ncols = 1024 if dn != "dwbf" else 4096
        rlo = e * 4096 * 1024 + t * 128 * ncols

        def emit(src=src, dst=dst, stg_w=stg_w, nm=nm, dn=dn, rlo=rlo, half=half):
            ek.dma("gpsimd", lambda g: g.dma_start(stg_w, src),
                   reads=[dreg(nm, rlo, rlo + 128 * 1024)],
                   writes=[(stg, half * 1024, (half + 1) * 1024)], slot="st" + str(half))
            ek.dma("sync", lambda sy: sy.dma_start(dst, stg_w),
                   reads=[(stg, half * 1024, (half + 1) * 1024)],
                   writes=[dreg(dn, rlo, rlo + 128 * 1024)], slot="ss" + str(half))
        chunks.append(emit)
    return chunks


def make_repack_chunks(ctx, b):
    ek = ctx.ek
    stg = b["rpk_stg"]
    chunks = []
    bi = 0
    for e in range(8):
        for dn, dT in (("gwbf", "gwT"), ("uwbf", "uwT")):
            for icg in range(8):
                for hcp in range(4):
                    half = bi % 2
                    bi += 1

                    def emit(e=e, dn=dn, dT=dT, icg=icg, hcp=hcp, half=half):
                        for hci in range(2):
                            hc = hcp * 2 + hci
                            src = ctx.dram[dn][e, icg * 512:(icg + 1) * 512, hc * 128:(hc + 1) * 128]
                            dst = stg.h[:, half * 1024 + hci * 512: half * 1024 + (hci + 1) * 512]
                            ek.dma("sync", lambda sy, src=src, dst=dst: sy.dma_start(dst, src, transpose=True),
                                   reads=[dreg(dn, e * 4096 * 1024 + icg * 512 * 1024,
                                               e * 4096 * 1024 + (icg + 1) * 512 * 1024)],
                                   writes=[(stg, half * 1024 + hci * 512, half * 1024 + (hci + 1) * 512)],
                                   slot="rpk" + str(half))
                        dstT = ctx.dram[dT][e, icg, :, hcp * 1024:(hcp + 1) * 1024]
                        ek.dma("gpsimd", lambda g, dstT=dstT, half=half: g.dma_start(
                            dstT, stg.h[:, half * 1024:(half + 1) * 1024]),
                            reads=[(stg, half * 1024, (half + 1) * 1024)],
                            writes=[dreg(dT, (e * 8 + icg) * 128 * 4096 + hcp * 128 * 1024,
                                         (e * 8 + icg) * 128 * 4096 + (hcp + 1) * 128 * 1024)],
                            slot="rps" + str(half))
                    chunks.append(emit)
        for ihalf in range(2):
            for ic in range(16):
                half = bi % 2
                bi += 1

                def emit(e=e, ihalf=ihalf, ic=ic, half=half):
                    icg_ = ihalf * 16 + ic
                    src = ctx.dram["dwbf"][e, :, icg_ * 128:(icg_ + 1) * 128]
                    dst = stg.h[:, half * 1024:(half + 1) * 1024]
                    ek.dma("sync", lambda sy, src=src, dst=dst: sy.dma_start(dst, src, transpose=True),
                           reads=[dreg("dwbf", e * 1024 * 4096, (e + 1) * 1024 * 4096)],
                           writes=[(stg, half * 1024, (half + 1) * 1024)],
                           slot="rpk" + str(half))
                    dstT = ctx.dram["dwT"][e, ihalf, :, ic * 1024:(ic + 1) * 1024]
                    ek.dma("gpsimd", lambda g, dstT=dstT, half=half: g.dma_start(
                        dstT, stg.h[:, half * 1024:(half + 1) * 1024]),
                        reads=[(stg, half * 1024, (half + 1) * 1024)],
                        writes=[dreg("dwT", (e * 2 + ihalf) * 128 * 16384 + ic * 128 * 1024,
                                     (e * 2 + ihalf) * 128 * 16384 + (ic + 1) * 128 * 1024)],
                        slot="rps" + str(half))
                chunks.append(emit)
    return chunks


def emit_phase_b_routing(ctx, b, tt0, te):
    ek, c, T = ctx.ek, ctx.c, ctx.T
    xn = b["xn"]
    for tci in range(te // 128):
        tc0 = tt0 + tci * 128
        lg_ap, lg_reg = pfb(ctx, 4, 0, 128)
        lhsTs = [xn[:, dc * T + tc0: dc * T + tc0 + 128] for dc in range(8)]
        rhss = [b["rwT"][:, dc * 8:(dc + 1) * 8] for dc in range(8)]
        mm_chain(ctx, lg_ap[:, 0:8], lg_reg, lhsTs, rhss,
                 reads=[b["rwT"]] + [(xn, dc * T + tc0, dc * T + tc0 + 128) for dc in range(8)])
        ek.op("vector", lambda v, lg_ap=lg_ap: v.tensor_add(b["l_sb"].ap, lg_ap[:, 0:8], b["bias_b"].ap),
              reads=[lg_reg, b["bias_b"]], writes=[b["l_sb"]])
        ek.op("vector", lambda v: v.reduce_max(b["m1"][:, 0:1], b["l_sb"].ap, axis=AX.X),
              reads=[b["l_sb"]], writes=[(b["m1"], 0, 1)])
        ek.op("vector", lambda v: v.tensor_single_scalar(b["msk"].ap, b["l_sb"].ap, b["m1"][:, 0:1], ALU.is_lt),
              reads=[b["l_sb"], (b["m1"], 0, 1)], writes=[b["msk"]])
        ek.op("vector", lambda v: v.memset(b["l_ms"].ap, NEG_BIG), writes=[b["l_ms"]])
        ek.op("vector", lambda v: v.copy_predicated(b["l_ms"].ap, b["msk"].ap, b["l_sb"].ap),
              reads=[b["msk"], b["l_sb"], b["l_ms"]], writes=[b["l_ms"]])
        ek.op("vector", lambda v: v.reduce_max(b["m1"][:, 1:2], b["l_ms"].ap, axis=AX.X),
              reads=[b["l_ms"]], writes=[(b["m1"], 1, 2)])
        ek.op("vector", lambda v: v.tensor_single_scalar(b["m1"][:, 2:3], b["m1"][:, 0:1], -1.0, ALU.mult),
              reads=[(b["m1"], 0, 1)], writes=[(b["m1"], 2, 3)])
        ek.op("scalar", lambda s: s.activation(b["el"].ap, b["l_sb"].ap, AF.Exp, bias=b["m1"][:, 2:3]),
              reads=[b["l_sb"], (b["m1"], 2, 3)], writes=[b["el"]])
        ek.op("vector", lambda v: v.tensor_add(b["m1"][:, 3:4], b["m1"][:, 1:2], b["m1"][:, 2:3]),
              reads=[(b["m1"], 1, 3)], writes=[(b["m1"], 3, 4)])
        ek.op("scalar", lambda s: s.activation(b["m1"][:, 3:4], b["m1"][:, 3:4], AF.Exp),
              reads=[(b["m1"], 3, 4)], writes=[(b["m1"], 3, 4)])
        ek.op("vector", lambda v: v.tensor_single_scalar(b["m1"][:, 3:4], b["m1"][:, 3:4], 1.0, ALU.add),
              reads=[(b["m1"], 3, 4)], writes=[(b["m1"], 3, 4)])
        ek.op("vector", lambda v: v.reciprocal(b["m1"][:, 3:4], b["m1"][:, 3:4]),
              reads=[(b["m1"], 3, 4)], writes=[(b["m1"], 3, 4)])
        ek.op("vector", lambda v: v.tensor_single_scalar(b["l_ms"].ap, b["l_sb"].ap, b["m1"][:, 1:2], ALU.is_ge),
              reads=[b["l_sb"], (b["m1"], 1, 2)], writes=[b["l_ms"]])
        ek.op("vector", lambda v: v.tensor_mul(b["el"].ap, b["el"].ap, b["l_ms"].ap),
              reads=[b["el"], b["l_ms"]], writes=[b["el"]])
        ek.op("vector", lambda v: v.tensor_scalar_mul(b["cw16"].ap, b["el"].ap, b["m1"][:, 3:4]),
              reads=[b["el"], (b["m1"], 3, 4)], writes=[b["cw16"]])
        for e in range(8):
            pt_ap, pt_reg = ptb(ctx, 0, e * 128, (e + 1) * 128)
            ek.op("tensor", lambda t, e=e, pt_ap=pt_ap: t.transpose(
                pt_ap[0:1, :], b["cw16"][:, e:e + 1], c["ident"].ap),
                reads=[(b["cw16"], e, e + 1), c["ident"]], writes=[pt_reg])
        for e in range(8):
            pt_ap = ctx.pt[0:1, e * 128:(e + 1) * 128]
            ek.op("vector", lambda v, e=e, pt_ap=pt_ap, tci=tci: v.tensor_copy(
                b["cwrow"][0:1, e * 1024 + tci * 128: e * 1024 + (tci + 1) * 128], pt_ap),
                reads=[(ctx.pt, e * 128, (e + 1) * 128)],
                writes=[(b["cwrow"], e * 1024 + tci * 128, e * 1024 + (tci + 1) * 128)])


def emit_phase_b(ctx, b, x_prev_name, x_next_name, stash_chunks=None):
    ek, c, T = ctx.ek, ctx.c, ctx.T
    xn = b["xn"]
    stash_chunks = list(stash_chunks or [])
    n_iter = (T // 1024) * 9
    per_iter = -(-len(stash_chunks) // n_iter) if stash_chunks else 0
    si = 0
    ek.dma("gpsimd", lambda g: g.dma_start(b["rtmp"][0:8, :], ctx.dram["L0_ffn_router_w"][:, :]),
           reads=[dreg("L0_ffn_router_w")], writes=[b["rtmp"]], slot="rwT")
    for dc in range(8):
        pt_ap, pt_reg = ptb(ctx, 1, 0, 8)
        ek.op("tensor", lambda t, dc=dc, pt_ap=pt_ap: t.transpose(
            pt_ap, b["rtmp"][0:8, dc * 128:(dc + 1) * 128], c["ident"][0:8, 0:8]),
            reads=[b["rtmp"], c["ident"]], writes=[pt_reg])
        ek.op("vector", lambda v, dc=dc, pt_ap=pt_ap: v.tensor_copy(b["rwT"][:, dc * 8:(dc + 1) * 8], pt_ap),
              reads=[pt_reg], writes=[(b["rwT"], dc * 8, (dc + 1) * 8)])
    ek.dma("sync", lambda sy: sy.dma_start(b["bias_row"][0:1, :], ctx.dram["L0_ffn_experts_bias"][None, :]),
           reads=[dreg("L0_ffn_experts_bias")], writes=[b["bias_row"]], slot="br")
    bb_ap, bb_reg = pfb(ctx, 5, 0, 8)
    ek.group("tensor", lambda t: [t.matmul(bb_ap[:, 0:8], c["ones_colf"].ap, b["bias_row"][0:1, :],
                                           start=True, stop=True)],
             reads=[c["ones_colf"], b["bias_row"]], writes=[bb_reg])
    ek.op("vector", lambda v: v.tensor_copy(b["bias_b"].ap, bb_ap[:, 0:8]), reads=[bb_reg], writes=[b["bias_b"]])
    for tt0 in range(0, T, 1024):
        te = min(1024, T - tt0)
        emit_phase_b_routing(ctx, b, tt0, te)
        for e in range(9):
            for _ in range(per_iter):
                if si < len(stash_chunks):
                    stash_chunks[si]()
                    si += 1
            shared = (e == 8)
            w1_name = "L0_ffn_sw1" if shared else "L0_ffn_w1"
            w3_name = "L0_ffn_sw3" if shared else "L0_ffn_w3"
            w2_name = "L0_ffn_sw2" if shared else "L0_ffn_w2"
            ei = 0 if shared else e
            if not shared:
                for q in range(te // 512):
                    cb_ap, cb_reg = pfb(ctx, 5, 0, 512)
                    ek.group("tensor", lambda t, e=e, q=q, cb_ap=cb_ap: [
                        t.matmul(cb_ap, c["ones_col"].ap,
                                 b["cwrow"][0:1, e * 1024 + q * 512: e * 1024 + (q + 1) * 512],
                                 start=True, stop=True)],
                        reads=[c["ones_col"], (b["cwrow"], e * 1024 + q * 512, e * 1024 + (q + 1) * 512)],
                        writes=[cb_reg])
                    ek.op("vector", lambda v, q=q, cb_ap=cb_ap: v.tensor_copy(
                        b["cwb"][:, q * 512:(q + 1) * 512], cb_ap),
                        reads=[cb_reg], writes=[(b["cwb"], q * 512, (q + 1) * 512)])
            for hh in range(2):
                for hc8 in range(8):
                    hc = hh * 8 + hc8
                    half = hc8 % 2
                    sei = 8 if shared else e
                    for wbuf, wname, bfn in ((b["w1s"], w1_name, "w1bf"), (b["w3s"], w3_name, "w3bf")):
                        dst = wbuf.h[:, half * 1024:(half + 1) * 1024].rearrange("p (dc n) -> p dc n", dc=8)
                        srcbf = ctx.dram[bfn][sei].rearrange("(dc p) n -> p dc n", p=128)[:, :, hc * 128:(hc + 1) * 128]
                        if tt0 == 0:
                            src = ctx.dram[wname][ei].rearrange("(dc p) n -> p dc n", p=128)[:, :, hc * 128:(hc + 1) * 128]
                            ek.dma("gpsimd", lambda g, src=src, dst=dst: g.dma_start(dst, src),
                                   reads=[dreg(wname, (ei * 16 + hc) * 131072, (ei * 16 + hc + 1) * 131072)],
                                   writes=[(wbuf, half * 1024, (half + 1) * 1024)],
                                   slot=wbuf.h.name + str(half))
                            srcs = wbuf.h[:, half * 1024:(half + 1) * 1024].rearrange("p (dc n) -> p dc n", dc=8)
                            ek.dma("sync", lambda sy, srcbf=srcbf, srcs=srcs: sy.dma_start(srcbf, srcs),
                                   reads=[(wbuf, half * 1024, (half + 1) * 1024)],
                                   writes=[dreg(bfn, (sei * 16 + hc) * 131072, (sei * 16 + hc + 1) * 131072)],
                                   slot="bs" + wbuf.h.name[-2:] + str(half))
                        else:
                            ek.dma("scalar", lambda sc, srcbf=srcbf, dst=dst: sc.dma_start(dst, srcbf),
                                   reads=[dreg(bfn, (sei * 16 + hc) * 131072, (sei * 16 + hc + 1) * 131072)],
                                   writes=[(wbuf, half * 1024, (half + 1) * 1024)],
                                   slot=wbuf.h.name + str(half))
                    for ts_ in range(te // 512):
                        tq0 = tt0 + ts_ * 512
                        g_ap, g_reg = pfb(ctx, 4, 0, 512)
                        u_ap, u_reg = pfb(ctx, 5, 0, 512)
                        for (wbuf, o_ap, o_reg) in ((b["w1s"], g_ap, g_reg), (b["w3s"], u_ap, u_reg)):
                            lhsTs = [wbuf[:, half * 1024 + dc * 128: half * 1024 + (dc + 1) * 128] for dc in range(8)]
                            rhss = [xn[:, dc * T + tq0: dc * T + tq0 + 512] for dc in range(8)]
                            mm_chain(ctx, o_ap, o_reg, lhsTs, rhss,
                                     reads=[(wbuf, half * 1024, (half + 1) * 1024)] +
                                           [(xn, dc * T + tq0, dc * T + tq0 + 512) for dc in range(8)])
                        st = b["stat_tmp"]
                        ek.op("scalar", lambda s, g_ap=g_ap, st=st: s.activation(st[:, 0:512], g_ap, AF.Silu),
                              reads=[g_reg], writes=[(st, 0, 512)])
                        eo_lo = hc8 * 1024 + ts_ * 512
                        if shared:
                            ek.op("vector", lambda v, u_ap=u_ap, eo_lo=eo_lo, st=st: v.scalar_tensor_tensor(
                                b["eo"][:, eo_lo:eo_lo + 512], u_ap, 1.0, st[:, 0:512], ALU.mult, ALU.mult),
                                reads=[u_reg, (st, 0, 512)], writes=[(b["eo"], eo_lo, eo_lo + 512)])
                        else:
                            ek.op("vector", lambda v, u_ap=u_ap, st=st: v.scalar_tensor_tensor(
                                st[:, 512:1024], u_ap, 1.0, st[:, 0:512], ALU.mult, ALU.mult),
                                reads=[u_reg, (st, 0, 512)], writes=[(st, 512, 1024)])
                            ek.op("vector", lambda v, eo_lo=eo_lo, ts_=ts_, st=st: v.tensor_mul(
                                b["eo"][:, eo_lo:eo_lo + 512], st[:, 512:1024], b["cwb"][:, ts_ * 512:(ts_ + 1) * 512]),
                                reads=[(st, 512, 1024), (b["cwb"], ts_ * 512, (ts_ + 1) * 512)],
                                writes=[(b["eo"], eo_lo, eo_lo + 512)])
                sei = 8 if shared else e
                for hc8 in range(8):
                    hc = hh * 8 + hc8
                    dst = b["w2s"].h[:, hc8 * 1024:(hc8 + 1) * 1024].rearrange("p (o n) -> p o n", o=1)
                    srcbf = ctx.dram["w2bf"][sei].rearrange("(hc p) n -> p hc n", p=128)[:, hc:hc + 1, :]
                    if tt0 == 0:
                        src = ctx.dram[w2_name][ei].rearrange("(hc p) n -> p hc n", p=128)[:, hc:hc + 1, :]
                        ek.dma("gpsimd", lambda g, src=src, dst=dst: g.dma_start(dst, src),
                               reads=[dreg(w2_name, (ei * 16 + hc) * 131072, (ei * 16 + hc + 1) * 131072)],
                               writes=[(b["w2s"], hc8 * 1024, (hc8 + 1) * 1024)],
                               slot="w2s" + str(hc8 % 2))
                        srcs = b["w2s"].h[:, hc8 * 1024:(hc8 + 1) * 1024].rearrange("p (o n) -> p o n", o=1)
                        ek.dma("sync", lambda sy, srcbf=srcbf, srcs=srcs: sy.dma_start(srcbf, srcs),
                               reads=[(b["w2s"], hc8 * 1024, (hc8 + 1) * 1024)],
                               writes=[dreg("w2bf", (sei * 16 + hc) * 131072, (sei * 16 + hc + 1) * 131072)],
                               slot="bs2" + str(hc8 % 2))
                    else:
                        ek.dma("scalar", lambda sc, srcbf=srcbf, dst=dst: sc.dma_start(dst, srcbf),
                               reads=[dreg("w2bf", (sei * 16 + hc) * 131072, (sei * 16 + hc + 1) * 131072)],
                               writes=[(b["w2s"], hc8 * 1024, (hc8 + 1) * 1024)],
                               slot="w2s" + str(hc8 % 2))
                for ts2 in range(te // 512):
                    for dc in range(8):
                        y_ap = ctx.pf[:, (dc % 4) * 512: (dc % 4) * 512 + 512]
                        y_reg = (ctx.pf, (dc % 4) * 512, (dc % 4) * 512 + 512)
                        lhsTs = [b["w2s"][:, hc8_ * 1024 + dc * 128: hc8_ * 1024 + (dc + 1) * 128] for hc8_ in range(8)]
                        rhss = [b["eo"][:, hc8_ * 1024 + ts2 * 512: hc8_ * 1024 + ts2 * 512 + 512] for hc8_ in range(8)]
                        mm_chain(ctx, y_ap, y_reg, lhsTs, rhss,
                                 reads=[b["w2s"], b["eo"]])
                        ya_lo = dc * 1024 + ts2 * 512
                        if e == 0 and hh == 0:
                            ek.op("vector", lambda v, y_ap=y_ap, ya_lo=ya_lo: v.tensor_copy(
                                b["y_acc"][:, ya_lo:ya_lo + 512], y_ap),
                                reads=[y_reg], writes=[(b["y_acc"], ya_lo, ya_lo + 512)])
                        else:
                            ek.op("vector", lambda v, y_ap=y_ap, ya_lo=ya_lo: v.tensor_add(
                                b["y_acc"][:, ya_lo:ya_lo + 512], b["y_acc"][:, ya_lo:ya_lo + 512], y_ap),
                                reads=[y_reg, (b["y_acc"], ya_lo, ya_lo + 512)],
                                writes=[(b["y_acc"], ya_lo, ya_lo + 512)])
        for cc4 in range(te // 256):
            t0 = tt0 + cc4 * 256
            emit_residual_tile_cols(ctx, b, t0, 256, "y_acc",
                                    lambda dc, cc4=cc4: dc * 1024 + cc4 * 256,
                                    x_prev_name, x_next_name)
            emit_ln_from_xtile(ctx, b, t0, 256)


def emit_residual_tile_cols(ctx, bufs, t0, W, y_src, ycol, x_prev_name, x_next_name, store=True):
    ek, T = ctx.ek, ctx.T
    xt, ysb = bufs["xtile"], bufs[y_src]
    xprev = ctx.dram[x_prev_name]
    src = xprev.rearrange("(c p) t -> p c t", p=128)[:, :, t0:t0 + W]
    ek.dma("sync", lambda sy: sy.dma_start(xt.h[:, 0:8 * W].rearrange("p (c w) -> p c w", c=8), src),
           reads=[dreg(x_prev_name, t0, t0 + W)], writes=[(xt, 0, 8 * W)], slot="xt")
    for cc in range(8):
        yl = ycol(cc)
        ek.op("vector", lambda v, cc=cc, yl=yl: v.tensor_add(
            xt[:, cc * W:(cc + 1) * W], ysb[:, yl:yl + W], xt[:, cc * W:(cc + 1) * W]),
            reads=[(ysb, yl, yl + W), (xt, cc * W, (cc + 1) * W)],
            writes=[(xt, cc * W, (cc + 1) * W)])
    if store:
        xnext = ctx.dram[x_next_name]
        dst = xnext.rearrange("(c p) t -> p c t", p=128)[:, :, t0:t0 + W]
        ek.dma("sync", lambda sy: sy.dma_start(dst, xt.h[:, 0:8 * W].rearrange("p (c w) -> p c w", c=8)),
               reads=[(xt, 0, 8 * W)], writes=[dreg(x_next_name, t0, t0 + W)], slot="xt")


def alloc_phase_c_bufs(ctx, b, with_repack=False):
    mem, T = ctx.mem, ctx.T
    b["hf"] = mem.sb("hf", 128, 8 * T, BF16)
    b["gma"] = mem.sb("gma", 128, 8, F32)
    b["bta"] = mem.sb("bta", 128, 8, F32)
    if with_repack:
        b["rpk_stg"] = mem.sb("rpk_stg", 128, 2 * 1024, BF16)


def _alloc_c_head_bufs(ctx, b):
    mem, T = ctx.mem, ctx.T
    b["wqh"] = mem.sb("wqh", 128, 512, BF16)
    b["wkh"] = mem.sb("wkh", 128, 512, BF16)
    b["wvh"] = mem.sb("wvh", 128, 512, BF16)
    b["q_f"] = mem.sb("q_f", 65, T, BF16)
    b["k_f"] = mem.sb("k_f", 65, T, BF16)
    b["v_tm"] = mem.sb("v_tm", 128, (T // 128) * 65, BF16)
    b["vstage"] = mem.sb("vstage", 64, 512, BF16)
    b["qrow"] = mem.sb("qrow", 1, T, BF16)
    b["mxk"] = mem.sb("mxk", 1, 2, F32)
    b["PT"] = mem.sb("PT", 128, 2 * 512, BF16)
    b["rs1"] = mem.sb("rs1", 1, 512, F32)
    b["ou_sb"] = mem.sb("ou_sb", 64, 512, F32)
    b["cstrip"] = mem.sb("cstrip", 128, 4 * 512, F32)


def _alloc_c_tail_bufs(ctx, b):
    mem = ctx.mem
    b["wo"] = mem.sb("wo", 128, 8 * 1024, BF16)
    b["yln"] = mem.sb("yln", 128, 8 * 256, F32)
    b["y_sb"] = mem.sb("y_sb_c", 128, 8 * 256, F32)


def emit_phase_c(ctx, b, x_prev_name, x_next_name, repack_chunks=None):
    ek, c, T = ctx.ek, ctx.c, ctx.T
    xn = b["xn"]
    repack_chunks = list(repack_chunks or [])
    per_head = -(-len(repack_chunks) // 16) if repack_chunks else 0
    ri = 0
    ek.dma("gpsimd", lambda g: g.dma_start(
        b["gma"].ap, ctx.dram["L1_attn_gamma"].rearrange("(dc p) -> p dc", p=128),
        allow_slow_non_contiguous=True),
        reads=[dreg("L1_attn_gamma")], writes=[b["gma"]], slot="gma")
    ek.dma("gpsimd", lambda g: g.dma_start(
        b["bta"].ap, ctx.dram["L1_attn_beta"].rearrange("(dc p) -> p dc", p=128),
        allow_slow_non_contiguous=True),
        reads=[dreg("L1_attn_beta")], writes=[b["bta"]], slot="bta")
    ctx.mem.push()
    _alloc_c_head_bufs(ctx, b)
    cs = b["cstrip"]
    ek.op("gpsimd", lambda g: g.iota(cs[:, 3 * 512:4 * 512], [[1, 512]], channel_multiplier=0,
                                     allow_small_or_imprecise_dtypes=True),
          writes=[(cs, 3 * 512, 4 * 512)])
    for j in (0, 1, 2, 3):
        dst = cs[:, j * 512:(j + 1) * 512]
        ek.op("vector", lambda v, dst=dst, j=j: v.tensor_scalar(
            dst, cs[:, 3 * 512:4 * 512], c["iota_p"][:, 0:1], float(128 * j),
            ALU.subtract, ALU.is_ge),
            reads=[(cs, 3 * 512, 4 * 512), (c["iota_p"], 0, 1)],
            writes=[(cs, j * 512, (j + 1) * 512)])
        ek.op("vector", lambda v, dst=dst: v.tensor_single_scalar(dst, dst, 1.0, ALU.subtract),
              reads=[(cs, j * 512, (j + 1) * 512)], writes=[(cs, j * 512, (j + 1) * 512)])
        ek.op("vector", lambda v, dst=dst: v.tensor_single_scalar(dst, dst, 1.0e30, ALU.mult),
              reads=[(cs, j * 512, (j + 1) * 512)], writes=[(cs, j * 512, (j + 1) * 512)])
    ek.op("vector", lambda v: v.memset(b["v_tm"].ap, 1.0), writes=[b["v_tm"]])
    for h in range(16):
        for _ in range(per_head):
            if ri < len(repack_chunks):
                repack_chunks[ri]()
                ri += 1
        _emit_c_head(ctx, b, h)
    ctx.mem.pop()
    _alloc_c_tail_bufs(ctx, b)
    load_weight_cast(ctx, b["wo"], ctx.dram["L1_attn_Wo"], "L1_attn_Wo", 8, 1024)
    for ti in range(T // 256):
        t0 = ti * 256
        if ti == 0 and "c_hf0" in ctx.dram:
            ek.dma("sync", lambda sy: sy.dma_start(ctx.dram["c_hf0"][:, :], b["hf"].h[:, 0:8 * T]),
                   reads=[(b["hf"], 0, 8 * T)], writes=[dreg("c_hf0")], slot="dbg2")
            ek.dma("sync", lambda sy: sy.dma_start(ctx.dram["c_wo0"][:, :], b["wo"].h[:, :]),
                   reads=[b["wo"]], writes=[dreg("c_wo0")], slot="dbg2")
            ek.dma("sync", lambda sy: sy.dma_start(ctx.dram["c_xn0"][:, :], b["xn"].h[:, 0:8 * T]),
                   reads=[(b["xn"], 0, 8 * T)], writes=[dreg("c_xn0")], slot="dbg2")
        for dc in range(8):
            y_ap, y_reg = pfb(ctx, dc // 2, (dc % 2) * 256, (dc % 2) * 256 + 256)
            lhsTs = [b["wo"][:, hec * 1024 + dc * 128: hec * 1024 + (dc + 1) * 128] for hec in range(8)]
            rhss = [b["hf"][:, hec * T + t0: hec * T + t0 + 256] for hec in range(8)]
            mm_chain(ctx, y_ap, y_reg, lhsTs, rhss,
                     reads=[b["wo"], b["hf"]])
            ek.op("vector", lambda v, dc=dc, y_ap=y_ap, t0=t0: v.tensor_add(
                b["y_sb"][:, dc * 256:(dc + 1) * 256], y_ap, xn[:, dc * T + t0: dc * T + t0 + 256]),
                reads=[y_reg, (xn, dc * T + t0, dc * T + t0 + 256)],
                writes=[(b["y_sb"], dc * 256, (dc + 1) * 256)])
        if ti == 0 and "c_ysb0" in ctx.dram:
            ek.dma("sync", lambda sy: sy.dma_start(ctx.dram["c_ysb0"][:, :], b["y_sb"].h[:, 0:8 * 256]),
                   reads=[(b["y_sb"], 0, 8 * 256)], writes=[dreg("c_ysb0")], slot="dbg2")
        _emit_c_inner_ln(ctx, b, 256)
        if ti == 0 and "c_yln0" in ctx.dram:
            ek.dma("sync", lambda sy: sy.dma_start(ctx.dram["c_yln0"][:, :], b["yln"].h[:, 0:8 * 256]),
                   reads=[(b["yln"], 0, 8 * 256)], writes=[dreg("c_yln0")], slot="dbg2")
        emit_residual_tile(ctx, b, t0, 256, "yln", x_prev_name, x_next_name)
        emit_ln_from_xtile(ctx, b, t0, 256)


def _emit_c_inner_ln(ctx, b, W):
    ek, c = ctx.ek, ctx.c
    ysb, tmp, st = b["y_sb"], b["stat_tmp"], b["stat_small"]
    s1_ap, s1_reg = pfb(ctx, 4, 0, W)
    s2_ap, s2_reg = pfb(ctx, 5, 0, W)

    def s1_fns(t):
        return [t.matmul(s1_ap[0:1, :], c["ones128f"].ap, ysb[:, cc * W:(cc + 1) * W],
                         start=(cc == 0), stop=(cc == 7)) for cc in range(8)]

    ek.group("tensor", s1_fns, reads=[(ysb, 0, 8 * W), c["ones128f"]], writes=[s1_reg])
    for half in range(2):
        for i in range(4):
            cc = half * 4 + i
            ek.op("scalar", lambda s, cc=cc, i=i: s.activation(
                tmp[:, i * W:(i + 1) * W], ysb[:, cc * W:(cc + 1) * W], AF.Square),
                reads=[(ysb, cc * W, (cc + 1) * W)], writes=[(tmp, i * W, (i + 1) * W)])

        def s2_fns(t, half=half):
            return [t.matmul(s2_ap[0:1, :], c["ones128f"].ap, tmp[:, i * W:(i + 1) * W],
                             start=(half == 0 and i == 0), stop=(half == 1 and i == 3))
                    for i in range(4)]

        ek.group("tensor", s2_fns, reads=[(tmp, 0, 4 * W), c["ones128f"]], writes=[s2_reg])
    m = st
    ek.op("vector", lambda v: v.tensor_single_scalar(m[0:1, 0:W], s1_ap[0:1, :], 1.0 / 1024.0, ALU.mult),
          reads=[s1_reg], writes=[(st, 0, W)])
    ek.op("vector", lambda v: v.tensor_single_scalar(m[0:1, 3 * W:4 * W], s2_ap[0:1, :], 1.0 / 1024.0, ALU.mult),
          reads=[s2_reg], writes=[(st, 3 * W, 4 * W)])
    ek.op("vector", lambda v: v.tensor_mul(m[0:1, W:2 * W], m[0:1, 0:W], m[0:1, 0:W]),
          reads=[(st, 0, W)], writes=[(st, W, 2 * W)])
    ek.op("vector", lambda v: v.tensor_sub(m[0:1, W:2 * W], m[0:1, 3 * W:4 * W], m[0:1, W:2 * W]),
          reads=[(st, W, 2 * W), (st, 3 * W, 4 * W)], writes=[(st, W, 2 * W)])
    ek.op("scalar", lambda s: s.activation(m[0:1, W:2 * W], m[0:1, W:2 * W], AF.Ln,
                                           bias=c["cpool"][0:1, 2:3], scale=1.0),
          reads=[(st, W, 2 * W), (c["cpool"], 2, 3)], writes=[(st, W, 2 * W)])
    ek.op("scalar", lambda s: s.activation(m[0:1, 2 * W:3 * W], m[0:1, W:2 * W], AF.Exp, scale=-0.5),
          reads=[(st, W, 2 * W)], writes=[(st, 2 * W, 3 * W)])
    mb_ap, mb_reg = pfb(ctx, 4, 0, W)
    rb_ap, rb_reg = pfb(ctx, 5, 0, W)
    ek.group("tensor", lambda t: [t.matmul(mb_ap, c["ones_colf"].ap, m[0:1, 0:W], start=True, stop=True)],
             reads=[(st, 0, W), c["ones_colf"]], writes=[mb_reg])
    ek.group("tensor", lambda t: [t.matmul(rb_ap, c["ones_colf"].ap, m[0:1, 2 * W:3 * W], start=True, stop=True)],
             reads=[(st, 2 * W, 3 * W), c["ones_colf"]], writes=[rb_reg])
    for cc in range(8):
        sl = (cc % 4) * W
        ek.op("vector", lambda v, cc=cc, sl=sl: v.tensor_sub(
            tmp[:, sl:sl + W], ysb[:, cc * W:(cc + 1) * W], mb_ap),
            reads=[(ysb, cc * W, (cc + 1) * W), mb_reg], writes=[(tmp, sl, sl + W)])
        ek.op("vector", lambda v, cc=cc, sl=sl: v.scalar_tensor_tensor(
            tmp[:, sl:sl + W], tmp[:, sl:sl + W], b["gma"][:, cc:cc + 1], rb_ap, ALU.mult, ALU.mult),
            reads=[(tmp, sl, sl + W), b["gma"], rb_reg], writes=[(tmp, sl, sl + W)])
        ek.op("vector", lambda v, cc=cc, sl=sl: v.tensor_scalar_add(
            b["yln"][:, cc * W:(cc + 1) * W], tmp[:, sl:sl + W], b["bta"][:, cc:cc + 1]),
            reads=[(tmp, sl, sl + W), b["bta"]], writes=[(b["yln"], cc * W, (cc + 1) * W)])


def _emit_c_head(ctx, b, h):
    ek, c, T = ctx.ek, ctx.c, ctx.T
    xn = b["xn"]
    hc2 = h // 2
    prow = (h % 2) * 64
    for wbuf, wname in ((b["wqh"], "L1_attn_Wq"), (b["wkh"], "L1_attn_Wk"), (b["wvh"], "L1_attn_Wv")):
        src = ctx.dram[wname][h].rearrange("(dc p) e -> p dc e", p=128)
        dst = wbuf.h[:, 0:512].rearrange("p (dc e) -> p dc e", dc=8)
        ek.dma("gpsimd", lambda g, src=src, dst=dst: g.dma_start(dst, src),
               reads=[dreg(wname, h, h + 1)], writes=[(wbuf, 0, 512)],
               slot=wbuf.h.name)
    for (wbuf, dstbuf, bank, scale) in ((b["wkh"], b["k_f"], 0, 1.0), (b["wqh"], b["q_f"], 1, 0.125)):
        for tt in range(T // 512):
            o_ap, o_reg = pfb(ctx, bank, 0, 512)
            lhsTs = [wbuf[:, dc * 64: (dc + 1) * 64] for dc in range(8)]
            rhss = [xn[:, dc * T + tt * 512: dc * T + (tt + 1) * 512] for dc in range(8)]
            mm_chain(ctx, o_ap[0:64, :], o_reg, lhsTs, rhss,
                     reads=[(wbuf, 0, 512)] +
                           [(xn, dc * T + tt * 512, dc * T + (tt + 1) * 512) for dc in range(8)])
            ek.op("scalar", lambda s, tt=tt, o_ap=o_ap, dstbuf=dstbuf, scale=scale: s.activation(
                dstbuf.h[0:64, tt * 512:(tt + 1) * 512], o_ap[0:64, :], AF.Copy, scale=scale),
                reads=[o_reg], writes=[(dstbuf, tt * 512, (tt + 1) * 512)])
    for tt in range(T // 512):
        o_ap, o_reg = pfb(ctx, 2, 0, 512)
        lhsTs = [b["wvh"][:, dc * 64: (dc + 1) * 64] for dc in range(8)]
        rhss = [xn[:, dc * T + tt * 512: dc * T + (tt + 1) * 512] for dc in range(8)]
        mm_chain(ctx, o_ap[0:64, :], o_reg, lhsTs, rhss,
                 reads=[(b["wvh"], 0, 512)] +
                       [(xn, dc * T + tt * 512, dc * T + (tt + 1) * 512) for dc in range(8)])
        ek.op("scalar", lambda s, o_ap=o_ap: s.copy(b["vstage"][0:64, :], o_ap[0:64, :]),
              reads=[o_reg], writes=[b["vstage"]])
        for j in range(4):
            pt_ap, pt_reg = ptb(ctx, 0, j * 128, j * 128 + 64)
            ek.op("tensor", lambda t, j=j, pt_ap=pt_ap: t.transpose(
                pt_ap, b["vstage"][0:64, j * 128:(j + 1) * 128], c["ident"][0:64, 0:64]),
                reads=[b["vstage"], c["ident"]], writes=[pt_reg])
        for j in range(4):
            tc = tt * 4 + j
            ek.op("vector", lambda v, j=j, tc=tc: v.tensor_copy(
                b["v_tm"][:, tc * 65: tc * 65 + 64], ctx.pt[:, j * 128: j * 128 + 64]),
                reads=[(ctx.pt, j * 128, j * 128 + 64)], writes=[(b["v_tm"], tc * 65, tc * 65 + 64)])
    for tt in range(T // 512):
        ek.op("scalar", lambda s, tt=tt: s.activation(
            b["stat_tmp"].h[0:64, 0:512], b["q_f"][0:64, tt * 512:(tt + 1) * 512], AF.Square),
            reads=[(b["q_f"], tt * 512, (tt + 1) * 512)], writes=[(b["stat_tmp"], 0, 512)])
        qs_ap, qs_reg = pfb(ctx, 4, 0, 512)
        ek.group("tensor", lambda t, qs_ap=qs_ap: [
            t.matmul(qs_ap[0:1, :], c["ones128f"][0:64, :], b["stat_tmp"].h[0:64, 0:512],
                     start=True, stop=True)],
            reads=[(b["stat_tmp"], 0, 512), c["ones128f"]], writes=[qs_reg])
        ek.op("vector", lambda v, tt=tt, qs_ap=qs_ap: v.tensor_copy(
            b["qrow"][0:1, tt * 512:(tt + 1) * 512], qs_ap[0:1, :]),
            reads=[qs_reg], writes=[(b["qrow"], tt * 512, (tt + 1) * 512)])
    ek.op("vector", lambda v: v.memset(b["mxk"][0:1, 0:1], 0.0), writes=[(b["mxk"], 0, 1)])
    for tt in range(T // 512):
        ek.op("scalar", lambda s, tt=tt: s.activation(
            b["stat_tmp"].h[0:64, 0:512], b["k_f"][0:64, tt * 512:(tt + 1) * 512], AF.Square),
            reads=[(b["k_f"], tt * 512, (tt + 1) * 512)], writes=[(b["stat_tmp"], 0, 512)])
        kn_ap, kn_reg = pfb(ctx, 5, 0, 512)
        ek.group("tensor", lambda t, kn_ap=kn_ap: [
            t.matmul(kn_ap[0:1, :], c["ones128f"][0:64, :], b["stat_tmp"].h[0:64, 0:512],
                     start=True, stop=True)],
            reads=[(b["stat_tmp"], 0, 512), c["ones128f"]], writes=[kn_reg])
        ek.op("vector", lambda v, kn_ap=kn_ap: v.reduce_max(
            b["stat_small"][0:1, 0:1], kn_ap[0:1, :], axis=AX.X),
            reads=[kn_reg], writes=[(b["stat_small"], 0, 1)])
        ek.op("vector", lambda v: v.tensor_max(b["mxk"][0:1, 0:1], b["mxk"][0:1, 0:1], b["stat_small"][0:1, 0:1]),
              reads=[(b["mxk"], 0, 1), (b["stat_small"], 0, 1)], writes=[(b["mxk"], 0, 1)])
    ek.op("vector", lambda v: v.tensor_scalar_mul(b["qrow"][0:1, :], b["qrow"][0:1, :], b["mxk"][0:1, 0:1]),
          reads=[(b["qrow"], 0, T), (b["mxk"], 0, 1)], writes=[(b["qrow"], 0, T)])
    ek.op("scalar", lambda s: s.activation(b["qrow"][0:1, :], b["qrow"][0:1, :], AF.Ln),
          reads=[(b["qrow"], 0, T)], writes=[(b["qrow"], 0, T)])
    ek.op("scalar", lambda s: s.activation(b["qrow"][0:1, :], b["qrow"][0:1, :], AF.Exp, scale=0.5),
          reads=[(b["qrow"], 0, T)], writes=[(b["qrow"], 0, T)])
    ek.op("vector", lambda v: v.tensor_single_scalar(b["q_f"].h[64:65, :], b["qrow"][0:1, :], -1.0, ALU.mult),
          reads=[(b["qrow"], 0, T)], writes=[(b["q_f"], 0, T)])
    ek.op("vector", lambda v: v.memset(b["k_f"].h[64:65, :], 1.0), writes=[(b["k_f"], 0, T)])
    for g in range(T // 512):
        _emit_c_group(ctx, b, h, g, hc2, prow)


def _emit_c_group(ctx, b, h, g, hc2, prow):
    ek, c, T = ctx.ek, ctx.c, ctx.T
    nk = (g + 1) * 4
    q0 = g * 512
    ou_ap, ou_reg = pfb(ctx, 2, 0, 512)

    def emit_av(ki, start, stop):
        slot = ki % 2
        ek.op("tensor", lambda t, ki=ki, slot=slot, start=start, stop=stop: t.matmul(
            ou_ap[0:65, :], b["v_tm"][:, ki * 65:(ki + 1) * 65],
            b["PT"][:, slot * 512:(slot + 1) * 512], start=start, stop=stop),
            reads=[(b["v_tm"], ki * 65, (ki + 1) * 65), (b["PT"], slot * 512, (slot + 1) * 512)],
            writes=[ou_reg])

    for ki in range(nk):
        bank = ki % 2
        s_ap, s_reg = pfb(ctx, bank, 0, 512)
        ek.group("tensor", lambda t, ki=ki, s_ap=s_ap: [t.matmul(
            s_ap, b["k_f"].h[0:65, ki * 128:(ki + 1) * 128], b["q_f"].h[0:65, q0:q0 + 512],
            start=True, stop=True)],
            reads=[(b["k_f"], ki * 128, (ki + 1) * 128), (b["q_f"], q0, q0 + 512)],
            writes=[s_reg])
        if ki >= g * 4:
            j = ki - g * 4
            ek.op("vector", lambda v, s_ap=s_ap, j=j: v.tensor_add(
                s_ap, s_ap, b["cstrip"][:, j * 512:(j + 1) * 512]),
                reads=[s_reg, (b["cstrip"], j * 512, (j + 1) * 512)], writes=[s_reg])
        slot = ki % 2
        ek.op("scalar", lambda s, s_ap=s_ap, slot=slot: s.activation(
            b["PT"][:, slot * 512:(slot + 1) * 512], s_ap, AF.Exp),
            reads=[s_reg], writes=[(b["PT"], slot * 512, (slot + 1) * 512)])
        if ki > 0:
            emit_av(ki - 1, ki - 1 == 0, False)
    emit_av(nk - 1, nk == 1, True)
    ek.op("vector", lambda v: v.tensor_copy(b["rs1"][0:1, :], ou_ap[64:65, :]),
          reads=[ou_reg], writes=[b["rs1"]])
    ek.op("vector", lambda v: v.reciprocal(b["rs1"][0:1, :], b["rs1"][0:1, :]),
          reads=[b["rs1"]], writes=[b["rs1"]])
    rb_ap, rb_reg = pfb(ctx, 3, 0, 512)
    ek.group("tensor", lambda t, rb_ap=rb_ap: [
        t.matmul(rb_ap[0:64, :], c["ones_colf"][0:1, 0:64], b["rs1"][0:1, :], start=True, stop=True)],
        reads=[c["ones_colf"], b["rs1"]], writes=[rb_reg])
    ek.op("scalar", lambda s: s.copy(b["ou_sb"][0:64, :], ou_ap[0:64, :]),
          reads=[ou_reg], writes=[b["ou_sb"]])
    hcol = hc2 * T + g * 512
    ek.op("vector", lambda v, hcol=hcol, rb_ap=rb_ap: v.tensor_mul(
        b["hf"].h[prow:prow + 64, hcol:hcol + 512], b["ou_sb"][0:64, :], rb_ap[0:64, :]),
        reads=[b["ou_sb"], rb_reg], writes=[(b["hf"], hcol, hcol + 512)])


def _unused():
    pass


def alloc_phase_d_bufs(ctx, b):
    mem = ctx.mem
    alloc_routing_bufs(ctx, b, "_d", tstride=512)
    b["wealth_b"] = mem.sb("wealth_b", 128, 8, F32)
    b["wealth_row"] = mem.sb("wealth_row", 1, 8, F32)
    b["bids"] = mem.sb("bids", 128, 8 * 8, F32)
    b["wslab"] = mem.sb("wslab", 128, 16384, BF16)
    b["dwb"] = mem.sb("dwb", 128, 16384, BF16)
    b["eoh"] = mem.sb("eoh", 128, 16 * 512, BF16)
    b["yacc_d"] = mem.sb("yacc_d", 128, 8 * 512, F32)


def emit_phase_d_routing(ctx, b, tt0, te):
    ek, c, T = ctx.ek, ctx.c, ctx.T
    xn = b["xn"]
    ntc = te // 128
    for tci in range(ntc):
        tc0 = tt0 + tci * 128
        lg_ap, lg_reg = pfb(ctx, 4, 0, 128)
        lhsTs = [xn[:, dc * T + tc0: dc * T + tc0 + 128] for dc in range(8)]
        rhss = [b["rwT"][:, dc * 8:(dc + 1) * 8] for dc in range(8)]
        mm_chain(ctx, lg_ap[:, 0:8], lg_reg, lhsTs, rhss,
                 reads=[b["rwT"]] + [(xn, dc * T + tc0, dc * T + tc0 + 128) for dc in range(8)])
        ek.op("vector", lambda v, lg_ap=lg_ap, tci=tci: v.tensor_add(
            b["bids"][:, tci * 8:(tci + 1) * 8], lg_ap[:, 0:8], b["bias_b"].ap),
            reads=[lg_reg, b["bias_b"]], writes=[(b["bids"], tci * 8, (tci + 1) * 8)])
        ek.op("scalar", lambda s, tci=tci: s.activation(
            b["bids"][:, tci * 8:(tci + 1) * 8], b["bids"][:, tci * 8:(tci + 1) * 8], AF.Sigmoid),
            reads=[(b["bids"], tci * 8, (tci + 1) * 8)], writes=[(b["bids"], tci * 8, (tci + 1) * 8)])
        ek.op("vector", lambda v, tci=tci: v.tensor_mul(
            b["bids"][:, tci * 8:(tci + 1) * 8], b["bids"][:, tci * 8:(tci + 1) * 8], b["wealth_b"].ap),
            reads=[(b["bids"], tci * 8, (tci + 1) * 8), b["wealth_b"]],
            writes=[(b["bids"], tci * 8, (tci + 1) * 8)])
    for tci in range(ntc):
        bsl = (b["bids"], tci * 8, (tci + 1) * 8)
        bap = b["bids"][:, tci * 8:(tci + 1) * 8]
        ek.op("vector", lambda v, bap=bap: v.reduce_max(b["m1"][:, 0:1], bap, axis=AX.X),
              reads=[bsl], writes=[(b["m1"], 0, 1)])
        ek.op("vector", lambda v, bap=bap: v.tensor_single_scalar(b["msk"].ap, bap, b["m1"][:, 0:1], ALU.is_lt),
              reads=[bsl, (b["m1"], 0, 1)], writes=[b["msk"]])
        ek.op("vector", lambda v: v.memset(b["l_ms"].ap, NEG_BIG), writes=[b["l_ms"]])
        ek.op("vector", lambda v, bap=bap: v.copy_predicated(b["l_ms"].ap, b["msk"].ap, bap),
              reads=[b["msk"], bsl, b["l_ms"]], writes=[b["l_ms"]])
        ek.op("vector", lambda v: v.reduce_max(b["m1"][:, 1:2], b["l_ms"].ap, axis=AX.X),
              reads=[b["l_ms"]], writes=[(b["m1"], 1, 2)])
        ek.op("vector", lambda v: v.tensor_single_scalar(b["m1"][:, 2:3], b["m1"][:, 0:1], -1.0, ALU.mult),
              reads=[(b["m1"], 0, 1)], writes=[(b["m1"], 2, 3)])
        ek.op("scalar", lambda s, bap=bap: s.activation(b["el"].ap, bap, AF.Exp, bias=b["m1"][:, 2:3]),
              reads=[bsl, (b["m1"], 2, 3)], writes=[b["el"]])
        ek.op("vector", lambda v: v.tensor_add(b["m1"][:, 3:4], b["m1"][:, 1:2], b["m1"][:, 2:3]),
              reads=[(b["m1"], 1, 3)], writes=[(b["m1"], 3, 4)])
        ek.op("scalar", lambda s: s.activation(b["m1"][:, 3:4], b["m1"][:, 3:4], AF.Exp),
              reads=[(b["m1"], 3, 4)], writes=[(b["m1"], 3, 4)])
        ek.op("vector", lambda v: v.tensor_single_scalar(b["m1"][:, 3:4], b["m1"][:, 3:4], 1.0, ALU.add),
              reads=[(b["m1"], 3, 4)], writes=[(b["m1"], 3, 4)])
        ek.op("vector", lambda v: v.reciprocal(b["m1"][:, 3:4], b["m1"][:, 3:4]),
              reads=[(b["m1"], 3, 4)], writes=[(b["m1"], 3, 4)])
        ek.op("vector", lambda v, bap=bap: v.tensor_single_scalar(b["l_ms"].ap, bap, b["m1"][:, 1:2], ALU.is_ge),
              reads=[bsl, (b["m1"], 1, 2)], writes=[b["l_ms"]])
        ek.op("vector", lambda v: v.tensor_mul(b["el"].ap, b["el"].ap, b["l_ms"].ap),
              reads=[b["el"], b["l_ms"]], writes=[b["el"]])
        ek.op("vector", lambda v: v.tensor_scalar_mul(b["cw16"].ap, b["el"].ap, b["m1"][:, 3:4]),
              reads=[b["el"], (b["m1"], 3, 4)], writes=[b["cw16"]])
        for e in range(8):
            pt_ap, pt_reg = ptb(ctx, 0, e * 128, (e + 1) * 128)
            ek.op("tensor", lambda t, e=e, pt_ap=pt_ap: t.transpose(
                pt_ap[0:1, :], b["cw16"][:, e:e + 1], c["ident"].ap),
                reads=[(b["cw16"], e, e + 1), c["ident"]], writes=[pt_reg])
        for e in range(8):
            pt_ap = ctx.pt[0:1, e * 128:(e + 1) * 128]
            ek.op("vector", lambda v, e=e, pt_ap=pt_ap, tci=tci: v.tensor_copy(
                b["cwrow"][0:1, e * 512 + tci * 128: e * 512 + (tci + 1) * 128], pt_ap),
                reads=[(ctx.pt, e * 128, (e + 1) * 128)],
                writes=[(b["cwrow"], e * 512 + tci * 128, e * 512 + (tci + 1) * 128)])


def emit_phase_d(ctx, b, x_prev_name, x_next_name):
    ek, c, T = ctx.ek, ctx.c, ctx.T
    xn = b["xn"]
    ek.dma("gpsimd", lambda g: g.dma_start(b["rtmp"][0:8, :], ctx.dram["L1_ffn_conf_w"][:, :]),
           reads=[dreg("L1_ffn_conf_w")], writes=[b["rtmp"]], slot="rwT")
    for dc in range(8):
        pt_ap, pt_reg = ptb(ctx, 1, 0, 8)
        ek.op("tensor", lambda t, dc=dc, pt_ap=pt_ap: t.transpose(
            pt_ap, b["rtmp"][0:8, dc * 128:(dc + 1) * 128], c["ident"][0:8, 0:8]),
            reads=[b["rtmp"], c["ident"]], writes=[pt_reg])
        ek.op("vector", lambda v, dc=dc, pt_ap=pt_ap: v.tensor_copy(b["rwT"][:, dc * 8:(dc + 1) * 8], pt_ap),
              reads=[pt_reg], writes=[(b["rwT"], dc * 8, (dc + 1) * 8)])
    ek.dma("sync", lambda sy: sy.dma_start(b["bias_row"][0:1, :], ctx.dram["L1_ffn_conf_b"][None, :]),
           reads=[dreg("L1_ffn_conf_b")], writes=[b["bias_row"]], slot="br")
    ek.dma("sync", lambda sy: sy.dma_start(b["wealth_row"][0:1, :], ctx.dram["L1_ffn_wealth"][None, :]),
           reads=[dreg("L1_ffn_wealth")], writes=[b["wealth_row"]], slot="br")
    for (row, dst) in ((b["bias_row"], b["bias_b"]), (b["wealth_row"], b["wealth_b"])):
        bb_ap, bb_reg = pfb(ctx, 5, 0, 8)
        ek.group("tensor", lambda t, row=row, bb_ap=bb_ap: [
            t.matmul(bb_ap[:, 0:8], c["ones_colf"].ap, row[0:1, :], start=True, stop=True)],
            reads=[c["ones_colf"], row], writes=[bb_reg])
        ek.op("vector", lambda v, dst=dst, bb_ap=bb_ap: v.tensor_copy(dst.ap, bb_ap[:, 0:8]),
              reads=[bb_reg], writes=[dst])

    for tt0 in range(0, T, 512):
        te = min(512, T - tt0)
        emit_phase_d_routing(ctx, b, tt0, te)
        for e in range(8):
            for q in range(te // 512):
                cb_ap, cb_reg = pfb(ctx, 5, 0, 512)
                ek.group("tensor", lambda t, e=e, q=q, cb_ap=cb_ap: [
                    t.matmul(cb_ap, c["ones_col"].ap,
                             b["cwrow"][0:1, e * 512 + q * 512: e * 512 + (q + 1) * 512],
                             start=True, stop=True)],
                    reads=[c["ones_col"], (b["cwrow"], e * 512 + q * 512, e * 512 + (q + 1) * 512)],
                    writes=[cb_reg])
                ek.op("vector", lambda v, q=q, cb_ap=cb_ap: v.tensor_copy(
                    b["cwb"][:, q * 512:(q + 1) * 512], cb_ap),
                    reads=[cb_reg], writes=[(b["cwb"], q * 512, (q + 1) * 512)])
            for ihalf in range(2):
                _emit_d_half(ctx, b, tt0, te, e, ihalf)
        for cc4 in range(te // 256):
            t0 = tt0 + cc4 * 256
            emit_residual_tile_cols(ctx, b, t0, 256, "yacc_d",
                                    lambda dc, cc4=cc4: dc * 512 + cc4 * 256,
                                    x_prev_name, x_next_name)
            emit_ln_from_xtile(ctx, b, t0, 256)


def _emit_d_half(ctx, b, tt0, te, e, ihalf):
    ek, c, T = ctx.ek, ctx.c, ctx.T
    xn = b["xn"]
    i0 = ihalf * 2048
    dlo = (e * 2 + ihalf) * 128 * 16384
    ek.dma("gpsimd", lambda g: g.dma_start(b["dwb"].ap, ctx.dram["dwT"][e, ihalf]),
           reads=[dreg("dwT", dlo, dlo + 128 * 16384)],
           writes=[b["dwb"]], slot="dwb")
    for icg in range(4):
        half = icg % 2
        wcols = 8192 * half
        icg_g = ihalf * 4 + icg
        for wi, dT in ((0, "gwT"), (1, "uwT")):
            col = wcols + wi * 4096
            rlo = (e * 8 + icg_g) * 128 * 4096
            ek.dma("sync", lambda sy, dT=dT, icg_g=icg_g, col=col: sy.dma_start(
                b["wslab"].h[:, col:col + 4096], ctx.dram[dT][e, icg_g]),
                reads=[dreg(dT, rlo, rlo + 128 * 4096)],
                writes=[(b["wslab"], col, col + 4096)], slot="ws" + str(half) + str(wi))
        for ic in range(4):
            for ts_ in range(te // 512):
                tq0 = tt0 + ts_ * 512
                g_ap, g_reg = pfb(ctx, 4, 0, 512)
                u_ap, u_reg = pfb(ctx, 5, 0, 512)
                for wi, (o_ap, o_reg) in ((0, (g_ap, g_reg)), (1, (u_ap, u_reg))):
                    lhsTs = [b["wslab"][:, wcols + wi * 4096 + hc * 512 + ic * 128:
                                        wcols + wi * 4096 + hc * 512 + ic * 128 + 128] for hc in range(8)]
                    rhss = [xn[:, hc * T + tq0: hc * T + tq0 + 512] for hc in range(8)]
                    mm_chain(ctx, o_ap, o_reg, lhsTs, rhss,
                             reads=[(b["wslab"], wcols + wi * 4096, wcols + (wi + 1) * 4096)] +
                                   [(xn, hc * T + tq0, hc * T + tq0 + 512) for hc in range(8)])
                st = b["stat_tmp"]
                ek.op("scalar", lambda s, g_ap=g_ap, st=st: s.activation(st[:, 512:1024], g_ap, AF.Silu),
                      reads=[g_reg], writes=[(st, 512, 1024)])
                eo_lo = (icg * 4 + ic) * 512 + ts_ * 512
                ek.op("vector", lambda v, u_ap=u_ap, st=st: v.scalar_tensor_tensor(
                    st[:, 0:512], u_ap, 1.0, st[:, 512:1024], ALU.mult, ALU.mult),
                    reads=[u_reg, (st, 512, 1024)], writes=[(st, 0, 512)])
                ek.op("vector", lambda v, eo_lo=eo_lo, ts_=ts_: v.tensor_mul(
                    b["eoh"][:, eo_lo:eo_lo + 512], b["stat_tmp"][:, 0:512],
                    b["cwb"][:, ts_ * 512:(ts_ + 1) * 512]),
                    reads=[(b["stat_tmp"], 0, 512), (b["cwb"], ts_ * 512, (ts_ + 1) * 512)],
                    writes=[(b["eoh"], eo_lo, eo_lo + 512)])
    for ts_ in range(te // 512):
        tq0s = ts_ * 512
        for dc in range(8):
            y_ap = ctx.pf[:, (dc % 4) * 512: (dc % 4) * 512 + 512]
            y_reg = (ctx.pf, (dc % 4) * 512, (dc % 4) * 512 + 512)
            lhsTs = [b["dwb"][:, ic * 1024 + dc * 128: ic * 1024 + (dc + 1) * 128] for ic in range(16)]
            rhss = [b["eoh"][:, ic * 512 + tq0s: ic * 512 + tq0s + 512] for ic in range(16)]
            mm_chain(ctx, y_ap, y_reg, lhsTs, rhss,
                     reads=[b["dwb"], b["eoh"]])
            ya_lo = dc * 512 + tq0s
            fresh = (e == 0 and ihalf == 0)
            if fresh:
                ek.op("vector", lambda v, y_ap=y_ap, ya_lo=ya_lo: v.tensor_copy(
                    b["yacc_d"][:, ya_lo:ya_lo + 512], y_ap),
                    reads=[y_reg], writes=[(b["yacc_d"], ya_lo, ya_lo + 512)])
            else:
                ek.op("vector", lambda v, y_ap=y_ap, ya_lo=ya_lo: v.tensor_add(
                    b["yacc_d"][:, ya_lo:ya_lo + 512], b["yacc_d"][:, ya_lo:ya_lo + 512], y_ap),
                    reads=[y_reg, (b["yacc_d"], ya_lo, ya_lo + 512)],
                    writes=[(b["yacc_d"], ya_lo, ya_lo + 512)])


INPUT_SHAPES = {
    "L0_attn_Wq": [1024, 512], "L0_attn_Wk": [1024, 512], "L0_attn_Wv": [1024, 1024],
    "L0_attn_Wg": [1024, 1024], "L0_attn_Wgk1": [1024, 16], "L0_attn_Wgk2": [16, 512],
    "L0_attn_bgk2": [512], "L0_attn_Wout": [1024, 1024], "L0_attn_rms_w": [256],
    "L0_ffn_router_w": [8, 1024], "L0_ffn_experts_bias": [8],
    "L0_ffn_w1": [8, 1024, 2048], "L0_ffn_w3": [8, 1024, 2048], "L0_ffn_w2": [8, 2048, 1024],
    "L0_ffn_sw1": [1, 1024, 2048], "L0_ffn_sw3": [1, 1024, 2048], "L0_ffn_sw2": [1, 2048, 1024],
    "L1_attn_Wq": [16, 1024, 64], "L1_attn_Wk": [16, 1024, 64], "L1_attn_Wv": [16, 1024, 64],
    "L1_attn_Wo": [1024, 1024], "L1_attn_gamma": [1024], "L1_attn_beta": [1024],
    "L1_ffn_conf_w": [8, 1024], "L1_ffn_conf_b": [8],
    "L1_ffn_gate_w": [8, 4096, 1024], "L1_ffn_up_w": [8, 4096, 1024],
    "L1_ffn_down_w": [8, 1024, 4096], "L1_ffn_wealth": [8],
}


def build_full(T, phases="ABCD", nc=None):
    if nc is None:
        nc = bass.Bass("TRN2", target_bir_lowering=False)
    ctx = Ctx(nc, T)
    ctx.dt("x_in", [1024, T], F32, kind="ExternalInput")
    ctx.dt("x_out", [1024, T], F32, kind="ExternalOutput")
    for n, sh in INPUT_SHAPES.items():
        ctx.dt(n, sh, F32, kind="ExternalInput")
    ctx.dt("x1", [1024, T], F32)
    ctx.dt("x2", [1024, T], F32)
    ctx.dt("x3", [1024, T], F32)
    ctx.dt("gwbf", [8, 4096, 1024], BF16)
    ctx.dt("uwbf", [8, 4096, 1024], BF16)
    ctx.dt("dwbf", [8, 1024, 4096], BF16)
    ctx.dt("w1bf", [9, 1024, 2048], BF16)
    ctx.dt("w3bf", [9, 1024, 2048], BF16)
    ctx.dt("w2bf", [9, 2048, 1024], BF16)
    ctx.dt("gwT", [8, 8, 128, 4096], BF16)
    ctx.dt("uwT", [8, 8, 128, 4096], BF16)
    ctx.dt("dwT", [8, 2, 128, 16384], BF16)
    emit_consts(ctx)
    b = alloc_shared_bufs(ctx)
    mem = ctx.mem
    peaks = {}
    xs = ["x_in"]
    for ph in phases[:-1]:
        xs.append({"A": "x1", "B": "x2", "C": "x3"}[ph])
    xs.append("x_out")

    i = 0
    if "A" in phases:
        mem.push()
        alloc_phase_a_bufs(ctx, b)
        peaks["A"] = mem.cursor
        pb = dict(b)
        emit_prologue(ctx, pb, xs[i])
        emit_phase_a(ctx, pb, xs[i], xs[i + 1])
        mem.pop()
        i += 1
    need_d = "D" in phases
    if "B" in phases:
        mem.push()
        alloc_phase_b_bufs(ctx, b, with_stash=need_d)
        peaks["B"] = mem.cursor
        pb = dict(b)
        sc = make_stash_chunks(ctx, pb) if need_d else None
        if i == 0:
            emit_prologue(ctx, pb, xs[i])
        emit_phase_b(ctx, pb, xs[i], xs[i + 1], stash_chunks=sc)
        mem.pop()
        i += 1
    if "C" in phases:
        mem.push()
        alloc_phase_c_bufs(ctx, b, with_repack=need_d)
        pb = dict(b)
        rc = make_repack_chunks(ctx, pb) if need_d else None
        if i == 0:
            emit_prologue(ctx, pb, xs[i])
        emit_phase_c(ctx, pb, xs[i], xs[i + 1], repack_chunks=rc)
        peaks["C"] = mem.cursor
        mem.pop()
        i += 1
    if "D" in phases:
        mem.push()
        alloc_phase_d_bufs(ctx, b)
        pb = dict(b)
        if i == 0:
            emit_prologue(ctx, pb, xs[i])
        if "B" not in phases:
            pb["stash_stg"] = mem.sb("stash_stg", 128, 2 * 1024, BF16)
            for chk in make_stash_chunks(ctx, pb):
                chk()
        if "C" not in phases:
            pb["rpk_stg"] = mem.sb("rpk_stg", 128, 2 * 4096, BF16)
            for chk in make_repack_chunks(ctx, pb):
                chk()
        peaks["D"] = mem.cursor
        emit_phase_d(ctx, pb, xs[i], xs[i + 1])
        mem.pop()
        i += 1
    ctx.ek.finish("sync")
    ctx.ek.build_and_emit()
    mem.check()
    return nc, ctx, peaks


_T = 4096
_NCORES = 8


def kernel(**inputs):
    from concourse.bass_utils import run_bass_kernel_spmd

    x = np.ascontiguousarray(inputs["L0_attn_x"], dtype=np.float32)
    assert x.shape == (_NCORES, _T, 1024)
    weights = {n: np.ascontiguousarray(inputs[n], dtype=np.float32)
               for n in INPUT_SHAPES}

    nc, ctx, _peaks = build_full(_T)
    in_maps = []
    for i in range(_NCORES):
        m = {"x_in": np.ascontiguousarray(x[i].T)}
        m.update(weights)
        in_maps.append(m)
    res = run_bass_kernel_spmd(nc, in_maps, core_ids=list(range(_NCORES)))
    out = np.stack([np.asarray(res.results[i]["x_out"]).T for i in range(_NCORES)], axis=0)
    return out.astype(np.float32)
